# version 07
# baseline (speedup 1.0000x reference)
import numpy as np
import ml_dtypes
from contextlib import ExitStack
import concourse.bass as bass
import concourse.mybir as mybir
from concourse.bass_utils import run_bass_kernel_spmd

F32 = mybir.dt.float32
BF16 = mybir.dt.bfloat16
AF = mybir.ActivationFunctionType
ALU = mybir.AluOpType
AXX = mybir.AxisListType.X

S = 4096
D = 4096
HKV = 8
DH = 128
KV = 1024
TOWN = 2048
NCH = 32
NT = 16
EPS = 1e-5
RSD = float(1.0 / np.sqrt(128.0))
P40 = float(2.0 ** 40)

SEM_NAMES = ["gl", "so", "pe", "dv", "sc", "gp",
             "sx0", "sx1", "sw0", "sw1", "sy0", "sy1",
             "st0", "st1", "gy0", "gy1",
             "gk0", "gk1", "gq0", "gq1", "gv0", "gv1"]


class Prog:

    def __init__(self, nc, ctx):
        self.nc = nc
        self.ctx = ctx
        self.sems = {}
        self.cnt = {}
        for name in SEM_NAMES:
            self._mk(name)

    def _mk(self, name):
        self.sems[name] = self.ctx.enter_context(self.nc.semaphore(name))
        self.cnt[name] = 0

    def need(self, name):
        if name not in self.sems:
            self._mk(name)

    def inc(self, instr, name, by=1):
        self.cnt[name] += by
        instr.then_inc(self.sems[name], by)
        return self.cnt[name]

    def dma(self, eng, outap, inap, name, **kw):
        ins = eng.dma_start(out=outap, in_=inap, **kw)
        return self.inc(ins, name, 16)

    def w(self, eng, name, val):
        assert val <= self.cnt[name], f"wait {name}>={val} emitted={self.cnt[name]}"
        if val > 0:
            eng.wait_ge(self.sems[name], val)

    def v(self, name):
        return self.cnt[name]

    def snap(self):
        return dict(self.cnt)

    def wall(self, eng, snapshot):
        for name, val in snapshot.items():
            self.w(eng, name, val)


def build_nc(debug=False, phases=7, n_tiles=32, n_groups=8):
    n_groups_own = max(1, n_groups // 2)
    nc = bass.Bass("TRN2", target_bir_lowering=False)

    xa = nc.dram_tensor("xa", [S, D], F32, kind="ExternalInput").ap()
    wq = nc.dram_tensor("wq", [D, D], F32, kind="ExternalInput").ap()
    wk = nc.dram_tensor("wk", [D, KV], F32, kind="ExternalInput").ap()
    wv = nc.dram_tensor("wv", [D, KV], F32, kind="ExternalInput").ap()
    wo = nc.dram_tensor("wo", [KV, D], F32, kind="ExternalInput").ap()
    Wb = nc.dram_tensor("Wb", [D, D], F32, kind="ExternalInput").ap()
    bb = nc.dram_tensor("bb", [1, D], F32, kind="ExternalInput").ap()
    Wr = nc.dram_tensor("Wr", [HKV, D], F32, kind="ExternalInput").ap()
    Ast = nc.dram_tensor("Ast", [8, 16, D], F32, kind="ExternalInput").ap()
    Bst = nc.dram_tensor("Bst", [8, D, 16], F32, kind="ExternalInput").ap()
    out = nc.dram_tensor("out", [TOWN, D], F32, kind="ExternalOutput").ap()

    ident_bf_d = nc.inline_tensor(np.eye(128, dtype=ml_dtypes.bfloat16), name="ident_bf").ap()
    ident_f_d = nc.inline_tensor(np.eye(128, dtype=np.float32), name="ident_f").ap()

    skind = "ExternalOutput" if debug else "Internal"
    y1t_sp = nc.dram_tensor("y1t_sp", [32, 128, D], BF16, kind=skind).ap()
    q_sp = nc.dram_tensor("q_sp", [HKV, 128, TOWN], BF16, kind=skind).ap()
    k_sp = nc.dram_tensor("k_sp", [HKV, 128, S], BF16, kind=skind).ap()
    v_sp = nc.dram_tensor("v_sp", [HKV, NCH, 128, 128], BF16, kind=skind).ap()
    x2_sp = nc.dram_tensor("x2_sp", [NT, 128, D], F32, kind=skind).ap()
    y2_sp = nc.dram_tensor("y2_sp", [NT, 128, D], BF16, kind=skind).ap()
    f_sp = nc.dram_tensor("f_sp", [NT, 128, D], BF16, kind=skind).ap()
    vt_dbg = None
    if debug:
        vt_dbg = nc.dram_tensor("vt_dbg", [128, HKV * TOWN], BF16, kind="ExternalOutput").ap()

    ctx = ExitStack()
    P = Prog(nc, ctx)

    ident = ctx.enter_context(nc.sbuf_tensor("ident", [128, 128], BF16))
    identf = ctx.enter_context(nc.sbuf_tensor("identf", [128, 128], F32))
    ones_r = ctx.enter_context(nc.sbuf_tensor("ones_r", [1, 128], BF16))
    ones_rf = ctx.enter_context(nc.sbuf_tensor("ones_rf", [1, 128], F32))
    onesc = ctx.enter_context(nc.sbuf_tensor("onesc", [128, 1], BF16))
    stat = ctx.enter_context(nc.sbuf_tensor("stat", [128, 64], F32))
    st1 = ctx.enter_context(nc.sbuf_tensor("st1", [1, 16], F32))
    st1b = ctx.enter_context(nc.sbuf_tensor("st1b", [1, 16], BF16))

    block = ctx.enter_context(nc.Block())

    with ctx:
        @block.gpsimd
        def _(g):
            P.dma(g, ident[:], ident_bf_d, "gl")
            P.dma(g, identf[:], ident_f_d, "gl")

        @block.vector
        def _(v):
            v.memset(ones_r[:], 1.0)
            v.memset(ones_rf[:], 1.0)
            P.inc(v.memset(onesc[:], 1.0), "dv")

        p2rctx = ExitStack()
        wk_h = p2rctx.enter_context(nc.sbuf_tensor("wk_h", [128, NCH * 512], BF16, side="right"))
        wv_h = p2rctx.enter_context(nc.sbuf_tensor("wv_h", [128, NCH * 512], BF16, side="right"))
        y1g = [p2rctx.enter_context(nc.sbuf_tensor(f"y1g{i}", [128, NCH * 512], BF16, side="right")) for i in range(2)]
        kstg = [p2rctx.enter_context(nc.sbuf_tensor(f"kstg{i}", [128, 512], BF16, side="right")) for i in range(2)]
        vstg = [p2rctx.enter_context(nc.sbuf_tensor(f"vstg{i}", [128, 512], BF16, side="right")) for i in range(2)]
        ps2a = p2rctx.enter_context(nc.psum_tensor("ps2a", [128, 2048], F32))
        ps2b = p2rctx.enter_context(nc.psum_tensor("ps2b", [128, 1024], F32))

        p1ctx = ExitStack()
        xin = [p1ctx.enter_context(nc.sbuf_tensor(f"xin{i}", [128, D], F32, side="right")) for i in range(2)]
        yb = [p1ctx.enter_context(nc.sbuf_tensor(f"yb{i}", [128, D], BF16, side="right")) for i in range(2)]
        ytr = [p1ctx.enter_context(nc.sbuf_tensor(f"ytr{i}", [128, D], BF16, side="right")) for i in range(2)]

        apply_sc, spill_gs, tpose_tr = [], [], []
        wkv_gl = {}

        def emit_wkv_cast(H, wait_pe=0):
            @block.gpsimd
            def _(g, H=H, wait_pe=wait_pe):
                if wait_pe:
                    P.w(g, "pe", wait_pe)
                src_k = wk[:, H * 512:(H + 1) * 512].rearrange("(c p) f -> p c f", p=128)
                P.dma(g, wk_h[:].rearrange("p (c f) -> p c f", c=NCH), src_k, "gl")
                src_v = wv[:, H * 512:(H + 1) * 512].rearrange("(c p) f -> p c f", p=128)
                P.dma(g, wv_h[:].rearrange("p (c f) -> p c f", c=NCH), src_v, "gl")
            wkv_gl[H] = P.v("gl")

        emit_wkv_cast(0)

        def emit_p1_tile(t):
            slot = t % 2
            sb = slot * 8

            @block.gpsimd
            def _(s, t=t, slot=slot):
                if t >= 2:
                    P.w(s, "sc", apply_sc[t - 2])
                P.dma(s, xin[slot][:], xa[t * 128:(t + 1) * 128, :], "sx%d" % slot)
            sl_now = P.v("sx%d" % slot)

            @block.vector
            def _(v, t=t, slot=slot, sb=sb, sl_now=sl_now):
                P.w(v, "sx%d" % slot, sl_now)
                v.reduce_sum(stat[:, sb: sb + 1], xin[slot][:], axis=AXX)
                v.drain()
                if t >= 2:
                    P.w(v, *tpose_tr[t - 2])
                v.scalar_tensor_tensor(yb[slot][:], xin[slot][:], 1.0, xin[slot][:],
                                       op0=ALU.mult, op1=ALU.mult,
                                       accum_out=stat[:, sb + 1: sb + 2])
                v.drain()
                v.tensor_scalar(stat[:, sb + 2: sb + 3], stat[:, sb: sb + 1], -1.0 / D, None, op0=ALU.mult)
                v.drain()
                v.scalar_tensor_tensor(stat[:, sb + 3: sb + 4], stat[:, sb + 2: sb + 3], -1.0,
                                       stat[:, sb + 2: sb + 3], op0=ALU.mult, op1=ALU.mult)
                v.drain()
                v.scalar_tensor_tensor(stat[:, sb + 3: sb + 4], stat[:, sb + 1: sb + 2], 1.0 / D,
                                       stat[:, sb + 3: sb + 4], op0=ALU.mult, op1=ALU.add)
                v.drain()
                ins = v.tensor_scalar(stat[:, sb + 3: sb + 4], stat[:, sb + 3: sb + 4], EPS, None, op0=ALU.add)
                P.inc(ins, "dv")
            dv_rec = P.v("dv")

            @block.scalar
            def _(s, sb=sb, dv_rec=dv_rec):
                P.w(s, "dv", dv_rec)
                s.activation(stat[:, sb + 4: sb + 5], stat[:, sb + 3: sb + 4], AF.Ln)
                s.drain()
                ins = s.activation(stat[:, sb + 4: sb + 5], stat[:, sb + 4: sb + 5], AF.Exp, scale=-0.5)
                P.inc(ins, "sc")
            sc_r = P.v("sc")

            @block.scalar
            def _(s, t=t, slot=slot, sb=sb, sc_r=sc_r):
                s.drain()
                s.activation(yb[slot][:], xin[slot][:], AF.Identity,
                             bias=stat[:, sb + 2: sb + 3])
                s.drain()
                ins = s.activation(yb[slot][:], yb[slot][:], AF.Identity,
                                   scale=stat[:, sb + 4: sb + 5])
                P.inc(ins, "sc")
            apply_sc.append(P.v("sc"))
            sc_now = P.v("sc")

            @block.sync
            def _(s, t=t, slot=slot, sc_now=sc_now):
                P.w(s, "sc", sc_now)
                if t >= 2:
                    P.w(s, *spill_gs[t - 2])
                srcy = yb[slot][:].rearrange("p (c f) -> p c f", c=NCH)
                dsty = ytr[slot][:].rearrange("p (c f) -> p c f", c=NCH)
                P.inc(s.dma_start_transpose(dsty, srcy), "st%d" % slot, 16)
            tpose_tr.append(("st%d" % slot, P.v("st%d" % slot)))
            tr_now = P.v("st%d" % slot)

            @block.gpsimd
            def _(g, t=t, slot=slot, tr_now=tr_now):
                P.w(g, "st%d" % slot, tr_now)
                P.dma(g, y1t_sp[t], ytr[slot][:], "gy%d" % slot)
            spill_gs.append(("gy%d" % slot, P.v("gy%d" % slot)))

        y1g_sl = {}
        y1g_done_pe = {}
        evac_hist = {"k": [], "q": [], "v": []}
        spill_hist = {"k": [], "q": [], "v": []}
        qkv_spill_snap = {}

        def emit_y1g_load(key, g, gslot, prev_pe):
            @block.gpsimd
            def _(s, g=g, gslot=gslot, prev_pe=prev_pe):
                if prev_pe:
                    P.w(s, "pe", prev_pe)
                P.w(s, *spill_gs[4 * g + 3])
                P.w(s, *spill_gs[4 * g + 2])
                for j in range(4):
                    srcg = y1t_sp[4 * g + j].rearrange("p (c f) -> p c f", c=NCH)
                    dstg = y1g[gslot][:].rearrange("p (c f) -> p c f", c=NCH)[:, :, j * 128:(j + 1) * 128]
                    P.dma(s, dstg, srcg, "sy%d" % gslot)
            y1g_sl[key] = ("sy%d" % gslot, P.v("sy%d" % gslot))

        def emit_k_unit(H, g, ct):
            gslot = g % 2
            n = len(evac_hist["k"])
            pslot = ct % 2

            @block.tensor
            def _(t, H=H, g=g, ct=ct, pslot=pslot, gslot=gslot, n=n):
                P.w(t, *y1g_sl[("kv", H, g)])
                if g == 0 and ct == 0:
                    P.w(t, "gl", wkv_gl[H])
                if n >= 2:
                    P.w(t, "dv", evac_hist["k"][n - 2])
                for ch in range(NCH):
                    ins = t.matmul(ps2a[:, pslot * 512:(pslot + 1) * 512],
                                   wk_h[:, ch * 512 + ct * 128: ch * 512 + (ct + 1) * 128],
                                   y1g[gslot][:, ch * 512:(ch + 1) * 512],
                                   start=(ch == 0), stop=(ch == NCH - 1))
                P.inc(ins, "pe")
            pe_now = P.v("pe")

            @block.vector
            def _(v, pe_now=pe_now, pslot=pslot, n=n):
                P.w(v, "pe", pe_now)
                if n >= 2:
                    P.w(v, *spill_hist["k"][n - 2])
                ins = v.tensor_copy(kstg[n % 2][:], ps2a[:, pslot * 512:(pslot + 1) * 512])
                P.inc(ins, "dv")
            evac_hist["k"].append(P.v("dv"))
            dv_now = P.v("dv")
            gsem = "gk%d" % (n % 2)

            @block.gpsimd
            def _(gp, H=H, g=g, ct=ct, dv_now=dv_now, n=n, gsem=gsem):
                P.w(gp, "dv", dv_now)
                P.dma(gp, k_sp[H * 4 + ct][:, g * 512:(g + 1) * 512], kstg[n % 2][:], gsem)
            spill_hist["k"].append((gsem, P.v(gsem)))

        def emit_v_unit(H, g, tt):
            gslot = g % 2
            n = len(evac_hist["v"])
            pslot = tt % 2

            @block.tensor
            def _(t, H=H, g=g, tt=tt, pslot=pslot, gslot=gslot, n=n):
                if n >= 2:
                    P.w(t, "dv", evac_hist["v"][n - 2])
                for ch in range(NCH):
                    ins = t.matmul(ps2b[:, pslot * 512:(pslot + 1) * 512],
                                   y1g[gslot][:, ch * 512 + tt * 128: ch * 512 + (tt + 1) * 128],
                                   wv_h[:, ch * 512:(ch + 1) * 512],
                                   start=(ch == 0), stop=(ch == NCH - 1))
                P.inc(ins, "pe")
            pe_now = P.v("pe")
            if tt == 3:
                y1g_done_pe[("kv", H, g)] = pe_now

            @block.vector
            def _(v, pe_now=pe_now, pslot=pslot, n=n):
                P.w(v, "pe", pe_now)
                if n >= 2:
                    P.w(v, *spill_hist["v"][n - 2])
                ins = v.tensor_copy(vstg[n % 2][:], ps2b[:, pslot * 512:(pslot + 1) * 512])
                P.inc(ins, "dv")
            evac_hist["v"].append(P.v("dv"))
            dv_now = P.v("dv")
            gsem = "gv%d" % (n % 2)

            @block.gpsimd
            def _(gp, H=H, g=g, tt=tt, dv_now=dv_now, n=n, gsem=gsem):
                P.w(gp, "dv", dv_now)
                for i in range(4):
                    P.dma(gp, v_sp[H * 4 + i, 4 * g + tt], vstg[n % 2][:, i * 128:(i + 1) * 128], gsem)
            spill_hist["v"].append((gsem, P.v(gsem)))

        def emit_kv_units(H, g, units):
            for u in units:
                if u < 4:
                    emit_k_unit(H, g, u)
                else:
                    emit_v_unit(H, g, u - 4)

        for t in range(n_tiles):
            emit_p1_tile(t)
            g = t // 4
            ph = t % 4
            if ph == 3:
                prev_pe = y1g_done_pe.get(("kv", 0, g - 2), 0)
                emit_y1g_load(("kv", 0, g), g, g % 2, prev_pe)
            if g >= 1:
                emit_kv_units(0, g - 1, [2 * ph, 2 * ph + 1])
        emit_kv_units(0, n_groups - 1, list(range(8)))
        last_h0_pe = P.v("pe")
        emit_wkv_cast(1, wait_pe=last_h0_pe)
        for g in range(n_groups):
            prev_pe = y1g_done_pe.get(("kv", 1, g - 2), y1g_done_pe.get(("kv", 0, g + n_groups - 2), 0))
            emit_y1g_load(("kv", 1, g), g, g % 2, prev_pe)
            emit_kv_units(1, g, list(range(8)))

        kv_end = P.snap()
        for hh in range(8):
            qkv_spill_snap[hh] = {nm: kv_end[nm] for nm in ("gk0", "gk1", "gv0", "gv1")}

        p1_end = P.snap()
        p1ctx.close()

        p2qctx = ExitStack()
        wqin = [p2qctx.enter_context(nc.sbuf_tensor(f"wqin{i}", [128, TOWN], F32, side="right")) for i in range(2)]
        wacc = [p2qctx.enter_context(nc.sbuf_tensor(f"wacc{i}", [128, 512], F32, side="right")) for i in range(2)]
        wq_eff = p2qctx.enter_context(nc.sbuf_tensor("wq_eff", [128, NCH * 512], BF16, side="right"))
        qstg = [p2qctx.enter_context(nc.sbuf_tensor(f"qstg{i}", [128, 512], BF16, side="right")) for i in range(2)]

        for engsec in (block.tensor, block.vector, block.scalar, block.sync, block.gpsimd):
            @engsec
            def _(e):
                for nm, val in p1_end.items():
                    P.w(e, nm, val)

        presum_list = []

        def emit_presum_chunk(H, ch, wait_pe=0):
            n = len(presum_list)
            slot = n % 2

            @block.gpsimd
            def _(s, H=H, ch=ch, slot=slot, n=n):
                if n >= 2:
                    P.w(s, "dv", presum_list[n - 2])
                P.dma(s, wqin[slot][:], wq[ch * 128:(ch + 1) * 128, H * TOWN:(H + 1) * TOWN], "sw%d" % slot)
            sl_now = P.v("sw%d" % slot)

            @block.vector
            def _(v, H=H, ch=ch, slot=slot, sl_now=sl_now, wait_pe=wait_pe):
                P.w(v, "sw%d" % slot, sl_now)
                if wait_pe and ch == 0:
                    P.w(v, "pe", wait_pe)
                srcq = wqin[slot][:].rearrange("p (kv g d) -> p kv g d", kv=4, g=4)
                acc = wacc[slot][:].rearrange("p (kv d) -> p kv d", kv=4)
                v.tensor_add(acc, srcq[:, :, 0, :], srcq[:, :, 1, :])
                v.drain()
                v.tensor_add(acc, acc, srcq[:, :, 2, :])
                v.drain()
                v.tensor_add(acc, acc, srcq[:, :, 3, :])
                v.drain()
                ins = v.tensor_scalar(wq_eff[:, ch * 512:(ch + 1) * 512], wacc[slot][:],
                                      RSD, None, op0=ALU.mult)
                P.inc(ins, "dv")
            presum_list.append(P.v("dv"))

        for H in range(2):
            wait_pe = 0
            if H == 1:
                wait_pe = P.v("pe")
            for ch in range(NCH):
                emit_presum_chunk(H, ch, wait_pe=wait_pe)
            presum_done = P.v("dv")

            for g in range(n_groups_own):
                gslot = g % 2
                prev_pe = y1g_done_pe.get(("q", H, g - 2), y1g_done_pe.get(("q", H - 1, g + n_groups_own - 2), 0))
                if H == 0 and g < 2:
                    prev_pe = 0
                emit_y1g_load(("q", H, g), g, gslot, prev_pe or kv_end["pe"])

                for ct in range(4):
                    n = len(evac_hist["q"])
                    pslot = 2 + ct % 2

                    @block.tensor
                    def _(t, H=H, g=g, ct=ct, pslot=pslot, gslot=gslot, n=n, presum_done=presum_done):
                        P.w(t, *y1g_sl[("q", H, g)])
                        if g == 0 and ct == 0:
                            P.w(t, "dv", presum_done)
                        if n >= 2:
                            P.w(t, "dv", evac_hist["q"][n - 2])
                        for ch in range(NCH):
                            ins = t.matmul(ps2a[:, pslot * 512:(pslot + 1) * 512],
                                           wq_eff[:, ch * 512 + ct * 128: ch * 512 + (ct + 1) * 128],
                                           y1g[gslot][:, ch * 512:(ch + 1) * 512],
                                           start=(ch == 0), stop=(ch == NCH - 1))
                        P.inc(ins, "pe")
                    pe_now = P.v("pe")
                    if ct == 3:
                        y1g_done_pe[("q", H, g)] = pe_now

                    @block.vector
                    def _(v, pe_now=pe_now, pslot=pslot, n=n):
                        P.w(v, "pe", pe_now)
                        if n >= 2:
                            P.w(v, *spill_hist["q"][n - 2])
                        ins = v.tensor_copy(qstg[n % 2][:], ps2a[:, pslot * 512:(pslot + 1) * 512])
                        P.inc(ins, "dv")
                    evac_hist["q"].append(P.v("dv"))
                    dv_now = P.v("dv")
                    gsem = "gq%d" % (n % 2)

                    @block.gpsimd
                    def _(gp, H=H, g=g, ct=ct, dv_now=dv_now, n=n, gsem=gsem):
                        P.w(gp, "dv", dv_now)
                        P.dma(gp, q_sp[H * 4 + ct][:, g * 512:(g + 1) * 512], qstg[n % 2][:], gsem)
                    spill_hist["q"].append((gsem, P.v(gsem)))

        q_end = P.snap()
        for hh in range(8):
            qkv_spill_snap[hh] = dict(qkv_spill_snap[hh])
            qkv_spill_snap[hh].update({nm: q_end[nm] for nm in ("gq0", "gq1")})
        p2qctx.close()

        p2_end = P.snap()
        p2rctx.close()

        for engsec in (block.tensor, block.vector, block.scalar, block.sync, block.gpsimd):
            @engsec
            def _(e):
                for nm, val in p2_end.items():
                    P.w(e, nm, val)

        p3vctx = ExitStack()
        valuesT = p3vctx.enter_context(nc.sbuf_tensor("valuesT", [128, HKV * TOWN], BF16, side="right"))
        p3ctx = ExitStack()
        kbuf = [p3ctx.enter_context(nc.sbuf_tensor(f"kbuf{i}", [128, S], BF16, side="right")) for i in range(2)]
        vbuf = [p3ctx.enter_context(nc.sbuf_tensor(f"vbuf{i}", [128, S], BF16, side="right")) for i in range(2)]
        qbuf = [p3ctx.enter_context(nc.sbuf_tensor(f"qbuf{i}", [128, TOWN], BF16, side="right")) for i in range(2)]
        PTb = [p3ctx.enter_context(nc.sbuf_tensor(f"PTb{i}", [128, 4 * S], BF16, side="right")) for i in range(2)]
        recd_sb = p3ctx.enter_context(nc.sbuf_tensor("recd_sb", [1, 512], F32, side="right"))
        mhjunk = p3ctx.enter_context(nc.sbuf_tensor("mhjunk", [128, S], BF16, side="right"))
        bias_h = p3ctx.enter_context(nc.sbuf_tensor("bias_h", [128, 2], F32, side="right"))
        bcast_sb = p3ctx.enter_context(nc.sbuf_tensor("bcast_sb", [128, 1024], F32, side="right"))
        psS = [p3ctx.enter_context(nc.psum_tensor(f"psS{i}", [128, 1024], F32)) for i in range(3)]
        psV3 = p3ctx.enter_context(nc.psum_tensor("psV3", [128, 512], F32))
        psX = p3ctx.enter_context(nc.psum_tensor("psX", [128, 512], F32))

        for nm in ("gld0", "gld1", "sp0", "sp1", "sp2", "sp3"):
            P.need(nm)

        @block.vector
        def _(v):
            P.inc(v.memset(valuesT[:], 0.0), "dv")

        NKEY = n_groups * 512
        NKC = NKEY // 128
        G3 = n_groups_own
        head_done_pe = {}
        pv_dv = {}
        recd_dv = {}
        bcast_dv = {}
        exp_hist_pergroup = {}
        exp_hist = {}
        n_heads = min(HKV, 8)

        for h in range(n_heads):
            hs = h % 2
            gld = "gld%d" % hs
            recd_dv = {}
            bcast_dv = dict(bcast_dv)
            exp_hist_pergroup = {}

            @block.gpsimd
            def _(g, h=h, hs=hs, gld=gld):
                if h >= 2:
                    P.w(g, "pe", head_done_pe[h - 2])
                for nm, val in qkv_spill_snap[h].items():
                    P.w(g, nm, val)
                P.dma(g, kbuf[hs][:, 0:NKEY], k_sp[h][:, 0:NKEY], gld)
                P.dma(g, vbuf[hs][:, 0:NKEY], v_sp[h][0:NKC].rearrange("c p j -> p c j"), gld)
                P.dma(g, qbuf[hs][:, 0:G3 * 512], q_sp[h][:, 0:G3 * 512], gld)
            gld_now = P.v(gld)

            @block.vector
            def _(v, h=h, hs=hs, gld_now=gld_now, gld=gld):
                P.w(v, gld, gld_now)
                if h >= 1:
                    P.w(v, "pe", P.v("pe"))
                ins = v.scalar_tensor_tensor(mhjunk[:, 0:G3 * 512], qbuf[hs][:, 0:G3 * 512], 1.0,
                                             qbuf[hs][:, 0:G3 * 512], op0=ALU.mult, op1=ALU.mult)
                P.inc(ins, "dv")
            sc_q2 = ("dv", P.v("dv"))

            for i in range(G3):
                @block.tensor
                def _(t, i=i, sc_q2=sc_q2):
                    P.w(t, *sc_q2)
                    if i > 0:
                        P.w(t, "dv", P.v("dv"))
                    ins = t.matmul(psX[0:1, :], onesc[:], mhjunk[:, i * 512:(i + 1) * 512],
                                   start=True, stop=True)
                    P.inc(ins, "pe")
                pe_now = P.v("pe")

                @block.vector
                def _(v, h=h, i=i, pe_now=pe_now):
                    P.w(v, "pe", pe_now)
                    v.reduce_max(st1[:, 1:2], psX[0:1, :], axis=AXX)
                    v.drain()
                    if i == 0:
                        ins = v.tensor_copy(st1[:, 2:3], st1[:, 1:2])
                    else:
                        ins = v.tensor_tensor(st1[:, 2:3], st1[:, 2:3], st1[:, 1:2], op=ALU.max)
                    v.drain()
                    P.inc(ins, "dv")

            pe_q2done = P.v("pe")

            @block.vector
            def _(v, h=h, hs=hs, pe_q2done=pe_q2done):
                P.w(v, "pe", pe_q2done)
                ins = v.scalar_tensor_tensor(mhjunk[:, 0:NKEY], kbuf[hs][:, 0:NKEY], 1.0,
                                             kbuf[hs][:, 0:NKEY], op0=ALU.mult, op1=ALU.mult)
                P.inc(ins, "dv")
            sc_k2 = ("dv", P.v("dv"))

            for i in range(NKEY // 512):
                @block.tensor
                def _(t, i=i, sc_k2=sc_k2):
                    P.w(t, *sc_k2)
                    P.w(t, "dv", P.v("dv"))
                    ins = t.matmul(psX[0:1, :], onesc[:], mhjunk[:, i * 512:(i + 1) * 512],
                                   start=True, stop=True)
                    P.inc(ins, "pe")
                pe_now = P.v("pe")

                @block.vector
                def _(v, h=h, i=i, pe_now=pe_now):
                    P.w(v, "pe", pe_now)
                    v.reduce_max(st1[:, 1:2], psX[0:1, :], axis=AXX)
                    v.drain()
                    if i == 0:
                        ins = v.tensor_copy(st1[:, 3:4], st1[:, 1:2])
                    else:
                        ins = v.tensor_tensor(st1[:, 3:4], st1[:, 3:4], st1[:, 1:2], op=ALU.max)
                    v.drain()
                    P.inc(ins, "dv")

            @block.vector
            def _(v):
                ins = v.tensor_mul(st1[:, 4:5], st1[:, 2:3], st1[:, 3:4])
                P.inc(ins, "dv")
            dv_m = P.v("dv")

            @block.scalar
            def _(s, dv_m=dv_m):
                P.w(s, "dv", dv_m)
                s.activation(st1[:, 4:5], st1[:, 4:5], AF.Ln)
                s.drain()
                ins = s.activation(st1[:, 4:5], st1[:, 4:5], AF.Exp, scale=0.5)
                P.inc(ins, "sc")
            sc_m = P.v("sc")

            @block.vector
            def _(v, sc_m=sc_m):
                P.w(v, "sc", sc_m)
                v.tensor_scalar(st1[:, 4:5], st1[:, 4:5], -1.02, -1.0,
                                op0=ALU.mult, op1=ALU.add)
                v.drain()
                ins = v.tensor_copy(st1b[:, 0:1], st1[:, 4:5])
                P.inc(ins, "dv")
            dv_mb = P.v("dv")

            @block.tensor
            def _(t, dv_mb=dv_mb):
                P.w(t, "dv", dv_mb)
                ins = t.matmul(psX[:, 0:1], ones_r[:], st1b[:, 0:1], start=True, stop=True)
                P.inc(ins, "pe")
            pe_b = P.v("pe")

            @block.vector
            def _(v, hs=hs, pe_b=pe_b):
                P.w(v, "pe", pe_b)
                ins = v.tensor_copy(bias_h[:, hs:hs + 1], psX[:, 0:1])
                P.inc(ins, "dv")
            sc_bias = ("dv", P.v("dv"))

            def emit_PV(g, h=h, hs=hs):
                gs3 = g % 2

                @block.tensor
                def _(t, g=g, gs3=gs3):
                    P.w(t, "sc", exp_hist_pergroup[g])
                    prevdv = pv_dv.get((h, g - 1)) if g >= 1 else (pv_dv.get((h - 1, G3 - 1)) if h >= 1 else None)
                    if prevdv is not None:
                        P.w(t, "dv", prevdv)
                    for kc in range(NKC):
                        ins = t.matmul(psV3[:], vbuf[hs][:, kc * 128:(kc + 1) * 128],
                                       PTb[gs3][:, kc * 512:(kc + 1) * 512],
                                       start=(kc == 0), stop=(kc == NKC - 1))
                    P.inc(ins, "pe")
                pe_pv = P.v("pe")

                @block.vector
                def _(v, g=g, gs3=gs3, pe_pv=pe_pv):
                    P.w(v, "pe", pe_pv)
                    P.w(v, *bcast_dv[g])
                    for qt in range(4):
                        ins = v.tensor_mul(valuesT[:, h * TOWN + g * 512 + qt * 128: h * TOWN + g * 512 + (qt + 1) * 128],
                                           psV3[:, qt * 128:(qt + 1) * 128],
                                           bcast_sb[:, gs3 * 512 + qt * 128: gs3 * 512 + (qt + 1) * 128])
                    P.inc(ins, "dv")
                pv_dv[(h, g)] = P.v("dv")

            for g in range(G3):
                gs3 = g % 2
                den_mm = {}

                for kp in range(NKC // 2):
                    sslot = kp % 3

                    @block.tensor
                    def _(t, g=g, kp=kp, sslot=sslot):
                        if (g, kp) == (0, 0):
                            P.w(t, *sc_bias)
                        prev = exp_hist.get(sslot)
                        if prev is not None:
                            P.w(t, "sc", prev)
                        for u in range(2):
                            kc = 2 * kp + u
                            ins = t.matmul(psS[sslot][:, u * 512:(u + 1) * 512],
                                           kbuf[hs][:, kc * 128:(kc + 1) * 128],
                                           qbuf[hs][:, g * 512:(g + 1) * 512],
                                           start=True, stop=True)
                        P.inc(ins, "pe")
                    pe_s = P.v("pe")

                    @block.scalar
                    def _(s, g=g, kp=kp, sslot=sslot, gs3=gs3, pe_s=pe_s):
                        P.w(s, "pe", pe_s)
                        if kp == 0:
                            prevpv = pv_dv.get((h, g - 2)) if g >= 2 else \
                                (pv_dv.get((h - 1, G3 - 2 + g)) if h >= 1 else None)
                            if prevpv is not None:
                                P.w(s, "dv", prevpv)
                        ins = s.activation(PTb[gs3][:, kp * 1024:(kp + 1) * 1024],
                                           psS[sslot][:, 0:1024], AF.Exp,
                                           bias=bias_h[:, hs:hs + 1])
                        P.inc(ins, "sc")
                    exp_hist[sslot] = P.v("sc")

                    @block.tensor
                    def _(t, g=g, kp=kp):
                        P.w(t, "sc", exp_hist[kp % 3])
                        if kp == 0:
                            prevrd = recd_dv.get(g - 1)
                            if prevrd is not None:
                                P.w(t, "dv", prevrd)
                        for u in range(2):
                            kc = 2 * kp + u
                            ins = t.matmul(psX[0:1, :], onesc[:],
                                           PTb[g % 2][:, kc * 512:(kc + 1) * 512],
                                           start=(kc == 0), stop=(kc == NKC - 1))
                        P.inc(ins, "pe")
                pe_den = P.v("pe")

                @block.vector
                def _(v, g=g, pe_den=pe_den):
                    P.w(v, "pe", pe_den)
                    v.tensor_scalar(recd_sb[0:1, 0:512], psX[0:1, :], P40, None, op0=ALU.mult)
                    v.drain()
                    v.reciprocal(recd_sb[0:1, 0:512], recd_sb[0:1, 0:512])
                    v.drain()
                    ins = v.tensor_scalar(recd_sb[0:1, 0:512], recd_sb[0:1, 0:512], P40, None, op0=ALU.mult)
                    P.inc(ins, "dv")
                recd_dv[g] = P.v("dv")
                dv_rd = P.v("dv")

                @block.tensor
                def _(t, g=g, dv_rd=dv_rd):
                    P.w(t, "dv", dv_rd)
                    prevbc = bcast_dv.get(g - 1)
                    if prevbc is not None:
                        P.w(t, *prevbc)
                    ins = t.matmul(psX[:, 0:512], ones_rf[:], recd_sb[0:1, 0:512],
                                   start=True, stop=True)
                    P.inc(ins, "pe")
                pe_bc = P.v("pe")

                @block.vector
                def _(v, g=g, gs3=gs3, pe_bc=pe_bc):
                    P.w(v, "pe", pe_bc)
                    prevpv = pv_dv.get((h, g - 2))
                    if prevpv is not None:
                        P.w(v, "dv", prevpv)
                    ins = v.tensor_copy(bcast_sb[:, gs3 * 512:(gs3 + 1) * 512], psX[:, 0:512])
                    P.inc(ins, "dv")
                bcast_dv[g] = ("dv", P.v("dv"))

                exp_hist_pergroup[g] = exp_hist[(NKC // 2 - 1) % 3]

                if g >= 1:
                    emit_PV(g - 1)
            emit_PV(G3 - 1)
            head_done_pe[h] = P.v("pe")
        p3_vt_dv = P.v("dv")

        if debug:
            @block.gpsimd
            def _(g):
                P.w(g, "dv", p3_vt_dv)
                P.dma(g, vt_dbg, valuesT[:], "gv0")

        p3_end = P.snap()
        p3ctx.close()

        for engsec in (block.tensor, block.vector, block.scalar, block.sync, block.gpsimd):
            @engsec
            def _(e):
                for nm, val in p3_end.items():
                    P.w(e, nm, val)

        NT4 = G3 * 4
        SGL = [(s, min(8, NT4 - s)) for s in range(0, NT4, 8)]

        bbrow = ctx.enter_context(nc.sbuf_tensor("bbrow", [1, D], BF16))
        AW = ctx.enter_context(nc.sbuf_tensor("AW", [128, NCH * 136], BF16))
        Bfl = ctx.enter_context(nc.sbuf_tensor("Bfl", [128, D], BF16))
        fsums = ctx.enter_context(nc.sbuf_tensor("fsums", [128, 64], F32))
        fssq = ctx.enter_context(nc.sbuf_tensor("fssq", [128, 64], F32))

        p4ctx = ExitStack()
        wo_sb = p4ctx.enter_context(nc.sbuf_tensor("wo_sb", [128, 8 * D], BF16, side="right"))
        attnsb = p4ctx.enter_context(nc.sbuf_tensor("attnsb", [128, D], F32, side="right"))
        _xo = p4ctx.enter_context(nc.sbuf_tensor("xo0", [128, D], F32, side="right"))
        _x2b = p4ctx.enter_context(nc.sbuf_tensor("x2b0", [128, D], F32, side="right"))
        _y2b = p4ctx.enter_context(nc.sbuf_tensor("y2b0", [128, D], BF16, side="right"))
        xo = [_xo, _xo]
        x2b = [_x2b, _x2b]
        y2b = [_y2b, _y2b]
        psW = p4ctx.enter_context(nc.psum_tensor("psW", [128, 3072], F32))
        psC = p4ctx.enter_context(nc.psum_tensor("psC", [128, 2048], BF16))
        p4sctx = ExitStack()
        A_nat = p4sctx.enter_context(nc.sbuf_tensor("A_nat", [128, D], BF16, side="right"))
        Wr_nat = p4sctx.enter_context(nc.sbuf_tensor("Wr_nat", [8, D], BF16, side="right"))
        Bexp = [p4sctx.enter_context(nc.sbuf_tensor(f"Bexp{i}", [128, 1024], BF16, side="right")) for i in range(2)]

        @block.gpsimd
        def _(g):
            P.dma(g, bbrow[:], bb, "gl")
            P.dma(g, A_nat[:], Ast.rearrange("e r d -> (e r) d"), "gl")
            P.dma(g, Wr_nat[:], Wr, "gl")
        aw_gl = P.v("gl")

        aw_sc = []
        for ch in range(NCH):
            cslot = ch % 2

            @block.tensor
            def _(t, ch=ch, cslot=cslot):
                if ch == 0:
                    P.w(t, "gl", aw_gl)
                if ch >= 2:
                    P.w(t, *aw_sc[ch - 2])
                t.transpose(psC[:, cslot * 1024: cslot * 1024 + 128],
                            A_nat[:, ch * 128:(ch + 1) * 128], ident[:])
                ins = t.transpose(psC[:, cslot * 1024 + 128: cslot * 1024 + 136],
                                  Wr_nat[:, ch * 128:(ch + 1) * 128], ident[0:8, 0:8])
                P.inc(ins, "pe")
            pe_now = P.v("pe")

            @block.vector
            def _(v, ch=ch, cslot=cslot, pe_now=pe_now):
                P.w(v, "pe", pe_now)
                v.tensor_copy(AW[:, ch * 136: ch * 136 + 128], psC[:, cslot * 1024: cslot * 1024 + 128])
                v.drain()
                ins = v.tensor_copy(AW[:, ch * 136 + 128: (ch + 1) * 136],
                                    psC[:, cslot * 1024 + 128: cslot * 1024 + 136])
                P.inc(ins, "dv")
            aw_sc.append(("dv", P.v("dv")))

        bexp_gl = {}
        bfl_sc = []
        for ep in range(4):
            es = ep % 2

            @block.gpsimd
            def _(g, ep=ep, es=es):
                if ep >= 2:
                    P.w(g, *bfl_sc[(ep - 2) * NCH + NCH - 1])
                P.need("gb%d" % es)
                for u in range(2):
                    srcb = Bst[2 * ep + u].rearrange("(oc p) r -> p oc r", p=128)
                    dstb = Bexp[es][:].rearrange("p (oc u r) -> p oc u r", u=2, r=16)[:, :, u, :]
                    P.dma(g, dstb, srcb, "gb%d" % es)
            bexp_gl[ep] = ("gb%d" % es, P.v("gb%d" % es))

            for oc in range(NCH):
                bs = oc % 2

                @block.tensor
                def _(t, ep=ep, oc=oc, es=es, bs=bs):
                    if oc == 0:
                        P.w(t, *bexp_gl[ep])
                    if ep == 0 and oc == 0:
                        P.w(t, *aw_sc[-1])
                    if len(bfl_sc) >= 2:
                        P.w(t, *bfl_sc[-2])
                    ins = t.transpose(psC[0:32, bs * 1024 + 256: bs * 1024 + 384],
                                      Bexp[es][:, oc * 32:(oc + 1) * 32], ident[:])
                    P.inc(ins, "pe")
                pe_now = P.v("pe")

                @block.vector
                def _(v, ep=ep, oc=oc, bs=bs, pe_now=pe_now):
                    P.w(v, "pe", pe_now)
                    ins = v.tensor_copy(Bfl[ep * 32:(ep + 1) * 32, oc * 128:(oc + 1) * 128],
                                        psC[0:32, bs * 1024 + 256: bs * 1024 + 384])
                    P.inc(ins, "dv")
                bfl_sc.append(("dv", P.v("dv")))

        cst_end = P.snap()
        p4sctx.close()
        for engsec in (block.tensor, block.vector, block.scalar, block.sync, block.gpsimd):
            @engsec
            def _(e, snap=cst_end):
                for nm, val in snap.items():
                    P.w(e, nm, val)

        P.need("gwo")
        with nc.sbuf_tensor("wor", [128, D], F32, side="right") as wor:
            wo_cast_gp = []
            for wc in range(8):
                @block.sync
                def _(s, wc=wc):
                    if wc >= 1:
                        P.w(s, "gp", wo_cast_gp[wc - 1])
                    P.dma(s, wor[:], wo[wc * 128:(wc + 1) * 128, :], "gwo")
                gl_wo = P.v("gwo")

                @block.gpsimd
                def _(g, wc=wc, gl_wo=gl_wo):
                    P.w(g, "gwo", gl_wo)
                    ins = g.tensor_copy(wo_sb[:, wc * D:(wc + 1) * D], wor[:])
                    P.inc(ins, "gp")
                wo_cast_gp.append(P.v("gp"))
        wo_gl = P.v("gp")

        wo_evac = {}
        x2done_dv = {}
        y2_done_sc = {}
        x2sp_gs = {}
        y2sp_gs = {}
        P.need("gx0"); P.need("gx1")

        for tt in range(NT4):
            ts4 = tt % 2
            sb4 = 40 + ts4 * 12

            P.need("sxo%d" % ts4)

            @block.sync
            def _(s, tt=tt, ts4=ts4):
                prev = x2done_dv.get(tt - 1)
                if prev:
                    P.w(s, "dv", prev)
                P.dma(s, xo[ts4][:], xa[tt * 128:(tt + 1) * 128, :], "sxo%d" % ts4)
            sl_xo = P.v("sxo%d" % ts4)

            for part, (obase, olen) in enumerate([(0, 3072), (3072, 1024)]):
                @block.tensor
                def _(t, tt=tt, part=part, obase=obase, olen=olen):
                    if tt == 0 and part == 0:
                        P.w(t, "gp", wo_gl)
                        P.w(t, "dv", p3_vt_dv)
                    prev = wo_evac.get((tt, part - 1), wo_evac.get((tt - 1, 1)))
                    if prev:
                        P.w(t, "sc", prev)
                    for ch in range(8):
                        for oc in range(olen // 512):
                            ins = t.matmul(psW[:, oc * 512:(oc + 1) * 512],
                                           valuesT[:, ch * TOWN + tt * 128:(ch * TOWN) + (tt + 1) * 128],
                                           wo_sb[:, ch * D + obase + oc * 512: ch * D + obase + (oc + 1) * 512],
                                           start=(ch == 0), stop=(ch == 7))
                    P.inc(ins, "pe")
                pe_wo = P.v("pe")

                @block.scalar
                def _(s, tt=tt, part=part, obase=obase, olen=olen, sb4=sb4, pe_wo=pe_wo):
                    P.w(s, "pe", pe_wo)
                    ins = s.activation(attnsb[:, obase:obase + olen], psW[:, 0:olen], AF.Identity,
                                       accum_out=stat[:, sb4 + part: sb4 + part + 1])
                    P.inc(ins, "sc")
                wo_evac[(tt, part)] = P.v("sc")
            sc_attn = P.v("sc")

            @block.vector
            def _(v, tt=tt, ts4=ts4, sb4=sb4, sc_attn=sc_attn):
                P.w(v, "sc", sc_attn)
                prev = y2sp_gs.get(tt - 1)
                if prev:
                    P.w(v, *prev)
                v.scalar_tensor_tensor(y2b[ts4][:, 0:2048], attnsb[:, 0:2048], 1.0,
                                       attnsb[:, 0:2048], op0=ALU.mult, op1=ALU.mult,
                                       accum_out=stat[:, sb4 + 2: sb4 + 3])
                v.drain()
                v.scalar_tensor_tensor(y2b[ts4][:, 0:2048], attnsb[:, 2048:4096], 1.0,
                                       attnsb[:, 2048:4096], op0=ALU.mult, op1=ALU.mult,
                                       accum_out=stat[:, sb4 + 3: sb4 + 4])
                v.drain()
                v.tensor_tensor(stat[:, sb4 + 4: sb4 + 5], stat[:, sb4: sb4 + 1],
                                stat[:, sb4 + 1: sb4 + 2], op=ALU.add)
                v.drain()
                v.tensor_scalar(stat[:, sb4 + 4: sb4 + 5], stat[:, sb4 + 4: sb4 + 5],
                                1.0 / D, None, op0=ALU.mult)
                v.drain()
                v.tensor_tensor(stat[:, sb4 + 5: sb4 + 6], stat[:, sb4 + 2: sb4 + 3],
                                stat[:, sb4 + 3: sb4 + 4], op=ALU.add)
                v.drain()
                v.scalar_tensor_tensor(stat[:, sb4 + 5: sb4 + 6], stat[:, sb4 + 5: sb4 + 6],
                                       1.0 / D, stat[:, sb4 + 4: sb4 + 5],
                                       op0=ALU.mult, op1=ALU.bypass)
                v.drain()
                v.scalar_tensor_tensor(stat[:, sb4 + 6: sb4 + 7], stat[:, sb4 + 4: sb4 + 5],
                                       -1.0, stat[:, sb4 + 4: sb4 + 5], op0=ALU.mult, op1=ALU.mult)
                v.drain()
                v.tensor_tensor(stat[:, sb4 + 6: sb4 + 7], stat[:, sb4 + 6: sb4 + 7],
                                stat[:, sb4 + 5: sb4 + 6], op=ALU.add)
                v.drain()
                ins = v.tensor_scalar(stat[:, sb4 + 6: sb4 + 7], stat[:, sb4 + 6: sb4 + 7],
                                      EPS, None, op0=ALU.add)
                P.inc(ins, "dv")
            dv_rec2 = P.v("dv")

            @block.scalar
            def _(s, sb4=sb4, dv_rec2=dv_rec2):
                P.w(s, "dv", dv_rec2)
                s.activation(stat[:, sb4 + 7: sb4 + 8], stat[:, sb4 + 6: sb4 + 7], AF.Ln)
                s.drain()
                ins = s.activation(stat[:, sb4 + 7: sb4 + 8], stat[:, sb4 + 7: sb4 + 8], AF.Exp, scale=-0.5)
                P.inc(ins, "sc")
            sc_r2 = P.v("sc")

            @block.vector
            def _(v, sb4=sb4, sc_r2=sc_r2):
                P.w(v, "sc", sc_r2)
                ins = v.scalar_tensor_tensor(stat[:, sb4 + 8: sb4 + 9], stat[:, sb4 + 4: sb4 + 5],
                                             -1.0, stat[:, sb4 + 7: sb4 + 8],
                                             op0=ALU.mult, op1=ALU.mult)
                P.inc(ins, "dv")
            dv_mb2 = P.v("dv")

            @block.scalar
            def _(s, tt=tt, ts4=ts4, sb4=sb4, dv_mb2=dv_mb2, sl_xo=sl_xo):
                P.w(s, "dv", dv_mb2)
                P.w(s, "sxo%d" % ts4, sl_xo)
                prevsp = x2sp_gs.get(tt - 1)
                if prevsp:
                    P.w(s, *prevsp)
                s.drain()
                ins = s.activation(x2b[ts4][:], xo[ts4][:], AF.Identity,
                                   bias=stat[:, sb4 + 8: sb4 + 9], scale=1.0)
                P.inc(ins, "sc")
            sc_xo = P.v("sc")

            @block.vector
            def _(v, tt=tt, ts4=ts4, sb4=sb4, sc_xo=sc_xo):
                P.w(v, "sc", sc_xo)
                v.drain()
                ins = v.scalar_tensor_tensor(x2b[ts4][:], attnsb[:], stat[:, sb4 + 7: sb4 + 8],
                                             x2b[ts4][:], op0=ALU.mult, op1=ALU.add)
                P.inc(ins, "dv")
            x2done_dv[tt] = P.v("dv")
            dv_x2 = P.v("dv")

            @block.gpsimd
            def _(g, tt=tt, ts4=ts4, dv_x2=dv_x2):
                P.w(g, "dv", dv_x2)
                P.dma(g, x2_sp[tt], x2b[ts4][:], "gx%d" % ts4)
            x2sp_gs[tt] = ("gx%d" % ts4, P.v("gx%d" % ts4))

            @block.vector
            def _(v, tt=tt, ts4=ts4, sb4=sb4, dv_x2=dv_x2):
                v.drain()
                ins = v.scalar_tensor_tensor(y2b[ts4][:], x2b[ts4][:], 1.0, x2b[ts4][:],
                                             op0=ALU.mult, op1=ALU.mult,
                                             accum_out=stat[:, sb4 + 9: sb4 + 10])
                P.inc(ins, "dv")
            sc_sq3 = ("dv", P.v("dv"))

            @block.vector
            def _(v, ts4=ts4, sb4=sb4, sc_sq3=sc_sq3):
                v.reduce_sum(stat[:, sb4 + 10: sb4 + 11], x2b[ts4][:], axis=AXX)
                v.drain()
                v.tensor_scalar(stat[:, sb4 + 10: sb4 + 11], stat[:, sb4 + 10: sb4 + 11],
                                1.0 / D, None, op0=ALU.mult)
                v.drain()
                v.scalar_tensor_tensor(stat[:, sb4 + 11: sb4 + 12], stat[:, sb4 + 10: sb4 + 11],
                                       -1.0, stat[:, sb4 + 10: sb4 + 11], op0=ALU.mult, op1=ALU.mult)
                v.drain()
                v.scalar_tensor_tensor(stat[:, sb4 + 11: sb4 + 12], stat[:, sb4 + 9: sb4 + 10],
                                       1.0 / D, stat[:, sb4 + 11: sb4 + 12], op0=ALU.mult, op1=ALU.add)
                v.drain()
                ins = v.tensor_scalar(stat[:, sb4 + 11: sb4 + 12], stat[:, sb4 + 11: sb4 + 12],
                                      EPS, None, op0=ALU.add)
                P.inc(ins, "dv")
            dv_rec3 = P.v("dv")

            @block.scalar
            def _(s, sb4=sb4, dv_rec3=dv_rec3):
                P.w(s, "dv", dv_rec3)
                s.activation(stat[:, sb4 + 11: sb4 + 12], stat[:, sb4 + 11: sb4 + 12], AF.Ln)
                s.drain()
                ins = s.activation(stat[:, sb4 + 11: sb4 + 12], stat[:, sb4 + 11: sb4 + 12], AF.Exp, scale=-0.5)
                P.inc(ins, "sc")
            sc_r3 = P.v("sc")

            @block.vector
            def _(v, sb4=sb4, sc_r3=sc_r3):
                P.w(v, "sc", sc_r3)
                ins = v.scalar_tensor_tensor(stat[:, sb4 + 10: sb4 + 11], stat[:, sb4 + 10: sb4 + 11],
                                             -1.0, stat[:, sb4 + 11: sb4 + 12],
                                             op0=ALU.mult, op1=ALU.mult)
                P.inc(ins, "dv")
            dv_mb3 = P.v("dv")

            @block.scalar
            def _(s, tt=tt, ts4=ts4, sb4=sb4, dv_mb3=dv_mb3):
                P.w(s, "dv", dv_mb3)
                s.drain()
                ins = s.activation(y2b[ts4][:], x2b[ts4][:], AF.Identity,
                                   bias=stat[:, sb4 + 10: sb4 + 11],
                                   scale=stat[:, sb4 + 11: sb4 + 12])
                P.inc(ins, "sc")
            y2_done_sc[tt] = P.v("sc")
            sc_y2 = P.v("sc")

            @block.gpsimd
            def _(g, tt=tt, ts4=ts4, sc_y2=sc_y2):
                P.w(g, "sc", sc_y2)
                P.dma(g, y2_sp[tt], y2b[ts4][:], "gy%d" % ts4)
            y2sp_gs[tt] = ("gy%d" % ts4, P.v("gy%d" % ts4))

        p4_end = P.snap()
        p4ctx.close()
        p3vctx.close()

        for engsec in (block.tensor, block.vector, block.scalar, block.sync, block.gpsimd):
            @engsec
            def _(e):
                for nm, val in p4_end.items():
                    P.w(e, nm, val)

        p5ctx = ExitStack()
        y2rd = [p5ctx.enter_context(nc.sbuf_tensor(f"y2rd{i}", [128, D], BF16, side="right")) for i in range(2)]
        y2T = p5ctx.enter_context(nc.sbuf_tensor("y2T", [128, NCH * 1024], BF16, side="right"))
        hT_sb = p5ctx.enter_context(nc.sbuf_tensor("hT_sb", [128, 8 * 128], BF16, side="right"))
        hp_sb = p5ctx.enter_context(nc.sbuf_tensor("hp_sb", [128, 128], F32, side="right"))
        cmb = p5ctx.enter_context(nc.sbuf_tensor("cmb", [128, 32], F32, side="right"))
        _fst = p5ctx.enter_context(nc.sbuf_tensor("fstage0", [128, 8 * 512], BF16, side="right"))
        fstage = [_fst, _fst]
        fjunk = p5ctx.enter_context(nc.sbuf_tensor("fjunk", [128, 512], BF16, side="right"))
        psF = p5ctx.enter_context(nc.psum_tensor("psF", [128, 4096], F32))

        P.need("gw0"); P.need("gw1"); P.need("swt0"); P.need("swt1")

        P.need("gf0"); P.need("gf1"); P.need("sf0"); P.need("sf1")
        P.need("sxx0"); P.need("sxx1")

        y2t_tr = {}
        hT_done = {}
        base_done_sc = {}
        fspill = {}
        wbt_tr = {}
        wbn_hist = {0: [], 1: []}
        wbr_hist = {0: [], 1: []}
        y2rd_hist = {0: [], 1: []}
        out_done = []

        for sgi, (sg0, sgn) in enumerate(SGL):
            sgctx = ExitStack()
            WbTb = [sgctx.enter_context(nc.sbuf_tensor(f"WbTb{sgi}_{i}", [128, NCH * 512], BF16, side="right")) for i in range(2)]
            _wbn = sgctx.enter_context(nc.sbuf_tensor(f"wbn{sgi}_0", [128, D], BF16, side="right"))
            wbn = [_wbn, _wbn]
            _wbr = sgctx.enter_context(nc.sbuf_tensor(f"wbr{sgi}_0", [128, D], F32, side="right"))
            wbr = [_wbr, _wbr]

            for j in range(sgn):
                tt = sg0 + j
                js = j % 2

                @block.gpsimd
                def _(s, tt=tt, js=js, sgi=sgi, j=j):
                    P.w(s, *y2sp_gs[tt])
                    if y2rd_hist[js]:
                        P.w(s, *y2rd_hist[js][-1])
                    P.dma(s, y2rd[js][:], y2_sp[tt], "sxx%d" % js)
                sl_y2 = P.v("sxx%d" % js)

                @block.sync
                def _(s, tt=tt, js=js, sgi=sgi, j=j, sl_y2=sl_y2):
                    P.w(s, "sxx%d" % js, sl_y2)
                    if sgi > 0 and j == 0:
                        P.w(s, "pe", base_done_sc[("pe", sgi - 1)])
                    srcy = y2rd[js][:].rearrange("p (c f) -> p c f", c=NCH)
                    dsty = y2T[:].rearrange("p (c f) -> p c f", c=NCH)[:, :, j * 128:(j + 1) * 128]
                    P.inc(s.dma_start_transpose(dsty, srcy), "swt%d" % js, 16)
                y2t_tr[(sgi, j)] = ("swt%d" % js, P.v("swt%d" % js))
                y2rd_hist[js].append(y2t_tr[(sgi, j)])

                @block.tensor
                def _(t, tt=tt, j=j, sgi=sgi):
                    P.w(t, *y2t_tr[(sgi, j)])
                    if (sgi, j) == (0, 0):
                        P.w(t, *bfl_sc[-1])
                    prev = hT_done.get((sgi, j - 2))
                    if prev:
                        P.w(t, *prev)
                    pbase = (j % 2) * 512
                    for ch in range(NCH):
                        ins = t.matmul(psF[:, pbase: pbase + 136],
                                       y2T[:, ch * 1024 + j * 128: ch * 1024 + (j + 1) * 128],
                                       AW[:, ch * 136:(ch + 1) * 136],
                                       start=(ch == 0), stop=(ch == NCH - 1))
                    P.inc(ins, "pe")
                pe_h = P.v("pe")

                cb = (j % 2) * 16

                @block.vector
                def _(v, j=j, cb=cb, pe_h=pe_h):
                    pbase = (j % 2) * 512
                    P.w(v, "pe", pe_h)
                    lg = psF[:, pbase + 128: pbase + 136]
                    v.reduce_max(cmb[:, cb: cb + 1], lg, axis=AXX)
                    v.drain()
                    v.tensor_scalar(cmb[:, cb + 8: cb + 16], lg, cmb[:, cb: cb + 1], None, op0=ALU.is_ge)
                    v.drain()
                    v.scalar_tensor_tensor(cmb[:, cb + 8: cb + 16], cmb[:, cb + 8: cb + 16], -1e30,
                                           lg, op0=ALU.mult, op1=ALU.add)
                    v.drain()
                    v.reduce_max(cmb[:, cb + 1: cb + 2], cmb[:, cb + 8: cb + 16], axis=AXX)
                    v.drain()
                    v.tensor_scalar(cmb[:, cb + 2: cb + 3], cmb[:, cb: cb + 1], -1.0, None, op0=ALU.mult)
                    v.drain()
                    ins = v.tensor_scalar(cmb[:, cb + 8: cb + 16], lg, cmb[:, cb + 1: cb + 2], None,
                                          op0=ALU.is_ge)
                    P.inc(ins, "dv")
                dv_c1 = P.v("dv")

                @block.scalar
                def _(s, j=j, cb=cb, dv_c1=dv_c1):
                    pbase = (j % 2) * 512
                    P.w(s, "dv", dv_c1)
                    ins = s.activation(cmb[:, cb + 4: cb + 12][:, 0:8], psF[:, pbase + 128: pbase + 136],
                                       AF.Exp, bias=cmb[:, cb + 2: cb + 3], scale=1.0)
                    P.inc(ins, "sc")
                sc_c1 = P.v("sc")

                @block.vector
                def _(v, cb=cb, sc_c1=sc_c1):
                    P.w(v, "sc", sc_c1)
                    v.tensor_mul(cmb[:, cb + 4: cb + 12], cmb[:, cb + 4: cb + 12], cmb[:, cb + 8: cb + 16])
                    v.drain()
                    v.reduce_sum(cmb[:, cb + 3: cb + 4], cmb[:, cb + 4: cb + 12], axis=AXX)
                    v.drain()
                    v.reciprocal(cmb[:, cb + 3: cb + 4], cmb[:, cb + 3: cb + 4])
                    v.drain()
                    ins = v.tensor_scalar(cmb[:, cb + 4: cb + 12], cmb[:, cb + 4: cb + 12],
                                          cmb[:, cb + 3: cb + 4], 2.0, op0=ALU.mult, op1=ALU.mult)
                    P.inc(ins, "dv")
                dv_c2 = P.v("dv")

                @block.vector
                def _(v, j=j, cb=cb, dv_c2=dv_c2):
                    pbase = (j % 2) * 512
                    v.drain()
                    for e8 in range(8):
                        ins = v.tensor_scalar(hp_sb[:, e8 * 16:(e8 + 1) * 16],
                                              psF[:, pbase + e8 * 16: pbase + (e8 + 1) * 16],
                                              cmb[:, cb + 4 + e8: cb + 5 + e8], None, op0=ALU.mult)
                    P.inc(ins, "dv")
                dv_hp = P.v("dv")

                @block.tensor
                def _(t, j=j, dv_hp=dv_hp):
                    pbase = (j % 2) * 512
                    P.w(t, "dv", dv_hp)
                    ins = t.transpose(psF[:, pbase + 256: pbase + 384], hp_sb[:], identf[:])
                    P.inc(ins, "pe")
                pe_hT = P.v("pe")

                @block.vector
                def _(v, j=j, pe_hT=pe_hT):
                    pbase = (j % 2) * 512
                    P.w(v, "pe", pe_hT)
                    ins = v.tensor_copy(hT_sb[:, j * 128:(j + 1) * 128], psF[:, pbase + 256: pbase + 384])
                    P.inc(ins, "dv")
                hT_done[(sgi, j)] = ("dv", P.v("dv"))
            sc_prep = P.v("sc")
            dv_prep = P.v("dv")

            evac_sc5 = {}
            for p in range(8):
                for ot in range(4):
                    ws = ot % 2

                    @block.sync
                    def _(g, sgi=sgi, p=p, ot=ot, ws=ws):
                        if wbr_hist[0] or wbr_hist[1]:
                            lastc = max(wbr_hist[0][-1][1] if wbr_hist[0] else 0,
                                        wbr_hist[1][-1][1] if wbr_hist[1] else 0)
                            P.w(g, "gp", lastc)
                        rows = Wb[p * 512 + ot * 128: p * 512 + (ot + 1) * 128, :]
                        P.dma(g, wbr[ws][:], rows, "gw%d" % ws)
                    gl_wb = P.v("gw%d" % ws)

                    @block.gpsimd
                    def _(g, sgi=sgi, p=p, ot=ot, ws=ws, gl_wb=gl_wb):
                        P.w(g, "gw%d" % ws, gl_wb)
                        if wbn_hist[0] or wbn_hist[1]:
                            lastt = None
                            for wsx in (0, 1):
                                if wbn_hist[wsx]:
                                    cand = wbn_hist[wsx][-1]
                                    if lastt is None:
                                        lastt = cand
                            for wsx in (0, 1):
                                if wbn_hist[wsx]:
                                    P.w(g, *wbn_hist[wsx][-1])
                        ins = g.tensor_copy(wbn[ws][:], wbr[ws][:])
                        P.inc(ins, "gp")
                    gp_cast = P.v("gp")
                    wbr_hist[ws].append(("gp", gp_cast))

                    tsem = "sf%d%d" % (p % 2, ot % 2)
                    P.need(tsem)

                    @block.sync
                    def _(s, sgi=sgi, p=p, ot=ot, ws=ws, gp_cast=gp_cast, tsem=tsem):
                        P.w(s, "gp", gp_cast)
                        if ot == 0:
                            prev = evac_sc5.get((sgi, p - 2), evac_sc5.get((sgi - 1, p + 6)))
                            if prev:
                                P.w(s, "sc", prev)
                        srcw = wbn[ws][:].rearrange("p (c f) -> p c f", c=NCH)
                        dstw = WbTb[p % 2][:, ot * D:(ot + 1) * D].rearrange("p (c f) -> p c f", c=NCH)
                        P.inc(s.dma_start_transpose(dstw, srcw), tsem, 16)
                    wbt_tr[(sgi, p, ot)] = (tsem, P.v(tsem))
                    wbn_hist[ws].append((tsem, P.v(tsem)))

                @block.tensor
                def _(t, sgi=sgi, p=p, sgn=sgn, sc_prep=sc_prep, dv_prep=dv_prep):
                    for ot in range(4):
                        P.w(t, *wbt_tr[(sgi, p, ot)])
                    if p == 0:
                        P.w(t, "sc", sc_prep)
                        P.w(t, "dv", dv_prep)
                    prev = evac_sc5.get((sgi, p - 1), evac_sc5.get((sgi - 1, 7)))
                    if prev:
                        P.w(t, "sc", prev)
                    wtv = WbTb[p % 2][:].rearrange("p (o c j) -> p o c j", o=4, j=128)
                    for ch in range(NCH):
                        for j in range(sgn):
                            t.matmul(psF[:, j * 512:(j + 1) * 512],
                                     y2T[:, ch * 1024 + j * 128: ch * 1024 + (j + 1) * 128],
                                     wtv[:, :, ch, :],
                                     start=(ch == 0), stop=False)
                    for j in range(sgn):
                        t.matmul(psF[:, j * 512:(j + 1) * 512],
                                 hT_sb[:, j * 128:(j + 1) * 128],
                                 Bfl[:, p * 512:(p + 1) * 512],
                                 start=False, stop=False)
                    for j in range(sgn):
                        ins = t.matmul(psF[:, j * 512:(j + 1) * 512],
                                       ones_r[:], bbrow[0:1, p * 512:(p + 1) * 512],
                                       start=False, stop=True)
                    P.inc(ins, "pe")
                pe_base = P.v("pe")

                @block.scalar
                def _(s, sgi=sgi, p=p, sgn=sgn, pe_base=pe_base):
                    P.w(s, "pe", pe_base)
                    prev = fspill.get((sgi, p - 1), fspill.get((sgi - 1, 7)))
                    if prev:
                        P.w(s, *prev)
                    for j in range(sgn):
                        ins = s.activation(fstage[p % 2][:, j * 512:(j + 1) * 512],
                                           psF[:, j * 512:(j + 1) * 512], AF.Identity,
                                           accum_out=fsums[:, p * 8 + j: p * 8 + j + 1])
                    P.inc(ins, "sc")
                sc_evac5 = P.v("sc")

                @block.vector
                def _(v, sgi=sgi, p=p, sgn=sgn, sc_evac5=sc_evac5):
                    P.w(v, "sc", sc_evac5)
                    for j in range(sgn):
                        v.drain()
                        ins = v.scalar_tensor_tensor(fjunk[:], fstage[p % 2][:, j * 512:(j + 1) * 512],
                                                     1.0, fstage[p % 2][:, j * 512:(j + 1) * 512],
                                                     op0=ALU.mult, op1=ALU.mult,
                                                     accum_out=fssq[:, p * 8 + j: p * 8 + j + 1])
                    P.inc(ins, "dv")
                evac_sc5[(sgi, p)] = P.v("sc")
                dv_ssq5 = P.v("dv")

                @block.gpsimd
                def _(g, sgi=sgi, sg0=sg0, p=p, sgn=sgn, sc_evac5=sc_evac5, dv_ssq5=dv_ssq5):
                    P.w(g, "sc", sc_evac5)
                    P.w(g, "dv", dv_ssq5)
                    for j in range(sgn):
                        P.dma(g, f_sp[sg0 + j][:, p * 512:(p + 1) * 512],
                              fstage[p % 2][:, j * 512:(j + 1) * 512], "gf%d" % (p % 2))
                fspill[(sgi, p)] = ("gf%d" % (p % 2), P.v("gf%d" % (p % 2)))
            base_done_sc[("pe", sgi)] = P.v("pe")
            sg_base_end = P.snap()

            sgctx.close()
            for engsec in (block.tensor, block.vector, block.scalar, block.sync, block.gpsimd):
                @engsec
                def _(e, snap=sg_base_end):
                    for nm, val in snap.items():
                        P.w(e, nm, val)

            apctx = ExitStack()
            fread = [apctx.enter_context(nc.sbuf_tensor(f"fread{sgi}_{i}", [128, D], BF16, side="right")) for i in range(2)]
            xb2 = [apctx.enter_context(nc.sbuf_tensor(f"xb2{sgi}_{i}", [128, D], F32, side="right")) for i in range(2)]
            outsb = [apctx.enter_context(nc.sbuf_tensor(f"outsb{sgi}_{i}", [128, D], F32, side="right")) for i in range(2)]

            ap_dv = {}
            out_st = {}
            for j in range(sgn):
                tt = sg0 + j
                js = j % 2
                ab = 40 + js * 12

                P.need("sap%d" % js)

                @block.sync
                def _(s, tt=tt, j=j, js=js, sgi=sgi):
                    prev = ap_dv.get(j - 2)
                    if prev:
                        P.w(s, "dv", prev)
                    P.w(s, *fspill[(sgi, 7)])
                    P.w(s, *x2sp_gs[tt])
                    P.dma(s, fread[js][:], f_sp[tt], "sap%d" % js)
                    P.dma(s, xb2[js][:], x2_sp[tt], "sap%d" % js)
                sl_ap = P.v("sap%d" % js)

                @block.vector
                def _(v, j=j, js=js, ab=ab, sl_ap=sl_ap):
                    v.tensor_reduce(stat[:, ab: ab + 1], fsums[:, j: 57 + j: 8][:, 0:8], axis=AXX, op=ALU.add)
                    v.drain()
                    v.tensor_reduce(stat[:, ab + 1: ab + 2], fssq[:, j: 57 + j: 8][:, 0:8], axis=AXX, op=ALU.add)
                    v.drain()
                    v.tensor_scalar(stat[:, ab: ab + 1], stat[:, ab: ab + 1], 1.0 / D, None, op0=ALU.mult)
                    v.drain()
                    v.scalar_tensor_tensor(stat[:, ab + 2: ab + 3], stat[:, ab: ab + 1], -1.0,
                                           stat[:, ab: ab + 1], op0=ALU.mult, op1=ALU.mult)
                    v.drain()
                    v.scalar_tensor_tensor(stat[:, ab + 2: ab + 3], stat[:, ab + 1: ab + 2], 1.0 / D,
                                           stat[:, ab + 2: ab + 3], op0=ALU.mult, op1=ALU.add)
                    v.drain()
                    ins = v.tensor_scalar(stat[:, ab + 2: ab + 3], stat[:, ab + 2: ab + 3], EPS, None, op0=ALU.add)
                    P.inc(ins, "dv")
                dv_r4 = P.v("dv")

                @block.scalar
                def _(s, ab=ab, dv_r4=dv_r4):
                    P.w(s, "dv", dv_r4)
                    s.activation(stat[:, ab + 2: ab + 3], stat[:, ab + 2: ab + 3], AF.Ln)
                    s.drain()
                    ins = s.activation(stat[:, ab + 2: ab + 3], stat[:, ab + 2: ab + 3], AF.Exp, scale=-0.5)
                    P.inc(ins, "sc")
                sc_r4 = P.v("sc")

                @block.vector
                def _(v, ab=ab, sc_r4=sc_r4):
                    P.w(v, "sc", sc_r4)
                    ins = v.scalar_tensor_tensor(stat[:, ab + 3: ab + 4], stat[:, ab: ab + 1], -1.0,
                                                 stat[:, ab + 2: ab + 3], op0=ALU.mult, op1=ALU.mult)
                    P.inc(ins, "dv")
                dv_mb4 = P.v("dv")

                @block.scalar
                def _(s, j=j, js=js, ab=ab, dv_mb4=dv_mb4, sl_ap=sl_ap):
                    P.w(s, "dv", dv_mb4)
                    P.w(s, "sap%d" % js, sl_ap)
                    prevst = out_st.get(j - 2)
                    if prevst:
                        P.w(s, *prevst)
                    s.drain()
                    ins = s.activation(outsb[js][:], fread[js][:], AF.Identity,
                                       bias=stat[:, ab + 3: ab + 4], scale=stat[:, ab + 2: ab + 3])
                    P.inc(ins, "sc")
                sc_ln4 = P.v("sc")

                @block.vector
                def _(v, js=js, sc_ln4=sc_ln4):
                    P.w(v, "sc", sc_ln4)
                    v.drain()
                    ins = v.tensor_add(outsb[js][:], outsb[js][:], xb2[js][:])
                    P.inc(ins, "dv")
                ap_dv[j] = P.v("dv")
                dv_out = P.v("dv")

                P.need("so%d" % js)

                @block.sync
                def _(s, tt=tt, js=js, dv_out=dv_out):
                    P.w(s, "dv", dv_out)
                    P.dma(s, out[tt * 128:(tt + 1) * 128, :], outsb[js][:], "so%d" % js)
                out_st[j] = ("so%d" % js, P.v("so%d" % js))
                out_done.append(out_st[j])
            ap_end = P.snap()
            apctx.close()
            for engsec in (block.tensor, block.vector, block.scalar, block.sync, block.gpsimd):
                @engsec
                def _(e, snap=ap_end):
                    for nm, val in snap.items():
                        P.w(e, nm, val)

        @block.sync
        def _(s):
            for nm in ("so0", "so1"):
                if nm in P.sems:
                    s.wait_ge(P.sems[nm], P.v(nm))

    return nc


def _in_maps(inputs):
    x = np.ascontiguousarray(np.asarray(inputs["L0_attn_x"], dtype=np.float32))
    wq = np.ascontiguousarray(np.asarray(inputs["L0_attn_wq"], dtype=np.float32))
    wk = np.ascontiguousarray(np.asarray(inputs["L0_attn_wk"], dtype=np.float32))
    wv = np.ascontiguousarray(np.asarray(inputs["L0_attn_wv"], dtype=np.float32))
    wo = np.ascontiguousarray(np.asarray(inputs["L0_attn_wo"], dtype=np.float32))
    Wb = np.ascontiguousarray(np.asarray(inputs["L0_ffn_W_base"], dtype=np.float32))
    bb = np.ascontiguousarray(np.asarray(inputs["L0_ffn_b_base"], dtype=np.float32)).reshape(1, D)
    Wr = np.ascontiguousarray(np.asarray(inputs["L0_ffn_W_router"], dtype=np.float32))
    Ast = np.ascontiguousarray(np.asarray(inputs["L0_ffn_A_stack"], dtype=np.float32))
    Bst = np.ascontiguousarray(np.asarray(inputs["L0_ffn_B_stack"], dtype=np.float32))
    maps = []
    for c in range(8):
        b, half = c // 2, c % 2
        xr = np.ascontiguousarray(np.roll(x[b], -half * TOWN, axis=0))
        maps.append({"xa": xr, "wq": wq, "wk": wk, "wv": wv, "wo": wo,
                     "Wb": Wb, "bb": bb, "Wr": Wr, "Ast": Ast, "Bst": Bst})
    return maps


def kernel(**inputs):
    nc = build_nc(debug=False)
    maps = _in_maps(inputs)
    res = run_bass_kernel_spmd(nc, maps, core_ids=list(range(8)))
    outf = np.empty((4, S, D), dtype=np.float32)
    for c in range(8):
        b, half = c // 2, c % 2
        outf[b, half * TOWN:(half + 1) * TOWN] = res.results[c]["out"]
    return outf


# version 08
# speedup vs baseline: 1.1618x; 1.1618x over previous
import numpy as np
import ml_dtypes
from contextlib import ExitStack
import concourse.bass as bass
import concourse.mybir as mybir
from concourse.bass_utils import run_bass_kernel_spmd

F32 = mybir.dt.float32
BF16 = mybir.dt.bfloat16
AF = mybir.ActivationFunctionType
ALU = mybir.AluOpType
AXX = mybir.AxisListType.X

S = 4096
D = 4096
HKV = 8
DH = 128
KV = 1024
TOWN = 2048
NCH = 32
NT = 16
EPS = 1e-5
RSD = float(1.0 / np.sqrt(128.0))
P40 = float(2.0 ** 40)

SEM_NAMES = ["gl", "so", "pe", "dv", "sc", "gp",
             "sx0", "sx1", "sw0", "sw1", "sy0", "sy1",
             "st0", "st1", "gy0", "gy1",
             "gk0", "gk1", "gq0", "gq1", "gv0", "gv1"]


class Prog:

    def __init__(self, nc, ctx):
        self.nc = nc
        self.ctx = ctx
        self.sems = {}
        self.cnt = {}
        for name in SEM_NAMES:
            self._mk(name)

    def _mk(self, name):
        self.sems[name] = self.ctx.enter_context(self.nc.semaphore(name))
        self.cnt[name] = 0

    def need(self, name):
        if name not in self.sems:
            self._mk(name)

    def inc(self, instr, name, by=1):
        self.cnt[name] += by
        instr.then_inc(self.sems[name], by)
        return self.cnt[name]

    def dma(self, eng, outap, inap, name, **kw):
        ins = eng.dma_start(out=outap, in_=inap, **kw)
        return self.inc(ins, name, 16)

    def w(self, eng, name, val):
        assert val <= self.cnt[name], f"wait {name}>={val} emitted={self.cnt[name]}"
        if val > 0:
            eng.wait_ge(self.sems[name], val)

    def v(self, name):
        return self.cnt[name]

    def snap(self):
        return dict(self.cnt)

    def wall(self, eng, snapshot):
        for name, val in snapshot.items():
            self.w(eng, name, val)


def build_nc(debug=False, phases=7, n_tiles=32, n_groups=8):
    n_groups_own = max(1, n_groups // 2)
    nc = bass.Bass("TRN2", target_bir_lowering=False)

    xa = nc.dram_tensor("xa", [S, D], F32, kind="ExternalInput").ap()
    wq = nc.dram_tensor("wq", [D, D], F32, kind="ExternalInput").ap()
    wk = nc.dram_tensor("wk", [D, KV], F32, kind="ExternalInput").ap()
    wv = nc.dram_tensor("wv", [D, KV], F32, kind="ExternalInput").ap()
    wo = nc.dram_tensor("wo", [KV, D], F32, kind="ExternalInput").ap()
    Wb = nc.dram_tensor("Wb", [D, D], F32, kind="ExternalInput").ap()
    bb = nc.dram_tensor("bb", [1, D], F32, kind="ExternalInput").ap()
    Wr = nc.dram_tensor("Wr", [HKV, D], F32, kind="ExternalInput").ap()
    Ast = nc.dram_tensor("Ast", [8, 16, D], F32, kind="ExternalInput").ap()
    Bst = nc.dram_tensor("Bst", [8, D, 16], F32, kind="ExternalInput").ap()
    out = nc.dram_tensor("out", [TOWN, D], F32, kind="ExternalOutput").ap()

    ident_bf_d = nc.inline_tensor(np.eye(128, dtype=ml_dtypes.bfloat16), name="ident_bf").ap()
    ident_f_d = nc.inline_tensor(np.eye(128, dtype=np.float32), name="ident_f").ap()

    skind = "ExternalOutput" if debug else "Internal"
    y1t_sp = nc.dram_tensor("y1t_sp", [32, 128, D], BF16, kind=skind).ap()
    q_sp = nc.dram_tensor("q_sp", [HKV, 128, TOWN], BF16, kind=skind).ap()
    k_sp = nc.dram_tensor("k_sp", [HKV, 128, S], BF16, kind=skind).ap()
    v_sp = nc.dram_tensor("v_sp", [HKV, NCH, 128, 128], BF16, kind=skind).ap()
    x2_sp = nc.dram_tensor("x2_sp", [NT, 128, D], F32, kind=skind).ap()
    y2_sp = nc.dram_tensor("y2_sp", [NT, 128, D], BF16, kind=skind).ap()
    f_sp = nc.dram_tensor("f_sp", [NT, 128, D], BF16, kind=skind).ap()
    vt_dbg = None
    if debug:
        vt_dbg = nc.dram_tensor("vt_dbg", [128, HKV * TOWN], BF16, kind="ExternalOutput").ap()

    ctx = ExitStack()
    P = Prog(nc, ctx)

    ident = ctx.enter_context(nc.sbuf_tensor("ident", [128, 128], BF16))
    identf = ctx.enter_context(nc.sbuf_tensor("identf", [128, 128], F32))
    ones_r = ctx.enter_context(nc.sbuf_tensor("ones_r", [1, 128], BF16))
    ones_rf = ctx.enter_context(nc.sbuf_tensor("ones_rf", [1, 128], F32))
    onesc = ctx.enter_context(nc.sbuf_tensor("onesc", [128, 1], BF16))
    stat = ctx.enter_context(nc.sbuf_tensor("stat", [128, 64], F32))
    st1 = ctx.enter_context(nc.sbuf_tensor("st1", [1, 16], F32))
    st1b = ctx.enter_context(nc.sbuf_tensor("st1b", [1, 16], BF16))

    block = ctx.enter_context(nc.Block())

    with ctx:
        @block.gpsimd
        def _(g):
            P.dma(g, ident[:], ident_bf_d, "gl")
            P.dma(g, identf[:], ident_f_d, "gl")

        @block.vector
        def _(v):
            v.memset(ones_r[:], 1.0)
            v.memset(ones_rf[:], 1.0)
            P.inc(v.memset(onesc[:], 1.0), "dv")

        p2rctx = ExitStack()
        wk_h = p2rctx.enter_context(nc.sbuf_tensor("wk_h", [128, NCH * 512], BF16, side="right"))
        wv_h = p2rctx.enter_context(nc.sbuf_tensor("wv_h", [128, NCH * 512], BF16, side="right"))
        y1g = [p2rctx.enter_context(nc.sbuf_tensor(f"y1g{i}", [128, NCH * 512], BF16, side="right")) for i in range(2)]
        kstg = [p2rctx.enter_context(nc.sbuf_tensor(f"kstg{i}", [128, 512], BF16, side="right")) for i in range(2)]
        vstg = [p2rctx.enter_context(nc.sbuf_tensor(f"vstg{i}", [128, 512], BF16, side="right")) for i in range(2)]
        ps2a = p2rctx.enter_context(nc.psum_tensor("ps2a", [128, 2048], F32))
        ps2b = p2rctx.enter_context(nc.psum_tensor("ps2b", [128, 1024], F32))

        p1ctx = ExitStack()
        xin = [p1ctx.enter_context(nc.sbuf_tensor(f"xin{i}", [128, D], F32, side="right")) for i in range(2)]
        yb = [p1ctx.enter_context(nc.sbuf_tensor(f"yb{i}", [128, D], BF16, side="right")) for i in range(2)]
        ytr = [p1ctx.enter_context(nc.sbuf_tensor(f"ytr{i}", [128, D], BF16, side="right")) for i in range(2)]

        apply_sc, spill_gs, tpose_tr = [], [], []
        wkv_gl = {}

        def emit_wkv_cast(H, wait_pe=0):
            @block.gpsimd
            def _(g, H=H, wait_pe=wait_pe):
                if wait_pe:
                    P.w(g, "pe", wait_pe)
                src_k = wk[:, H * 512:(H + 1) * 512].rearrange("(c p) f -> p c f", p=128)
                P.dma(g, wk_h[:].rearrange("p (c f) -> p c f", c=NCH), src_k, "gl")
                src_v = wv[:, H * 512:(H + 1) * 512].rearrange("(c p) f -> p c f", p=128)
                P.dma(g, wv_h[:].rearrange("p (c f) -> p c f", c=NCH), src_v, "gl")
            wkv_gl[H] = P.v("gl")

        emit_wkv_cast(0)

        def emit_p1_tile(t):
            slot = t % 2
            sb = slot * 8

            @block.sync
            def _(s, t=t, slot=slot):
                if t >= 2:
                    P.w(s, "sc", apply_sc[t - 2])
                P.dma(s, xin[slot][:], xa[t * 128:(t + 1) * 128, :], "sx%d" % slot)
            sl_now = P.v("sx%d" % slot)

            @block.vector
            def _(v, t=t, slot=slot, sb=sb, sl_now=sl_now):
                P.w(v, "sx%d" % slot, sl_now)
                v.reduce_sum(stat[:, sb: sb + 1], xin[slot][:], axis=AXX)
                v.drain()
                if t >= 2:
                    P.w(v, *tpose_tr[t - 2])
                v.scalar_tensor_tensor(yb[slot][:], xin[slot][:], 1.0, xin[slot][:],
                                       op0=ALU.mult, op1=ALU.mult,
                                       accum_out=stat[:, sb + 1: sb + 2])
                v.drain()
                v.tensor_scalar(stat[:, sb + 2: sb + 3], stat[:, sb: sb + 1], -1.0 / D, None, op0=ALU.mult)
                v.drain()
                v.scalar_tensor_tensor(stat[:, sb + 3: sb + 4], stat[:, sb + 2: sb + 3], -1.0,
                                       stat[:, sb + 2: sb + 3], op0=ALU.mult, op1=ALU.mult)
                v.drain()
                v.scalar_tensor_tensor(stat[:, sb + 3: sb + 4], stat[:, sb + 1: sb + 2], 1.0 / D,
                                       stat[:, sb + 3: sb + 4], op0=ALU.mult, op1=ALU.add)
                v.drain()
                ins = v.tensor_scalar(stat[:, sb + 3: sb + 4], stat[:, sb + 3: sb + 4], EPS, None, op0=ALU.add)
                P.inc(ins, "dv")
            dv_rec = P.v("dv")

            @block.scalar
            def _(s, sb=sb, dv_rec=dv_rec):
                P.w(s, "dv", dv_rec)
                s.activation(stat[:, sb + 4: sb + 5], stat[:, sb + 3: sb + 4], AF.Ln)
                s.drain()
                ins = s.activation(stat[:, sb + 4: sb + 5], stat[:, sb + 4: sb + 5], AF.Exp, scale=-0.5)
                P.inc(ins, "sc")
            sc_r = P.v("sc")

            @block.scalar
            def _(s, t=t, slot=slot, sb=sb, sc_r=sc_r):
                s.drain()
                s.activation(yb[slot][:], xin[slot][:], AF.Identity,
                             bias=stat[:, sb + 2: sb + 3])
                s.drain()
                ins = s.activation(yb[slot][:], yb[slot][:], AF.Identity,
                                   scale=stat[:, sb + 4: sb + 5])
                P.inc(ins, "sc")
            apply_sc.append(P.v("sc"))
            sc_now = P.v("sc")

            @block.sync
            def _(s, t=t, slot=slot, sc_now=sc_now):
                P.w(s, "sc", sc_now)
                if t >= 2:
                    P.w(s, *spill_gs[t - 2])
                srcy = yb[slot][:].rearrange("p (c f) -> p c f", c=NCH)
                dsty = ytr[slot][:].rearrange("p (c f) -> p c f", c=NCH)
                P.inc(s.dma_start_transpose(dsty, srcy), "st%d" % slot, 16)
            tpose_tr.append(("st%d" % slot, P.v("st%d" % slot)))
            tr_now = P.v("st%d" % slot)

            @block.gpsimd
            def _(g, t=t, slot=slot, tr_now=tr_now):
                P.w(g, "st%d" % slot, tr_now)
                P.dma(g, y1t_sp[t], ytr[slot][:], "gy%d" % slot)
            spill_gs.append(("gy%d" % slot, P.v("gy%d" % slot)))

        y1g_sl = {}
        y1g_done_pe = {}
        evac_hist = {"k": [], "q": [], "v": []}
        spill_hist = {"k": [], "q": [], "v": []}
        qkv_spill_snap = {}

        def emit_y1g_load(key, g, gslot, prev_pe):
            @block.sync
            def _(s, g=g, gslot=gslot, prev_pe=prev_pe):
                if prev_pe:
                    P.w(s, "pe", prev_pe)
                P.w(s, *spill_gs[4 * g + 3])
                P.w(s, *spill_gs[4 * g + 2])
                for j in range(4):
                    srcg = y1t_sp[4 * g + j].rearrange("p (c f) -> p c f", c=NCH)
                    dstg = y1g[gslot][:].rearrange("p (c f) -> p c f", c=NCH)[:, :, j * 128:(j + 1) * 128]
                    P.dma(s, dstg, srcg, "sy%d" % gslot)
            y1g_sl[key] = ("sy%d" % gslot, P.v("sy%d" % gslot))

        def emit_k_unit(H, g, ct):
            gslot = g % 2
            n = len(evac_hist["k"])
            pslot = ct % 2

            @block.tensor
            def _(t, H=H, g=g, ct=ct, pslot=pslot, gslot=gslot, n=n):
                P.w(t, *y1g_sl[("kv", H, g)])
                if g == 0 and ct == 0:
                    P.w(t, "gl", wkv_gl[H])
                if n >= 2:
                    P.w(t, "dv", evac_hist["k"][n - 2])
                for ch in range(NCH):
                    ins = t.matmul(ps2a[:, pslot * 512:(pslot + 1) * 512],
                                   wk_h[:, ch * 512 + ct * 128: ch * 512 + (ct + 1) * 128],
                                   y1g[gslot][:, ch * 512:(ch + 1) * 512],
                                   start=(ch == 0), stop=(ch == NCH - 1))
                P.inc(ins, "pe")
            pe_now = P.v("pe")

            @block.vector
            def _(v, pe_now=pe_now, pslot=pslot, n=n):
                P.w(v, "pe", pe_now)
                if n >= 2:
                    P.w(v, *spill_hist["k"][n - 2])
                ins = v.tensor_copy(kstg[n % 2][:], ps2a[:, pslot * 512:(pslot + 1) * 512])
                P.inc(ins, "dv")
            evac_hist["k"].append(P.v("dv"))
            dv_now = P.v("dv")
            gsem = "gk%d" % (n % 2)

            @block.gpsimd
            def _(gp, H=H, g=g, ct=ct, dv_now=dv_now, n=n, gsem=gsem):
                P.w(gp, "dv", dv_now)
                P.dma(gp, k_sp[H * 4 + ct][:, g * 512:(g + 1) * 512], kstg[n % 2][:], gsem)
            spill_hist["k"].append((gsem, P.v(gsem)))

        def emit_v_unit(H, g, tt):
            gslot = g % 2
            n = len(evac_hist["v"])
            pslot = tt % 2

            @block.tensor
            def _(t, H=H, g=g, tt=tt, pslot=pslot, gslot=gslot, n=n):
                if n >= 2:
                    P.w(t, "dv", evac_hist["v"][n - 2])
                for ch in range(NCH):
                    ins = t.matmul(ps2b[:, pslot * 512:(pslot + 1) * 512],
                                   y1g[gslot][:, ch * 512 + tt * 128: ch * 512 + (tt + 1) * 128],
                                   wv_h[:, ch * 512:(ch + 1) * 512],
                                   start=(ch == 0), stop=(ch == NCH - 1))
                P.inc(ins, "pe")
            pe_now = P.v("pe")
            if tt == 3:
                y1g_done_pe[("kv", H, g)] = pe_now

            @block.vector
            def _(v, pe_now=pe_now, pslot=pslot, n=n):
                P.w(v, "pe", pe_now)
                if n >= 2:
                    P.w(v, *spill_hist["v"][n - 2])
                ins = v.tensor_copy(vstg[n % 2][:], ps2b[:, pslot * 512:(pslot + 1) * 512])
                P.inc(ins, "dv")
            evac_hist["v"].append(P.v("dv"))
            dv_now = P.v("dv")
            gsem = "gv%d" % (n % 2)

            @block.gpsimd
            def _(gp, H=H, g=g, tt=tt, dv_now=dv_now, n=n, gsem=gsem):
                P.w(gp, "dv", dv_now)
                for i in range(4):
                    P.dma(gp, v_sp[H * 4 + i, 4 * g + tt], vstg[n % 2][:, i * 128:(i + 1) * 128], gsem)
            spill_hist["v"].append((gsem, P.v(gsem)))

        def emit_kv_units(H, g, units):
            for u in units:
                if u < 4:
                    emit_k_unit(H, g, u)
                else:
                    emit_v_unit(H, g, u - 4)

        for t in range(n_tiles):
            emit_p1_tile(t)
            g = t // 4
            ph = t % 4
            if ph == 3:
                prev_pe = y1g_done_pe.get(("kv", 0, g - 2), 0)
                emit_y1g_load(("kv", 0, g), g, g % 2, prev_pe)
            if g >= 1:
                emit_kv_units(0, g - 1, [2 * ph, 2 * ph + 1])
        emit_kv_units(0, n_groups - 1, list(range(8)))
        last_h0_pe = P.v("pe")
        emit_wkv_cast(1, wait_pe=last_h0_pe)
        for g in range(n_groups):
            prev_pe = y1g_done_pe.get(("kv", 1, g - 2), y1g_done_pe.get(("kv", 0, g + n_groups - 2), 0))
            emit_y1g_load(("kv", 1, g), g, g % 2, prev_pe)
            emit_kv_units(1, g, list(range(8)))

        kv_end = P.snap()
        for hh in range(8):
            qkv_spill_snap[hh] = {nm: kv_end[nm] for nm in ("gk0", "gk1", "gv0", "gv1")}

        p1_end = P.snap()
        p1ctx.close()

        p2qctx = ExitStack()
        wqin = [p2qctx.enter_context(nc.sbuf_tensor(f"wqin{i}", [128, TOWN], F32, side="right")) for i in range(2)]
        wacc = [p2qctx.enter_context(nc.sbuf_tensor(f"wacc{i}", [128, 512], F32, side="right")) for i in range(2)]
        wq_eff = p2qctx.enter_context(nc.sbuf_tensor("wq_eff", [128, NCH * 512], BF16, side="right"))
        qstg = [p2qctx.enter_context(nc.sbuf_tensor(f"qstg{i}", [128, 512], BF16, side="right")) for i in range(2)]

        for engsec in (block.tensor, block.vector, block.scalar, block.sync, block.gpsimd):
            @engsec
            def _(e):
                for nm, val in p1_end.items():
                    P.w(e, nm, val)

        presum_list = []

        def emit_presum_chunk(H, ch, wait_pe=0):
            n = len(presum_list)
            slot = n % 2

            @block.sync
            def _(s, H=H, ch=ch, slot=slot, n=n):
                if n >= 2:
                    P.w(s, "dv", presum_list[n - 2])
                P.dma(s, wqin[slot][:], wq[ch * 128:(ch + 1) * 128, H * TOWN:(H + 1) * TOWN], "sw%d" % slot)
            sl_now = P.v("sw%d" % slot)

            @block.vector
            def _(v, H=H, ch=ch, slot=slot, sl_now=sl_now, wait_pe=wait_pe):
                P.w(v, "sw%d" % slot, sl_now)
                if wait_pe and ch == 0:
                    P.w(v, "pe", wait_pe)
                srcq = wqin[slot][:].rearrange("p (kv g d) -> p kv g d", kv=4, g=4)
                acc = wacc[slot][:].rearrange("p (kv d) -> p kv d", kv=4)
                v.tensor_add(acc, srcq[:, :, 0, :], srcq[:, :, 1, :])
                v.drain()
                v.tensor_add(acc, acc, srcq[:, :, 2, :])
                v.drain()
                v.tensor_add(acc, acc, srcq[:, :, 3, :])
                v.drain()
                ins = v.tensor_scalar(wq_eff[:, ch * 512:(ch + 1) * 512], wacc[slot][:],
                                      RSD, None, op0=ALU.mult)
                P.inc(ins, "dv")
            presum_list.append(P.v("dv"))

        for H in range(2):
            wait_pe = 0
            if H == 1:
                wait_pe = P.v("pe")
            for ch in range(NCH):
                emit_presum_chunk(H, ch, wait_pe=wait_pe)
            presum_done = P.v("dv")

            for g in range(n_groups_own):
                gslot = g % 2
                prev_pe = y1g_done_pe.get(("q", H, g - 2), y1g_done_pe.get(("q", H - 1, g + n_groups_own - 2), 0))
                if H == 0 and g < 2:
                    prev_pe = 0
                emit_y1g_load(("q", H, g), g, gslot, prev_pe or kv_end["pe"])

                for ct in range(4):
                    n = len(evac_hist["q"])
                    pslot = 2 + ct % 2

                    @block.tensor
                    def _(t, H=H, g=g, ct=ct, pslot=pslot, gslot=gslot, n=n, presum_done=presum_done):
                        P.w(t, *y1g_sl[("q", H, g)])
                        if g == 0 and ct == 0:
                            P.w(t, "dv", presum_done)
                        if n >= 2:
                            P.w(t, "dv", evac_hist["q"][n - 2])
                        for ch in range(NCH):
                            ins = t.matmul(ps2a[:, pslot * 512:(pslot + 1) * 512],
                                           wq_eff[:, ch * 512 + ct * 128: ch * 512 + (ct + 1) * 128],
                                           y1g[gslot][:, ch * 512:(ch + 1) * 512],
                                           start=(ch == 0), stop=(ch == NCH - 1))
                        P.inc(ins, "pe")
                    pe_now = P.v("pe")
                    if ct == 3:
                        y1g_done_pe[("q", H, g)] = pe_now

                    @block.vector
                    def _(v, pe_now=pe_now, pslot=pslot, n=n):
                        P.w(v, "pe", pe_now)
                        if n >= 2:
                            P.w(v, *spill_hist["q"][n - 2])
                        ins = v.tensor_copy(qstg[n % 2][:], ps2a[:, pslot * 512:(pslot + 1) * 512])
                        P.inc(ins, "dv")
                    evac_hist["q"].append(P.v("dv"))
                    dv_now = P.v("dv")
                    gsem = "gq%d" % (n % 2)

                    @block.gpsimd
                    def _(gp, H=H, g=g, ct=ct, dv_now=dv_now, n=n, gsem=gsem):
                        P.w(gp, "dv", dv_now)
                        P.dma(gp, q_sp[H * 4 + ct][:, g * 512:(g + 1) * 512], qstg[n % 2][:], gsem)
                    spill_hist["q"].append((gsem, P.v(gsem)))

        q_end = P.snap()
        for hh in range(8):
            qkv_spill_snap[hh] = dict(qkv_spill_snap[hh])
            qkv_spill_snap[hh].update({nm: q_end[nm] for nm in ("gq0", "gq1")})
        p2qctx.close()

        p2_end = P.snap()
        p2rctx.close()

        for engsec in (block.tensor, block.vector, block.scalar, block.sync, block.gpsimd):
            @engsec
            def _(e):
                for nm, val in p2_end.items():
                    P.w(e, nm, val)

        p3vctx = ExitStack()
        valuesT = p3vctx.enter_context(nc.sbuf_tensor("valuesT", [128, HKV * TOWN], BF16, side="right"))
        p3ctx = ExitStack()
        kbuf = [p3ctx.enter_context(nc.sbuf_tensor(f"kbuf{i}", [128, S], BF16, side="right")) for i in range(2)]
        vbuf = [p3ctx.enter_context(nc.sbuf_tensor(f"vbuf{i}", [128, S], BF16, side="right")) for i in range(2)]
        qbuf = [p3ctx.enter_context(nc.sbuf_tensor(f"qbuf{i}", [128, TOWN], BF16, side="right")) for i in range(2)]
        PTb = [p3ctx.enter_context(nc.sbuf_tensor(f"PTb{i}", [128, 4 * S], BF16, side="right")) for i in range(2)]
        recd_sb = p3ctx.enter_context(nc.sbuf_tensor("recd_sb", [1, 512], F32, side="right"))
        mhjunk = p3ctx.enter_context(nc.sbuf_tensor("mhjunk", [128, S], BF16, side="right"))
        bias_h = p3ctx.enter_context(nc.sbuf_tensor("bias_h", [128, 2], F32, side="right"))
        bcast_sb = p3ctx.enter_context(nc.sbuf_tensor("bcast_sb", [128, 1024], F32, side="right"))
        psS = [p3ctx.enter_context(nc.psum_tensor(f"psS{i}", [128, 1024], F32)) for i in range(3)]
        psV3 = p3ctx.enter_context(nc.psum_tensor("psV3", [128, 512], F32))
        psX = p3ctx.enter_context(nc.psum_tensor("psX", [128, 512], F32))

        for nm in ("gld0", "gld1", "sp0", "sp1", "sp2", "sp3"):
            P.need(nm)

        @block.vector
        def _(v):
            P.inc(v.memset(valuesT[:], 0.0), "dv")

        NKEY = n_groups * 512
        NKC = NKEY // 128
        G3 = n_groups_own
        head_done_pe = {}
        pv_dv = {}
        recd_dv = {}
        bcast_dv = {}
        exp_hist_pergroup = {}
        exp_hist = {}
        n_heads = min(HKV, 8)

        for h in range(n_heads):
            hs = h % 2
            gld = "gld%d" % hs
            recd_dv = {}
            bcast_dv = dict(bcast_dv)
            exp_hist_pergroup = {}

            @block.gpsimd
            def _(g, h=h, hs=hs, gld=gld):
                if h >= 2:
                    P.w(g, "pe", head_done_pe[h - 2])
                for nm, val in qkv_spill_snap[h].items():
                    P.w(g, nm, val)
                P.dma(g, kbuf[hs][:, 0:NKEY], k_sp[h][:, 0:NKEY], gld)
                P.dma(g, vbuf[hs][:, 0:NKEY], v_sp[h][0:NKC].rearrange("c p j -> p c j"), gld)
                P.dma(g, qbuf[hs][:, 0:G3 * 512], q_sp[h][:, 0:G3 * 512], gld)
            gld_now = P.v(gld)

            @block.vector
            def _(v, h=h, hs=hs, gld_now=gld_now, gld=gld):
                P.w(v, gld, gld_now)
                if h >= 1:
                    P.w(v, "pe", P.v("pe"))
                ins = v.scalar_tensor_tensor(mhjunk[:, 0:G3 * 512], qbuf[hs][:, 0:G3 * 512], 1.0,
                                             qbuf[hs][:, 0:G3 * 512], op0=ALU.mult, op1=ALU.mult)
                P.inc(ins, "dv")
            sc_q2 = ("dv", P.v("dv"))

            for i in range(G3):
                @block.tensor
                def _(t, i=i, sc_q2=sc_q2):
                    P.w(t, *sc_q2)
                    if i > 0:
                        P.w(t, "dv", P.v("dv"))
                    ins = t.matmul(psX[0:1, :], onesc[:], mhjunk[:, i * 512:(i + 1) * 512],
                                   start=True, stop=True)
                    P.inc(ins, "pe")
                pe_now = P.v("pe")

                @block.vector
                def _(v, h=h, i=i, pe_now=pe_now):
                    P.w(v, "pe", pe_now)
                    v.reduce_max(st1[:, 1:2], psX[0:1, :], axis=AXX)
                    v.drain()
                    if i == 0:
                        ins = v.tensor_copy(st1[:, 2:3], st1[:, 1:2])
                    else:
                        ins = v.tensor_tensor(st1[:, 2:3], st1[:, 2:3], st1[:, 1:2], op=ALU.max)
                    v.drain()
                    P.inc(ins, "dv")

            pe_q2done = P.v("pe")

            @block.vector
            def _(v, h=h, hs=hs, pe_q2done=pe_q2done):
                P.w(v, "pe", pe_q2done)
                ins = v.scalar_tensor_tensor(mhjunk[:, 0:NKEY], kbuf[hs][:, 0:NKEY], 1.0,
                                             kbuf[hs][:, 0:NKEY], op0=ALU.mult, op1=ALU.mult)
                P.inc(ins, "dv")
            sc_k2 = ("dv", P.v("dv"))

            for i in range(NKEY // 512):
                @block.tensor
                def _(t, i=i, sc_k2=sc_k2):
                    P.w(t, *sc_k2)
                    P.w(t, "dv", P.v("dv"))
                    ins = t.matmul(psX[0:1, :], onesc[:], mhjunk[:, i * 512:(i + 1) * 512],
                                   start=True, stop=True)
                    P.inc(ins, "pe")
                pe_now = P.v("pe")

                @block.vector
                def _(v, h=h, i=i, pe_now=pe_now):
                    P.w(v, "pe", pe_now)
                    v.reduce_max(st1[:, 1:2], psX[0:1, :], axis=AXX)
                    v.drain()
                    if i == 0:
                        ins = v.tensor_copy(st1[:, 3:4], st1[:, 1:2])
                    else:
                        ins = v.tensor_tensor(st1[:, 3:4], st1[:, 3:4], st1[:, 1:2], op=ALU.max)
                    v.drain()
                    P.inc(ins, "dv")

            @block.vector
            def _(v):
                ins = v.tensor_mul(st1[:, 4:5], st1[:, 2:3], st1[:, 3:4])
                P.inc(ins, "dv")
            dv_m = P.v("dv")

            @block.scalar
            def _(s, dv_m=dv_m):
                P.w(s, "dv", dv_m)
                s.activation(st1[:, 4:5], st1[:, 4:5], AF.Ln)
                s.drain()
                ins = s.activation(st1[:, 4:5], st1[:, 4:5], AF.Exp, scale=0.5)
                P.inc(ins, "sc")
            sc_m = P.v("sc")

            @block.vector
            def _(v, sc_m=sc_m):
                P.w(v, "sc", sc_m)
                v.tensor_scalar(st1[:, 4:5], st1[:, 4:5], -1.02, -1.0,
                                op0=ALU.mult, op1=ALU.add)
                v.drain()
                ins = v.tensor_copy(st1b[:, 0:1], st1[:, 4:5])
                P.inc(ins, "dv")
            dv_mb = P.v("dv")

            @block.tensor
            def _(t, dv_mb=dv_mb):
                P.w(t, "dv", dv_mb)
                ins = t.matmul(psX[:, 0:1], ones_r[:], st1b[:, 0:1], start=True, stop=True)
                P.inc(ins, "pe")
            pe_b = P.v("pe")

            @block.vector
            def _(v, hs=hs, pe_b=pe_b):
                P.w(v, "pe", pe_b)
                ins = v.tensor_copy(bias_h[:, hs:hs + 1], psX[:, 0:1])
                P.inc(ins, "dv")
            sc_bias = ("dv", P.v("dv"))

            def emit_PV(g, h=h, hs=hs):
                gs3 = g % 2

                @block.tensor
                def _(t, g=g, gs3=gs3):
                    P.w(t, "sc", exp_hist_pergroup[g])
                    prevdv = pv_dv.get((h, g - 1)) if g >= 1 else (pv_dv.get((h - 1, G3 - 1)) if h >= 1 else None)
                    if prevdv is not None:
                        P.w(t, "dv", prevdv)
                    for kc in range(NKC):
                        ins = t.matmul(psV3[:], vbuf[hs][:, kc * 128:(kc + 1) * 128],
                                       PTb[gs3][:, kc * 512:(kc + 1) * 512],
                                       start=(kc == 0), stop=(kc == NKC - 1))
                    P.inc(ins, "pe")
                pe_pv = P.v("pe")

                @block.vector
                def _(v, g=g, gs3=gs3, pe_pv=pe_pv):
                    P.w(v, "pe", pe_pv)
                    P.w(v, *bcast_dv[g])
                    for qt in range(4):
                        ins = v.tensor_mul(valuesT[:, h * TOWN + g * 512 + qt * 128: h * TOWN + g * 512 + (qt + 1) * 128],
                                           psV3[:, qt * 128:(qt + 1) * 128],
                                           bcast_sb[:, gs3 * 512 + qt * 128: gs3 * 512 + (qt + 1) * 128])
                    P.inc(ins, "dv")
                pv_dv[(h, g)] = P.v("dv")

            for g in range(G3):
                gs3 = g % 2
                den_mm = {}

                for kp in range(NKC // 2):
                    sslot = kp % 3

                    @block.tensor
                    def _(t, g=g, kp=kp, sslot=sslot):
                        if (g, kp) == (0, 0):
                            P.w(t, *sc_bias)
                        prev = exp_hist.get(sslot)
                        if prev is not None:
                            P.w(t, "sc", prev)
                        for u in range(2):
                            kc = 2 * kp + u
                            ins = t.matmul(psS[sslot][:, u * 512:(u + 1) * 512],
                                           kbuf[hs][:, kc * 128:(kc + 1) * 128],
                                           qbuf[hs][:, g * 512:(g + 1) * 512],
                                           start=True, stop=True)
                        P.inc(ins, "pe")
                    pe_s = P.v("pe")

                    @block.scalar
                    def _(s, g=g, kp=kp, sslot=sslot, gs3=gs3, pe_s=pe_s):
                        P.w(s, "pe", pe_s)
                        if kp == 0:
                            prevpv = pv_dv.get((h, g - 2)) if g >= 2 else \
                                (pv_dv.get((h - 1, G3 - 2 + g)) if h >= 1 else None)
                            if prevpv is not None:
                                P.w(s, "dv", prevpv)
                        ins = s.activation(PTb[gs3][:, kp * 1024:(kp + 1) * 1024],
                                           psS[sslot][:, 0:1024], AF.Exp,
                                           bias=bias_h[:, hs:hs + 1])
                        P.inc(ins, "sc")
                    exp_hist[sslot] = P.v("sc")

                    @block.tensor
                    def _(t, g=g, kp=kp):
                        P.w(t, "sc", exp_hist[kp % 3])
                        if kp == 0:
                            prevrd = recd_dv.get(g - 1)
                            if prevrd is not None:
                                P.w(t, "dv", prevrd)
                        for u in range(2):
                            kc = 2 * kp + u
                            ins = t.matmul(psX[0:1, :], onesc[:],
                                           PTb[g % 2][:, kc * 512:(kc + 1) * 512],
                                           start=(kc == 0), stop=(kc == NKC - 1))
                        P.inc(ins, "pe")
                pe_den = P.v("pe")

                @block.vector
                def _(v, g=g, pe_den=pe_den):
                    P.w(v, "pe", pe_den)
                    v.tensor_scalar(recd_sb[0:1, 0:512], psX[0:1, :], P40, None, op0=ALU.mult)
                    v.drain()
                    v.reciprocal(recd_sb[0:1, 0:512], recd_sb[0:1, 0:512])
                    v.drain()
                    ins = v.tensor_scalar(recd_sb[0:1, 0:512], recd_sb[0:1, 0:512], P40, None, op0=ALU.mult)
                    P.inc(ins, "dv")
                recd_dv[g] = P.v("dv")
                dv_rd = P.v("dv")

                @block.tensor
                def _(t, g=g, dv_rd=dv_rd):
                    P.w(t, "dv", dv_rd)
                    prevbc = bcast_dv.get(g - 1)
                    if prevbc is not None:
                        P.w(t, *prevbc)
                    ins = t.matmul(psX[:, 0:512], ones_rf[:], recd_sb[0:1, 0:512],
                                   start=True, stop=True)
                    P.inc(ins, "pe")
                pe_bc = P.v("pe")

                @block.vector
                def _(v, g=g, gs3=gs3, pe_bc=pe_bc):
                    P.w(v, "pe", pe_bc)
                    prevpv = pv_dv.get((h, g - 2))
                    if prevpv is not None:
                        P.w(v, "dv", prevpv)
                    ins = v.tensor_copy(bcast_sb[:, gs3 * 512:(gs3 + 1) * 512], psX[:, 0:512])
                    P.inc(ins, "dv")
                bcast_dv[g] = ("dv", P.v("dv"))

                exp_hist_pergroup[g] = exp_hist[(NKC // 2 - 1) % 3]

                if g >= 1:
                    emit_PV(g - 1)
            emit_PV(G3 - 1)
            head_done_pe[h] = P.v("pe")
        p3_vt_dv = P.v("dv")

        if debug:
            @block.gpsimd
            def _(g):
                P.w(g, "dv", p3_vt_dv)
                P.dma(g, vt_dbg, valuesT[:], "gv0")

        p3_end = P.snap()
        p3ctx.close()

        for engsec in (block.tensor, block.vector, block.scalar, block.sync, block.gpsimd):
            @engsec
            def _(e):
                for nm, val in p3_end.items():
                    P.w(e, nm, val)

        NT4 = G3 * 4
        SGL = [(s, min(8, NT4 - s)) for s in range(0, NT4, 8)]

        bbrow = ctx.enter_context(nc.sbuf_tensor("bbrow", [1, D], BF16))
        AW = ctx.enter_context(nc.sbuf_tensor("AW", [128, NCH * 136], BF16))
        Bfl = ctx.enter_context(nc.sbuf_tensor("Bfl", [128, D], BF16))
        fsums = ctx.enter_context(nc.sbuf_tensor("fsums", [128, 64], F32))
        fssq = ctx.enter_context(nc.sbuf_tensor("fssq", [128, 64], F32))

        p4ctx = ExitStack()
        wo_sb = p4ctx.enter_context(nc.sbuf_tensor("wo_sb", [128, 8 * D], BF16, side="right"))
        attnsb = p4ctx.enter_context(nc.sbuf_tensor("attnsb", [128, D], F32, side="right"))
        _xo = p4ctx.enter_context(nc.sbuf_tensor("xo0", [128, D], F32, side="right"))
        _x2b = p4ctx.enter_context(nc.sbuf_tensor("x2b0", [128, D], F32, side="right"))
        _y2b = p4ctx.enter_context(nc.sbuf_tensor("y2b0", [128, D], BF16, side="right"))
        xo = [_xo, _xo]
        x2b = [_x2b, _x2b]
        y2b = [_y2b, _y2b]
        psW = p4ctx.enter_context(nc.psum_tensor("psW", [128, 3072], F32))
        psC = p4ctx.enter_context(nc.psum_tensor("psC", [128, 2048], BF16))
        p4sctx = ExitStack()
        A_nat = p4sctx.enter_context(nc.sbuf_tensor("A_nat", [128, D], BF16, side="right"))
        Wr_nat = p4sctx.enter_context(nc.sbuf_tensor("Wr_nat", [8, D], BF16, side="right"))
        Bexp = [p4sctx.enter_context(nc.sbuf_tensor(f"Bexp{i}", [128, 1024], BF16, side="right")) for i in range(2)]

        @block.gpsimd
        def _(g):
            P.dma(g, bbrow[:], bb, "gl")
            P.dma(g, A_nat[:], Ast.rearrange("e r d -> (e r) d"), "gl")
            P.dma(g, Wr_nat[:], Wr, "gl")
        aw_gl = P.v("gl")

        aw_sc = []
        for ch in range(NCH):
            cslot = ch % 2

            @block.tensor
            def _(t, ch=ch, cslot=cslot):
                if ch == 0:
                    P.w(t, "gl", aw_gl)
                if ch >= 2:
                    P.w(t, *aw_sc[ch - 2])
                t.transpose(psC[:, cslot * 1024: cslot * 1024 + 128],
                            A_nat[:, ch * 128:(ch + 1) * 128], ident[:])
                ins = t.transpose(psC[:, cslot * 1024 + 128: cslot * 1024 + 136],
                                  Wr_nat[:, ch * 128:(ch + 1) * 128], ident[0:8, 0:8])
                P.inc(ins, "pe")
            pe_now = P.v("pe")

            @block.vector
            def _(v, ch=ch, cslot=cslot, pe_now=pe_now):
                P.w(v, "pe", pe_now)
                v.tensor_copy(AW[:, ch * 136: ch * 136 + 128], psC[:, cslot * 1024: cslot * 1024 + 128])
                v.drain()
                ins = v.tensor_copy(AW[:, ch * 136 + 128: (ch + 1) * 136],
                                    psC[:, cslot * 1024 + 128: cslot * 1024 + 136])
                P.inc(ins, "dv")
            aw_sc.append(("dv", P.v("dv")))

        bexp_gl = {}
        bfl_sc = []
        for ep in range(4):
            es = ep % 2

            @block.gpsimd
            def _(g, ep=ep, es=es):
                if ep >= 2:
                    P.w(g, *bfl_sc[(ep - 2) * NCH + NCH - 1])
                P.need("gb%d" % es)
                for u in range(2):
                    srcb = Bst[2 * ep + u].rearrange("(oc p) r -> p oc r", p=128)
                    dstb = Bexp[es][:].rearrange("p (oc u r) -> p oc u r", u=2, r=16)[:, :, u, :]
                    P.dma(g, dstb, srcb, "gb%d" % es)
            bexp_gl[ep] = ("gb%d" % es, P.v("gb%d" % es))

            for oc in range(NCH):
                bs = oc % 2

                @block.tensor
                def _(t, ep=ep, oc=oc, es=es, bs=bs):
                    if oc == 0:
                        P.w(t, *bexp_gl[ep])
                    if ep == 0 and oc == 0:
                        P.w(t, *aw_sc[-1])
                    if len(bfl_sc) >= 2:
                        P.w(t, *bfl_sc[-2])
                    ins = t.transpose(psC[0:32, bs * 1024 + 256: bs * 1024 + 384],
                                      Bexp[es][:, oc * 32:(oc + 1) * 32], ident[:])
                    P.inc(ins, "pe")
                pe_now = P.v("pe")

                @block.vector
                def _(v, ep=ep, oc=oc, bs=bs, pe_now=pe_now):
                    P.w(v, "pe", pe_now)
                    ins = v.tensor_copy(Bfl[ep * 32:(ep + 1) * 32, oc * 128:(oc + 1) * 128],
                                        psC[0:32, bs * 1024 + 256: bs * 1024 + 384])
                    P.inc(ins, "dv")
                bfl_sc.append(("dv", P.v("dv")))

        cst_end = P.snap()
        p4sctx.close()
        for engsec in (block.tensor, block.vector, block.scalar, block.sync, block.gpsimd):
            @engsec
            def _(e, snap=cst_end):
                for nm, val in snap.items():
                    P.w(e, nm, val)

        P.need("gwo")

        @block.gpsimd
        def _(g):
            srcw = wo.rearrange("(c p) f -> p c f", p=128)
            P.dma(g, wo_sb[:].rearrange("p (c f) -> p c f", c=8), srcw, "gwo")
        wo_gl = P.v("gwo")

        wo_evac = {}
        x2done_dv = {}
        y2_done_sc = {}
        x2sp_gs = {}
        y2sp_gs = {}
        P.need("gx0"); P.need("gx1")

        for tt in range(NT4):
            ts4 = tt % 2
            sb4 = 40 + ts4 * 12

            P.need("sxo%d" % ts4)

            @block.sync
            def _(s, tt=tt, ts4=ts4):
                prev = x2done_dv.get(tt - 1)
                if prev:
                    P.w(s, "dv", prev)
                P.dma(s, xo[ts4][:], xa[tt * 128:(tt + 1) * 128, :], "sxo%d" % ts4)
            sl_xo = P.v("sxo%d" % ts4)

            for part, (obase, olen) in enumerate([(0, 3072), (3072, 1024)]):
                @block.tensor
                def _(t, tt=tt, part=part, obase=obase, olen=olen):
                    if tt == 0 and part == 0:
                        P.w(t, "gwo", wo_gl)
                        P.w(t, "dv", p3_vt_dv)
                    prev = wo_evac.get((tt, part - 1), wo_evac.get((tt - 1, 1)))
                    if prev:
                        P.w(t, "sc", prev)
                    for ch in range(8):
                        for oc in range(olen // 512):
                            ins = t.matmul(psW[:, oc * 512:(oc + 1) * 512],
                                           valuesT[:, ch * TOWN + tt * 128:(ch * TOWN) + (tt + 1) * 128],
                                           wo_sb[:, ch * D + obase + oc * 512: ch * D + obase + (oc + 1) * 512],
                                           start=(ch == 0), stop=(ch == 7))
                    P.inc(ins, "pe")
                pe_wo = P.v("pe")

                @block.scalar
                def _(s, tt=tt, part=part, obase=obase, olen=olen, sb4=sb4, pe_wo=pe_wo):
                    P.w(s, "pe", pe_wo)
                    ins = s.activation(attnsb[:, obase:obase + olen], psW[:, 0:olen], AF.Identity,
                                       accum_out=stat[:, sb4 + part: sb4 + part + 1])
                    P.inc(ins, "sc")
                wo_evac[(tt, part)] = P.v("sc")
            sc_attn = P.v("sc")

            @block.vector
            def _(v, tt=tt, ts4=ts4, sb4=sb4, sc_attn=sc_attn):
                P.w(v, "sc", sc_attn)
                prev = y2sp_gs.get(tt - 1)
                if prev:
                    P.w(v, *prev)
                v.scalar_tensor_tensor(y2b[ts4][:, 0:2048], attnsb[:, 0:2048], 1.0,
                                       attnsb[:, 0:2048], op0=ALU.mult, op1=ALU.mult,
                                       accum_out=stat[:, sb4 + 2: sb4 + 3])
                v.drain()
                v.scalar_tensor_tensor(y2b[ts4][:, 0:2048], attnsb[:, 2048:4096], 1.0,
                                       attnsb[:, 2048:4096], op0=ALU.mult, op1=ALU.mult,
                                       accum_out=stat[:, sb4 + 3: sb4 + 4])
                v.drain()
                v.tensor_tensor(stat[:, sb4 + 4: sb4 + 5], stat[:, sb4: sb4 + 1],
                                stat[:, sb4 + 1: sb4 + 2], op=ALU.add)
                v.drain()
                v.tensor_scalar(stat[:, sb4 + 4: sb4 + 5], stat[:, sb4 + 4: sb4 + 5],
                                1.0 / D, None, op0=ALU.mult)
                v.drain()
                v.tensor_tensor(stat[:, sb4 + 5: sb4 + 6], stat[:, sb4 + 2: sb4 + 3],
                                stat[:, sb4 + 3: sb4 + 4], op=ALU.add)
                v.drain()
                v.scalar_tensor_tensor(stat[:, sb4 + 5: sb4 + 6], stat[:, sb4 + 5: sb4 + 6],
                                       1.0 / D, stat[:, sb4 + 4: sb4 + 5],
                                       op0=ALU.mult, op1=ALU.bypass)
                v.drain()
                v.scalar_tensor_tensor(stat[:, sb4 + 6: sb4 + 7], stat[:, sb4 + 4: sb4 + 5],
                                       -1.0, stat[:, sb4 + 4: sb4 + 5], op0=ALU.mult, op1=ALU.mult)
                v.drain()
                v.tensor_tensor(stat[:, sb4 + 6: sb4 + 7], stat[:, sb4 + 6: sb4 + 7],
                                stat[:, sb4 + 5: sb4 + 6], op=ALU.add)
                v.drain()
                ins = v.tensor_scalar(stat[:, sb4 + 6: sb4 + 7], stat[:, sb4 + 6: sb4 + 7],
                                      EPS, None, op0=ALU.add)
                P.inc(ins, "dv")
            dv_rec2 = P.v("dv")

            @block.scalar
            def _(s, sb4=sb4, dv_rec2=dv_rec2):
                P.w(s, "dv", dv_rec2)
                s.activation(stat[:, sb4 + 7: sb4 + 8], stat[:, sb4 + 6: sb4 + 7], AF.Ln)
                s.drain()
                ins = s.activation(stat[:, sb4 + 7: sb4 + 8], stat[:, sb4 + 7: sb4 + 8], AF.Exp, scale=-0.5)
                P.inc(ins, "sc")
            sc_r2 = P.v("sc")

            @block.vector
            def _(v, sb4=sb4, sc_r2=sc_r2):
                P.w(v, "sc", sc_r2)
                ins = v.scalar_tensor_tensor(stat[:, sb4 + 8: sb4 + 9], stat[:, sb4 + 4: sb4 + 5],
                                             -1.0, stat[:, sb4 + 7: sb4 + 8],
                                             op0=ALU.mult, op1=ALU.mult)
                P.inc(ins, "dv")
            dv_mb2 = P.v("dv")

            @block.scalar
            def _(s, tt=tt, ts4=ts4, sb4=sb4, dv_mb2=dv_mb2, sl_xo=sl_xo):
                P.w(s, "dv", dv_mb2)
                P.w(s, "sxo%d" % ts4, sl_xo)
                prevsp = x2sp_gs.get(tt - 1)
                if prevsp:
                    P.w(s, *prevsp)
                s.drain()
                ins = s.activation(x2b[ts4][:], xo[ts4][:], AF.Identity,
                                   bias=stat[:, sb4 + 8: sb4 + 9], scale=1.0)
                P.inc(ins, "sc")
            sc_xo = P.v("sc")

            @block.vector
            def _(v, tt=tt, ts4=ts4, sb4=sb4, sc_xo=sc_xo):
                P.w(v, "sc", sc_xo)
                v.drain()
                ins = v.scalar_tensor_tensor(x2b[ts4][:], attnsb[:], stat[:, sb4 + 7: sb4 + 8],
                                             x2b[ts4][:], op0=ALU.mult, op1=ALU.add)
                P.inc(ins, "dv")
            x2done_dv[tt] = P.v("dv")
            dv_x2 = P.v("dv")

            @block.gpsimd
            def _(g, tt=tt, ts4=ts4, dv_x2=dv_x2):
                P.w(g, "dv", dv_x2)
                P.dma(g, x2_sp[tt], x2b[ts4][:], "gx%d" % ts4)
            x2sp_gs[tt] = ("gx%d" % ts4, P.v("gx%d" % ts4))

            @block.vector
            def _(v, tt=tt, ts4=ts4, sb4=sb4, dv_x2=dv_x2):
                v.drain()
                ins = v.scalar_tensor_tensor(y2b[ts4][:], x2b[ts4][:], 1.0, x2b[ts4][:],
                                             op0=ALU.mult, op1=ALU.mult,
                                             accum_out=stat[:, sb4 + 9: sb4 + 10])
                P.inc(ins, "dv")
            sc_sq3 = ("dv", P.v("dv"))

            @block.vector
            def _(v, ts4=ts4, sb4=sb4, sc_sq3=sc_sq3):
                v.reduce_sum(stat[:, sb4 + 10: sb4 + 11], x2b[ts4][:], axis=AXX)
                v.drain()
                v.tensor_scalar(stat[:, sb4 + 10: sb4 + 11], stat[:, sb4 + 10: sb4 + 11],
                                1.0 / D, None, op0=ALU.mult)
                v.drain()
                v.scalar_tensor_tensor(stat[:, sb4 + 11: sb4 + 12], stat[:, sb4 + 10: sb4 + 11],
                                       -1.0, stat[:, sb4 + 10: sb4 + 11], op0=ALU.mult, op1=ALU.mult)
                v.drain()
                v.scalar_tensor_tensor(stat[:, sb4 + 11: sb4 + 12], stat[:, sb4 + 9: sb4 + 10],
                                       1.0 / D, stat[:, sb4 + 11: sb4 + 12], op0=ALU.mult, op1=ALU.add)
                v.drain()
                ins = v.tensor_scalar(stat[:, sb4 + 11: sb4 + 12], stat[:, sb4 + 11: sb4 + 12],
                                      EPS, None, op0=ALU.add)
                P.inc(ins, "dv")
            dv_rec3 = P.v("dv")

            @block.scalar
            def _(s, sb4=sb4, dv_rec3=dv_rec3):
                P.w(s, "dv", dv_rec3)
                s.activation(stat[:, sb4 + 11: sb4 + 12], stat[:, sb4 + 11: sb4 + 12], AF.Ln)
                s.drain()
                ins = s.activation(stat[:, sb4 + 11: sb4 + 12], stat[:, sb4 + 11: sb4 + 12], AF.Exp, scale=-0.5)
                P.inc(ins, "sc")
            sc_r3 = P.v("sc")

            @block.vector
            def _(v, sb4=sb4, sc_r3=sc_r3):
                P.w(v, "sc", sc_r3)
                ins = v.scalar_tensor_tensor(stat[:, sb4 + 10: sb4 + 11], stat[:, sb4 + 10: sb4 + 11],
                                             -1.0, stat[:, sb4 + 11: sb4 + 12],
                                             op0=ALU.mult, op1=ALU.mult)
                P.inc(ins, "dv")
            dv_mb3 = P.v("dv")

            @block.scalar
            def _(s, tt=tt, ts4=ts4, sb4=sb4, dv_mb3=dv_mb3):
                P.w(s, "dv", dv_mb3)
                s.drain()
                ins = s.activation(y2b[ts4][:], x2b[ts4][:], AF.Identity,
                                   bias=stat[:, sb4 + 10: sb4 + 11],
                                   scale=stat[:, sb4 + 11: sb4 + 12])
                P.inc(ins, "sc")
            y2_done_sc[tt] = P.v("sc")
            sc_y2 = P.v("sc")

            @block.gpsimd
            def _(g, tt=tt, ts4=ts4, sc_y2=sc_y2):
                P.w(g, "sc", sc_y2)
                P.dma(g, y2_sp[tt], y2b[ts4][:], "gy%d" % ts4)
            y2sp_gs[tt] = ("gy%d" % ts4, P.v("gy%d" % ts4))

        p4_end = P.snap()
        p4ctx.close()
        p3vctx.close()

        for engsec in (block.tensor, block.vector, block.scalar, block.sync, block.gpsimd):
            @engsec
            def _(e):
                for nm, val in p4_end.items():
                    P.w(e, nm, val)

        p5ctx = ExitStack()
        y2rd = [p5ctx.enter_context(nc.sbuf_tensor(f"y2rd{i}", [128, D], BF16, side="right")) for i in range(2)]
        y2T = p5ctx.enter_context(nc.sbuf_tensor("y2T", [128, NCH * 1024], BF16, side="right"))
        hT_sb = p5ctx.enter_context(nc.sbuf_tensor("hT_sb", [128, 8 * 128], BF16, side="right"))
        hp_sb = p5ctx.enter_context(nc.sbuf_tensor("hp_sb", [128, 128], F32, side="right"))
        cmb = p5ctx.enter_context(nc.sbuf_tensor("cmb", [128, 32], F32, side="right"))
        _fst = p5ctx.enter_context(nc.sbuf_tensor("fstage0", [128, 8 * 512], BF16, side="right"))
        fstage = [_fst, _fst]
        fjunk = p5ctx.enter_context(nc.sbuf_tensor("fjunk", [128, 512], BF16, side="right"))
        psF = p5ctx.enter_context(nc.psum_tensor("psF", [128, 4096], F32))

        P.need("gw0"); P.need("gw1"); P.need("swt0"); P.need("swt1")

        P.need("gf0"); P.need("gf1"); P.need("sf0"); P.need("sf1")
        P.need("sxx0"); P.need("sxx1")

        y2t_tr = {}
        hT_done = {}
        base_done_sc = {}
        fspill = {}
        wbt_tr = {}
        wbn_hist = {0: [], 1: []}
        wbr_hist = {0: [], 1: []}
        y2rd_hist = {0: [], 1: []}
        out_done = []

        for sgi, (sg0, sgn) in enumerate(SGL):
            sgctx = ExitStack()
            WbTb = [sgctx.enter_context(nc.sbuf_tensor(f"WbTb{sgi}_{i}", [128, NCH * 512], BF16, side="right")) for i in range(2)]
            _wbn = sgctx.enter_context(nc.sbuf_tensor(f"wbn{sgi}_0", [128, D], BF16, side="right"))
            wbn = [_wbn, _wbn]

            for j in range(sgn):
                tt = sg0 + j
                js = j % 2

                @block.sync
                def _(s, tt=tt, js=js, sgi=sgi, j=j):
                    P.w(s, *y2sp_gs[tt])
                    if y2rd_hist[js]:
                        P.w(s, *y2rd_hist[js][-1])
                    P.dma(s, y2rd[js][:], y2_sp[tt], "sxx%d" % js)
                sl_y2 = P.v("sxx%d" % js)

                @block.sync
                def _(s, tt=tt, js=js, sgi=sgi, j=j, sl_y2=sl_y2):
                    P.w(s, "sxx%d" % js, sl_y2)
                    if sgi > 0 and j == 0:
                        P.w(s, "pe", base_done_sc[("pe", sgi - 1)])
                    srcy = y2rd[js][:].rearrange("p (c f) -> p c f", c=NCH)
                    dsty = y2T[:].rearrange("p (c f) -> p c f", c=NCH)[:, :, j * 128:(j + 1) * 128]
                    P.inc(s.dma_start_transpose(dsty, srcy), "swt%d" % js, 16)
                y2t_tr[(sgi, j)] = ("swt%d" % js, P.v("swt%d" % js))
                y2rd_hist[js].append(y2t_tr[(sgi, j)])

                @block.tensor
                def _(t, tt=tt, j=j, sgi=sgi):
                    P.w(t, *y2t_tr[(sgi, j)])
                    if (sgi, j) == (0, 0):
                        P.w(t, *bfl_sc[-1])
                    prev = hT_done.get((sgi, j - 2))
                    if prev:
                        P.w(t, *prev)
                    pbase = (j % 2) * 512
                    for ch in range(NCH):
                        ins = t.matmul(psF[:, pbase: pbase + 136],
                                       y2T[:, ch * 1024 + j * 128: ch * 1024 + (j + 1) * 128],
                                       AW[:, ch * 136:(ch + 1) * 136],
                                       start=(ch == 0), stop=(ch == NCH - 1))
                    P.inc(ins, "pe")
                pe_h = P.v("pe")

                cb = (j % 2) * 16

                @block.vector
                def _(v, j=j, cb=cb, pe_h=pe_h):
                    pbase = (j % 2) * 512
                    P.w(v, "pe", pe_h)
                    lg = psF[:, pbase + 128: pbase + 136]
                    v.reduce_max(cmb[:, cb: cb + 1], lg, axis=AXX)
                    v.drain()
                    v.tensor_scalar(cmb[:, cb + 8: cb + 16], lg, cmb[:, cb: cb + 1], None, op0=ALU.is_ge)
                    v.drain()
                    v.scalar_tensor_tensor(cmb[:, cb + 8: cb + 16], cmb[:, cb + 8: cb + 16], -1e30,
                                           lg, op0=ALU.mult, op1=ALU.add)
                    v.drain()
                    v.reduce_max(cmb[:, cb + 1: cb + 2], cmb[:, cb + 8: cb + 16], axis=AXX)
                    v.drain()
                    v.tensor_scalar(cmb[:, cb + 2: cb + 3], cmb[:, cb: cb + 1], -1.0, None, op0=ALU.mult)
                    v.drain()
                    ins = v.tensor_scalar(cmb[:, cb + 8: cb + 16], lg, cmb[:, cb + 1: cb + 2], None,
                                          op0=ALU.is_ge)
                    P.inc(ins, "dv")
                dv_c1 = P.v("dv")

                @block.scalar
                def _(s, j=j, cb=cb, dv_c1=dv_c1):
                    pbase = (j % 2) * 512
                    P.w(s, "dv", dv_c1)
                    ins = s.activation(cmb[:, cb + 4: cb + 12][:, 0:8], psF[:, pbase + 128: pbase + 136],
                                       AF.Exp, bias=cmb[:, cb + 2: cb + 3], scale=1.0)
                    P.inc(ins, "sc")
                sc_c1 = P.v("sc")

                @block.vector
                def _(v, cb=cb, sc_c1=sc_c1):
                    P.w(v, "sc", sc_c1)
                    v.tensor_mul(cmb[:, cb + 4: cb + 12], cmb[:, cb + 4: cb + 12], cmb[:, cb + 8: cb + 16])
                    v.drain()
                    v.reduce_sum(cmb[:, cb + 3: cb + 4], cmb[:, cb + 4: cb + 12], axis=AXX)
                    v.drain()
                    v.reciprocal(cmb[:, cb + 3: cb + 4], cmb[:, cb + 3: cb + 4])
                    v.drain()
                    ins = v.tensor_scalar(cmb[:, cb + 4: cb + 12], cmb[:, cb + 4: cb + 12],
                                          cmb[:, cb + 3: cb + 4], 2.0, op0=ALU.mult, op1=ALU.mult)
                    P.inc(ins, "dv")
                dv_c2 = P.v("dv")

                @block.vector
                def _(v, j=j, cb=cb, dv_c2=dv_c2):
                    pbase = (j % 2) * 512
                    v.drain()
                    for e8 in range(8):
                        ins = v.tensor_scalar(hp_sb[:, e8 * 16:(e8 + 1) * 16],
                                              psF[:, pbase + e8 * 16: pbase + (e8 + 1) * 16],
                                              cmb[:, cb + 4 + e8: cb + 5 + e8], None, op0=ALU.mult)
                    P.inc(ins, "dv")
                dv_hp = P.v("dv")

                @block.tensor
                def _(t, j=j, dv_hp=dv_hp):
                    pbase = (j % 2) * 512
                    P.w(t, "dv", dv_hp)
                    ins = t.transpose(psF[:, pbase + 256: pbase + 384], hp_sb[:], identf[:])
                    P.inc(ins, "pe")
                pe_hT = P.v("pe")

                @block.vector
                def _(v, j=j, pe_hT=pe_hT):
                    pbase = (j % 2) * 512
                    P.w(v, "pe", pe_hT)
                    ins = v.tensor_copy(hT_sb[:, j * 128:(j + 1) * 128], psF[:, pbase + 256: pbase + 384])
                    P.inc(ins, "dv")
                hT_done[(sgi, j)] = ("dv", P.v("dv"))
            sc_prep = P.v("sc")
            dv_prep = P.v("dv")

            evac_sc5 = {}
            for p in range(8):
                for ot in range(4):
                    ws = ot % 2

                    @block.gpsimd
                    def _(g, sgi=sgi, p=p, ot=ot, ws=ws):
                        for wsx in (0, 1):
                            if wbn_hist[wsx]:
                                P.w(g, *wbn_hist[wsx][-1])
                        rows = Wb[p * 512 + ot * 128: p * 512 + (ot + 1) * 128, :]
                        P.dma(g, wbn[ws][:], rows, "gw%d" % ws)
                    gl_wb = P.v("gw%d" % ws)

                    tsem = "sf%d%d" % (p % 2, ot % 2)
                    P.need(tsem)

                    @block.sync
                    def _(s, sgi=sgi, p=p, ot=ot, ws=ws, gl_wb=gl_wb, tsem=tsem):
                        P.w(s, "gw%d" % ws, gl_wb)
                        if ot == 0:
                            prev = evac_sc5.get((sgi, p - 2), evac_sc5.get((sgi - 1, p + 6)))
                            if prev:
                                P.w(s, "sc", prev)
                        srcw = wbn[ws][:].rearrange("p (c f) -> p c f", c=NCH)
                        dstw = WbTb[p % 2][:, ot * D:(ot + 1) * D].rearrange("p (c f) -> p c f", c=NCH)
                        P.inc(s.dma_start_transpose(dstw, srcw), tsem, 16)
                    wbt_tr[(sgi, p, ot)] = (tsem, P.v(tsem))
                    wbn_hist[ws].append((tsem, P.v(tsem)))

                @block.tensor
                def _(t, sgi=sgi, p=p, sgn=sgn, sc_prep=sc_prep, dv_prep=dv_prep):
                    for ot in range(4):
                        P.w(t, *wbt_tr[(sgi, p, ot)])
                    if p == 0:
                        P.w(t, "sc", sc_prep)
                        P.w(t, "dv", dv_prep)
                    prev = evac_sc5.get((sgi, p - 1), evac_sc5.get((sgi - 1, 7)))
                    if prev:
                        P.w(t, "sc", prev)
                    wtv = WbTb[p % 2][:].rearrange("p (o c j) -> p o c j", o=4, j=128)
                    for ch in range(NCH):
                        for j in range(sgn):
                            t.matmul(psF[:, j * 512:(j + 1) * 512],
                                     y2T[:, ch * 1024 + j * 128: ch * 1024 + (j + 1) * 128],
                                     wtv[:, :, ch, :],
                                     start=(ch == 0), stop=False)
                    for j in range(sgn):
                        t.matmul(psF[:, j * 512:(j + 1) * 512],
                                 hT_sb[:, j * 128:(j + 1) * 128],
                                 Bfl[:, p * 512:(p + 1) * 512],
                                 start=False, stop=False)
                    for j in range(sgn):
                        ins = t.matmul(psF[:, j * 512:(j + 1) * 512],
                                       ones_r[:], bbrow[0:1, p * 512:(p + 1) * 512],
                                       start=False, stop=True)
                    P.inc(ins, "pe")
                pe_base = P.v("pe")

                @block.scalar
                def _(s, sgi=sgi, p=p, sgn=sgn, pe_base=pe_base):
                    P.w(s, "pe", pe_base)
                    prev = fspill.get((sgi, p - 1), fspill.get((sgi - 1, 7)))
                    if prev:
                        P.w(s, *prev)
                    for j in range(sgn):
                        ins = s.activation(fstage[p % 2][:, j * 512:(j + 1) * 512],
                                           psF[:, j * 512:(j + 1) * 512], AF.Identity,
                                           accum_out=fsums[:, p * 8 + j: p * 8 + j + 1])
                    P.inc(ins, "sc")
                sc_evac5 = P.v("sc")

                @block.vector
                def _(v, sgi=sgi, p=p, sgn=sgn, sc_evac5=sc_evac5):
                    P.w(v, "sc", sc_evac5)
                    for j in range(sgn):
                        v.drain()
                        ins = v.scalar_tensor_tensor(fjunk[:], fstage[p % 2][:, j * 512:(j + 1) * 512],
                                                     1.0, fstage[p % 2][:, j * 512:(j + 1) * 512],
                                                     op0=ALU.mult, op1=ALU.mult,
                                                     accum_out=fssq[:, p * 8 + j: p * 8 + j + 1])
                    P.inc(ins, "dv")
                evac_sc5[(sgi, p)] = P.v("sc")
                dv_ssq5 = P.v("dv")

                @block.gpsimd
                def _(g, sgi=sgi, sg0=sg0, p=p, sgn=sgn, sc_evac5=sc_evac5, dv_ssq5=dv_ssq5):
                    P.w(g, "sc", sc_evac5)
                    P.w(g, "dv", dv_ssq5)
                    for j in range(sgn):
                        P.dma(g, f_sp[sg0 + j][:, p * 512:(p + 1) * 512],
                              fstage[p % 2][:, j * 512:(j + 1) * 512], "gf%d" % (p % 2))
                fspill[(sgi, p)] = ("gf%d" % (p % 2), P.v("gf%d" % (p % 2)))
            base_done_sc[("pe", sgi)] = P.v("pe")
            sg_base_end = P.snap()

            sgctx.close()
            for engsec in (block.tensor, block.vector, block.scalar, block.sync, block.gpsimd):
                @engsec
                def _(e, snap=sg_base_end):
                    for nm, val in snap.items():
                        P.w(e, nm, val)

            apctx = ExitStack()
            fread = [apctx.enter_context(nc.sbuf_tensor(f"fread{sgi}_{i}", [128, D], BF16, side="right")) for i in range(2)]
            xb2 = [apctx.enter_context(nc.sbuf_tensor(f"xb2{sgi}_{i}", [128, D], F32, side="right")) for i in range(2)]
            outsb = [apctx.enter_context(nc.sbuf_tensor(f"outsb{sgi}_{i}", [128, D], F32, side="right")) for i in range(2)]

            ap_dv = {}
            out_st = {}
            for j in range(sgn):
                tt = sg0 + j
                js = j % 2
                ab = 40 + js * 12

                P.need("sap%d" % js)

                @block.sync
                def _(s, tt=tt, j=j, js=js, sgi=sgi):
                    prev = ap_dv.get(j - 2)
                    if prev:
                        P.w(s, "dv", prev)
                    P.w(s, *fspill[(sgi, 7)])
                    P.w(s, *x2sp_gs[tt])
                    P.dma(s, fread[js][:], f_sp[tt], "sap%d" % js)
                    P.dma(s, xb2[js][:], x2_sp[tt], "sap%d" % js)
                sl_ap = P.v("sap%d" % js)

                @block.vector
                def _(v, j=j, js=js, ab=ab, sl_ap=sl_ap):
                    v.tensor_reduce(stat[:, ab: ab + 1], fsums[:, j: 57 + j: 8][:, 0:8], axis=AXX, op=ALU.add)
                    v.drain()
                    v.tensor_reduce(stat[:, ab + 1: ab + 2], fssq[:, j: 57 + j: 8][:, 0:8], axis=AXX, op=ALU.add)
                    v.drain()
                    v.tensor_scalar(stat[:, ab: ab + 1], stat[:, ab: ab + 1], 1.0 / D, None, op0=ALU.mult)
                    v.drain()
                    v.scalar_tensor_tensor(stat[:, ab + 2: ab + 3], stat[:, ab: ab + 1], -1.0,
                                           stat[:, ab: ab + 1], op0=ALU.mult, op1=ALU.mult)
                    v.drain()
                    v.scalar_tensor_tensor(stat[:, ab + 2: ab + 3], stat[:, ab + 1: ab + 2], 1.0 / D,
                                           stat[:, ab + 2: ab + 3], op0=ALU.mult, op1=ALU.add)
                    v.drain()
                    ins = v.tensor_scalar(stat[:, ab + 2: ab + 3], stat[:, ab + 2: ab + 3], EPS, None, op0=ALU.add)
                    P.inc(ins, "dv")
                dv_r4 = P.v("dv")

                @block.scalar
                def _(s, ab=ab, dv_r4=dv_r4):
                    P.w(s, "dv", dv_r4)
                    s.activation(stat[:, ab + 2: ab + 3], stat[:, ab + 2: ab + 3], AF.Ln)
                    s.drain()
                    ins = s.activation(stat[:, ab + 2: ab + 3], stat[:, ab + 2: ab + 3], AF.Exp, scale=-0.5)
                    P.inc(ins, "sc")
                sc_r4 = P.v("sc")

                @block.vector
                def _(v, ab=ab, sc_r4=sc_r4):
                    P.w(v, "sc", sc_r4)
                    ins = v.scalar_tensor_tensor(stat[:, ab + 3: ab + 4], stat[:, ab: ab + 1], -1.0,
                                                 stat[:, ab + 2: ab + 3], op0=ALU.mult, op1=ALU.mult)
                    P.inc(ins, "dv")
                dv_mb4 = P.v("dv")

                @block.scalar
                def _(s, j=j, js=js, ab=ab, dv_mb4=dv_mb4, sl_ap=sl_ap):
                    P.w(s, "dv", dv_mb4)
                    P.w(s, "sap%d" % js, sl_ap)
                    prevst = out_st.get(j - 2)
                    if prevst:
                        P.w(s, *prevst)
                    s.drain()
                    ins = s.activation(outsb[js][:], fread[js][:], AF.Identity,
                                       bias=stat[:, ab + 3: ab + 4], scale=stat[:, ab + 2: ab + 3])
                    P.inc(ins, "sc")
                sc_ln4 = P.v("sc")

                @block.vector
                def _(v, js=js, sc_ln4=sc_ln4):
                    P.w(v, "sc", sc_ln4)
                    v.drain()
                    ins = v.tensor_add(outsb[js][:], outsb[js][:], xb2[js][:])
                    P.inc(ins, "dv")
                ap_dv[j] = P.v("dv")
                dv_out = P.v("dv")

                P.need("so%d" % js)

                @block.sync
                def _(s, tt=tt, js=js, dv_out=dv_out):
                    P.w(s, "dv", dv_out)
                    P.dma(s, out[tt * 128:(tt + 1) * 128, :], outsb[js][:], "so%d" % js)
                out_st[j] = ("so%d" % js, P.v("so%d" % js))
                out_done.append(out_st[j])
            ap_end = P.snap()
            apctx.close()
            for engsec in (block.tensor, block.vector, block.scalar, block.sync, block.gpsimd):
                @engsec
                def _(e, snap=ap_end):
                    for nm, val in snap.items():
                        P.w(e, nm, val)

        @block.sync
        def _(s):
            for nm in ("so0", "so1"):
                if nm in P.sems:
                    s.wait_ge(P.sems[nm], P.v(nm))

    return nc


def _in_maps(inputs):
    x = np.ascontiguousarray(np.asarray(inputs["L0_attn_x"], dtype=np.float32))
    wq = np.ascontiguousarray(np.asarray(inputs["L0_attn_wq"], dtype=np.float32))
    wk = np.ascontiguousarray(np.asarray(inputs["L0_attn_wk"], dtype=np.float32))
    wv = np.ascontiguousarray(np.asarray(inputs["L0_attn_wv"], dtype=np.float32))
    wo = np.ascontiguousarray(np.asarray(inputs["L0_attn_wo"], dtype=np.float32))
    Wb = np.ascontiguousarray(np.asarray(inputs["L0_ffn_W_base"], dtype=np.float32))
    bb = np.ascontiguousarray(np.asarray(inputs["L0_ffn_b_base"], dtype=np.float32)).reshape(1, D)
    Wr = np.ascontiguousarray(np.asarray(inputs["L0_ffn_W_router"], dtype=np.float32))
    Ast = np.ascontiguousarray(np.asarray(inputs["L0_ffn_A_stack"], dtype=np.float32))
    Bst = np.ascontiguousarray(np.asarray(inputs["L0_ffn_B_stack"], dtype=np.float32))
    maps = []
    for c in range(8):
        b, half = c // 2, c % 2
        xr = np.ascontiguousarray(np.roll(x[b], -half * TOWN, axis=0))
        maps.append({"xa": xr, "wq": wq, "wk": wk, "wv": wv, "wo": wo,
                     "Wb": Wb, "bb": bb, "Wr": Wr, "Ast": Ast, "Bst": Bst})
    return maps


def kernel(**inputs):
    nc = build_nc(debug=False)
    maps = _in_maps(inputs)
    res = run_bass_kernel_spmd(nc, maps, core_ids=list(range(8)))
    outf = np.empty((4, S, D), dtype=np.float32)
    for c in range(8):
        b, half = c // 2, c % 2
        outf[b, half * TOWN:(half + 1) * TOWN] = res.results[c]["out"]
    return outf


# version 10
# speedup vs baseline: 1.2640x; 1.0880x over previous
import numpy as np
import ml_dtypes
from contextlib import ExitStack
import concourse.bass as bass
import concourse.mybir as mybir
from concourse.bass_utils import run_bass_kernel_spmd

F32 = mybir.dt.float32
BF16 = mybir.dt.bfloat16
AF = mybir.ActivationFunctionType
ALU = mybir.AluOpType
AXX = mybir.AxisListType.X

S = 4096
D = 4096
HKV = 8
DH = 128
KV = 1024
TOWN = 2048
NCH = 32
NT = 16
EPS = 1e-5
RSD = float(1.0 / np.sqrt(128.0))
P40 = float(2.0 ** 40)

SEM_NAMES = ["gl", "so", "pe", "dv", "sc", "gp",
             "sx0", "sx1", "sw0", "sw1", "sy0", "sy1",
             "st0", "st1", "gy0", "gy1",
             "gk0", "gk1", "gq0", "gq1", "gv0", "gv1"]


class Prog:

    def __init__(self, nc, ctx):
        self.nc = nc
        self.ctx = ctx
        self.sems = {}
        self.cnt = {}
        for name in SEM_NAMES:
            self._mk(name)

    def _mk(self, name):
        self.sems[name] = self.ctx.enter_context(self.nc.semaphore(name))
        self.cnt[name] = 0

    def need(self, name):
        if name not in self.sems:
            self._mk(name)

    def inc(self, instr, name, by=1):
        self.cnt[name] += by
        instr.then_inc(self.sems[name], by)
        return self.cnt[name]

    def dma(self, eng, outap, inap, name, **kw):
        ins = eng.dma_start(out=outap, in_=inap, **kw)
        return self.inc(ins, name, 16)

    def w(self, eng, name, val):
        assert val <= self.cnt[name], f"wait {name}>={val} emitted={self.cnt[name]}"
        if val > 0:
            eng.wait_ge(self.sems[name], val)

    def wfut(self, eng, name, val):
        if val > 0:
            eng.wait_ge(self.sems[name], val)

    def v(self, name):
        return self.cnt[name]

    def snap(self):
        return dict(self.cnt)

    def wall(self, eng, snapshot):
        for name, val in snapshot.items():
            self.w(eng, name, val)


def build_nc(debug=False, phases=7, n_tiles=32, n_groups=8):
    n_groups_own = max(1, n_groups // 2)
    nc = bass.Bass("TRN2", target_bir_lowering=False)

    xa = nc.dram_tensor("xa", [S, D], F32, kind="ExternalInput").ap()
    wq = nc.dram_tensor("wq", [D, D], F32, kind="ExternalInput").ap()
    wk = nc.dram_tensor("wk", [D, KV], F32, kind="ExternalInput").ap()
    wv = nc.dram_tensor("wv", [D, KV], F32, kind="ExternalInput").ap()
    wo = nc.dram_tensor("wo", [KV, D], F32, kind="ExternalInput").ap()
    Wb = nc.dram_tensor("Wb", [D, D], F32, kind="ExternalInput").ap()
    bb = nc.dram_tensor("bb", [1, D], F32, kind="ExternalInput").ap()
    Wr = nc.dram_tensor("Wr", [HKV, D], F32, kind="ExternalInput").ap()
    Ast = nc.dram_tensor("Ast", [8, 16, D], F32, kind="ExternalInput").ap()
    Bst = nc.dram_tensor("Bst", [8, D, 16], F32, kind="ExternalInput").ap()
    out = nc.dram_tensor("out", [TOWN, D], F32, kind="ExternalOutput").ap()

    ident_bf_d = nc.inline_tensor(np.eye(128, dtype=ml_dtypes.bfloat16), name="ident_bf").ap()
    ident_f_d = nc.inline_tensor(np.eye(128, dtype=np.float32), name="ident_f").ap()

    skind = "ExternalOutput" if debug else "Internal"
    y1t_sp = nc.dram_tensor("y1t_sp", [32, 128, D], BF16, kind=skind).ap()
    q_sp = nc.dram_tensor("q_sp", [HKV, 128, TOWN], BF16, kind=skind).ap()
    k_sp = nc.dram_tensor("k_sp", [HKV, 128, S], BF16, kind=skind).ap()
    v_sp = nc.dram_tensor("v_sp", [HKV, NCH, 128, 128], BF16, kind=skind).ap()
    x2_sp = nc.dram_tensor("x2_sp", [NT, 128, D], F32, kind=skind).ap()
    y2_sp = nc.dram_tensor("y2_sp", [NT, 128, D], BF16, kind=skind).ap()
    f_sp = nc.dram_tensor("f_sp", [NT, 128, D], BF16, kind=skind).ap()
    vt_dbg = None
    if debug:
        vt_dbg = nc.dram_tensor("vt_dbg", [128, HKV * TOWN], BF16, kind="ExternalOutput").ap()

    ctx = ExitStack()
    P = Prog(nc, ctx)

    ident = ctx.enter_context(nc.sbuf_tensor("ident", [128, 128], BF16))
    identf = ctx.enter_context(nc.sbuf_tensor("identf", [128, 128], F32))
    ones_r = ctx.enter_context(nc.sbuf_tensor("ones_r", [1, 128], BF16))
    ones_rf = ctx.enter_context(nc.sbuf_tensor("ones_rf", [1, 128], F32))
    onesc = ctx.enter_context(nc.sbuf_tensor("onesc", [128, 1], BF16))
    stat = ctx.enter_context(nc.sbuf_tensor("stat", [128, 64], F32))
    st1 = ctx.enter_context(nc.sbuf_tensor("st1", [1, 16], F32))
    st1b = ctx.enter_context(nc.sbuf_tensor("st1b", [1, 16], BF16))

    block = ctx.enter_context(nc.Block())

    with ctx:
        @block.gpsimd
        def _(g):
            P.dma(g, ident[:], ident_bf_d, "gl")
            P.dma(g, identf[:], ident_f_d, "gl")

        @block.vector
        def _(v):
            v.memset(ones_r[:], 1.0)
            v.memset(ones_rf[:], 1.0)
            P.inc(v.memset(onesc[:], 1.0), "dv")

        p2rctx = ExitStack()
        wk_h = p2rctx.enter_context(nc.sbuf_tensor("wk_h", [128, NCH * 512], BF16, side="right"))
        wv_h = p2rctx.enter_context(nc.sbuf_tensor("wv_h", [128, NCH * 512], BF16, side="right"))
        y1g = [p2rctx.enter_context(nc.sbuf_tensor(f"y1g{i}", [128, NCH * 512], BF16, side="right")) for i in range(2)]
        kstg = [p2rctx.enter_context(nc.sbuf_tensor(f"kstg{i}", [128, 512], BF16, side="right")) for i in range(2)]
        vstg = [p2rctx.enter_context(nc.sbuf_tensor(f"vstg{i}", [128, 512], BF16, side="right")) for i in range(2)]
        ps2a = p2rctx.enter_context(nc.psum_tensor("ps2a", [128, 2048], F32))
        ps2b = p2rctx.enter_context(nc.psum_tensor("ps2b", [128, 1024], F32))

        p1ctx = ExitStack()
        xin = [p1ctx.enter_context(nc.sbuf_tensor(f"xin{i}", [128, D], F32, side="right")) for i in range(2)]
        yb = [p1ctx.enter_context(nc.sbuf_tensor(f"yb{i}", [128, D], BF16, side="right")) for i in range(2)]
        ytr = [p1ctx.enter_context(nc.sbuf_tensor(f"ytr{i}", [128, D], BF16, side="right")) for i in range(2)]

        apply_sc, spill_gs, tpose_tr = [], [], []
        wkv_gl = {}

        def emit_wkv_cast(H, wait_pe=0):
            @block.gpsimd
            def _(g, H=H, wait_pe=wait_pe):
                if wait_pe:
                    P.w(g, "pe", wait_pe)
                src_k = wk[:, H * 512:(H + 1) * 512].rearrange("(c p) f -> p c f", p=128)
                P.dma(g, wk_h[:].rearrange("p (c f) -> p c f", c=NCH), src_k, "gl")
                src_v = wv[:, H * 512:(H + 1) * 512].rearrange("(c p) f -> p c f", p=128)
                P.dma(g, wv_h[:].rearrange("p (c f) -> p c f", c=NCH), src_v, "gl")
            wkv_gl[H] = P.v("gl")

        emit_wkv_cast(0)

        def emit_p1_tile(t):
            slot = t % 2
            sb = slot * 8

            @block.sync
            def _(s, t=t, slot=slot):
                if t >= 2:
                    P.w(s, "sc", apply_sc[t - 2])
                P.dma(s, xin[slot][:], xa[t * 128:(t + 1) * 128, :], "sx%d" % slot)
            sl_now = P.v("sx%d" % slot)

            @block.vector
            def _(v, t=t, slot=slot, sb=sb, sl_now=sl_now):
                P.w(v, "sx%d" % slot, sl_now)
                v.reduce_sum(stat[:, sb: sb + 1], xin[slot][:], axis=AXX)
                v.drain()
                if t >= 2:
                    P.w(v, *spill_gs[t - 2])
                v.scalar_tensor_tensor(ytr[slot][:], xin[slot][:], 1.0, xin[slot][:],
                                       op0=ALU.mult, op1=ALU.mult,
                                       accum_out=stat[:, sb + 1: sb + 2])
                v.drain()
                v.tensor_scalar(stat[:, sb + 2: sb + 3], stat[:, sb: sb + 1], -1.0 / D, None, op0=ALU.mult)
                v.drain()
                v.scalar_tensor_tensor(stat[:, sb + 3: sb + 4], stat[:, sb + 2: sb + 3], -1.0,
                                       stat[:, sb + 2: sb + 3], op0=ALU.mult, op1=ALU.mult)
                v.drain()
                v.scalar_tensor_tensor(stat[:, sb + 3: sb + 4], stat[:, sb + 1: sb + 2], 1.0 / D,
                                       stat[:, sb + 3: sb + 4], op0=ALU.mult, op1=ALU.add)
                v.drain()
                ins = v.tensor_scalar(stat[:, sb + 3: sb + 4], stat[:, sb + 3: sb + 4], EPS, None, op0=ALU.add)
                P.inc(ins, "dv")
            dv_rec = P.v("dv")

            @block.scalar
            def _(s, t=t, slot=slot, sb=sb, dv_rec=dv_rec):
                P.w(s, "dv", dv_rec)
                if t >= 2:
                    P.w(s, *tpose_tr[t - 2])
                s.activation(stat[:, sb + 4: sb + 5], stat[:, sb + 3: sb + 4], AF.Ln)
                s.activation(yb[slot][:], xin[slot][:], AF.Identity,
                             bias=stat[:, sb + 2: sb + 3])
                s.drain()
                s.activation(stat[:, sb + 4: sb + 5], stat[:, sb + 4: sb + 5], AF.Exp, scale=-0.5)
                s.drain()
                ins = s.activation(yb[slot][:], yb[slot][:], AF.Identity,
                                   scale=stat[:, sb + 4: sb + 5])
                P.inc(ins, "sc")
            apply_sc.append(P.v("sc"))
            sc_now = P.v("sc")

            @block.sync
            def _(s, t=t, slot=slot, sc_now=sc_now):
                P.w(s, "sc", sc_now)
                if t >= 2:
                    P.w(s, *spill_gs[t - 2])
                srcy = yb[slot][:].rearrange("p (c f) -> p c f", c=NCH)
                dsty = ytr[slot][:].rearrange("p (c f) -> p c f", c=NCH)
                P.inc(s.dma_start_transpose(dsty, srcy), "st%d" % slot, 16)
            tpose_tr.append(("st%d" % slot, P.v("st%d" % slot)))
            tr_now = P.v("st%d" % slot)

            @block.gpsimd
            def _(g, t=t, slot=slot, tr_now=tr_now):
                P.w(g, "st%d" % slot, tr_now)
                P.dma(g, y1t_sp[t], ytr[slot][:], "gy%d" % slot)
            spill_gs.append(("gy%d" % slot, P.v("gy%d" % slot)))

        y1g_sl = {}
        y1g_done_pe = {}
        evac_hist = {"k": [], "q": [], "v": []}
        spill_hist = {"k": [], "q": [], "v": []}
        qkv_spill_snap = {}

        def emit_y1g_load(key, g, gslot, prev_pe):
            @block.gpsimd
            def _(s, g=g, gslot=gslot, prev_pe=prev_pe):
                if prev_pe:
                    P.w(s, "pe", prev_pe)
                P.w(s, *spill_gs[4 * g + 3])
                P.w(s, *spill_gs[4 * g + 2])
                for j in range(4):
                    srcg = y1t_sp[4 * g + j].rearrange("p (c f) -> p c f", c=NCH)
                    dstg = y1g[gslot][:].rearrange("p (c f) -> p c f", c=NCH)[:, :, j * 128:(j + 1) * 128]
                    P.dma(s, dstg, srcg, "sy%d" % gslot)
            y1g_sl[key] = ("sy%d" % gslot, P.v("sy%d" % gslot))

        def emit_k_unit(H, g, ct):
            gslot = g % 2
            n = len(evac_hist["k"])
            pslot = ct % 2

            @block.tensor
            def _(t, H=H, g=g, ct=ct, pslot=pslot, gslot=gslot, n=n):
                P.w(t, *y1g_sl[("kv", H, g)])
                if g == 0 and ct == 0:
                    P.w(t, "gl", wkv_gl[H])
                if n >= 2:
                    P.w(t, "dv", evac_hist["k"][n - 2])
                for ch in range(NCH):
                    ins = t.matmul(ps2a[:, pslot * 512:(pslot + 1) * 512],
                                   wk_h[:, ch * 512 + ct * 128: ch * 512 + (ct + 1) * 128],
                                   y1g[gslot][:, ch * 512:(ch + 1) * 512],
                                   start=(ch == 0), stop=(ch == NCH - 1))
                P.inc(ins, "pe")
            pe_now = P.v("pe")

            @block.vector
            def _(v, pe_now=pe_now, pslot=pslot, n=n):
                P.w(v, "pe", pe_now)
                if n >= 2:
                    P.w(v, *spill_hist["k"][n - 2])
                ins = v.tensor_copy(kstg[n % 2][:], ps2a[:, pslot * 512:(pslot + 1) * 512])
                P.inc(ins, "dv")
            evac_hist["k"].append(P.v("dv"))
            dv_now = P.v("dv")
            gsem = "gk%d" % (n % 2)

            @block.gpsimd
            def _(gp, H=H, g=g, ct=ct, dv_now=dv_now, n=n, gsem=gsem):
                P.w(gp, "dv", dv_now)
                P.dma(gp, k_sp[H * 4 + ct][:, g * 512:(g + 1) * 512], kstg[n % 2][:], gsem)
            spill_hist["k"].append((gsem, P.v(gsem)))

        def emit_v_unit(H, g, tt):
            gslot = g % 2
            n = len(evac_hist["v"])
            pslot = tt % 2

            @block.tensor
            def _(t, H=H, g=g, tt=tt, pslot=pslot, gslot=gslot, n=n):
                if n >= 2:
                    P.w(t, "dv", evac_hist["v"][n - 2])
                for ch in range(NCH):
                    ins = t.matmul(ps2b[:, pslot * 512:(pslot + 1) * 512],
                                   y1g[gslot][:, ch * 512 + tt * 128: ch * 512 + (tt + 1) * 128],
                                   wv_h[:, ch * 512:(ch + 1) * 512],
                                   start=(ch == 0), stop=(ch == NCH - 1))
                P.inc(ins, "pe")
            pe_now = P.v("pe")
            if tt == 3:
                y1g_done_pe[("kv", H, g)] = pe_now

            @block.vector
            def _(v, pe_now=pe_now, pslot=pslot, n=n):
                P.w(v, "pe", pe_now)
                if n >= 2:
                    P.w(v, *spill_hist["v"][n - 2])
                ins = v.tensor_copy(vstg[n % 2][:], ps2b[:, pslot * 512:(pslot + 1) * 512])
                P.inc(ins, "dv")
            evac_hist["v"].append(P.v("dv"))
            dv_now = P.v("dv")
            gsem = "gv%d" % (n % 2)

            @block.gpsimd
            def _(gp, H=H, g=g, tt=tt, dv_now=dv_now, n=n, gsem=gsem):
                P.w(gp, "dv", dv_now)
                for i in range(4):
                    P.dma(gp, v_sp[H * 4 + i, 4 * g + tt], vstg[n % 2][:, i * 128:(i + 1) * 128], gsem)
            spill_hist["v"].append((gsem, P.v(gsem)))

        def emit_kv_units(H, g, units):
            for u in units:
                if u < 4:
                    emit_k_unit(H, g, u)
                else:
                    emit_v_unit(H, g, u - 4)

        for t in range(n_tiles):
            emit_p1_tile(t)
            g = t // 4
            ph = t % 4
            if ph == 3:
                prev_pe = y1g_done_pe.get(("kv", 0, g - 2), 0)
                emit_y1g_load(("kv", 0, g), g, g % 2, prev_pe)
            if g >= 1:
                emit_kv_units(0, g - 1, [2 * ph, 2 * ph + 1])
        emit_kv_units(0, n_groups - 1, list(range(8)))
        last_h0_pe = P.v("pe")
        emit_wkv_cast(1, wait_pe=last_h0_pe)
        for g in range(n_groups):
            prev_pe = y1g_done_pe.get(("kv", 1, g - 2), y1g_done_pe.get(("kv", 0, g + n_groups - 2), 0))
            emit_y1g_load(("kv", 1, g), g, g % 2, prev_pe)
            emit_kv_units(1, g, list(range(8)))

        kv_end = P.snap()
        for hh in range(8):
            HH, cth = hh // 4, hh % 4
            w = {}
            nk = (HH * n_groups + (n_groups - 1)) * 4 + cth
            w.update(dict([spill_hist["k"][nk]]))
            for sl in range(2):
                lastv = None
                for i, ent in enumerate(spill_hist["v"]):
                    if i < HH * n_groups * 4 or i >= (HH + 1) * n_groups * 4:
                        continue
                    if ent[0] == "gv%d" % sl:
                        lastv = ent
                if lastv:
                    w[lastv[0]] = max(w.get(lastv[0], 0), lastv[1])
            qkv_spill_snap[hh] = w

        p1_end = P.snap()
        p1ctx.close()

        p2qctx = ExitStack()
        wqin = [p2qctx.enter_context(nc.sbuf_tensor(f"wqin{i}", [128, TOWN], F32, side="right")) for i in range(2)]
        wacc = [p2qctx.enter_context(nc.sbuf_tensor(f"wacc{i}", [128, 512], F32, side="right")) for i in range(2)]
        wq_eff = p2qctx.enter_context(nc.sbuf_tensor("wq_eff", [128, NCH * 512], BF16, side="right"))
        qstg = [p2qctx.enter_context(nc.sbuf_tensor(f"qstg{i}", [128, 512], BF16, side="right")) for i in range(2)]

        for engsec in (block.tensor, block.vector, block.scalar, block.sync, block.gpsimd):
            @engsec
            def _(e):
                for nm, val in p1_end.items():
                    P.w(e, nm, val)

        presum_list = []

        def emit_presum_chunk(H, ch, wait_pe=0):
            n = len(presum_list)
            slot = n % 2

            @block.sync
            def _(s, H=H, ch=ch, slot=slot, n=n):
                if n >= 2:
                    P.w(s, "dv", presum_list[n - 2])
                P.dma(s, wqin[slot][:], wq[ch * 128:(ch + 1) * 128, H * TOWN:(H + 1) * TOWN], "sw%d" % slot)
            sl_now = P.v("sw%d" % slot)

            @block.vector
            def _(v, H=H, ch=ch, slot=slot, sl_now=sl_now, wait_pe=wait_pe):
                P.w(v, "sw%d" % slot, sl_now)
                if wait_pe and ch == 0:
                    P.w(v, "pe", wait_pe)
                srcq = wqin[slot][:].rearrange("p (kv g d) -> p kv g d", kv=4, g=4)
                acc = wacc[slot][:].rearrange("p (kv d) -> p kv d", kv=4)
                v.tensor_add(acc, srcq[:, :, 0, :], srcq[:, :, 1, :])
                v.drain()
                v.tensor_add(acc, acc, srcq[:, :, 2, :])
                v.drain()
                v.tensor_add(acc, acc, srcq[:, :, 3, :])
                v.drain()
                ins = v.tensor_scalar(wq_eff[:, ch * 512:(ch + 1) * 512], wacc[slot][:],
                                      RSD, None, op0=ALU.mult)
                P.inc(ins, "dv")
            presum_list.append(P.v("dv"))

        for H in range(2):
            wait_pe = 0
            if H == 1:
                wait_pe = P.v("pe")
            for ch in range(NCH):
                emit_presum_chunk(H, ch, wait_pe=wait_pe)
            presum_done = P.v("dv")

            for g in range(n_groups_own):
                gslot = g % 2
                prev_pe = y1g_done_pe.get(("q", H, g - 2), y1g_done_pe.get(("q", H - 1, g + n_groups_own - 2), 0))
                if H == 0 and g < 2:
                    prev_pe = 0
                emit_y1g_load(("q", H, g), g, gslot, prev_pe or kv_end["pe"])

                for ct in range(4):
                    n = len(evac_hist["q"])
                    pslot = 2 + ct % 2

                    @block.tensor
                    def _(t, H=H, g=g, ct=ct, pslot=pslot, gslot=gslot, n=n, presum_done=presum_done):
                        P.w(t, *y1g_sl[("q", H, g)])
                        if g == 0 and ct == 0:
                            P.w(t, "dv", presum_done)
                        if n >= 2:
                            P.w(t, "dv", evac_hist["q"][n - 2])
                        for ch in range(NCH):
                            ins = t.matmul(ps2a[:, pslot * 512:(pslot + 1) * 512],
                                           wq_eff[:, ch * 512 + ct * 128: ch * 512 + (ct + 1) * 128],
                                           y1g[gslot][:, ch * 512:(ch + 1) * 512],
                                           start=(ch == 0), stop=(ch == NCH - 1))
                        P.inc(ins, "pe")
                    pe_now = P.v("pe")
                    if ct == 3:
                        y1g_done_pe[("q", H, g)] = pe_now

                    @block.vector
                    def _(v, pe_now=pe_now, pslot=pslot, n=n):
                        P.w(v, "pe", pe_now)
                        if n >= 2:
                            P.w(v, *spill_hist["q"][n - 2])
                        ins = v.tensor_copy(qstg[n % 2][:], ps2a[:, pslot * 512:(pslot + 1) * 512])
                        P.inc(ins, "dv")
                    evac_hist["q"].append(P.v("dv"))
                    dv_now = P.v("dv")
                    gsem = "gq%d" % (n % 2)

                    @block.gpsimd
                    def _(gp, H=H, g=g, ct=ct, dv_now=dv_now, n=n, gsem=gsem):
                        P.w(gp, "dv", dv_now)
                        P.dma(gp, q_sp[H * 4 + ct][:, g * 512:(g + 1) * 512], qstg[n % 2][:], gsem)
                    spill_hist["q"].append((gsem, P.v(gsem)))

        for hh in range(8):
            HH, cth = hh // 4, hh % 4
            nq = (HH * n_groups_own + (n_groups_own - 1)) * 4 + cth
            ent = spill_hist["q"][nq]
            qkv_spill_snap[hh] = dict(qkv_spill_snap[hh])
            qkv_spill_snap[hh][ent[0]] = max(qkv_spill_snap[hh].get(ent[0], 0), ent[1])
        p2qctx.close()

        p2_end = P.snap()
        p2rctx.close()

        for engsec in (block.tensor, block.vector, block.scalar, block.sync, block.gpsimd):
            @engsec
            def _(e):
                for nm, val in p2_end.items():
                    P.w(e, nm, val)

        p3vctx = ExitStack()
        valuesT = p3vctx.enter_context(nc.sbuf_tensor("valuesT", [128, HKV * TOWN], BF16, side="right"))
        p3ctx = ExitStack()
        kbuf = [p3ctx.enter_context(nc.sbuf_tensor(f"kbuf{i}", [128, S], BF16, side="right")) for i in range(2)]
        vbuf = [p3ctx.enter_context(nc.sbuf_tensor(f"vbuf{i}", [128, S], BF16, side="right")) for i in range(2)]
        qbuf = [p3ctx.enter_context(nc.sbuf_tensor(f"qbuf{i}", [128, TOWN], BF16, side="right")) for i in range(2)]
        PTb = [p3ctx.enter_context(nc.sbuf_tensor(f"PTb{i}", [128, 4 * S], BF16, side="right")) for i in range(2)]
        recd_sb = p3ctx.enter_context(nc.sbuf_tensor("recd_sb", [1, 512], F32, side="right"))
        mhjunk = p3ctx.enter_context(nc.sbuf_tensor("mhjunk", [128, S], BF16, side="right"))
        bias_h = p3ctx.enter_context(nc.sbuf_tensor("bias_h", [128, 2], F32, side="right"))
        bcast_sb = p3ctx.enter_context(nc.sbuf_tensor("bcast_sb", [128, 1024], F32, side="right"))
        psS = [p3ctx.enter_context(nc.psum_tensor(f"psS{i}", [128, 1024], F32)) for i in range(3)]
        psV3 = p3ctx.enter_context(nc.psum_tensor("psV3", [128, 512], F32))
        psX = p3ctx.enter_context(nc.psum_tensor("psX", [128, 512], F32))

        for nm in ("gld0", "gld1", "sp0", "sp1", "sp2", "sp3"):
            P.need(nm)

        @block.vector
        def _(v):
            P.inc(v.memset(valuesT[:], 0.0), "dv")

        NKEY = n_groups * 512
        NKC = NKEY // 128
        G3 = n_groups_own
        head_done_pe = {}
        pv_dv = {}
        recd_dv = {}
        bcast_dv = {}
        exp_hist_pergroup = {}
        exp_hist = {}
        n_heads = min(HKV, 8)

        for h in range(n_heads):
            hs = h % 2
            gld = "gld%d" % hs
            recd_dv = {}
            bcast_dv = dict(bcast_dv)
            exp_hist_pergroup = {}

            @block.gpsimd
            def _(g, h=h, hs=hs, gld=gld):
                if h >= 2:
                    P.w(g, "pe", head_done_pe[h - 2])
                for nm, val in qkv_spill_snap[h].items():
                    P.w(g, nm, val)
                P.dma(g, kbuf[hs][:, 0:NKEY], k_sp[h][:, 0:NKEY], gld)
                P.dma(g, vbuf[hs][:, 0:NKEY], v_sp[h][0:NKC].rearrange("c p j -> p c j"), gld)
                P.dma(g, qbuf[hs][:, 0:G3 * 512], q_sp[h][:, 0:G3 * 512], gld)
            gld_now = P.v(gld)

            @block.vector
            def _(v, h=h, hs=hs, gld_now=gld_now, gld=gld):
                P.w(v, gld, gld_now)
                if h >= 1:
                    P.w(v, "pe", P.v("pe"))
                ins = v.scalar_tensor_tensor(mhjunk[:, 0:G3 * 512], qbuf[hs][:, 0:G3 * 512], 1.0,
                                             qbuf[hs][:, 0:G3 * 512], op0=ALU.mult, op1=ALU.mult)
                P.inc(ins, "dv")
            sc_q2 = ("dv", P.v("dv"))

            for i in range(G3):
                @block.tensor
                def _(t, i=i, sc_q2=sc_q2):
                    P.w(t, *sc_q2)
                    if i > 0:
                        P.w(t, "dv", P.v("dv"))
                    ins = t.matmul(psX[0:1, :], onesc[:], mhjunk[:, i * 512:(i + 1) * 512],
                                   start=True, stop=True)
                    P.inc(ins, "pe")
                pe_now = P.v("pe")

                @block.vector
                def _(v, h=h, i=i, pe_now=pe_now):
                    P.w(v, "pe", pe_now)
                    v.reduce_max(st1[:, 1:2], psX[0:1, :], axis=AXX)
                    v.drain()
                    if i == 0:
                        ins = v.tensor_copy(st1[:, 2:3], st1[:, 1:2])
                    else:
                        ins = v.tensor_tensor(st1[:, 2:3], st1[:, 2:3], st1[:, 1:2], op=ALU.max)
                    v.drain()
                    P.inc(ins, "dv")

            pe_q2done = P.v("pe")

            @block.vector
            def _(v, h=h, hs=hs, pe_q2done=pe_q2done):
                P.w(v, "pe", pe_q2done)
                ins = v.scalar_tensor_tensor(mhjunk[:, 0:NKEY], kbuf[hs][:, 0:NKEY], 1.0,
                                             kbuf[hs][:, 0:NKEY], op0=ALU.mult, op1=ALU.mult)
                P.inc(ins, "dv")
            sc_k2 = ("dv", P.v("dv"))

            for i in range(NKEY // 512):
                @block.tensor
                def _(t, i=i, sc_k2=sc_k2):
                    P.w(t, *sc_k2)
                    P.w(t, "dv", P.v("dv"))
                    ins = t.matmul(psX[0:1, :], onesc[:], mhjunk[:, i * 512:(i + 1) * 512],
                                   start=True, stop=True)
                    P.inc(ins, "pe")
                pe_now = P.v("pe")

                @block.vector
                def _(v, h=h, i=i, pe_now=pe_now):
                    P.w(v, "pe", pe_now)
                    v.reduce_max(st1[:, 1:2], psX[0:1, :], axis=AXX)
                    v.drain()
                    if i == 0:
                        ins = v.tensor_copy(st1[:, 3:4], st1[:, 1:2])
                    else:
                        ins = v.tensor_tensor(st1[:, 3:4], st1[:, 3:4], st1[:, 1:2], op=ALU.max)
                    v.drain()
                    P.inc(ins, "dv")

            @block.vector
            def _(v):
                ins = v.tensor_mul(st1[:, 4:5], st1[:, 2:3], st1[:, 3:4])
                P.inc(ins, "dv")
            dv_m = P.v("dv")

            @block.scalar
            def _(s, dv_m=dv_m):
                P.w(s, "dv", dv_m)
                s.activation(st1[:, 4:5], st1[:, 4:5], AF.Ln)
                s.drain()
                ins = s.activation(st1[:, 4:5], st1[:, 4:5], AF.Exp, scale=0.5)
                P.inc(ins, "sc")
            sc_m = P.v("sc")

            @block.vector
            def _(v, sc_m=sc_m):
                P.w(v, "sc", sc_m)
                v.tensor_scalar(st1[:, 4:5], st1[:, 4:5], -1.02, -1.0,
                                op0=ALU.mult, op1=ALU.add)
                v.drain()
                ins = v.tensor_copy(st1b[:, 0:1], st1[:, 4:5])
                P.inc(ins, "dv")
            dv_mb = P.v("dv")

            @block.tensor
            def _(t, dv_mb=dv_mb):
                P.w(t, "dv", dv_mb)
                ins = t.matmul(psX[:, 0:1], ones_r[:], st1b[:, 0:1], start=True, stop=True)
                P.inc(ins, "pe")
            pe_b = P.v("pe")

            @block.vector
            def _(v, hs=hs, pe_b=pe_b):
                P.w(v, "pe", pe_b)
                ins = v.tensor_copy(bias_h[:, hs:hs + 1], psX[:, 0:1])
                P.inc(ins, "dv")
            sc_bias = ("dv", P.v("dv"))

            def emit_PV(g, h=h, hs=hs):
                gs3 = g % 2

                @block.tensor
                def _(t, g=g, gs3=gs3):
                    P.w(t, "sc", exp_hist_pergroup[g])
                    prevdv = pv_dv.get((h, g - 1)) if g >= 1 else (pv_dv.get((h - 1, G3 - 1)) if h >= 1 else None)
                    if prevdv is not None:
                        P.w(t, "dv", prevdv)
                    for kc in range(NKC):
                        ins = t.matmul(psV3[:], vbuf[hs][:, kc * 128:(kc + 1) * 128],
                                       PTb[gs3][:, kc * 512:(kc + 1) * 512],
                                       start=(kc == 0), stop=(kc == NKC - 1))
                    P.inc(ins, "pe")
                pe_pv = P.v("pe")

                @block.vector
                def _(v, g=g, gs3=gs3, pe_pv=pe_pv):
                    P.w(v, "pe", pe_pv)
                    P.w(v, *bcast_dv[g])
                    for qt in range(4):
                        ins = v.tensor_mul(valuesT[:, h * TOWN + g * 512 + qt * 128: h * TOWN + g * 512 + (qt + 1) * 128],
                                           psV3[:, qt * 128:(qt + 1) * 128],
                                           bcast_sb[:, gs3 * 512 + qt * 128: gs3 * 512 + (qt + 1) * 128])
                    P.inc(ins, "dv")
                pv_dv[(h, g)] = P.v("dv")

            for g in range(G3):
                gs3 = g % 2
                den_mm = {}

                sc0 = P.v("sc")
                pe_s_list = []
                for kp in range(NKC // 2):
                    sslot = kp % 3

                    @block.tensor
                    def _(t, g=g, kp=kp, sslot=sslot, sc0=sc0):
                        if (g, kp) == (0, 0):
                            P.w(t, *sc_bias)
                        if kp >= 3:
                            P.wfut(t, "sc", sc0 + kp - 2)
                        for u in range(2):
                            kc = 2 * kp + u
                            ins = t.matmul(psS[sslot][:, u * 512:(u + 1) * 512],
                                           kbuf[hs][:, kc * 128:(kc + 1) * 128],
                                           qbuf[hs][:, g * 512:(g + 1) * 512],
                                           start=True, stop=True)
                        P.inc(ins, "pe")
                    pe_s_list.append(P.v("pe"))

                if g >= 1:
                    emit_PV(g - 1)

                @block.scalar
                def _(s, g=g, pe_s_list=pe_s_list):
                    for kp in range(NKC // 2):
                        P.w(s, "pe", pe_s_list[kp])
                        if kp == 0:
                            prevpv = pv_dv.get((h, g - 2)) if g >= 2 else \
                                (pv_dv.get((h - 1, G3 - 2 + g)) if h >= 1 else None)
                            if prevpv is not None:
                                P.w(s, "dv", prevpv)
                        ins = s.activation(PTb[g % 2][:, kp * 1024:(kp + 1) * 1024],
                                           psS[kp % 3][:, 0:1024], AF.Exp,
                                           bias=bias_h[:, hs:hs + 1])
                        P.inc(ins, "sc")
                assert P.v("sc") == sc0 + NKC // 2
                for kp in range(NKC // 2):
                    exp_hist[kp % 3] = sc0 + kp + 1

                @block.tensor
                def _(t, g=g, sc0=sc0):
                    prevrd = recd_dv.get(g - 1)
                    if prevrd is not None:
                        P.w(t, "dv", prevrd)
                    prevbc = bcast_dv.get(g - 1)
                    if prevbc is not None:
                        P.w(t, *prevbc)
                    for kp in range(NKC // 2):
                        P.w(t, "sc", sc0 + kp + 1)
                        for u in range(2):
                            kc = 2 * kp + u
                            ins = t.matmul(psX[0:1, :], onesc[:],
                                           PTb[g % 2][:, kc * 512:(kc + 1) * 512],
                                           start=(kc == 0), stop=(kc == NKC - 1))
                    P.inc(ins, "pe")
                pe_den = P.v("pe")

                @block.vector
                def _(v, g=g, pe_den=pe_den):
                    P.w(v, "pe", pe_den)
                    v.tensor_scalar(recd_sb[0:1, 0:512], psX[0:1, :], P40, None, op0=ALU.mult)
                    v.drain()
                    v.reciprocal(recd_sb[0:1, 0:512], recd_sb[0:1, 0:512])
                    v.drain()
                    ins = v.tensor_scalar(recd_sb[0:1, 0:512], recd_sb[0:1, 0:512], P40, None, op0=ALU.mult)
                    P.inc(ins, "dv")
                recd_dv[g] = P.v("dv")
                dv_rd = P.v("dv")

                @block.tensor
                def _(t, g=g, dv_rd=dv_rd):
                    P.w(t, "dv", dv_rd)
                    prevbc = bcast_dv.get(g - 1)
                    if prevbc is not None:
                        P.w(t, *prevbc)
                    ins = t.matmul(psX[:, 0:512], ones_rf[:], recd_sb[0:1, 0:512],
                                   start=True, stop=True)
                    P.inc(ins, "pe")
                pe_bc = P.v("pe")

                @block.vector
                def _(v, g=g, gs3=gs3, pe_bc=pe_bc):
                    P.w(v, "pe", pe_bc)
                    prevpv = pv_dv.get((h, g - 2))
                    if prevpv is not None:
                        P.w(v, "dv", prevpv)
                    ins = v.tensor_copy(bcast_sb[:, gs3 * 512:(gs3 + 1) * 512], psX[:, 0:512])
                    P.inc(ins, "dv")
                bcast_dv[g] = ("dv", P.v("dv"))

                exp_hist_pergroup[g] = sc0 + NKC // 2
            emit_PV(G3 - 1)
            head_done_pe[h] = P.v("pe")
        p3_vt_dv = P.v("dv")

        if debug:
            @block.gpsimd
            def _(g):
                P.w(g, "dv", p3_vt_dv)
                P.dma(g, vt_dbg, valuesT[:], "gv0")

        p3_end = P.snap()
        p3ctx.close()

        for engsec in (block.tensor, block.vector, block.scalar, block.sync, block.gpsimd):
            @engsec
            def _(e):
                for nm, val in p3_end.items():
                    P.w(e, nm, val)

        NT4 = G3 * 4
        SGL = [(s, min(8, NT4 - s)) for s in range(0, NT4, 8)]

        bbrow = ctx.enter_context(nc.sbuf_tensor("bbrow", [1, D], BF16))
        AW = ctx.enter_context(nc.sbuf_tensor("AW", [128, NCH * 136], BF16))
        Bfl = ctx.enter_context(nc.sbuf_tensor("Bfl", [128, D], BF16))
        fsums = ctx.enter_context(nc.sbuf_tensor("fsums", [128, 64], F32))
        fssq = ctx.enter_context(nc.sbuf_tensor("fssq", [128, 64], F32))

        p4ctx = ExitStack()
        wo_sb = p4ctx.enter_context(nc.sbuf_tensor("wo_sb", [128, 8 * D], BF16, side="right"))
        attnsb = p4ctx.enter_context(nc.sbuf_tensor("attnsb", [128, D], F32, side="right"))
        _xo = p4ctx.enter_context(nc.sbuf_tensor("xo0", [128, D], F32, side="right"))
        _x2b = p4ctx.enter_context(nc.sbuf_tensor("x2b0", [128, D], F32, side="right"))
        _y2b = p4ctx.enter_context(nc.sbuf_tensor("y2b0", [128, D], BF16, side="right"))
        xo = [_xo, _xo]
        x2b = [_x2b, _x2b]
        y2b = [_y2b, _y2b]
        psW = p4ctx.enter_context(nc.psum_tensor("psW", [128, 3072], F32))
        psC = p4ctx.enter_context(nc.psum_tensor("psC", [128, 2048], BF16))
        p4sctx = ExitStack()
        A_nat = p4sctx.enter_context(nc.sbuf_tensor("A_nat", [128, D], BF16, side="right"))
        Wr_nat = p4sctx.enter_context(nc.sbuf_tensor("Wr_nat", [8, D], BF16, side="right"))
        Bexp = [p4sctx.enter_context(nc.sbuf_tensor(f"Bexp{i}", [128, 1024], BF16, side="right")) for i in range(2)]

        @block.gpsimd
        def _(g):
            P.dma(g, bbrow[:], bb, "gl")
            P.dma(g, A_nat[:], Ast.rearrange("e r d -> (e r) d"), "gl")
            P.dma(g, Wr_nat[:], Wr, "gl")
        aw_gl = P.v("gl")

        aw_sc = []
        for ch in range(NCH):
            cslot = ch % 2

            @block.tensor
            def _(t, ch=ch, cslot=cslot):
                if ch == 0:
                    P.w(t, "gl", aw_gl)
                if ch >= 2:
                    P.w(t, *aw_sc[ch - 2])
                t.transpose(psC[:, cslot * 1024: cslot * 1024 + 128],
                            A_nat[:, ch * 128:(ch + 1) * 128], ident[:])
                ins = t.transpose(psC[:, cslot * 1024 + 128: cslot * 1024 + 136],
                                  Wr_nat[:, ch * 128:(ch + 1) * 128], ident[0:8, 0:8])
                P.inc(ins, "pe")
            pe_now = P.v("pe")

            @block.vector
            def _(v, ch=ch, cslot=cslot, pe_now=pe_now):
                P.w(v, "pe", pe_now)
                v.tensor_copy(AW[:, ch * 136: ch * 136 + 128], psC[:, cslot * 1024: cslot * 1024 + 128])
                v.drain()
                ins = v.tensor_copy(AW[:, ch * 136 + 128: (ch + 1) * 136],
                                    psC[:, cslot * 1024 + 128: cslot * 1024 + 136])
                P.inc(ins, "dv")
            aw_sc.append(("dv", P.v("dv")))

        bexp_gl = {}
        bfl_sc = []
        for ep in range(4):
            es = ep % 2

            @block.gpsimd
            def _(g, ep=ep, es=es):
                if ep >= 2:
                    P.w(g, *bfl_sc[(ep - 2) * NCH + NCH - 1])
                P.need("gb%d" % es)
                for u in range(2):
                    srcb = Bst[2 * ep + u].rearrange("(oc p) r -> p oc r", p=128)
                    dstb = Bexp[es][:].rearrange("p (oc u r) -> p oc u r", u=2, r=16)[:, :, u, :]
                    P.dma(g, dstb, srcb, "gb%d" % es)
            bexp_gl[ep] = ("gb%d" % es, P.v("gb%d" % es))

            for oc in range(NCH):
                bs = oc % 2

                @block.tensor
                def _(t, ep=ep, oc=oc, es=es, bs=bs):
                    if oc == 0:
                        P.w(t, *bexp_gl[ep])
                    if ep == 0 and oc == 0:
                        P.w(t, *aw_sc[-1])
                    if len(bfl_sc) >= 2:
                        P.w(t, *bfl_sc[-2])
                    ins = t.transpose(psC[0:32, bs * 1024 + 256: bs * 1024 + 384],
                                      Bexp[es][:, oc * 32:(oc + 1) * 32], ident[:])
                    P.inc(ins, "pe")
                pe_now = P.v("pe")

                @block.vector
                def _(v, ep=ep, oc=oc, bs=bs, pe_now=pe_now):
                    P.w(v, "pe", pe_now)
                    ins = v.tensor_copy(Bfl[ep * 32:(ep + 1) * 32, oc * 128:(oc + 1) * 128],
                                        psC[0:32, bs * 1024 + 256: bs * 1024 + 384])
                    P.inc(ins, "dv")
                bfl_sc.append(("dv", P.v("dv")))

        cst_end = P.snap()
        p4sctx.close()
        for engsec in (block.tensor, block.vector, block.scalar, block.sync, block.gpsimd):
            @engsec
            def _(e, snap=cst_end):
                for nm, val in snap.items():
                    P.w(e, nm, val)

        P.need("gwo")

        @block.gpsimd
        def _(g):
            srcw = wo.rearrange("(c p) f -> p c f", p=128)
            P.dma(g, wo_sb[:].rearrange("p (c f) -> p c f", c=8), srcw, "gwo")
        wo_gl = P.v("gwo")

        wo_evac = {}
        x2done_dv = {}
        y2_done_sc = {}
        x2sp_gs = {}
        y2sp_gs = {}
        P.need("gx0"); P.need("gx1")

        for tt in range(NT4):
            ts4 = tt % 2
            sb4 = 40 + ts4 * 12

            @block.sync
            def _(s, tt=tt, ts4=ts4):
                prev = x2done_dv.get(tt - 1)
                if prev:
                    P.w(s, "dv", prev)
                P.need("sxo%d" % ts4)
                P.dma(s, xo[ts4][:], xa[tt * 128:(tt + 1) * 128, :], "sxo%d" % ts4)
            sl_xo = P.v("sxo%d" % ts4)

            pe_parts = []
            for part, (obase, olen) in enumerate([(0, 3072), (3072, 1024)]):
                @block.tensor
                def _(t, tt=tt, part=part, obase=obase, olen=olen):
                    if tt == 0 and part == 0:
                        P.w(t, "gwo", wo_gl)
                        P.w(t, "dv", p3_vt_dv)
                    prev = wo_evac.get((tt, part - 1), wo_evac.get((tt - 1, 1)))
                    if prev:
                        P.wfut(t, "sc", prev)
                    for ch in range(8):
                        for oc in range(olen // 512):
                            ins = t.matmul(psW[:, oc * 512:(oc + 1) * 512],
                                           valuesT[:, ch * TOWN + tt * 128:(ch * TOWN) + (tt + 1) * 128],
                                           wo_sb[:, ch * D + obase + oc * 512: ch * D + obase + (oc + 1) * 512],
                                           start=(ch == 0), stop=(ch == 7))
                    P.inc(ins, "pe")
                pe_parts.append(P.v("pe"))
                wo_evac[(tt, part)] = P.v("sc") + part + 1

            @block.scalar
            def _(s, tt=tt, sb4=sb4, pe_parts=pe_parts):
                prevx2 = x2done_dv.get(tt - 1)
                if prevx2:
                    P.w(s, "dv", prevx2)
                for part, (obase, olen) in enumerate([(0, 3072), (3072, 1024)]):
                    P.w(s, "pe", pe_parts[part])
                    ins = s.activation(attnsb[:, obase:obase + olen], psW[:, 0:olen], AF.Identity,
                                       accum_out=stat[:, sb4 + part: sb4 + part + 1])
                    P.inc(ins, "sc")
            sc_attn = P.v("sc")
            assert wo_evac[(tt, 0)] == sc_attn - 1 and wo_evac[(tt, 1)] == sc_attn

            @block.vector
            def _(v, tt=tt, ts4=ts4, sb4=sb4, sc_attn=sc_attn):
                P.w(v, "sc", sc_attn)
                prev = y2sp_gs.get(tt - 1)
                if prev:
                    P.w(v, *prev)
                v.scalar_tensor_tensor(y2b[ts4][:, 0:2048], attnsb[:, 0:2048], 1.0,
                                       attnsb[:, 0:2048], op0=ALU.mult, op1=ALU.mult,
                                       accum_out=stat[:, sb4 + 2: sb4 + 3])
                v.drain()
                v.scalar_tensor_tensor(y2b[ts4][:, 0:2048], attnsb[:, 2048:4096], 1.0,
                                       attnsb[:, 2048:4096], op0=ALU.mult, op1=ALU.mult,
                                       accum_out=stat[:, sb4 + 3: sb4 + 4])
                v.drain()
                v.tensor_tensor(stat[:, sb4 + 4: sb4 + 5], stat[:, sb4: sb4 + 1],
                                stat[:, sb4 + 1: sb4 + 2], op=ALU.add)
                v.drain()
                v.tensor_scalar(stat[:, sb4 + 4: sb4 + 5], stat[:, sb4 + 4: sb4 + 5],
                                -1.0 / D, None, op0=ALU.mult)
                v.drain()
                v.tensor_tensor(stat[:, sb4 + 5: sb4 + 6], stat[:, sb4 + 2: sb4 + 3],
                                stat[:, sb4 + 3: sb4 + 4], op=ALU.add)
                v.drain()
                v.scalar_tensor_tensor(stat[:, sb4 + 6: sb4 + 7], stat[:, sb4 + 4: sb4 + 5],
                                       -1.0, stat[:, sb4 + 4: sb4 + 5], op0=ALU.mult, op1=ALU.mult)
                v.drain()
                v.scalar_tensor_tensor(stat[:, sb4 + 6: sb4 + 7], stat[:, sb4 + 5: sb4 + 6],
                                       1.0 / D, stat[:, sb4 + 6: sb4 + 7], op0=ALU.mult, op1=ALU.add)
                v.drain()
                ins = v.tensor_scalar(stat[:, sb4 + 6: sb4 + 7], stat[:, sb4 + 6: sb4 + 7],
                                      EPS, None, op0=ALU.add)
                P.inc(ins, "dv")
            dv_rec2 = P.v("dv")

            @block.scalar
            def _(s, sb4=sb4, dv_rec2=dv_rec2):
                P.w(s, "dv", dv_rec2)
                s.activation(stat[:, sb4 + 7: sb4 + 8], stat[:, sb4 + 6: sb4 + 7], AF.Ln)
                s.drain()
                ins = s.activation(stat[:, sb4 + 7: sb4 + 8], stat[:, sb4 + 7: sb4 + 8], AF.Exp, scale=-0.5)
                P.inc(ins, "sc")
            sc_r2 = P.v("sc")

            @block.vector
            def _(v, tt=tt, ts4=ts4, sb4=sb4, sc_r2=sc_r2, sl_xo=sl_xo):
                P.w(v, "sc", sc_r2)
                P.w(v, "sxo%d" % ts4, sl_xo)
                prevsp = x2sp_gs.get(tt - 1)
                if prevsp:
                    P.w(v, *prevsp)
                prevy2 = y2_done_sc.get(tt - 1)
                if prevy2:
                    P.w(v, "sc", prevy2)
                v.drain()
                v.tensor_scalar(attnsb[:], attnsb[:], stat[:, sb4 + 4: sb4 + 5], None, op0=ALU.add)
                v.drain()
                ins = v.scalar_tensor_tensor(x2b[ts4][:], attnsb[:], stat[:, sb4 + 7: sb4 + 8],
                                             xo[ts4][:], op0=ALU.mult, op1=ALU.add)
                P.inc(ins, "dv")
            x2done_dv[tt] = P.v("dv")
            dv_x2 = P.v("dv")

            @block.gpsimd
            def _(g, tt=tt, ts4=ts4, dv_x2=dv_x2):
                P.w(g, "dv", dv_x2)
                P.dma(g, x2_sp[tt], x2b[ts4][:], "gx%d" % ts4)
            x2sp_gs[tt] = ("gx%d" % ts4, P.v("gx%d" % ts4))

            @block.vector
            def _(v, tt=tt, ts4=ts4, sb4=sb4):
                v.drain()
                v.scalar_tensor_tensor(y2b[ts4][:], x2b[ts4][:], 1.0, x2b[ts4][:],
                                       op0=ALU.mult, op1=ALU.mult,
                                       accum_out=stat[:, sb4 + 9: sb4 + 10])
                v.drain()
                v.reduce_sum(stat[:, sb4 + 10: sb4 + 11], x2b[ts4][:], axis=AXX)
                v.drain()
                v.tensor_scalar(stat[:, sb4 + 10: sb4 + 11], stat[:, sb4 + 10: sb4 + 11],
                                -1.0 / D, None, op0=ALU.mult)
                v.drain()
                v.scalar_tensor_tensor(stat[:, sb4 + 11: sb4 + 12], stat[:, sb4 + 10: sb4 + 11],
                                       -1.0, stat[:, sb4 + 10: sb4 + 11], op0=ALU.mult, op1=ALU.mult)
                v.drain()
                v.scalar_tensor_tensor(stat[:, sb4 + 11: sb4 + 12], stat[:, sb4 + 9: sb4 + 10],
                                       1.0 / D, stat[:, sb4 + 11: sb4 + 12], op0=ALU.mult, op1=ALU.add)
                v.drain()
                ins = v.tensor_scalar(stat[:, sb4 + 11: sb4 + 12], stat[:, sb4 + 11: sb4 + 12],
                                      EPS, None, op0=ALU.add)
                P.inc(ins, "dv")
            dv_rec3 = P.v("dv")

            @block.scalar
            def _(s, tt=tt, ts4=ts4, sb4=sb4, dv_rec3=dv_rec3):
                P.w(s, "dv", dv_rec3)
                s.activation(stat[:, sb4 + 11: sb4 + 12], stat[:, sb4 + 11: sb4 + 12], AF.Ln)
                s.drain()
                s.activation(stat[:, sb4 + 11: sb4 + 12], stat[:, sb4 + 11: sb4 + 12], AF.Exp, scale=-0.5)
                s.activation(y2b[ts4][:], x2b[ts4][:], AF.Identity,
                             bias=stat[:, sb4 + 10: sb4 + 11])
                s.drain()
                ins = s.activation(y2b[ts4][:], y2b[ts4][:], AF.Identity,
                                   scale=stat[:, sb4 + 11: sb4 + 12])
                P.inc(ins, "sc")
            y2_done_sc[tt] = P.v("sc")
            sc_y2 = P.v("sc")

            @block.gpsimd
            def _(g, tt=tt, ts4=ts4, sc_y2=sc_y2):
                P.w(g, "sc", sc_y2)
                P.dma(g, y2_sp[tt], y2b[ts4][:], "gy%d" % ts4)
            y2sp_gs[tt] = ("gy%d" % ts4, P.v("gy%d" % ts4))

        p4_end = P.snap()
        p4ctx.close()
        p3vctx.close()

        for engsec in (block.tensor, block.vector, block.scalar, block.sync, block.gpsimd):
            @engsec
            def _(e):
                for nm, val in p4_end.items():
                    P.w(e, nm, val)

        p5ctx = ExitStack()
        _y2rd = p5ctx.enter_context(nc.sbuf_tensor("y2rd0", [128, D], BF16, side="right"))
        y2rd = [_y2rd, _y2rd]
        y2T = p5ctx.enter_context(nc.sbuf_tensor("y2T", [128, NCH * 1024], BF16, side="right"))
        hT_sb = p5ctx.enter_context(nc.sbuf_tensor("hT_sb", [128, 8 * 128], BF16, side="right"))
        hp_sb = p5ctx.enter_context(nc.sbuf_tensor("hp_sb", [128, 128], F32, side="right"))
        cmb = p5ctx.enter_context(nc.sbuf_tensor("cmb", [128, 32], F32, side="right"))
        _fst = p5ctx.enter_context(nc.sbuf_tensor("fstage0", [128, 8 * 512], BF16, side="right"))
        fstage = [_fst, _fst]
        fjunk = p5ctx.enter_context(nc.sbuf_tensor("fjunk", [128, 512], BF16, side="right"))
        psF = p5ctx.enter_context(nc.psum_tensor("psF", [128, 4096], F32))

        P.need("gw0"); P.need("gw1"); P.need("swt0"); P.need("swt1")

        P.need("gf0"); P.need("gf1"); P.need("sf0"); P.need("sf1")
        P.need("sxx0"); P.need("sxx1")

        y2t_tr = {}
        hT_done = {}
        base_done_sc = {}
        fspill = {}
        wbt_tr = {}
        wbn_hist = {0: [], 1: []}
        wbr_hist = []
        y2rd_hist = {0: [], 1: []}
        out_done = []

        for sgi, (sg0, sgn) in enumerate(SGL):
            sgctx = ExitStack()
            WbTb = [sgctx.enter_context(nc.sbuf_tensor(f"WbTb{sgi}_{i}", [128, NCH * 512], BF16, side="right")) for i in range(2)]
            wbn = [sgctx.enter_context(nc.sbuf_tensor(f"wbn{sgi}_{i}", [128, D], BF16, side="right")) for i in range(2)]
            _wbr = sgctx.enter_context(nc.sbuf_tensor(f"wbr{sgi}_0", [128, D], F32, side="right"))
            wbr = [_wbr, _wbr]

            for j in range(sgn):
                tt = sg0 + j
                js = j % 2

                @block.sync
                def _(s, tt=tt, js=js, sgi=sgi, j=j):
                    P.w(s, *y2sp_gs[tt])
                    for jsx in (0, 1):
                        if y2rd_hist[jsx]:
                            P.w(s, *y2rd_hist[jsx][-1])
                    P.dma(s, y2rd[js][:], y2_sp[tt], "sxx%d" % js)
                sl_y2 = P.v("sxx%d" % js)

                @block.sync
                def _(s, tt=tt, js=js, sgi=sgi, j=j, sl_y2=sl_y2):
                    P.w(s, "sxx%d" % js, sl_y2)
                    if sgi > 0 and j == 0:
                        P.w(s, "pe", base_done_sc[("pe", sgi - 1)])
                    srcy = y2rd[js][:].rearrange("p (c f) -> p c f", c=NCH)
                    dsty = y2T[:].rearrange("p (c f) -> p c f", c=NCH)[:, :, j * 128:(j + 1) * 128]
                    P.inc(s.dma_start_transpose(dsty, srcy), "swt%d" % js, 16)
                y2t_tr[(sgi, j)] = ("swt%d" % js, P.v("swt%d" % js))
                y2rd_hist[js].append(y2t_tr[(sgi, j)])

                @block.tensor
                def _(t, tt=tt, j=j, sgi=sgi):
                    P.w(t, *y2t_tr[(sgi, j)])
                    if (sgi, j) == (0, 0):
                        P.w(t, *bfl_sc[-1])
                    prev = hT_done.get((sgi, j - 2))
                    if prev:
                        P.w(t, *prev)
                    pbase = (j % 2) * 512
                    for ch in range(NCH):
                        ins = t.matmul(psF[:, pbase: pbase + 136],
                                       y2T[:, ch * 1024 + j * 128: ch * 1024 + (j + 1) * 128],
                                       AW[:, ch * 136:(ch + 1) * 136],
                                       start=(ch == 0), stop=(ch == NCH - 1))
                    P.inc(ins, "pe")
                pe_h = P.v("pe")

                cb = (j % 2) * 16

                @block.vector
                def _(v, j=j, cb=cb, pe_h=pe_h):
                    pbase = (j % 2) * 512
                    P.w(v, "pe", pe_h)
                    lg = psF[:, pbase + 128: pbase + 136]
                    v.reduce_max(cmb[:, cb: cb + 1], lg, axis=AXX)
                    v.drain()
                    v.tensor_scalar(cmb[:, cb + 8: cb + 16], lg, cmb[:, cb: cb + 1], None, op0=ALU.is_ge)
                    v.drain()
                    v.scalar_tensor_tensor(cmb[:, cb + 8: cb + 16], cmb[:, cb + 8: cb + 16], -1e30,
                                           lg, op0=ALU.mult, op1=ALU.add)
                    v.drain()
                    v.reduce_max(cmb[:, cb + 1: cb + 2], cmb[:, cb + 8: cb + 16], axis=AXX)
                    v.drain()
                    v.tensor_scalar(cmb[:, cb + 2: cb + 3], cmb[:, cb: cb + 1], -1.0, None, op0=ALU.mult)
                    v.drain()
                    ins = v.tensor_scalar(cmb[:, cb + 8: cb + 16], lg, cmb[:, cb + 1: cb + 2], None,
                                          op0=ALU.is_ge)
                    P.inc(ins, "dv")
                dv_c1 = P.v("dv")

                @block.scalar
                def _(s, j=j, cb=cb, dv_c1=dv_c1):
                    pbase = (j % 2) * 512
                    P.w(s, "dv", dv_c1)
                    ins = s.activation(cmb[:, cb + 4: cb + 12][:, 0:8], psF[:, pbase + 128: pbase + 136],
                                       AF.Exp, bias=cmb[:, cb + 2: cb + 3], scale=1.0)
                    P.inc(ins, "sc")
                sc_c1 = P.v("sc")

                @block.vector
                def _(v, cb=cb, sc_c1=sc_c1):
                    P.w(v, "sc", sc_c1)
                    v.tensor_mul(cmb[:, cb + 4: cb + 12], cmb[:, cb + 4: cb + 12], cmb[:, cb + 8: cb + 16])
                    v.drain()
                    v.reduce_sum(cmb[:, cb + 3: cb + 4], cmb[:, cb + 4: cb + 12], axis=AXX)
                    v.drain()
                    v.reciprocal(cmb[:, cb + 3: cb + 4], cmb[:, cb + 3: cb + 4])
                    v.drain()
                    ins = v.tensor_scalar(cmb[:, cb + 4: cb + 12], cmb[:, cb + 4: cb + 12],
                                          cmb[:, cb + 3: cb + 4], 2.0, op0=ALU.mult, op1=ALU.mult)
                    P.inc(ins, "dv")
                dv_c2 = P.v("dv")

                @block.vector
                def _(v, j=j, cb=cb, dv_c2=dv_c2):
                    pbase = (j % 2) * 512
                    v.drain()
                    for e8 in range(8):
                        ins = v.tensor_scalar(hp_sb[:, e8 * 16:(e8 + 1) * 16],
                                              psF[:, pbase + e8 * 16: pbase + (e8 + 1) * 16],
                                              cmb[:, cb + 4 + e8: cb + 5 + e8], None, op0=ALU.mult)
                    P.inc(ins, "dv")
                dv_hp = P.v("dv")

                @block.tensor
                def _(t, j=j, dv_hp=dv_hp):
                    pbase = (j % 2) * 512
                    P.w(t, "dv", dv_hp)
                    ins = t.transpose(psF[:, pbase + 256: pbase + 384], hp_sb[:], identf[:])
                    P.inc(ins, "pe")
                pe_hT = P.v("pe")

                @block.vector
                def _(v, j=j, pe_hT=pe_hT):
                    pbase = (j % 2) * 512
                    P.w(v, "pe", pe_hT)
                    ins = v.tensor_copy(hT_sb[:, j * 128:(j + 1) * 128], psF[:, pbase + 256: pbase + 384])
                    P.inc(ins, "dv")
                hT_done[(sgi, j)] = ("dv", P.v("dv"))
            sc_prep = P.v("sc")
            dv_prep = P.v("dv")

            evac_sc5 = {}
            for p in range(8):
                for ot in range(4):
                    ws = ot % 2

                    @block.gpsimd
                    def _(g, sgi=sgi, p=p, ot=ot, ws=ws):
                        if wbr_hist:
                            P.w(g, "dv", wbr_hist[-1])
                        rows = Wb[p * 512 + ot * 128: p * 512 + (ot + 1) * 128, :]
                        P.dma(g, wbr[ws][:], rows, "gw%d" % ws)
                    gl_wb = P.v("gw%d" % ws)

                    @block.vector
                    def _(v, ws=ws, gl_wb=gl_wb):
                        P.w(v, "gw%d" % ws, gl_wb)
                        if wbn_hist[ws]:
                            P.w(v, *wbn_hist[ws][-1])
                        ins = v.tensor_copy(wbn[ws][:], wbr[ws][:])
                        P.inc(ins, "dv")
                    wbr_hist.append(P.v("dv"))
                    dv_cast = P.v("dv")

                    tsem = "sf%d%d" % (p % 2, ot % 2)
                    P.need(tsem)

                    @block.sync
                    def _(s, sgi=sgi, p=p, ot=ot, ws=ws, dv_cast=dv_cast, tsem=tsem):
                        P.w(s, "dv", dv_cast)
                        if ot == 0:
                            prev = evac_sc5.get((sgi, p - 2), evac_sc5.get((sgi - 1, p + 6)))
                            if prev:
                                P.w(s, "sc", prev)
                        srcw = wbn[ws][:].rearrange("p (c f) -> p c f", c=NCH)
                        dstw = WbTb[p % 2][:, ot * D:(ot + 1) * D].rearrange("p (c f) -> p c f", c=NCH)
                        P.inc(s.dma_start_transpose(dstw, srcw), tsem, 16)
                    wbt_tr[(sgi, p, ot)] = (tsem, P.v(tsem))
                    wbn_hist[ws].append((tsem, P.v(tsem)))

                @block.tensor
                def _(t, sgi=sgi, p=p, sgn=sgn, sc_prep=sc_prep, dv_prep=dv_prep):
                    for ot in range(4):
                        P.w(t, *wbt_tr[(sgi, p, ot)])
                    if p == 0:
                        P.w(t, "sc", sc_prep)
                        P.w(t, "dv", dv_prep)
                    prev = evac_sc5.get((sgi, p - 1), evac_sc5.get((sgi - 1, 7)))
                    if prev:
                        P.w(t, "sc", prev)
                    wtv = WbTb[p % 2][:].rearrange("p (o c j) -> p o c j", o=4, j=128)
                    for ch in range(NCH):
                        for j in range(sgn):
                            t.matmul(psF[:, j * 512:(j + 1) * 512],
                                     y2T[:, ch * 1024 + j * 128: ch * 1024 + (j + 1) * 128],
                                     wtv[:, :, ch, :],
                                     start=(ch == 0), stop=False)
                    for j in range(sgn):
                        t.matmul(psF[:, j * 512:(j + 1) * 512],
                                 hT_sb[:, j * 128:(j + 1) * 128],
                                 Bfl[:, p * 512:(p + 1) * 512],
                                 start=False, stop=False)
                    for j in range(sgn):
                        ins = t.matmul(psF[:, j * 512:(j + 1) * 512],
                                       ones_r[:], bbrow[0:1, p * 512:(p + 1) * 512],
                                       start=False, stop=True)
                    P.inc(ins, "pe")
                pe_base = P.v("pe")

                @block.scalar
                def _(s, sgi=sgi, p=p, sgn=sgn, pe_base=pe_base):
                    P.w(s, "pe", pe_base)
                    prev = fspill.get((sgi, p - 1), fspill.get((sgi - 1, 7)))
                    if prev:
                        P.w(s, *prev)
                    for j in range(sgn):
                        ins = s.activation(fstage[p % 2][:, j * 512:(j + 1) * 512],
                                           psF[:, j * 512:(j + 1) * 512], AF.Identity,
                                           accum_out=fsums[:, p * 8 + j: p * 8 + j + 1])
                    P.inc(ins, "sc")
                sc_evac5 = P.v("sc")

                @block.vector
                def _(v, sgi=sgi, p=p, sgn=sgn, sc_evac5=sc_evac5):
                    P.w(v, "sc", sc_evac5)
                    for j in range(sgn):
                        v.drain()
                        ins = v.scalar_tensor_tensor(fjunk[:], fstage[p % 2][:, j * 512:(j + 1) * 512],
                                                     1.0, fstage[p % 2][:, j * 512:(j + 1) * 512],
                                                     op0=ALU.mult, op1=ALU.mult,
                                                     accum_out=fssq[:, p * 8 + j: p * 8 + j + 1])
                    P.inc(ins, "dv")
                evac_sc5[(sgi, p)] = P.v("sc")
                dv_ssq5 = P.v("dv")

                @block.gpsimd
                def _(g, sgi=sgi, sg0=sg0, p=p, sgn=sgn, sc_evac5=sc_evac5, dv_ssq5=dv_ssq5):
                    P.w(g, "sc", sc_evac5)
                    P.w(g, "dv", dv_ssq5)
                    for j in range(sgn):
                        P.dma(g, f_sp[sg0 + j][:, p * 512:(p + 1) * 512],
                              fstage[p % 2][:, j * 512:(j + 1) * 512], "gf%d" % (p % 2))
                fspill[(sgi, p)] = ("gf%d" % (p % 2), P.v("gf%d" % (p % 2)))
            base_done_sc[("pe", sgi)] = P.v("pe")
            sg_base_end = P.snap()

            sgctx.close()
            for engsec in (block.tensor, block.vector, block.scalar, block.sync, block.gpsimd):
                @engsec
                def _(e, snap=sg_base_end):
                    for nm, val in snap.items():
                        P.w(e, nm, val)

            apctx = ExitStack()
            fread = [apctx.enter_context(nc.sbuf_tensor(f"fread{sgi}_{i}", [128, D], BF16, side="right")) for i in range(2)]
            xb2 = [apctx.enter_context(nc.sbuf_tensor(f"xb2{sgi}_{i}", [128, D], F32, side="right")) for i in range(2)]
            outsb = [apctx.enter_context(nc.sbuf_tensor(f"outsb{sgi}_{i}", [128, D], F32, side="right")) for i in range(2)]

            ap_dv = {}
            out_st = {}
            for j in range(sgn):
                tt = sg0 + j
                js = j % 2
                ab = 40 + js * 12

                P.need("sap%d" % js)

                @block.sync
                def _(s, tt=tt, j=j, js=js, sgi=sgi):
                    prev = ap_dv.get(j - 2)
                    if prev:
                        P.w(s, "dv", prev)
                    P.w(s, *fspill[(sgi, 7)])
                    P.w(s, *x2sp_gs[tt])
                    P.dma(s, fread[js][:], f_sp[tt], "sap%d" % js)
                    P.dma(s, xb2[js][:], x2_sp[tt], "sap%d" % js)
                sl_ap = P.v("sap%d" % js)

                @block.vector
                def _(v, j=j, js=js, ab=ab, sl_ap=sl_ap):
                    v.tensor_reduce(stat[:, ab: ab + 1], fsums[:, j: 57 + j: 8][:, 0:8], axis=AXX, op=ALU.add)
                    v.drain()
                    v.tensor_reduce(stat[:, ab + 1: ab + 2], fssq[:, j: 57 + j: 8][:, 0:8], axis=AXX, op=ALU.add)
                    v.drain()
                    v.tensor_scalar(stat[:, ab: ab + 1], stat[:, ab: ab + 1], -1.0 / D, None, op0=ALU.mult)
                    v.drain()
                    v.scalar_tensor_tensor(stat[:, ab + 2: ab + 3], stat[:, ab: ab + 1], -1.0,
                                           stat[:, ab: ab + 1], op0=ALU.mult, op1=ALU.mult)
                    v.drain()
                    v.scalar_tensor_tensor(stat[:, ab + 2: ab + 3], stat[:, ab + 1: ab + 2], 1.0 / D,
                                           stat[:, ab + 2: ab + 3], op0=ALU.mult, op1=ALU.add)
                    v.drain()
                    ins = v.tensor_scalar(stat[:, ab + 2: ab + 3], stat[:, ab + 2: ab + 3], EPS, None, op0=ALU.add)
                    P.inc(ins, "dv")
                dv_r4 = P.v("dv")

                @block.scalar
                def _(s, j=j, js=js, ab=ab, dv_r4=dv_r4, sl_ap=sl_ap):
                    P.w(s, "dv", dv_r4)
                    P.w(s, "sap%d" % js, sl_ap)
                    prevst = out_st.get(j - 2)
                    if prevst:
                        P.w(s, *prevst)
                    s.activation(stat[:, ab + 2: ab + 3], stat[:, ab + 2: ab + 3], AF.Ln)
                    s.drain()
                    s.activation(stat[:, ab + 2: ab + 3], stat[:, ab + 2: ab + 3], AF.Exp, scale=-0.5)
                    s.activation(outsb[js][:], fread[js][:], AF.Identity,
                                 bias=stat[:, ab: ab + 1])
                    s.drain()
                    ins = s.activation(outsb[js][:], outsb[js][:], AF.Identity,
                                       scale=stat[:, ab + 2: ab + 3])
                    P.inc(ins, "sc")
                sc_ln4 = P.v("sc")

                @block.vector
                def _(v, js=js, sc_ln4=sc_ln4):
                    P.w(v, "sc", sc_ln4)
                    v.drain()
                    ins = v.tensor_add(outsb[js][:], outsb[js][:], xb2[js][:])
                    P.inc(ins, "dv")
                ap_dv[j] = P.v("dv")
                dv_out = P.v("dv")

                P.need("so%d" % js)

                @block.sync
                def _(s, tt=tt, js=js, dv_out=dv_out):
                    P.w(s, "dv", dv_out)
                    P.dma(s, out[tt * 128:(tt + 1) * 128, :], outsb[js][:], "so%d" % js)
                out_st[j] = ("so%d" % js, P.v("so%d" % js))
                out_done.append(out_st[j])
            ap_end = P.snap()
            apctx.close()
            for engsec in (block.tensor, block.vector, block.scalar, block.sync, block.gpsimd):
                @engsec
                def _(e, snap=ap_end):
                    for nm, val in snap.items():
                        P.w(e, nm, val)

        @block.sync
        def _(s):
            for nm in ("so0", "so1"):
                if nm in P.sems:
                    s.wait_ge(P.sems[nm], P.v(nm))

    return nc


def _in_maps(inputs):
    x = np.ascontiguousarray(np.asarray(inputs["L0_attn_x"], dtype=np.float32))
    wq = np.ascontiguousarray(np.asarray(inputs["L0_attn_wq"], dtype=np.float32))
    wk = np.ascontiguousarray(np.asarray(inputs["L0_attn_wk"], dtype=np.float32))
    wv = np.ascontiguousarray(np.asarray(inputs["L0_attn_wv"], dtype=np.float32))
    wo = np.ascontiguousarray(np.asarray(inputs["L0_attn_wo"], dtype=np.float32))
    Wb = np.ascontiguousarray(np.asarray(inputs["L0_ffn_W_base"], dtype=np.float32))
    bb = np.ascontiguousarray(np.asarray(inputs["L0_ffn_b_base"], dtype=np.float32)).reshape(1, D)
    Wr = np.ascontiguousarray(np.asarray(inputs["L0_ffn_W_router"], dtype=np.float32))
    Ast = np.ascontiguousarray(np.asarray(inputs["L0_ffn_A_stack"], dtype=np.float32))
    Bst = np.ascontiguousarray(np.asarray(inputs["L0_ffn_B_stack"], dtype=np.float32))
    maps = []
    for c in range(8):
        b, half = c // 2, c % 2
        xr = np.ascontiguousarray(np.roll(x[b], -half * TOWN, axis=0))
        maps.append({"xa": xr, "wq": wq, "wk": wk, "wv": wv, "wo": wo,
                     "Wb": Wb, "bb": bb, "Wr": Wr, "Ast": Ast, "Bst": Bst})
    return maps


def kernel(**inputs):
    nc = build_nc(debug=False)
    maps = _in_maps(inputs)
    res = run_bass_kernel_spmd(nc, maps, core_ids=list(range(8)))
    outf = np.empty((4, S, D), dtype=np.float32)
    for c in range(8):
        b, half = c // 2, c % 2
        outf[b, half * TOWN:(half + 1) * TOWN] = res.results[c]["out"]
    return outf


# version 11
# speedup vs baseline: 1.3624x; 1.0779x over previous
import numpy as np
import ml_dtypes
from contextlib import ExitStack
import concourse.bass as bass
import concourse.mybir as mybir
from concourse.bass_utils import run_bass_kernel_spmd

F32 = mybir.dt.float32
BF16 = mybir.dt.bfloat16
AF = mybir.ActivationFunctionType
ALU = mybir.AluOpType
AXX = mybir.AxisListType.X

S = 4096
D = 4096
HKV = 8
DH = 128
KV = 1024
TOWN = 2048
NCH = 32
NT = 16
EPS = 1e-5
RSD = float(1.0 / np.sqrt(128.0))
P40 = float(2.0 ** 40)

SEM_NAMES = ["gl", "so", "pe", "dv", "sc", "gp",
             "sx0", "sx1", "sw0", "sw1", "sy0", "sy1",
             "st0", "st1", "gy0", "gy1",
             "gk0", "gk1", "gq0", "gq1", "gv0", "gv1"]


class Prog:

    def __init__(self, nc, ctx):
        self.nc = nc
        self.ctx = ctx
        self.sems = {}
        self.cnt = {}
        for name in SEM_NAMES:
            self._mk(name)

    def _mk(self, name):
        self.sems[name] = self.ctx.enter_context(self.nc.semaphore(name))
        self.cnt[name] = 0

    def need(self, name):
        if name not in self.sems:
            self._mk(name)

    def inc(self, instr, name, by=1):
        self.cnt[name] += by
        instr.then_inc(self.sems[name], by)
        return self.cnt[name]

    def dma(self, eng, outap, inap, name, **kw):
        ins = eng.dma_start(out=outap, in_=inap, **kw)
        return self.inc(ins, name, 16)

    def w(self, eng, name, val):
        assert val <= self.cnt[name], f"wait {name}>={val} emitted={self.cnt[name]}"
        if val > 0:
            eng.wait_ge(self.sems[name], val)

    def wfut(self, eng, name, val):
        if val > 0:
            eng.wait_ge(self.sems[name], val)

    def v(self, name):
        return self.cnt[name]

    def snap(self):
        return dict(self.cnt)

    def wall(self, eng, snapshot):
        for name, val in snapshot.items():
            self.w(eng, name, val)


def build_nc(debug=False, phases=7, n_tiles=32, n_groups=8):
    n_groups_own = max(1, n_groups // 2)
    nc = bass.Bass("TRN2", target_bir_lowering=False)

    xa = nc.dram_tensor("xa", [S, D], F32, kind="ExternalInput").ap()
    wq = nc.dram_tensor("wq", [D, D], F32, kind="ExternalInput").ap()
    wk = nc.dram_tensor("wk", [D, KV], F32, kind="ExternalInput").ap()
    wv = nc.dram_tensor("wv", [D, KV], F32, kind="ExternalInput").ap()
    wo = nc.dram_tensor("wo", [KV, D], F32, kind="ExternalInput").ap()
    Wb = nc.dram_tensor("Wb", [D, D], F32, kind="ExternalInput").ap()
    bb = nc.dram_tensor("bb", [1, D], F32, kind="ExternalInput").ap()
    Wr = nc.dram_tensor("Wr", [HKV, D], F32, kind="ExternalInput").ap()
    Ast = nc.dram_tensor("Ast", [8, 16, D], F32, kind="ExternalInput").ap()
    Bst = nc.dram_tensor("Bst", [8, D, 16], F32, kind="ExternalInput").ap()
    out = nc.dram_tensor("out", [TOWN, D], F32, kind="ExternalOutput").ap()

    ident_bf_d = nc.inline_tensor(np.eye(128, dtype=ml_dtypes.bfloat16), name="ident_bf").ap()
    ident_f_d = nc.inline_tensor(np.eye(128, dtype=np.float32), name="ident_f").ap()

    skind = "ExternalOutput" if debug else "Internal"
    y1t_sp = nc.dram_tensor("y1t_sp", [32, 128, D], BF16, kind=skind).ap()
    q_sp = nc.dram_tensor("q_sp", [HKV, 128, TOWN], BF16, kind=skind).ap()
    k_sp = nc.dram_tensor("k_sp", [HKV, 128, S], BF16, kind=skind).ap()
    v_sp = nc.dram_tensor("v_sp", [HKV, NCH, 128, 128], BF16, kind=skind).ap()
    x2_sp = nc.dram_tensor("x2_sp", [NT, 128, D], F32, kind=skind).ap()
    y2_sp = nc.dram_tensor("y2_sp", [NT, 128, D], BF16, kind=skind).ap()
    f_sp = nc.dram_tensor("f_sp", [NT, 128, D], BF16, kind=skind).ap()
    vt_dbg = None
    if debug:
        vt_dbg = nc.dram_tensor("vt_dbg", [128, HKV * TOWN], BF16, kind="ExternalOutput").ap()

    ctx = ExitStack()
    P = Prog(nc, ctx)

    ident = ctx.enter_context(nc.sbuf_tensor("ident", [128, 128], BF16))
    identf = ctx.enter_context(nc.sbuf_tensor("identf", [128, 128], F32))
    ones_r = ctx.enter_context(nc.sbuf_tensor("ones_r", [1, 128], BF16))
    ones_rf = ctx.enter_context(nc.sbuf_tensor("ones_rf", [1, 128], F32))
    onesc = ctx.enter_context(nc.sbuf_tensor("onesc", [128, 1], BF16))
    stat = ctx.enter_context(nc.sbuf_tensor("stat", [128, 64], F32))
    st1 = ctx.enter_context(nc.sbuf_tensor("st1", [1, 16], F32))
    st1b = ctx.enter_context(nc.sbuf_tensor("st1b", [1, 16], BF16))

    block = ctx.enter_context(nc.Block())

    with ctx:
        @block.gpsimd
        def _(g):
            P.dma(g, ident[:], ident_bf_d, "gl")
            P.dma(g, identf[:], ident_f_d, "gl")

        @block.vector
        def _(v):
            v.memset(ones_r[:], 1.0)
            v.memset(ones_rf[:], 1.0)
            P.inc(v.memset(onesc[:], 1.0), "dv")

        p2rctx = ExitStack()
        wk_h = p2rctx.enter_context(nc.sbuf_tensor("wk_h", [128, NCH * 512], BF16, side="right"))
        wv_h = p2rctx.enter_context(nc.sbuf_tensor("wv_h", [128, NCH * 512], BF16, side="right"))
        y1g = [p2rctx.enter_context(nc.sbuf_tensor(f"y1g{i}", [128, NCH * 512], BF16, side="right")) for i in range(2)]
        kstg = [p2rctx.enter_context(nc.sbuf_tensor(f"kstg{i}", [128, 512], BF16, side="right")) for i in range(2)]
        vstg = [p2rctx.enter_context(nc.sbuf_tensor(f"vstg{i}", [128, 512], BF16, side="right")) for i in range(2)]
        ps2a = p2rctx.enter_context(nc.psum_tensor("ps2a", [128, 2048], F32))
        ps2b = p2rctx.enter_context(nc.psum_tensor("ps2b", [128, 1024], F32))

        p1ctx = ExitStack()
        xin = [p1ctx.enter_context(nc.sbuf_tensor(f"xin{i}", [128, D], F32, side="right")) for i in range(2)]
        yb = [p1ctx.enter_context(nc.sbuf_tensor(f"yb{i}", [128, D], BF16, side="right")) for i in range(2)]
        ytr = [p1ctx.enter_context(nc.sbuf_tensor(f"ytr{i}", [128, D], BF16, side="right")) for i in range(2)]

        apply_sc, spill_gs, tpose_tr = [], [], []
        wkv_gl = {}

        def emit_wkv_cast(H, wait_pe=0):
            @block.gpsimd
            def _(g, H=H, wait_pe=wait_pe):
                if wait_pe:
                    P.w(g, "pe", wait_pe)
                src_k = wk[:, H * 512:(H + 1) * 512].rearrange("(c p) f -> p c f", p=128)
                P.dma(g, wk_h[:].rearrange("p (c f) -> p c f", c=NCH), src_k, "gl")
                src_v = wv[:, H * 512:(H + 1) * 512].rearrange("(c p) f -> p c f", p=128)
                P.dma(g, wv_h[:].rearrange("p (c f) -> p c f", c=NCH), src_v, "gl")
            wkv_gl[H] = P.v("gl")

        emit_wkv_cast(0)

        def emit_p1_tile(t):
            slot = t % 2
            sb = slot * 8

            @block.sync
            def _(s, t=t, slot=slot):
                if t >= 2:
                    P.w(s, "sc", apply_sc[t - 2])
                P.dma(s, xin[slot][:], xa[t * 128:(t + 1) * 128, :], "sx%d" % slot)
            sl_now = P.v("sx%d" % slot)

            @block.vector
            def _(v, t=t, slot=slot, sb=sb, sl_now=sl_now):
                P.w(v, "sx%d" % slot, sl_now)
                v.reduce_sum(stat[:, sb: sb + 1], xin[slot][:], axis=AXX)
                v.drain()
                if t >= 2:
                    P.w(v, *spill_gs[t - 2])
                v.scalar_tensor_tensor(ytr[slot][:], xin[slot][:], 1.0, xin[slot][:],
                                       op0=ALU.mult, op1=ALU.mult,
                                       accum_out=stat[:, sb + 1: sb + 2])
                v.drain()
                v.tensor_scalar(stat[:, sb + 2: sb + 3], stat[:, sb: sb + 1], -1.0 / D, None, op0=ALU.mult)
                v.drain()
                v.scalar_tensor_tensor(stat[:, sb + 3: sb + 4], stat[:, sb + 2: sb + 3], -1.0,
                                       stat[:, sb + 2: sb + 3], op0=ALU.mult, op1=ALU.mult)
                v.drain()
                v.scalar_tensor_tensor(stat[:, sb + 3: sb + 4], stat[:, sb + 1: sb + 2], 1.0 / D,
                                       stat[:, sb + 3: sb + 4], op0=ALU.mult, op1=ALU.add)
                v.drain()
                ins = v.tensor_scalar(stat[:, sb + 3: sb + 4], stat[:, sb + 3: sb + 4], EPS, None, op0=ALU.add)
                P.inc(ins, "dv")
            dv_rec = P.v("dv")

            @block.scalar
            def _(s, t=t, slot=slot, sb=sb, dv_rec=dv_rec):
                P.w(s, "dv", dv_rec)
                if t >= 2:
                    P.w(s, *tpose_tr[t - 2])
                s.activation(stat[:, sb + 4: sb + 5], stat[:, sb + 3: sb + 4], AF.Ln)
                s.activation(yb[slot][:], xin[slot][:], AF.Identity,
                             bias=stat[:, sb + 2: sb + 3])
                s.drain()
                s.activation(stat[:, sb + 4: sb + 5], stat[:, sb + 4: sb + 5], AF.Exp, scale=-0.5)
                s.drain()
                ins = s.activation(yb[slot][:], yb[slot][:], AF.Identity,
                                   scale=stat[:, sb + 4: sb + 5])
                P.inc(ins, "sc")
            apply_sc.append(P.v("sc"))
            sc_now = P.v("sc")

            @block.sync
            def _(s, t=t, slot=slot, sc_now=sc_now):
                P.w(s, "sc", sc_now)
                if t >= 2:
                    P.w(s, *spill_gs[t - 2])
                srcy = yb[slot][:].rearrange("p (c f) -> p c f", c=NCH)
                dsty = ytr[slot][:].rearrange("p (c f) -> p c f", c=NCH)
                P.inc(s.dma_start_transpose(dsty, srcy), "st%d" % slot, 16)
            tpose_tr.append(("st%d" % slot, P.v("st%d" % slot)))
            tr_now = P.v("st%d" % slot)

            @block.gpsimd
            def _(g, t=t, slot=slot, tr_now=tr_now):
                P.w(g, "st%d" % slot, tr_now)
                P.dma(g, y1t_sp[t], ytr[slot][:], "gy%d" % slot)
            spill_gs.append(("gy%d" % slot, P.v("gy%d" % slot)))

        y1g_sl = {}
        y1g_done_pe = {}
        evac_hist = {"k": [], "q": [], "v": []}
        spill_hist = {"k": [], "q": [], "v": []}
        qkv_spill_snap = {}

        def emit_y1g_load(key, g, gslot, prev_pe):
            @block.sync
            def _(s, g=g, gslot=gslot, prev_pe=prev_pe):
                if prev_pe:
                    P.w(s, "pe", prev_pe)
                P.w(s, *spill_gs[4 * g + 3])
                P.w(s, *spill_gs[4 * g + 2])
                for j in range(4):
                    srcg = y1t_sp[4 * g + j].rearrange("p (c f) -> p c f", c=NCH)
                    dstg = y1g[gslot][:].rearrange("p (c f) -> p c f", c=NCH)[:, :, j * 128:(j + 1) * 128]
                    P.dma(s, dstg, srcg, "sy%d" % gslot)
            y1g_sl[key] = ("sy%d" % gslot, P.v("sy%d" % gslot))

        def emit_k_unit(H, g, ct):
            gslot = g % 2
            n = len(evac_hist["k"])
            pslot = ct % 2

            @block.tensor
            def _(t, H=H, g=g, ct=ct, pslot=pslot, gslot=gslot, n=n):
                P.w(t, *y1g_sl[("kv", H, g)])
                if g == 0 and ct == 0:
                    P.w(t, "gl", wkv_gl[H])
                if n >= 2:
                    P.w(t, "dv", evac_hist["k"][n - 2])
                for ch in range(NCH):
                    ins = t.matmul(ps2a[:, pslot * 512:(pslot + 1) * 512],
                                   wk_h[:, ch * 512 + ct * 128: ch * 512 + (ct + 1) * 128],
                                   y1g[gslot][:, ch * 512:(ch + 1) * 512],
                                   start=(ch == 0), stop=(ch == NCH - 1))
                P.inc(ins, "pe")
            pe_now = P.v("pe")

            @block.vector
            def _(v, pe_now=pe_now, pslot=pslot, n=n):
                P.w(v, "pe", pe_now)
                if n >= 2:
                    P.w(v, *spill_hist["k"][n - 2])
                ins = v.tensor_copy(kstg[n % 2][:], ps2a[:, pslot * 512:(pslot + 1) * 512])
                P.inc(ins, "dv")
            evac_hist["k"].append(P.v("dv"))
            dv_now = P.v("dv")
            gsem = "gk%d" % (n % 2)

            @block.gpsimd
            def _(gp, H=H, g=g, ct=ct, dv_now=dv_now, n=n, gsem=gsem):
                P.w(gp, "dv", dv_now)
                P.dma(gp, k_sp[H * 4 + ct][:, g * 512:(g + 1) * 512], kstg[n % 2][:], gsem)
            spill_hist["k"].append((gsem, P.v(gsem)))

        def emit_v_unit(H, g, tt):
            gslot = g % 2
            n = len(evac_hist["v"])
            pslot = tt % 2

            @block.tensor
            def _(t, H=H, g=g, tt=tt, pslot=pslot, gslot=gslot, n=n):
                if n >= 2:
                    P.w(t, "dv", evac_hist["v"][n - 2])
                for ch in range(NCH):
                    ins = t.matmul(ps2b[:, pslot * 512:(pslot + 1) * 512],
                                   y1g[gslot][:, ch * 512 + tt * 128: ch * 512 + (tt + 1) * 128],
                                   wv_h[:, ch * 512:(ch + 1) * 512],
                                   start=(ch == 0), stop=(ch == NCH - 1))
                P.inc(ins, "pe")
            pe_now = P.v("pe")
            if tt == 3:
                y1g_done_pe[("kv", H, g)] = pe_now

            @block.vector
            def _(v, pe_now=pe_now, pslot=pslot, n=n):
                P.w(v, "pe", pe_now)
                if n >= 2:
                    P.w(v, *spill_hist["v"][n - 2])
                ins = v.tensor_copy(vstg[n % 2][:], ps2b[:, pslot * 512:(pslot + 1) * 512])
                P.inc(ins, "dv")
            evac_hist["v"].append(P.v("dv"))
            dv_now = P.v("dv")
            gsem = "gv%d" % (n % 2)

            @block.gpsimd
            def _(gp, H=H, g=g, tt=tt, dv_now=dv_now, n=n, gsem=gsem):
                P.w(gp, "dv", dv_now)
                for i in range(4):
                    P.dma(gp, v_sp[H * 4 + i, 4 * g + tt], vstg[n % 2][:, i * 128:(i + 1) * 128], gsem)
            spill_hist["v"].append((gsem, P.v(gsem)))

        def emit_kv_units(H, g, units):
            for u in units:
                if u < 4:
                    emit_k_unit(H, g, u)
                else:
                    emit_v_unit(H, g, u - 4)

        for t in range(n_tiles):
            emit_p1_tile(t)
            g = t // 4
            ph = t % 4
            if ph == 3:
                prev_pe = y1g_done_pe.get(("kv", 0, g - 2), 0)
                emit_y1g_load(("kv", 0, g), g, g % 2, prev_pe)
            if g >= 1:
                emit_kv_units(0, g - 1, [2 * ph, 2 * ph + 1])
        emit_kv_units(0, n_groups - 1, list(range(8)))
        last_h0_pe = P.v("pe")
        emit_wkv_cast(1, wait_pe=last_h0_pe)
        for g in range(n_groups):
            prev_pe = y1g_done_pe.get(("kv", 1, g - 2), y1g_done_pe.get(("kv", 0, g + n_groups - 2), 0))
            emit_y1g_load(("kv", 1, g), g, g % 2, prev_pe)
            emit_kv_units(1, g, list(range(8)))

        kv_end = P.snap()
        for hh in range(8):
            HH, cth = hh // 4, hh % 4
            w = {}
            nk = (HH * n_groups + (n_groups - 1)) * 4 + cth
            w.update(dict([spill_hist["k"][nk]]))
            for sl in range(2):
                lastv = None
                for i, ent in enumerate(spill_hist["v"]):
                    if i < HH * n_groups * 4 or i >= (HH + 1) * n_groups * 4:
                        continue
                    if ent[0] == "gv%d" % sl:
                        lastv = ent
                if lastv:
                    w[lastv[0]] = max(w.get(lastv[0], 0), lastv[1])
            qkv_spill_snap[hh] = w

        p1_end = P.snap()
        p1ctx.close()

        p2qctx = ExitStack()
        wqin = [p2qctx.enter_context(nc.sbuf_tensor(f"wqin{i}", [128, TOWN], F32, side="right")) for i in range(2)]
        wacc = [p2qctx.enter_context(nc.sbuf_tensor(f"wacc{i}", [128, 512], F32, side="right")) for i in range(2)]
        wq_eff = p2qctx.enter_context(nc.sbuf_tensor("wq_eff", [128, NCH * 512], BF16, side="right"))
        qstg = [p2qctx.enter_context(nc.sbuf_tensor(f"qstg{i}", [128, 512], BF16, side="right")) for i in range(2)]

        for engsec in (block.tensor, block.vector, block.scalar, block.sync, block.gpsimd):
            @engsec
            def _(e):
                for nm, val in p1_end.items():
                    P.w(e, nm, val)

        presum_list = []

        def emit_presum_chunk(H, ch, wait_pe=0):
            n = len(presum_list)
            slot = n % 2

            @block.sync
            def _(s, H=H, ch=ch, slot=slot, n=n):
                if n >= 2:
                    P.w(s, "dv", presum_list[n - 2])
                P.dma(s, wqin[slot][:], wq[ch * 128:(ch + 1) * 128, H * TOWN:(H + 1) * TOWN], "sw%d" % slot)
            sl_now = P.v("sw%d" % slot)

            @block.vector
            def _(v, H=H, ch=ch, slot=slot, sl_now=sl_now, wait_pe=wait_pe):
                P.w(v, "sw%d" % slot, sl_now)
                if wait_pe and ch == 0:
                    P.w(v, "pe", wait_pe)
                srcq = wqin[slot][:].rearrange("p (kv g d) -> p kv g d", kv=4, g=4)
                acc = wacc[slot][:].rearrange("p (kv d) -> p kv d", kv=4)
                v.tensor_add(acc, srcq[:, :, 0, :], srcq[:, :, 1, :])
                v.drain()
                v.tensor_add(acc, acc, srcq[:, :, 2, :])
                v.drain()
                v.tensor_add(acc, acc, srcq[:, :, 3, :])
                v.drain()
                ins = v.tensor_scalar(wq_eff[:, ch * 512:(ch + 1) * 512], wacc[slot][:],
                                      RSD, None, op0=ALU.mult)
                P.inc(ins, "dv")
            presum_list.append(P.v("dv"))

        for H in range(2):
            wait_pe = 0
            if H == 1:
                wait_pe = P.v("pe")
            for ch in range(NCH):
                emit_presum_chunk(H, ch, wait_pe=wait_pe)
            presum_done = P.v("dv")

            for g in range(n_groups_own):
                gslot = g % 2
                prev_pe = y1g_done_pe.get(("q", H, g - 2), y1g_done_pe.get(("q", H - 1, g + n_groups_own - 2), 0))
                if H == 0 and g < 2:
                    prev_pe = 0
                emit_y1g_load(("q", H, g), g, gslot, prev_pe or kv_end["pe"])

                for ct in range(4):
                    n = len(evac_hist["q"])
                    pslot = 2 + ct % 2

                    @block.tensor
                    def _(t, H=H, g=g, ct=ct, pslot=pslot, gslot=gslot, n=n, presum_done=presum_done):
                        P.w(t, *y1g_sl[("q", H, g)])
                        if g == 0 and ct == 0:
                            P.w(t, "dv", presum_done)
                        if n >= 2:
                            P.w(t, "dv", evac_hist["q"][n - 2])
                        for ch in range(NCH):
                            ins = t.matmul(ps2a[:, pslot * 512:(pslot + 1) * 512],
                                           wq_eff[:, ch * 512 + ct * 128: ch * 512 + (ct + 1) * 128],
                                           y1g[gslot][:, ch * 512:(ch + 1) * 512],
                                           start=(ch == 0), stop=(ch == NCH - 1))
                        P.inc(ins, "pe")
                    pe_now = P.v("pe")
                    if ct == 3:
                        y1g_done_pe[("q", H, g)] = pe_now

                    @block.vector
                    def _(v, pe_now=pe_now, pslot=pslot, n=n):
                        P.w(v, "pe", pe_now)
                        if n >= 2:
                            P.w(v, *spill_hist["q"][n - 2])
                        ins = v.tensor_copy(qstg[n % 2][:], ps2a[:, pslot * 512:(pslot + 1) * 512])
                        P.inc(ins, "dv")
                    evac_hist["q"].append(P.v("dv"))
                    dv_now = P.v("dv")
                    gsem = "gq%d" % (n % 2)

                    @block.gpsimd
                    def _(gp, H=H, g=g, ct=ct, dv_now=dv_now, n=n, gsem=gsem):
                        P.w(gp, "dv", dv_now)
                        P.dma(gp, q_sp[H * 4 + ct][:, g * 512:(g + 1) * 512], qstg[n % 2][:], gsem)
                    spill_hist["q"].append((gsem, P.v(gsem)))

        for hh in range(8):
            HH, cth = hh // 4, hh % 4
            nq = (HH * n_groups_own + (n_groups_own - 1)) * 4 + cth
            ent = spill_hist["q"][nq]
            qkv_spill_snap[hh] = dict(qkv_spill_snap[hh])
            qkv_spill_snap[hh][ent[0]] = max(qkv_spill_snap[hh].get(ent[0], 0), ent[1])
        p2qctx.close()

        p2_end = P.snap()
        p2rctx.close()

        for engsec in (block.tensor, block.vector, block.scalar, block.sync, block.gpsimd):
            @engsec
            def _(e):
                for nm, val in p2_end.items():
                    P.w(e, nm, val)

        p3vctx = ExitStack()
        valuesT = p3vctx.enter_context(nc.sbuf_tensor("valuesT", [128, HKV * TOWN], BF16, side="right"))
        p3ctx = ExitStack()
        kbuf = [p3ctx.enter_context(nc.sbuf_tensor(f"kbuf{i}", [128, S], BF16, side="right")) for i in range(2)]
        vbuf = [p3ctx.enter_context(nc.sbuf_tensor(f"vbuf{i}", [128, S], BF16, side="right")) for i in range(2)]
        qbuf = [p3ctx.enter_context(nc.sbuf_tensor(f"qbuf{i}", [128, TOWN], BF16, side="right")) for i in range(2)]
        PTb = [p3ctx.enter_context(nc.sbuf_tensor(f"PTb{i}", [128, 4 * S], BF16, side="right")) for i in range(2)]
        recd_sb = p3ctx.enter_context(nc.sbuf_tensor("recd_sb", [1, 512], F32, side="right"))
        mhjunk = p3ctx.enter_context(nc.sbuf_tensor("mhjunk", [128, S], BF16, side="right"))
        bias_h = p3ctx.enter_context(nc.sbuf_tensor("bias_h", [128, 2], F32, side="right"))
        bcast_sb = p3ctx.enter_context(nc.sbuf_tensor("bcast_sb", [128, 1024], F32, side="right"))
        psS = [p3ctx.enter_context(nc.psum_tensor(f"psS{i}", [128, 1024], F32)) for i in range(3)]
        psV3 = p3ctx.enter_context(nc.psum_tensor("psV3", [128, 512], F32))
        psX = p3ctx.enter_context(nc.psum_tensor("psX", [128, 512], F32))

        for nm in ("gld0", "gld1", "sp0", "sp1", "sp2", "sp3"):
            P.need(nm)

        @block.vector
        def _(v):
            P.inc(v.memset(valuesT[:], 0.0), "dv")

        NKEY = n_groups * 512
        NKC = NKEY // 128
        G3 = n_groups_own
        head_done_pe = {}
        pv_dv = {}
        recd_dv = {}
        bcast_dv = {}
        exp_hist_pergroup = {}
        exp_hist = {}
        n_heads = min(HKV, 8)

        for h in range(n_heads):
            hs = h % 2
            gld = "gld%d" % hs
            recd_dv = {}
            bcast_dv = dict(bcast_dv)
            exp_hist_pergroup = {}

            @block.gpsimd
            def _(g, h=h, hs=hs, gld=gld):
                if h >= 2:
                    P.w(g, "pe", head_done_pe[h - 2])
                for nm, val in qkv_spill_snap[h].items():
                    P.w(g, nm, val)
                P.dma(g, kbuf[hs][:, 0:NKEY], k_sp[h][:, 0:NKEY], gld)
                P.dma(g, vbuf[hs][:, 0:NKEY], v_sp[h][0:NKC].rearrange("c p j -> p c j"), gld)
                P.dma(g, qbuf[hs][:, 0:G3 * 512], q_sp[h][:, 0:G3 * 512], gld)
            gld_now = P.v(gld)

            @block.vector
            def _(v, h=h, hs=hs, gld_now=gld_now, gld=gld):
                P.w(v, gld, gld_now)
                if h >= 1:
                    P.w(v, "pe", P.v("pe"))
                ins = v.scalar_tensor_tensor(mhjunk[:, 0:G3 * 512], qbuf[hs][:, 0:G3 * 512], 1.0,
                                             qbuf[hs][:, 0:G3 * 512], op0=ALU.mult, op1=ALU.mult)
                P.inc(ins, "dv")
            sc_q2 = ("dv", P.v("dv"))

            for i in range(G3):
                @block.tensor
                def _(t, i=i, sc_q2=sc_q2):
                    P.w(t, *sc_q2)
                    if i > 0:
                        P.w(t, "dv", P.v("dv"))
                    ins = t.matmul(psX[0:1, :], onesc[:], mhjunk[:, i * 512:(i + 1) * 512],
                                   start=True, stop=True)
                    P.inc(ins, "pe")
                pe_now = P.v("pe")

                @block.vector
                def _(v, h=h, i=i, pe_now=pe_now):
                    P.w(v, "pe", pe_now)
                    v.reduce_max(st1[:, 1:2], psX[0:1, :], axis=AXX)
                    v.drain()
                    if i == 0:
                        ins = v.tensor_copy(st1[:, 2:3], st1[:, 1:2])
                    else:
                        ins = v.tensor_tensor(st1[:, 2:3], st1[:, 2:3], st1[:, 1:2], op=ALU.max)
                    v.drain()
                    P.inc(ins, "dv")

            pe_q2done = P.v("pe")

            @block.vector
            def _(v, h=h, hs=hs, pe_q2done=pe_q2done):
                P.w(v, "pe", pe_q2done)
                ins = v.scalar_tensor_tensor(mhjunk[:, 0:NKEY], kbuf[hs][:, 0:NKEY], 1.0,
                                             kbuf[hs][:, 0:NKEY], op0=ALU.mult, op1=ALU.mult)
                P.inc(ins, "dv")
            sc_k2 = ("dv", P.v("dv"))

            for i in range(NKEY // 512):
                @block.tensor
                def _(t, i=i, sc_k2=sc_k2):
                    P.w(t, *sc_k2)
                    P.w(t, "dv", P.v("dv"))
                    ins = t.matmul(psX[0:1, :], onesc[:], mhjunk[:, i * 512:(i + 1) * 512],
                                   start=True, stop=True)
                    P.inc(ins, "pe")
                pe_now = P.v("pe")

                @block.vector
                def _(v, h=h, i=i, pe_now=pe_now):
                    P.w(v, "pe", pe_now)
                    v.reduce_max(st1[:, 1:2], psX[0:1, :], axis=AXX)
                    v.drain()
                    if i == 0:
                        ins = v.tensor_copy(st1[:, 3:4], st1[:, 1:2])
                    else:
                        ins = v.tensor_tensor(st1[:, 3:4], st1[:, 3:4], st1[:, 1:2], op=ALU.max)
                    v.drain()
                    P.inc(ins, "dv")

            @block.vector
            def _(v):
                ins = v.tensor_mul(st1[:, 4:5], st1[:, 2:3], st1[:, 3:4])
                P.inc(ins, "dv")
            dv_m = P.v("dv")

            @block.scalar
            def _(s, dv_m=dv_m):
                P.w(s, "dv", dv_m)
                s.activation(st1[:, 4:5], st1[:, 4:5], AF.Ln)
                s.drain()
                ins = s.activation(st1[:, 4:5], st1[:, 4:5], AF.Exp, scale=0.5)
                P.inc(ins, "sc")
            sc_m = P.v("sc")

            @block.vector
            def _(v, sc_m=sc_m):
                P.w(v, "sc", sc_m)
                v.tensor_scalar(st1[:, 4:5], st1[:, 4:5], -1.02, -1.0,
                                op0=ALU.mult, op1=ALU.add)
                v.drain()
                ins = v.tensor_copy(st1b[:, 0:1], st1[:, 4:5])
                P.inc(ins, "dv")
            dv_mb = P.v("dv")

            @block.tensor
            def _(t, dv_mb=dv_mb):
                P.w(t, "dv", dv_mb)
                ins = t.matmul(psX[:, 0:1], ones_r[:], st1b[:, 0:1], start=True, stop=True)
                P.inc(ins, "pe")
            pe_b = P.v("pe")

            @block.vector
            def _(v, hs=hs, pe_b=pe_b):
                P.w(v, "pe", pe_b)
                ins = v.tensor_copy(bias_h[:, hs:hs + 1], psX[:, 0:1])
                P.inc(ins, "dv")
            sc_bias = ("dv", P.v("dv"))

            def emit_PV(g, h=h, hs=hs):
                gs3 = g % 2

                @block.tensor
                def _(t, g=g, gs3=gs3):
                    P.w(t, "sc", exp_hist_pergroup[g])
                    prevdv = pv_dv.get((h, g - 1)) if g >= 1 else (pv_dv.get((h - 1, G3 - 1)) if h >= 1 else None)
                    if prevdv is not None:
                        P.w(t, "dv", prevdv)
                    for kc in range(NKC):
                        ins = t.matmul(psV3[:], vbuf[hs][:, kc * 128:(kc + 1) * 128],
                                       PTb[gs3][:, kc * 512:(kc + 1) * 512],
                                       start=(kc == 0), stop=(kc == NKC - 1))
                    P.inc(ins, "pe")
                pe_pv = P.v("pe")

                @block.vector
                def _(v, g=g, gs3=gs3, pe_pv=pe_pv):
                    P.w(v, "pe", pe_pv)
                    P.w(v, *bcast_dv[g])
                    for qt in range(4):
                        ins = v.tensor_mul(valuesT[:, h * TOWN + g * 512 + qt * 128: h * TOWN + g * 512 + (qt + 1) * 128],
                                           psV3[:, qt * 128:(qt + 1) * 128],
                                           bcast_sb[:, gs3 * 512 + qt * 128: gs3 * 512 + (qt + 1) * 128])
                    P.inc(ins, "dv")
                pv_dv[(h, g)] = P.v("dv")

            for g in range(G3):
                gs3 = g % 2
                den_mm = {}

                sc0 = P.v("sc")
                pe_s_list = []
                for kp in range(NKC // 2):
                    sslot = kp % 3

                    @block.tensor
                    def _(t, g=g, kp=kp, sslot=sslot, sc0=sc0):
                        if (g, kp) == (0, 0):
                            P.w(t, *sc_bias)
                        if kp >= 3:
                            P.wfut(t, "sc", sc0 + kp - 2)
                        for u in range(2):
                            kc = 2 * kp + u
                            ins = t.matmul(psS[sslot][:, u * 512:(u + 1) * 512],
                                           kbuf[hs][:, kc * 128:(kc + 1) * 128],
                                           qbuf[hs][:, g * 512:(g + 1) * 512],
                                           start=True, stop=True)
                        P.inc(ins, "pe")
                    pe_s_list.append(P.v("pe"))

                if g >= 1:
                    emit_PV(g - 1)

                @block.scalar
                def _(s, g=g, pe_s_list=pe_s_list):
                    for kp in range(NKC // 2):
                        P.w(s, "pe", pe_s_list[kp])
                        if kp == 0:
                            prevpv = pv_dv.get((h, g - 2)) if g >= 2 else \
                                (pv_dv.get((h - 1, G3 - 2 + g)) if h >= 1 else None)
                            if prevpv is not None:
                                P.w(s, "dv", prevpv)
                        ins = s.activation(PTb[g % 2][:, kp * 1024:(kp + 1) * 1024],
                                           psS[kp % 3][:, 0:1024], AF.Exp,
                                           bias=bias_h[:, hs:hs + 1])
                        P.inc(ins, "sc")
                assert P.v("sc") == sc0 + NKC // 2
                for kp in range(NKC // 2):
                    exp_hist[kp % 3] = sc0 + kp + 1

                @block.tensor
                def _(t, g=g, sc0=sc0):
                    prevrd = recd_dv.get(g - 1)
                    if prevrd is not None:
                        P.w(t, "dv", prevrd)
                    prevbc = bcast_dv.get(g - 1)
                    if prevbc is not None:
                        P.w(t, *prevbc)
                    for kp in range(NKC // 2):
                        P.w(t, "sc", sc0 + kp + 1)
                        for u in range(2):
                            kc = 2 * kp + u
                            ins = t.matmul(psX[0:1, :], onesc[:],
                                           PTb[g % 2][:, kc * 512:(kc + 1) * 512],
                                           start=(kc == 0), stop=(kc == NKC - 1))
                    P.inc(ins, "pe")
                pe_den = P.v("pe")

                @block.vector
                def _(v, g=g, pe_den=pe_den):
                    P.w(v, "pe", pe_den)
                    v.tensor_scalar(recd_sb[0:1, 0:512], psX[0:1, :], P40, None, op0=ALU.mult)
                    v.drain()
                    v.reciprocal(recd_sb[0:1, 0:512], recd_sb[0:1, 0:512])
                    v.drain()
                    ins = v.tensor_scalar(recd_sb[0:1, 0:512], recd_sb[0:1, 0:512], P40, None, op0=ALU.mult)
                    P.inc(ins, "dv")
                recd_dv[g] = P.v("dv")
                dv_rd = P.v("dv")

                @block.tensor
                def _(t, g=g, dv_rd=dv_rd):
                    P.w(t, "dv", dv_rd)
                    prevbc = bcast_dv.get(g - 1)
                    if prevbc is not None:
                        P.w(t, *prevbc)
                    ins = t.matmul(psX[:, 0:512], ones_rf[:], recd_sb[0:1, 0:512],
                                   start=True, stop=True)
                    P.inc(ins, "pe")
                pe_bc = P.v("pe")

                @block.vector
                def _(v, g=g, gs3=gs3, pe_bc=pe_bc):
                    P.w(v, "pe", pe_bc)
                    prevpv = pv_dv.get((h, g - 2))
                    if prevpv is not None:
                        P.w(v, "dv", prevpv)
                    ins = v.tensor_copy(bcast_sb[:, gs3 * 512:(gs3 + 1) * 512], psX[:, 0:512])
                    P.inc(ins, "dv")
                bcast_dv[g] = ("dv", P.v("dv"))

                exp_hist_pergroup[g] = sc0 + NKC // 2
            emit_PV(G3 - 1)
            head_done_pe[h] = P.v("pe")
        p3_vt_dv = P.v("dv")

        if debug:
            @block.gpsimd
            def _(g):
                P.w(g, "dv", p3_vt_dv)
                P.dma(g, vt_dbg, valuesT[:], "gv0")

        p3_end = P.snap()
        p3ctx.close()

        for engsec in (block.tensor, block.vector, block.scalar, block.sync, block.gpsimd):
            @engsec
            def _(e):
                for nm, val in p3_end.items():
                    P.w(e, nm, val)

        NT4 = G3 * 4
        SGL = [(s, min(8, NT4 - s)) for s in range(0, NT4, 8)]

        bbrow = ctx.enter_context(nc.sbuf_tensor("bbrow", [1, D], BF16))
        AW = ctx.enter_context(nc.sbuf_tensor("AW", [128, NCH * 136], BF16))
        Bfl = ctx.enter_context(nc.sbuf_tensor("Bfl", [128, D], BF16))
        fsums = ctx.enter_context(nc.sbuf_tensor("fsums", [128, 64], F32))
        fssq = ctx.enter_context(nc.sbuf_tensor("fssq", [128, 64], F32))

        p4ctx = ExitStack()
        wo_sb = p4ctx.enter_context(nc.sbuf_tensor("wo_sb", [128, 8 * D], BF16, side="right"))
        attnsb = p4ctx.enter_context(nc.sbuf_tensor("attnsb", [128, D], F32, side="right"))
        _xo = p4ctx.enter_context(nc.sbuf_tensor("xo0", [128, D], F32, side="right"))
        _x2b = p4ctx.enter_context(nc.sbuf_tensor("x2b0", [128, D], F32, side="right"))
        _y2b = p4ctx.enter_context(nc.sbuf_tensor("y2b0", [128, D], BF16, side="right"))
        xo = [_xo, _xo]
        x2b = [_x2b, _x2b]
        y2b = [_y2b, _y2b]
        psW = p4ctx.enter_context(nc.psum_tensor("psW", [128, 3072], F32))
        psC = p4ctx.enter_context(nc.psum_tensor("psC", [128, 2048], BF16))
        p4sctx = ExitStack()
        A_nat = p4sctx.enter_context(nc.sbuf_tensor("A_nat", [128, D], BF16, side="right"))
        Wr_nat = p4sctx.enter_context(nc.sbuf_tensor("Wr_nat", [8, D], BF16, side="right"))
        Bexp = [p4sctx.enter_context(nc.sbuf_tensor(f"Bexp{i}", [128, 1024], BF16, side="right")) for i in range(2)]

        @block.gpsimd
        def _(g):
            P.dma(g, bbrow[:], bb, "gl")
            P.dma(g, A_nat[:], Ast.rearrange("e r d -> (e r) d"), "gl")
            P.dma(g, Wr_nat[:], Wr, "gl")
        aw_gl = P.v("gl")

        aw_sc = []
        for ch in range(NCH):
            cslot = ch % 2

            @block.tensor
            def _(t, ch=ch, cslot=cslot):
                if ch == 0:
                    P.w(t, "gl", aw_gl)
                if ch >= 2:
                    P.w(t, *aw_sc[ch - 2])
                t.transpose(psC[:, cslot * 1024: cslot * 1024 + 128],
                            A_nat[:, ch * 128:(ch + 1) * 128], ident[:])
                ins = t.transpose(psC[:, cslot * 1024 + 128: cslot * 1024 + 136],
                                  Wr_nat[:, ch * 128:(ch + 1) * 128], ident[0:8, 0:8])
                P.inc(ins, "pe")
            pe_now = P.v("pe")

            @block.vector
            def _(v, ch=ch, cslot=cslot, pe_now=pe_now):
                P.w(v, "pe", pe_now)
                v.tensor_copy(AW[:, ch * 136: ch * 136 + 128], psC[:, cslot * 1024: cslot * 1024 + 128])
                v.drain()
                ins = v.tensor_copy(AW[:, ch * 136 + 128: (ch + 1) * 136],
                                    psC[:, cslot * 1024 + 128: cslot * 1024 + 136])
                P.inc(ins, "dv")
            aw_sc.append(("dv", P.v("dv")))

        bexp_gl = {}
        bfl_sc = []
        for ep in range(4):
            es = ep % 2

            @block.gpsimd
            def _(g, ep=ep, es=es):
                if ep >= 2:
                    P.w(g, *bfl_sc[(ep - 2) * NCH + NCH - 1])
                P.need("gb%d" % es)
                for u in range(2):
                    srcb = Bst[2 * ep + u].rearrange("(oc p) r -> p oc r", p=128)
                    dstb = Bexp[es][:].rearrange("p (oc u r) -> p oc u r", u=2, r=16)[:, :, u, :]
                    P.dma(g, dstb, srcb, "gb%d" % es)
            bexp_gl[ep] = ("gb%d" % es, P.v("gb%d" % es))

            for oc in range(NCH):
                bs = oc % 2

                @block.tensor
                def _(t, ep=ep, oc=oc, es=es, bs=bs):
                    if oc == 0:
                        P.w(t, *bexp_gl[ep])
                    if ep == 0 and oc == 0:
                        P.w(t, *aw_sc[-1])
                    if len(bfl_sc) >= 2:
                        P.w(t, *bfl_sc[-2])
                    ins = t.transpose(psC[0:32, bs * 1024 + 256: bs * 1024 + 384],
                                      Bexp[es][:, oc * 32:(oc + 1) * 32], ident[:])
                    P.inc(ins, "pe")
                pe_now = P.v("pe")

                @block.vector
                def _(v, ep=ep, oc=oc, bs=bs, pe_now=pe_now):
                    P.w(v, "pe", pe_now)
                    ins = v.tensor_copy(Bfl[ep * 32:(ep + 1) * 32, oc * 128:(oc + 1) * 128],
                                        psC[0:32, bs * 1024 + 256: bs * 1024 + 384])
                    P.inc(ins, "dv")
                bfl_sc.append(("dv", P.v("dv")))

        cst_end = P.snap()
        p4sctx.close()
        for engsec in (block.tensor, block.vector, block.scalar, block.sync, block.gpsimd):
            @engsec
            def _(e, snap=cst_end):
                for nm, val in snap.items():
                    P.w(e, nm, val)

        P.need("gwo")

        @block.gpsimd
        def _(g):
            srcw = wo.rearrange("(c p) f -> p c f", p=128)
            P.dma(g, wo_sb[:].rearrange("p (c f) -> p c f", c=8), srcw, "gwo")
        wo_gl = P.v("gwo")

        wo_evac = {}
        x2done_dv = {}
        y2_done_sc = {}
        x2sp_gs = {}
        y2sp_gs = {}
        P.need("gx0"); P.need("gx1")

        for tt in range(NT4):
            ts4 = tt % 2
            sb4 = 40 + ts4 * 12

            @block.sync
            def _(s, tt=tt, ts4=ts4):
                prev = x2done_dv.get(tt - 1)
                if prev:
                    P.w(s, "dv", prev)
                P.need("sxo%d" % ts4)
                P.dma(s, xo[ts4][:], xa[tt * 128:(tt + 1) * 128, :], "sxo%d" % ts4)
            sl_xo = P.v("sxo%d" % ts4)

            pe_parts = []
            for part, (obase, olen) in enumerate([(0, 3072), (3072, 1024)]):
                @block.tensor
                def _(t, tt=tt, part=part, obase=obase, olen=olen):
                    if tt == 0 and part == 0:
                        P.w(t, "gwo", wo_gl)
                        P.w(t, "dv", p3_vt_dv)
                    prev = wo_evac.get((tt, part - 1), wo_evac.get((tt - 1, 1)))
                    if prev:
                        P.wfut(t, "sc", prev)
                    for ch in range(8):
                        for oc in range(olen // 512):
                            ins = t.matmul(psW[:, oc * 512:(oc + 1) * 512],
                                           valuesT[:, ch * TOWN + tt * 128:(ch * TOWN) + (tt + 1) * 128],
                                           wo_sb[:, ch * D + obase + oc * 512: ch * D + obase + (oc + 1) * 512],
                                           start=(ch == 0), stop=(ch == 7))
                    P.inc(ins, "pe")
                pe_parts.append(P.v("pe"))
                wo_evac[(tt, part)] = P.v("sc") + part + 1

            @block.scalar
            def _(s, tt=tt, sb4=sb4, pe_parts=pe_parts):
                prevx2 = x2done_dv.get(tt - 1)
                if prevx2:
                    P.w(s, "dv", prevx2)
                for part, (obase, olen) in enumerate([(0, 3072), (3072, 1024)]):
                    P.w(s, "pe", pe_parts[part])
                    ins = s.activation(attnsb[:, obase:obase + olen], psW[:, 0:olen], AF.Identity,
                                       accum_out=stat[:, sb4 + part: sb4 + part + 1])
                    P.inc(ins, "sc")
            sc_attn = P.v("sc")
            assert wo_evac[(tt, 0)] == sc_attn - 1 and wo_evac[(tt, 1)] == sc_attn

            @block.vector
            def _(v, tt=tt, ts4=ts4, sb4=sb4, sc_attn=sc_attn):
                P.w(v, "sc", sc_attn)
                prev = y2sp_gs.get(tt - 1)
                if prev:
                    P.w(v, *prev)
                v.scalar_tensor_tensor(y2b[ts4][:, 0:2048], attnsb[:, 0:2048], 1.0,
                                       attnsb[:, 0:2048], op0=ALU.mult, op1=ALU.mult,
                                       accum_out=stat[:, sb4 + 2: sb4 + 3])
                v.drain()
                v.scalar_tensor_tensor(y2b[ts4][:, 0:2048], attnsb[:, 2048:4096], 1.0,
                                       attnsb[:, 2048:4096], op0=ALU.mult, op1=ALU.mult,
                                       accum_out=stat[:, sb4 + 3: sb4 + 4])
                v.drain()
                v.tensor_tensor(stat[:, sb4 + 4: sb4 + 5], stat[:, sb4: sb4 + 1],
                                stat[:, sb4 + 1: sb4 + 2], op=ALU.add)
                v.drain()
                v.tensor_scalar(stat[:, sb4 + 4: sb4 + 5], stat[:, sb4 + 4: sb4 + 5],
                                -1.0 / D, None, op0=ALU.mult)
                v.drain()
                v.tensor_tensor(stat[:, sb4 + 5: sb4 + 6], stat[:, sb4 + 2: sb4 + 3],
                                stat[:, sb4 + 3: sb4 + 4], op=ALU.add)
                v.drain()
                v.scalar_tensor_tensor(stat[:, sb4 + 6: sb4 + 7], stat[:, sb4 + 4: sb4 + 5],
                                       -1.0, stat[:, sb4 + 4: sb4 + 5], op0=ALU.mult, op1=ALU.mult)
                v.drain()
                v.scalar_tensor_tensor(stat[:, sb4 + 6: sb4 + 7], stat[:, sb4 + 5: sb4 + 6],
                                       1.0 / D, stat[:, sb4 + 6: sb4 + 7], op0=ALU.mult, op1=ALU.add)
                v.drain()
                ins = v.tensor_scalar(stat[:, sb4 + 6: sb4 + 7], stat[:, sb4 + 6: sb4 + 7],
                                      EPS, None, op0=ALU.add)
                P.inc(ins, "dv")
            dv_rec2 = P.v("dv")

            @block.scalar
            def _(s, sb4=sb4, dv_rec2=dv_rec2):
                P.w(s, "dv", dv_rec2)
                s.activation(stat[:, sb4 + 7: sb4 + 8], stat[:, sb4 + 6: sb4 + 7], AF.Ln)
                s.drain()
                ins = s.activation(stat[:, sb4 + 7: sb4 + 8], stat[:, sb4 + 7: sb4 + 8], AF.Exp, scale=-0.5)
                P.inc(ins, "sc")
            sc_r2 = P.v("sc")

            @block.vector
            def _(v, tt=tt, ts4=ts4, sb4=sb4, sc_r2=sc_r2, sl_xo=sl_xo):
                P.w(v, "sc", sc_r2)
                P.w(v, "sxo%d" % ts4, sl_xo)
                prevsp = x2sp_gs.get(tt - 1)
                if prevsp:
                    P.w(v, *prevsp)
                prevy2 = y2_done_sc.get(tt - 1)
                if prevy2:
                    P.w(v, "sc", prevy2)
                v.drain()
                v.tensor_scalar(attnsb[:], attnsb[:], stat[:, sb4 + 4: sb4 + 5], None, op0=ALU.add)
                v.drain()
                ins = v.scalar_tensor_tensor(x2b[ts4][:], attnsb[:], stat[:, sb4 + 7: sb4 + 8],
                                             xo[ts4][:], op0=ALU.mult, op1=ALU.add)
                P.inc(ins, "dv")
            x2done_dv[tt] = P.v("dv")
            dv_x2 = P.v("dv")

            @block.gpsimd
            def _(g, tt=tt, ts4=ts4, dv_x2=dv_x2):
                P.w(g, "dv", dv_x2)
                P.dma(g, x2_sp[tt], x2b[ts4][:], "gx%d" % ts4)
            x2sp_gs[tt] = ("gx%d" % ts4, P.v("gx%d" % ts4))

            @block.vector
            def _(v, tt=tt, ts4=ts4, sb4=sb4):
                v.drain()
                v.scalar_tensor_tensor(y2b[ts4][:], x2b[ts4][:], 1.0, x2b[ts4][:],
                                       op0=ALU.mult, op1=ALU.mult,
                                       accum_out=stat[:, sb4 + 9: sb4 + 10])
                v.drain()
                v.reduce_sum(stat[:, sb4 + 10: sb4 + 11], x2b[ts4][:], axis=AXX)
                v.drain()
                v.tensor_scalar(stat[:, sb4 + 10: sb4 + 11], stat[:, sb4 + 10: sb4 + 11],
                                -1.0 / D, None, op0=ALU.mult)
                v.drain()
                v.scalar_tensor_tensor(stat[:, sb4 + 11: sb4 + 12], stat[:, sb4 + 10: sb4 + 11],
                                       -1.0, stat[:, sb4 + 10: sb4 + 11], op0=ALU.mult, op1=ALU.mult)
                v.drain()
                v.scalar_tensor_tensor(stat[:, sb4 + 11: sb4 + 12], stat[:, sb4 + 9: sb4 + 10],
                                       1.0 / D, stat[:, sb4 + 11: sb4 + 12], op0=ALU.mult, op1=ALU.add)
                v.drain()
                ins = v.tensor_scalar(stat[:, sb4 + 11: sb4 + 12], stat[:, sb4 + 11: sb4 + 12],
                                      EPS, None, op0=ALU.add)
                P.inc(ins, "dv")
            dv_rec3 = P.v("dv")

            @block.scalar
            def _(s, tt=tt, ts4=ts4, sb4=sb4, dv_rec3=dv_rec3):
                P.w(s, "dv", dv_rec3)
                s.activation(stat[:, sb4 + 11: sb4 + 12], stat[:, sb4 + 11: sb4 + 12], AF.Ln)
                s.drain()
                s.activation(stat[:, sb4 + 11: sb4 + 12], stat[:, sb4 + 11: sb4 + 12], AF.Exp, scale=-0.5)
                s.activation(y2b[ts4][:], x2b[ts4][:], AF.Identity,
                             bias=stat[:, sb4 + 10: sb4 + 11])
                s.drain()
                ins = s.activation(y2b[ts4][:], y2b[ts4][:], AF.Identity,
                                   scale=stat[:, sb4 + 11: sb4 + 12])
                P.inc(ins, "sc")
            y2_done_sc[tt] = P.v("sc")
            sc_y2 = P.v("sc")

            @block.gpsimd
            def _(g, tt=tt, ts4=ts4, sc_y2=sc_y2):
                P.w(g, "sc", sc_y2)
                P.dma(g, y2_sp[tt], y2b[ts4][:], "gy%d" % ts4)
            y2sp_gs[tt] = ("gy%d" % ts4, P.v("gy%d" % ts4))

        p4_end = P.snap()
        p4ctx.close()
        p3vctx.close()

        for engsec in (block.tensor, block.vector, block.scalar, block.sync, block.gpsimd):
            @engsec
            def _(e):
                for nm, val in p4_end.items():
                    P.w(e, nm, val)

        p5ctx = ExitStack()
        _y2rd = p5ctx.enter_context(nc.sbuf_tensor("y2rd0", [128, D], BF16, side="right"))
        y2rd = [_y2rd, _y2rd]
        y2T = p5ctx.enter_context(nc.sbuf_tensor("y2T", [128, NCH * 1024], BF16, side="right"))
        hT_sb = p5ctx.enter_context(nc.sbuf_tensor("hT_sb", [128, 8 * 128], BF16, side="right"))
        hp_sb = p5ctx.enter_context(nc.sbuf_tensor("hp_sb", [128, 128], F32, side="right"))
        cmb = p5ctx.enter_context(nc.sbuf_tensor("cmb", [128, 32], F32, side="right"))
        _fst = p5ctx.enter_context(nc.sbuf_tensor("fstage0", [128, 8 * 512], BF16, side="right"))
        fstage = [_fst, _fst]
        fjunk = p5ctx.enter_context(nc.sbuf_tensor("fjunk", [128, 512], BF16, side="right"))
        psF = p5ctx.enter_context(nc.psum_tensor("psF", [128, 4096], F32))

        P.need("gw0"); P.need("gw1"); P.need("swt0"); P.need("swt1")

        P.need("gf0"); P.need("gf1"); P.need("sf0"); P.need("sf1")
        P.need("sxx0"); P.need("sxx1")

        y2t_tr = {}
        hT_done = {}
        base_done_sc = {}
        fspill = {}
        wbt_tr = {}
        wbn_hist = {0: [], 1: []}
        wbr_hist = []
        y2rd_hist = {0: [], 1: []}
        out_done = []

        for sgi, (sg0, sgn) in enumerate(SGL):
            sgctx = ExitStack()
            WbTb = [sgctx.enter_context(nc.sbuf_tensor(f"WbTb{sgi}_{i}", [128, NCH * 512], BF16, side="right")) for i in range(2)]
            wbn = [sgctx.enter_context(nc.sbuf_tensor(f"wbn{sgi}_{i}", [128, D], BF16, side="right")) for i in range(2)]
            _wbr = sgctx.enter_context(nc.sbuf_tensor(f"wbr{sgi}_0", [128, D], F32, side="right"))
            wbr = [_wbr, _wbr]

            for j in range(sgn):
                tt = sg0 + j
                js = j % 2

                @block.sync
                def _(s, tt=tt, js=js, sgi=sgi, j=j):
                    P.w(s, *y2sp_gs[tt])
                    for jsx in (0, 1):
                        if y2rd_hist[jsx]:
                            P.w(s, *y2rd_hist[jsx][-1])
                    P.dma(s, y2rd[js][:], y2_sp[tt], "sxx%d" % js)
                sl_y2 = P.v("sxx%d" % js)

                @block.sync
                def _(s, tt=tt, js=js, sgi=sgi, j=j, sl_y2=sl_y2):
                    P.w(s, "sxx%d" % js, sl_y2)
                    if sgi > 0 and j == 0:
                        P.w(s, "pe", base_done_sc[("pe", sgi - 1)])
                    srcy = y2rd[js][:].rearrange("p (c f) -> p c f", c=NCH)
                    dsty = y2T[:].rearrange("p (c f) -> p c f", c=NCH)[:, :, j * 128:(j + 1) * 128]
                    P.inc(s.dma_start_transpose(dsty, srcy), "swt%d" % js, 16)
                y2t_tr[(sgi, j)] = ("swt%d" % js, P.v("swt%d" % js))
                y2rd_hist[js].append(y2t_tr[(sgi, j)])

                @block.tensor
                def _(t, tt=tt, j=j, sgi=sgi):
                    P.w(t, *y2t_tr[(sgi, j)])
                    if (sgi, j) == (0, 0):
                        P.w(t, *bfl_sc[-1])
                    prev = hT_done.get((sgi, j - 2))
                    if prev:
                        P.w(t, *prev)
                    pbase = (j % 2) * 512
                    for ch in range(NCH):
                        ins = t.matmul(psF[:, pbase: pbase + 136],
                                       y2T[:, ch * 1024 + j * 128: ch * 1024 + (j + 1) * 128],
                                       AW[:, ch * 136:(ch + 1) * 136],
                                       start=(ch == 0), stop=(ch == NCH - 1))
                    P.inc(ins, "pe")
                pe_h = P.v("pe")

                cb = (j % 2) * 16

                @block.vector
                def _(v, j=j, cb=cb, pe_h=pe_h):
                    pbase = (j % 2) * 512
                    P.w(v, "pe", pe_h)
                    lg = psF[:, pbase + 128: pbase + 136]
                    v.reduce_max(cmb[:, cb: cb + 1], lg, axis=AXX)
                    v.drain()
                    v.tensor_scalar(cmb[:, cb + 8: cb + 16], lg, cmb[:, cb: cb + 1], None, op0=ALU.is_ge)
                    v.drain()
                    v.scalar_tensor_tensor(cmb[:, cb + 8: cb + 16], cmb[:, cb + 8: cb + 16], -1e30,
                                           lg, op0=ALU.mult, op1=ALU.add)
                    v.drain()
                    v.reduce_max(cmb[:, cb + 1: cb + 2], cmb[:, cb + 8: cb + 16], axis=AXX)
                    v.drain()
                    v.tensor_scalar(cmb[:, cb + 2: cb + 3], cmb[:, cb: cb + 1], -1.0, None, op0=ALU.mult)
                    v.drain()
                    ins = v.tensor_scalar(cmb[:, cb + 8: cb + 16], lg, cmb[:, cb + 1: cb + 2], None,
                                          op0=ALU.is_ge)
                    P.inc(ins, "dv")
                dv_c1 = P.v("dv")

                @block.scalar
                def _(s, j=j, cb=cb, dv_c1=dv_c1):
                    pbase = (j % 2) * 512
                    P.w(s, "dv", dv_c1)
                    ins = s.activation(cmb[:, cb + 4: cb + 12][:, 0:8], psF[:, pbase + 128: pbase + 136],
                                       AF.Exp, bias=cmb[:, cb + 2: cb + 3], scale=1.0)
                    P.inc(ins, "sc")
                sc_c1 = P.v("sc")

                @block.vector
                def _(v, cb=cb, sc_c1=sc_c1):
                    P.w(v, "sc", sc_c1)
                    v.tensor_mul(cmb[:, cb + 4: cb + 12], cmb[:, cb + 4: cb + 12], cmb[:, cb + 8: cb + 16])
                    v.drain()
                    v.reduce_sum(cmb[:, cb + 3: cb + 4], cmb[:, cb + 4: cb + 12], axis=AXX)
                    v.drain()
                    v.reciprocal(cmb[:, cb + 3: cb + 4], cmb[:, cb + 3: cb + 4])
                    v.drain()
                    ins = v.tensor_scalar(cmb[:, cb + 4: cb + 12], cmb[:, cb + 4: cb + 12],
                                          cmb[:, cb + 3: cb + 4], 2.0, op0=ALU.mult, op1=ALU.mult)
                    P.inc(ins, "dv")
                dv_c2 = P.v("dv")

                @block.vector
                def _(v, j=j, cb=cb, dv_c2=dv_c2):
                    pbase = (j % 2) * 512
                    v.drain()
                    for e8 in range(8):
                        ins = v.tensor_scalar(hp_sb[:, e8 * 16:(e8 + 1) * 16],
                                              psF[:, pbase + e8 * 16: pbase + (e8 + 1) * 16],
                                              cmb[:, cb + 4 + e8: cb + 5 + e8], None, op0=ALU.mult)
                    P.inc(ins, "dv")
                dv_hp = P.v("dv")

                @block.tensor
                def _(t, j=j, dv_hp=dv_hp):
                    pbase = (j % 2) * 512
                    P.w(t, "dv", dv_hp)
                    ins = t.transpose(psF[:, pbase + 256: pbase + 384], hp_sb[:], identf[:])
                    P.inc(ins, "pe")
                pe_hT = P.v("pe")

                @block.vector
                def _(v, j=j, pe_hT=pe_hT):
                    pbase = (j % 2) * 512
                    P.w(v, "pe", pe_hT)
                    ins = v.tensor_copy(hT_sb[:, j * 128:(j + 1) * 128], psF[:, pbase + 256: pbase + 384])
                    P.inc(ins, "dv")
                hT_done[(sgi, j)] = ("dv", P.v("dv"))
            sc_prep = P.v("sc")
            dv_prep = P.v("dv")

            evac_sc5 = {}
            for p in range(8):
                for ot in range(4):
                    ws = ot % 2

                    @block.gpsimd
                    def _(g, sgi=sgi, p=p, ot=ot, ws=ws):
                        if wbr_hist:
                            P.w(g, "dv", wbr_hist[-1])
                        rows = Wb[p * 512 + ot * 128: p * 512 + (ot + 1) * 128, :]
                        P.dma(g, wbr[ws][:], rows, "gw%d" % ws)
                    gl_wb = P.v("gw%d" % ws)

                    @block.vector
                    def _(v, ws=ws, gl_wb=gl_wb):
                        P.w(v, "gw%d" % ws, gl_wb)
                        if wbn_hist[ws]:
                            P.w(v, *wbn_hist[ws][-1])
                        ins = v.tensor_copy(wbn[ws][:], wbr[ws][:])
                        P.inc(ins, "dv")
                    wbr_hist.append(P.v("dv"))
                    dv_cast = P.v("dv")

                    tsem = "sf%d%d" % (p % 2, ot % 2)
                    P.need(tsem)

                    @block.sync
                    def _(s, sgi=sgi, p=p, ot=ot, ws=ws, dv_cast=dv_cast, tsem=tsem):
                        P.w(s, "dv", dv_cast)
                        if ot == 0:
                            prev = evac_sc5.get((sgi, p - 2), evac_sc5.get((sgi - 1, p + 6)))
                            if prev:
                                P.w(s, "sc", prev)
                        srcw = wbn[ws][:].rearrange("p (c f) -> p c f", c=NCH)
                        dstw = WbTb[p % 2][:, ot * D:(ot + 1) * D].rearrange("p (c f) -> p c f", c=NCH)
                        P.inc(s.dma_start_transpose(dstw, srcw), tsem, 16)
                    wbt_tr[(sgi, p, ot)] = (tsem, P.v(tsem))
                    wbn_hist[ws].append((tsem, P.v(tsem)))

                @block.tensor
                def _(t, sgi=sgi, p=p, sgn=sgn, sc_prep=sc_prep, dv_prep=dv_prep):
                    for ot in range(4):
                        P.w(t, *wbt_tr[(sgi, p, ot)])
                    if p == 0:
                        P.w(t, "sc", sc_prep)
                        P.w(t, "dv", dv_prep)
                    prev = evac_sc5.get((sgi, p - 1), evac_sc5.get((sgi - 1, 7)))
                    if prev:
                        P.w(t, "sc", prev)
                    wtv = WbTb[p % 2][:].rearrange("p (o c j) -> p o c j", o=4, j=128)
                    for ch in range(NCH):
                        for j in range(sgn):
                            t.matmul(psF[:, j * 512:(j + 1) * 512],
                                     y2T[:, ch * 1024 + j * 128: ch * 1024 + (j + 1) * 128],
                                     wtv[:, :, ch, :],
                                     start=(ch == 0), stop=False)
                    for j in range(sgn):
                        t.matmul(psF[:, j * 512:(j + 1) * 512],
                                 hT_sb[:, j * 128:(j + 1) * 128],
                                 Bfl[:, p * 512:(p + 1) * 512],
                                 start=False, stop=False)
                    for j in range(sgn):
                        ins = t.matmul(psF[:, j * 512:(j + 1) * 512],
                                       ones_r[:], bbrow[0:1, p * 512:(p + 1) * 512],
                                       start=False, stop=True)
                    P.inc(ins, "pe")
                pe_base = P.v("pe")

                @block.scalar
                def _(s, sgi=sgi, p=p, sgn=sgn, pe_base=pe_base):
                    P.w(s, "pe", pe_base)
                    prev = fspill.get((sgi, p - 1), fspill.get((sgi - 1, 7)))
                    if prev:
                        P.w(s, *prev)
                    for j in range(sgn):
                        ins = s.activation(fstage[p % 2][:, j * 512:(j + 1) * 512],
                                           psF[:, j * 512:(j + 1) * 512], AF.Identity,
                                           accum_out=fsums[:, p * 8 + j: p * 8 + j + 1])
                    P.inc(ins, "sc")
                sc_evac5 = P.v("sc")

                @block.vector
                def _(v, sgi=sgi, p=p, sgn=sgn, sc_evac5=sc_evac5):
                    P.w(v, "sc", sc_evac5)
                    for j in range(sgn):
                        v.drain()
                        ins = v.scalar_tensor_tensor(fjunk[:], fstage[p % 2][:, j * 512:(j + 1) * 512],
                                                     1.0, fstage[p % 2][:, j * 512:(j + 1) * 512],
                                                     op0=ALU.mult, op1=ALU.mult,
                                                     accum_out=fssq[:, p * 8 + j: p * 8 + j + 1])
                    P.inc(ins, "dv")
                evac_sc5[(sgi, p)] = P.v("sc")
                dv_ssq5 = P.v("dv")

                @block.gpsimd
                def _(g, sgi=sgi, sg0=sg0, p=p, sgn=sgn, sc_evac5=sc_evac5, dv_ssq5=dv_ssq5):
                    P.w(g, "sc", sc_evac5)
                    P.w(g, "dv", dv_ssq5)
                    for j in range(sgn):
                        P.dma(g, f_sp[sg0 + j][:, p * 512:(p + 1) * 512],
                              fstage[p % 2][:, j * 512:(j + 1) * 512], "gf%d" % (p % 2))
                fspill[(sgi, p)] = ("gf%d" % (p % 2), P.v("gf%d" % (p % 2)))
            base_done_sc[("pe", sgi)] = P.v("pe")
            sg_base_end = P.snap()

            sgctx.close()
            for engsec in (block.tensor, block.vector, block.scalar, block.sync, block.gpsimd):
                @engsec
                def _(e, snap=sg_base_end):
                    for nm, val in snap.items():
                        P.w(e, nm, val)

            apctx = ExitStack()
            fread = [apctx.enter_context(nc.sbuf_tensor(f"fread{sgi}_{i}", [128, D], BF16, side="right")) for i in range(2)]
            xb2 = [apctx.enter_context(nc.sbuf_tensor(f"xb2{sgi}_{i}", [128, D], F32, side="right")) for i in range(2)]
            outsb = [apctx.enter_context(nc.sbuf_tensor(f"outsb{sgi}_{i}", [128, D], F32, side="right")) for i in range(2)]

            ap_dv = {}
            out_st = {}
            for j in range(sgn):
                tt = sg0 + j
                js = j % 2
                ab = 40 + js * 12

                P.need("sap%d" % js)

                @block.sync
                def _(s, tt=tt, j=j, js=js, sgi=sgi):
                    prev = ap_dv.get(j - 2)
                    if prev:
                        P.w(s, "dv", prev)
                    P.w(s, *fspill[(sgi, 7)])
                    P.w(s, *x2sp_gs[tt])
                    P.dma(s, fread[js][:], f_sp[tt], "sap%d" % js)
                    P.dma(s, xb2[js][:], x2_sp[tt], "sap%d" % js)
                sl_ap = P.v("sap%d" % js)

                @block.vector
                def _(v, j=j, js=js, ab=ab, sl_ap=sl_ap):
                    v.tensor_reduce(stat[:, ab: ab + 1], fsums[:, j: 57 + j: 8][:, 0:8], axis=AXX, op=ALU.add)
                    v.drain()
                    v.tensor_reduce(stat[:, ab + 1: ab + 2], fssq[:, j: 57 + j: 8][:, 0:8], axis=AXX, op=ALU.add)
                    v.drain()
                    v.tensor_scalar(stat[:, ab: ab + 1], stat[:, ab: ab + 1], -1.0 / D, None, op0=ALU.mult)
                    v.drain()
                    v.scalar_tensor_tensor(stat[:, ab + 2: ab + 3], stat[:, ab: ab + 1], -1.0,
                                           stat[:, ab: ab + 1], op0=ALU.mult, op1=ALU.mult)
                    v.drain()
                    v.scalar_tensor_tensor(stat[:, ab + 2: ab + 3], stat[:, ab + 1: ab + 2], 1.0 / D,
                                           stat[:, ab + 2: ab + 3], op0=ALU.mult, op1=ALU.add)
                    v.drain()
                    ins = v.tensor_scalar(stat[:, ab + 2: ab + 3], stat[:, ab + 2: ab + 3], EPS, None, op0=ALU.add)
                    P.inc(ins, "dv")
                dv_r4 = P.v("dv")

                @block.scalar
                def _(s, j=j, js=js, ab=ab, dv_r4=dv_r4, sl_ap=sl_ap):
                    P.w(s, "dv", dv_r4)
                    P.w(s, "sap%d" % js, sl_ap)
                    prevst = out_st.get(j - 2)
                    if prevst:
                        P.w(s, *prevst)
                    s.activation(stat[:, ab + 2: ab + 3], stat[:, ab + 2: ab + 3], AF.Ln)
                    s.drain()
                    s.activation(stat[:, ab + 2: ab + 3], stat[:, ab + 2: ab + 3], AF.Exp, scale=-0.5)
                    s.activation(outsb[js][:], fread[js][:], AF.Identity,
                                 bias=stat[:, ab: ab + 1])
                    s.drain()
                    ins = s.activation(outsb[js][:], outsb[js][:], AF.Identity,
                                       scale=stat[:, ab + 2: ab + 3])
                    P.inc(ins, "sc")
                sc_ln4 = P.v("sc")

                @block.vector
                def _(v, js=js, sc_ln4=sc_ln4):
                    P.w(v, "sc", sc_ln4)
                    v.drain()
                    ins = v.tensor_add(outsb[js][:], outsb[js][:], xb2[js][:])
                    P.inc(ins, "dv")
                ap_dv[j] = P.v("dv")
                dv_out = P.v("dv")

                P.need("so%d" % js)

                @block.sync
                def _(s, tt=tt, js=js, dv_out=dv_out):
                    P.w(s, "dv", dv_out)
                    P.dma(s, out[tt * 128:(tt + 1) * 128, :], outsb[js][:], "so%d" % js)
                out_st[j] = ("so%d" % js, P.v("so%d" % js))
                out_done.append(out_st[j])
            ap_end = P.snap()
            apctx.close()
            for engsec in (block.tensor, block.vector, block.scalar, block.sync, block.gpsimd):
                @engsec
                def _(e, snap=ap_end):
                    for nm, val in snap.items():
                        P.w(e, nm, val)

        @block.sync
        def _(s):
            for nm in ("so0", "so1"):
                if nm in P.sems:
                    s.wait_ge(P.sems[nm], P.v(nm))

    return nc


def _in_maps(inputs):
    x = np.ascontiguousarray(np.asarray(inputs["L0_attn_x"], dtype=np.float32))
    wq = np.ascontiguousarray(np.asarray(inputs["L0_attn_wq"], dtype=np.float32))
    wk = np.ascontiguousarray(np.asarray(inputs["L0_attn_wk"], dtype=np.float32))
    wv = np.ascontiguousarray(np.asarray(inputs["L0_attn_wv"], dtype=np.float32))
    wo = np.ascontiguousarray(np.asarray(inputs["L0_attn_wo"], dtype=np.float32))
    Wb = np.ascontiguousarray(np.asarray(inputs["L0_ffn_W_base"], dtype=np.float32))
    bb = np.ascontiguousarray(np.asarray(inputs["L0_ffn_b_base"], dtype=np.float32)).reshape(1, D)
    Wr = np.ascontiguousarray(np.asarray(inputs["L0_ffn_W_router"], dtype=np.float32))
    Ast = np.ascontiguousarray(np.asarray(inputs["L0_ffn_A_stack"], dtype=np.float32))
    Bst = np.ascontiguousarray(np.asarray(inputs["L0_ffn_B_stack"], dtype=np.float32))
    maps = []
    for c in range(8):
        b, half = c // 2, c % 2
        xr = np.ascontiguousarray(np.roll(x[b], -half * TOWN, axis=0))
        maps.append({"xa": xr, "wq": wq, "wk": wk, "wv": wv, "wo": wo,
                     "Wb": Wb, "bb": bb, "Wr": Wr, "Ast": Ast, "Bst": Bst})
    return maps


def kernel(**inputs):
    nc = build_nc(debug=False)
    maps = _in_maps(inputs)
    res = run_bass_kernel_spmd(nc, maps, core_ids=list(range(8)))
    outf = np.empty((4, S, D), dtype=np.float32)
    for c in range(8):
        b, half = c // 2, c % 2
        outf[b, half * TOWN:(half + 1) * TOWN] = res.results[c]["out"]
    return outf


# version 12
# speedup vs baseline: 1.5516x; 1.1388x over previous
import numpy as np
import ml_dtypes
from contextlib import ExitStack
import concourse.bass as bass
import concourse.mybir as mybir
from concourse.bass_utils import run_bass_kernel_spmd

F32 = mybir.dt.float32
BF16 = mybir.dt.bfloat16
AF = mybir.ActivationFunctionType
ALU = mybir.AluOpType
AXX = mybir.AxisListType.X

S = 4096
D = 4096
HKV = 8
DH = 128
KV = 1024
TOWN = 2048
NCH = 32
NT = 16
EPS = 1e-5
RSD = float(1.0 / np.sqrt(128.0))
P40 = float(2.0 ** 40)

SEM_NAMES = ["gl", "so", "pe", "dv", "sc", "gp",
             "sx0", "sx1", "sw0", "sw1", "sy0", "sy1",
             "st0", "st1", "gy0", "gy1",
             "gk0", "gk1", "gq0", "gq1", "gv0", "gv1"]


class Prog:

    def __init__(self, nc, ctx):
        self.nc = nc
        self.ctx = ctx
        self.sems = {}
        self.cnt = {}
        for name in SEM_NAMES:
            self._mk(name)

    def _mk(self, name):
        self.sems[name] = self.ctx.enter_context(self.nc.semaphore(name))
        self.cnt[name] = 0

    def need(self, name):
        if name not in self.sems:
            self._mk(name)

    def inc(self, instr, name, by=1):
        self.cnt[name] += by
        instr.then_inc(self.sems[name], by)
        return self.cnt[name]

    def dma(self, eng, outap, inap, name, **kw):
        ins = eng.dma_start(out=outap, in_=inap, **kw)
        return self.inc(ins, name, 16)

    def w(self, eng, name, val):
        assert val <= self.cnt[name], f"wait {name}>={val} emitted={self.cnt[name]}"
        if val > 0:
            eng.wait_ge(self.sems[name], val)

    def wfut(self, eng, name, val):
        if val > 0:
            eng.wait_ge(self.sems[name], val)

    def v(self, name):
        return self.cnt[name]

    def snap(self):
        return dict(self.cnt)

    def wall(self, eng, snapshot):
        for name, val in snapshot.items():
            self.w(eng, name, val)


def build_nc(debug=False, phases=7, n_tiles=32, n_groups=8):
    n_groups_own = max(1, n_groups // 2)
    nc = bass.Bass("TRN2", target_bir_lowering=False)

    xa = nc.dram_tensor("xa", [S, D], F32, kind="ExternalInput").ap()
    wq = nc.dram_tensor("wq", [D, D], F32, kind="ExternalInput").ap()
    wk = nc.dram_tensor("wk", [D, KV], F32, kind="ExternalInput").ap()
    wv = nc.dram_tensor("wv", [D, KV], F32, kind="ExternalInput").ap()
    wo = nc.dram_tensor("wo", [KV, D], F32, kind="ExternalInput").ap()
    Wb = nc.dram_tensor("Wb", [D, D], F32, kind="ExternalInput").ap()
    bb = nc.dram_tensor("bb", [1, D], F32, kind="ExternalInput").ap()
    Wr = nc.dram_tensor("Wr", [HKV, D], F32, kind="ExternalInput").ap()
    Ast = nc.dram_tensor("Ast", [8, 16, D], F32, kind="ExternalInput").ap()
    Bst = nc.dram_tensor("Bst", [8, D, 16], F32, kind="ExternalInput").ap()
    out = nc.dram_tensor("out", [TOWN, D], F32, kind="ExternalOutput").ap()

    ident_bf_d = nc.inline_tensor(np.eye(128, dtype=ml_dtypes.bfloat16), name="ident_bf").ap()
    ident_f_d = nc.inline_tensor(np.eye(128, dtype=np.float32), name="ident_f").ap()

    skind = "ExternalOutput" if debug else "Internal"
    y1t_sp = nc.dram_tensor("y1t_sp", [32, 128, D], BF16, kind=skind).ap()
    q_sp = nc.dram_tensor("q_sp", [HKV, 128, TOWN], BF16, kind=skind).ap()
    k_sp = nc.dram_tensor("k_sp", [HKV, 128, S], BF16, kind=skind).ap()
    v_sp = nc.dram_tensor("v_sp", [HKV, NCH, 128, 128], BF16, kind=skind).ap()
    x2_sp = nc.dram_tensor("x2_sp", [NT, 128, D], F32, kind=skind).ap()
    y2_sp = nc.dram_tensor("y2_sp", [NT, 128, D], BF16, kind=skind).ap()
    f_sp = nc.dram_tensor("f_sp", [NT, 128, D], BF16, kind=skind).ap()
    vt_dbg = None
    if debug:
        vt_dbg = nc.dram_tensor("vt_dbg", [128, HKV * TOWN], BF16, kind="ExternalOutput").ap()

    ctx = ExitStack()
    P = Prog(nc, ctx)

    ident = ctx.enter_context(nc.sbuf_tensor("ident", [128, 128], BF16))
    identf = ctx.enter_context(nc.sbuf_tensor("identf", [128, 128], F32))
    ones_r = ctx.enter_context(nc.sbuf_tensor("ones_r", [1, 128], BF16))
    ones_rf = ctx.enter_context(nc.sbuf_tensor("ones_rf", [1, 128], F32))
    onesc = ctx.enter_context(nc.sbuf_tensor("onesc", [128, 1], BF16))
    stat = ctx.enter_context(nc.sbuf_tensor("stat", [128, 64], F32))
    st1 = ctx.enter_context(nc.sbuf_tensor("st1", [1, 16], F32))
    st1b = ctx.enter_context(nc.sbuf_tensor("st1b", [1, 16], BF16))

    block = ctx.enter_context(nc.Block())

    with ctx:
        @block.gpsimd
        def _(g):
            P.dma(g, ident[:], ident_bf_d, "gl")
            P.dma(g, identf[:], ident_f_d, "gl")

        @block.vector
        def _(v):
            v.memset(ones_r[:], 1.0)
            v.memset(ones_rf[:], 1.0)
            P.inc(v.memset(onesc[:], 1.0), "dv")

        p2rctx = ExitStack()
        wk_h = p2rctx.enter_context(nc.sbuf_tensor("wk_h", [128, NCH * 512], BF16, side="right"))
        wv_h = p2rctx.enter_context(nc.sbuf_tensor("wv_h", [128, NCH * 512], BF16, side="right"))
        y1g = [p2rctx.enter_context(nc.sbuf_tensor(f"y1g{i}", [128, NCH * 512], BF16, side="right")) for i in range(2)]
        kstg = [p2rctx.enter_context(nc.sbuf_tensor(f"kstg{i}", [128, 512], BF16, side="right")) for i in range(2)]
        vstg = [p2rctx.enter_context(nc.sbuf_tensor(f"vstg{i}", [128, 512], BF16, side="right")) for i in range(2)]
        ps2a = p2rctx.enter_context(nc.psum_tensor("ps2a", [128, 2048], F32))
        ps2b = p2rctx.enter_context(nc.psum_tensor("ps2b", [128, 1024], F32))

        p1ctx = ExitStack()
        xin = [p1ctx.enter_context(nc.sbuf_tensor(f"xin{i}", [128, D], F32, side="right")) for i in range(2)]
        yb = [p1ctx.enter_context(nc.sbuf_tensor(f"yb{i}", [128, D], BF16, side="right")) for i in range(2)]
        ytr = [p1ctx.enter_context(nc.sbuf_tensor(f"ytr{i}", [128, D], BF16, side="right")) for i in range(2)]

        apply_sc, spill_gs, tpose_tr = [], [], []
        wkv_gl = {}

        def emit_wkv_cast(H, wait_pe=0):
            @block.gpsimd
            def _(g, H=H, wait_pe=wait_pe):
                if wait_pe:
                    P.w(g, "pe", wait_pe)
                src_k = wk[:, H * 512:(H + 1) * 512].rearrange("(c p) f -> p c f", p=128)
                P.dma(g, wk_h[:].rearrange("p (c f) -> p c f", c=NCH), src_k, "gl")
                src_v = wv[:, H * 512:(H + 1) * 512].rearrange("(c p) f -> p c f", p=128)
                P.dma(g, wv_h[:].rearrange("p (c f) -> p c f", c=NCH), src_v, "gl")
            wkv_gl[H] = P.v("gl")

        emit_wkv_cast(0)

        def emit_p1_tile(t):
            slot = t % 2
            sb = slot * 8

            @block.sync
            def _(s, t=t, slot=slot):
                if t >= 2:
                    P.w(s, "sc", apply_sc[t - 2])
                P.dma(s, xin[slot][:], xa[t * 128:(t + 1) * 128, :], "sx%d" % slot)
            sl_now = P.v("sx%d" % slot)

            @block.vector
            def _(v, t=t, slot=slot, sb=sb, sl_now=sl_now):
                P.w(v, "sx%d" % slot, sl_now)
                v.reduce_sum(stat[:, sb: sb + 1], xin[slot][:], axis=AXX)
                v.drain()
                if t >= 2:
                    P.w(v, *spill_gs[t - 2])
                v.scalar_tensor_tensor(ytr[slot][:], xin[slot][:], 1.0, xin[slot][:],
                                       op0=ALU.mult, op1=ALU.mult,
                                       accum_out=stat[:, sb + 1: sb + 2])
                v.drain()
                v.tensor_scalar(stat[:, sb + 2: sb + 3], stat[:, sb: sb + 1], -1.0 / D, None, op0=ALU.mult)
                v.drain()
                v.scalar_tensor_tensor(stat[:, sb + 3: sb + 4], stat[:, sb + 2: sb + 3], -1.0,
                                       stat[:, sb + 2: sb + 3], op0=ALU.mult, op1=ALU.mult)
                v.drain()
                v.scalar_tensor_tensor(stat[:, sb + 3: sb + 4], stat[:, sb + 1: sb + 2], 1.0 / D,
                                       stat[:, sb + 3: sb + 4], op0=ALU.mult, op1=ALU.add)
                v.drain()
                ins = v.tensor_scalar(stat[:, sb + 3: sb + 4], stat[:, sb + 3: sb + 4], EPS, None, op0=ALU.add)
                P.inc(ins, "dv")
            dv_rec = P.v("dv")

            @block.scalar
            def _(s, t=t, slot=slot, sb=sb, dv_rec=dv_rec):
                P.w(s, "dv", dv_rec)
                if t >= 2:
                    P.w(s, *tpose_tr[t - 2])
                s.activation(stat[:, sb + 4: sb + 5], stat[:, sb + 3: sb + 4], AF.Ln)
                s.activation(yb[slot][:], xin[slot][:], AF.Identity,
                             bias=stat[:, sb + 2: sb + 3])
                s.drain()
                s.activation(stat[:, sb + 4: sb + 5], stat[:, sb + 4: sb + 5], AF.Exp, scale=-0.5)
                s.drain()
                ins = s.activation(yb[slot][:], yb[slot][:], AF.Identity,
                                   scale=stat[:, sb + 4: sb + 5])
                P.inc(ins, "sc")
            apply_sc.append(P.v("sc"))
            sc_now = P.v("sc")

            @block.sync
            def _(s, t=t, slot=slot, sc_now=sc_now):
                P.w(s, "sc", sc_now)
                if t >= 2:
                    P.w(s, *spill_gs[t - 2])
                srcy = yb[slot][:].rearrange("p (c f) -> p c f", c=NCH)
                dsty = ytr[slot][:].rearrange("p (c f) -> p c f", c=NCH)
                P.inc(s.dma_start_transpose(dsty, srcy), "st%d" % slot, 16)
            tpose_tr.append(("st%d" % slot, P.v("st%d" % slot)))
            tr_now = P.v("st%d" % slot)

            @block.gpsimd
            def _(g, t=t, slot=slot, tr_now=tr_now):
                P.w(g, "st%d" % slot, tr_now)
                P.dma(g, y1t_sp[t], ytr[slot][:], "gy%d" % slot)
            spill_gs.append(("gy%d" % slot, P.v("gy%d" % slot)))

        y1g_sl = {}
        y1g_done_pe = {}
        evac_hist = {"k": [], "q": [], "v": []}
        spill_hist = {"k": [], "q": [], "v": []}
        qkv_spill_snap = {}

        def emit_y1g_load(key, g, gslot, prev_pe):
            @block.sync
            def _(s, g=g, gslot=gslot, prev_pe=prev_pe):
                if prev_pe:
                    P.w(s, "pe", prev_pe)
                P.w(s, *spill_gs[4 * g + 3])
                P.w(s, *spill_gs[4 * g + 2])
                for j in range(4):
                    srcg = y1t_sp[4 * g + j].rearrange("p (c f) -> p c f", c=NCH)
                    dstg = y1g[gslot][:].rearrange("p (c f) -> p c f", c=NCH)[:, :, j * 128:(j + 1) * 128]
                    P.dma(s, dstg, srcg, "sy%d" % gslot)
            y1g_sl[key] = ("sy%d" % gslot, P.v("sy%d" % gslot))

        def emit_k_unit(H, g, ct):
            gslot = g % 2
            n = len(evac_hist["k"])
            pslot = ct % 2

            @block.tensor
            def _(t, H=H, g=g, ct=ct, pslot=pslot, gslot=gslot, n=n):
                P.w(t, *y1g_sl[("kv", H, g)])
                if g == 0 and ct == 0:
                    P.w(t, "gl", wkv_gl[H])
                if n >= 2:
                    P.w(t, "dv", evac_hist["k"][n - 2])
                for ch in range(NCH):
                    ins = t.matmul(ps2a[:, pslot * 512:(pslot + 1) * 512],
                                   wk_h[:, ch * 512 + ct * 128: ch * 512 + (ct + 1) * 128],
                                   y1g[gslot][:, ch * 512:(ch + 1) * 512],
                                   start=(ch == 0), stop=(ch == NCH - 1))
                P.inc(ins, "pe")
            pe_now = P.v("pe")

            @block.vector
            def _(v, pe_now=pe_now, pslot=pslot, n=n):
                P.w(v, "pe", pe_now)
                if n >= 2:
                    P.w(v, *spill_hist["k"][n - 2])
                ins = v.tensor_copy(kstg[n % 2][:], ps2a[:, pslot * 512:(pslot + 1) * 512])
                P.inc(ins, "dv")
            evac_hist["k"].append(P.v("dv"))
            dv_now = P.v("dv")
            gsem = "gk%d" % (n % 2)

            @block.gpsimd
            def _(gp, H=H, g=g, ct=ct, dv_now=dv_now, n=n, gsem=gsem):
                P.w(gp, "dv", dv_now)
                P.dma(gp, k_sp[H * 4 + ct][:, g * 512:(g + 1) * 512], kstg[n % 2][:], gsem)
            spill_hist["k"].append((gsem, P.v(gsem)))

        def emit_v_unit(H, g, tt):
            gslot = g % 2
            n = len(evac_hist["v"])
            pslot = tt % 2

            @block.tensor
            def _(t, H=H, g=g, tt=tt, pslot=pslot, gslot=gslot, n=n):
                if n >= 2:
                    P.w(t, "dv", evac_hist["v"][n - 2])
                for ch in range(NCH):
                    ins = t.matmul(ps2b[:, pslot * 512:(pslot + 1) * 512],
                                   y1g[gslot][:, ch * 512 + tt * 128: ch * 512 + (tt + 1) * 128],
                                   wv_h[:, ch * 512:(ch + 1) * 512],
                                   start=(ch == 0), stop=(ch == NCH - 1))
                P.inc(ins, "pe")
            pe_now = P.v("pe")
            if tt == 3:
                y1g_done_pe[("kv", H, g)] = pe_now

            @block.vector
            def _(v, pe_now=pe_now, pslot=pslot, n=n):
                P.w(v, "pe", pe_now)
                if n >= 2:
                    P.w(v, *spill_hist["v"][n - 2])
                ins = v.tensor_copy(vstg[n % 2][:], ps2b[:, pslot * 512:(pslot + 1) * 512])
                P.inc(ins, "dv")
            evac_hist["v"].append(P.v("dv"))
            dv_now = P.v("dv")
            gsem = "gv%d" % (n % 2)

            @block.gpsimd
            def _(gp, H=H, g=g, tt=tt, dv_now=dv_now, n=n, gsem=gsem):
                P.w(gp, "dv", dv_now)
                for i in range(4):
                    P.dma(gp, v_sp[H * 4 + i, 4 * g + tt], vstg[n % 2][:, i * 128:(i + 1) * 128], gsem)
            spill_hist["v"].append((gsem, P.v(gsem)))

        def emit_kv_units(H, g, units):
            for u in units:
                if u < 4:
                    emit_k_unit(H, g, u)
                else:
                    emit_v_unit(H, g, u - 4)

        for t in range(n_tiles):
            emit_p1_tile(t)
            g = t // 4
            ph = t % 4
            if ph == 3:
                prev_pe = y1g_done_pe.get(("kv", 0, g - 2), 0)
                emit_y1g_load(("kv", 0, g), g, g % 2, prev_pe)
            if g >= 1:
                emit_kv_units(0, g - 1, [2 * ph, 2 * ph + 1])
        emit_kv_units(0, n_groups - 1, list(range(8)))
        last_h0_pe = P.v("pe")
        emit_wkv_cast(1, wait_pe=last_h0_pe)
        for g in range(n_groups):
            prev_pe = y1g_done_pe.get(("kv", 1, g - 2), y1g_done_pe.get(("kv", 0, g + n_groups - 2), 0))
            emit_y1g_load(("kv", 1, g), g, g % 2, prev_pe)
            emit_kv_units(1, g, list(range(8)))

        kv_end = P.snap()
        for hh in range(8):
            HH, cth = hh // 4, hh % 4
            w = {}
            nk = (HH * n_groups + (n_groups - 1)) * 4 + cth
            w.update(dict([spill_hist["k"][nk]]))
            for sl in range(2):
                lastv = None
                for i, ent in enumerate(spill_hist["v"]):
                    if i < HH * n_groups * 4 or i >= (HH + 1) * n_groups * 4:
                        continue
                    if ent[0] == "gv%d" % sl:
                        lastv = ent
                if lastv:
                    w[lastv[0]] = max(w.get(lastv[0], 0), lastv[1])
            qkv_spill_snap[hh] = w

        p1_end = P.snap()
        p1ctx.close()

        p2qctx = ExitStack()
        wqin = [p2qctx.enter_context(nc.sbuf_tensor(f"wqin{i}", [128, TOWN], F32, side="right")) for i in range(2)]
        wacc = [p2qctx.enter_context(nc.sbuf_tensor(f"wacc{i}", [128, 512], F32, side="right")) for i in range(2)]
        wq_eff = p2qctx.enter_context(nc.sbuf_tensor("wq_eff", [128, NCH * 512], BF16, side="right"))
        qstg = [p2qctx.enter_context(nc.sbuf_tensor(f"qstg{i}", [128, 512], BF16, side="right")) for i in range(2)]

        for engsec in (block.tensor, block.vector, block.scalar, block.sync, block.gpsimd):
            @engsec
            def _(e):
                for nm, val in p1_end.items():
                    P.w(e, nm, val)

        presum_list = []

        def emit_presum_chunk(H, ch, wait_pe=0):
            n = len(presum_list)
            slot = n % 2

            @block.sync
            def _(s, H=H, ch=ch, slot=slot, n=n):
                if n >= 2:
                    P.w(s, "dv", presum_list[n - 2])
                P.dma(s, wqin[slot][:], wq[ch * 128:(ch + 1) * 128, H * TOWN:(H + 1) * TOWN], "sw%d" % slot)
            sl_now = P.v("sw%d" % slot)

            @block.vector
            def _(v, H=H, ch=ch, slot=slot, sl_now=sl_now, wait_pe=wait_pe):
                P.w(v, "sw%d" % slot, sl_now)
                if wait_pe and ch == 0:
                    P.w(v, "pe", wait_pe)
                srcq = wqin[slot][:].rearrange("p (kv g d) -> p kv g d", kv=4, g=4)
                acc = wacc[slot][:].rearrange("p (kv d) -> p kv d", kv=4)
                v.tensor_add(acc, srcq[:, :, 0, :], srcq[:, :, 1, :])
                v.drain()
                v.tensor_add(acc, acc, srcq[:, :, 2, :])
                v.drain()
                v.tensor_add(acc, acc, srcq[:, :, 3, :])
                v.drain()
                ins = v.tensor_scalar(wq_eff[:, ch * 512:(ch + 1) * 512], wacc[slot][:],
                                      RSD, None, op0=ALU.mult)
                P.inc(ins, "dv")
            presum_list.append(P.v("dv"))

        for H in range(2):
            wait_pe = 0
            if H == 1:
                wait_pe = P.v("pe")
            for ch in range(NCH):
                emit_presum_chunk(H, ch, wait_pe=wait_pe)
            presum_done = P.v("dv")

            for g in range(n_groups_own):
                gslot = g % 2
                prev_pe = y1g_done_pe.get(("q", H, g - 2), y1g_done_pe.get(("q", H - 1, g + n_groups_own - 2), 0))
                if H == 0 and g < 2:
                    prev_pe = 0
                emit_y1g_load(("q", H, g), g, gslot, prev_pe or kv_end["pe"])

                for ct in range(4):
                    n = len(evac_hist["q"])
                    pslot = 2 + ct % 2

                    @block.tensor
                    def _(t, H=H, g=g, ct=ct, pslot=pslot, gslot=gslot, n=n, presum_done=presum_done):
                        P.w(t, *y1g_sl[("q", H, g)])
                        if g == 0 and ct == 0:
                            P.w(t, "dv", presum_done)
                        if n >= 2:
                            P.w(t, "dv", evac_hist["q"][n - 2])
                        for ch in range(NCH):
                            ins = t.matmul(ps2a[:, pslot * 512:(pslot + 1) * 512],
                                           wq_eff[:, ch * 512 + ct * 128: ch * 512 + (ct + 1) * 128],
                                           y1g[gslot][:, ch * 512:(ch + 1) * 512],
                                           start=(ch == 0), stop=(ch == NCH - 1))
                        P.inc(ins, "pe")
                    pe_now = P.v("pe")
                    if ct == 3:
                        y1g_done_pe[("q", H, g)] = pe_now

                    @block.vector
                    def _(v, pe_now=pe_now, pslot=pslot, n=n):
                        P.w(v, "pe", pe_now)
                        if n >= 2:
                            P.w(v, *spill_hist["q"][n - 2])
                        ins = v.tensor_copy(qstg[n % 2][:], ps2a[:, pslot * 512:(pslot + 1) * 512])
                        P.inc(ins, "dv")
                    evac_hist["q"].append(P.v("dv"))
                    dv_now = P.v("dv")
                    gsem = "gq%d" % (n % 2)

                    @block.gpsimd
                    def _(gp, H=H, g=g, ct=ct, dv_now=dv_now, n=n, gsem=gsem):
                        P.w(gp, "dv", dv_now)
                        P.dma(gp, q_sp[H * 4 + ct][:, g * 512:(g + 1) * 512], qstg[n % 2][:], gsem)
                    spill_hist["q"].append((gsem, P.v(gsem)))

        for hh in range(8):
            HH, cth = hh // 4, hh % 4
            nq = (HH * n_groups_own + (n_groups_own - 1)) * 4 + cth
            ent = spill_hist["q"][nq]
            qkv_spill_snap[hh] = dict(qkv_spill_snap[hh])
            qkv_spill_snap[hh][ent[0]] = max(qkv_spill_snap[hh].get(ent[0], 0), ent[1])
        p2qctx.close()

        p2_end = P.snap()
        p2rctx.close()

        for engsec in (block.tensor, block.vector, block.scalar, block.sync, block.gpsimd):
            @engsec
            def _(e):
                for nm, val in p2_end.items():
                    P.w(e, nm, val)

        p3vctx = ExitStack()
        valuesT = p3vctx.enter_context(nc.sbuf_tensor("valuesT", [128, HKV * TOWN], BF16, side="right"))
        p3ctx = ExitStack()
        kbuf = [p3ctx.enter_context(nc.sbuf_tensor(f"kbuf{i}", [128, S], BF16, side="right")) for i in range(2)]
        vbuf = [p3ctx.enter_context(nc.sbuf_tensor(f"vbuf{i}", [128, S], BF16, side="right")) for i in range(2)]
        qbuf = [p3ctx.enter_context(nc.sbuf_tensor(f"qbuf{i}", [128, TOWN], BF16, side="right")) for i in range(2)]
        PTb = [p3ctx.enter_context(nc.sbuf_tensor(f"PTb{i}", [128, 4 * S], BF16, side="right")) for i in range(2)]
        recd_sb = p3ctx.enter_context(nc.sbuf_tensor("recd_sb", [1, 512], F32, side="right"))
        mhjunk = p3ctx.enter_context(nc.sbuf_tensor("mhjunk", [128, S], BF16, side="right"))
        bias_h = p3ctx.enter_context(nc.sbuf_tensor("bias_h", [128, 2], F32, side="right"))
        bcast_sb = p3ctx.enter_context(nc.sbuf_tensor("bcast_sb", [128, 1024], F32, side="right"))
        psS = [p3ctx.enter_context(nc.psum_tensor(f"psS{i}", [128, 1024], F32)) for i in range(3)]
        psV3 = p3ctx.enter_context(nc.psum_tensor("psV3", [128, 512], F32))
        psX = p3ctx.enter_context(nc.psum_tensor("psX", [128, 512], F32))

        for nm in ("gld0", "gld1", "sp0", "sp1", "sp2", "sp3"):
            P.need(nm)

        @block.vector
        def _(v):
            P.inc(v.memset(valuesT[:], 0.0), "dv")

        NKEY = n_groups * 512
        NKC = NKEY // 128
        G3 = n_groups_own
        head_done_pe = {}
        pv_dv = {}
        recd_dv = {}
        bcast_dv = {}
        exp_hist_pergroup = {}
        exp_hist = {}
        n_heads = min(HKV, 8)

        for h in range(n_heads):
            hs = h % 2
            gld = "gld%d" % hs
            recd_dv = {}
            bcast_dv = dict(bcast_dv)
            exp_hist_pergroup = {}

            @block.gpsimd
            def _(g, h=h, hs=hs, gld=gld):
                if h >= 2:
                    P.w(g, "pe", head_done_pe[h - 2])
                for nm, val in qkv_spill_snap[h].items():
                    P.w(g, nm, val)
                P.dma(g, kbuf[hs][:, 0:NKEY], k_sp[h][:, 0:NKEY], gld)
                P.dma(g, vbuf[hs][:, 0:NKEY], v_sp[h][0:NKC].rearrange("c p j -> p c j"), gld)
                P.dma(g, qbuf[hs][:, 0:G3 * 512], q_sp[h][:, 0:G3 * 512], gld)
            gld_now = P.v(gld)

            @block.vector
            def _(v, h=h, hs=hs, gld_now=gld_now, gld=gld):
                P.w(v, gld, gld_now)
                if h >= 1:
                    P.w(v, "pe", P.v("pe"))
                ins = v.scalar_tensor_tensor(mhjunk[:, 0:G3 * 512], qbuf[hs][:, 0:G3 * 512], 1.0,
                                             qbuf[hs][:, 0:G3 * 512], op0=ALU.mult, op1=ALU.mult)
                P.inc(ins, "dv")
            sc_q2 = ("dv", P.v("dv"))

            for i in range(G3):
                @block.tensor
                def _(t, i=i, sc_q2=sc_q2):
                    P.w(t, *sc_q2)
                    if i > 0:
                        P.w(t, "dv", P.v("dv"))
                    ins = t.matmul(psX[0:1, :], onesc[:], mhjunk[:, i * 512:(i + 1) * 512],
                                   start=True, stop=True)
                    P.inc(ins, "pe")
                pe_now = P.v("pe")

                @block.vector
                def _(v, h=h, i=i, pe_now=pe_now):
                    P.w(v, "pe", pe_now)
                    v.reduce_max(st1[:, 1:2], psX[0:1, :], axis=AXX)
                    v.drain()
                    if i == 0:
                        ins = v.tensor_copy(st1[:, 2:3], st1[:, 1:2])
                    else:
                        ins = v.tensor_tensor(st1[:, 2:3], st1[:, 2:3], st1[:, 1:2], op=ALU.max)
                    v.drain()
                    P.inc(ins, "dv")

            pe_q2done = P.v("pe")

            @block.vector
            def _(v, h=h, hs=hs, pe_q2done=pe_q2done):
                P.w(v, "pe", pe_q2done)
                ins = v.scalar_tensor_tensor(mhjunk[:, 0:NKEY], kbuf[hs][:, 0:NKEY], 1.0,
                                             kbuf[hs][:, 0:NKEY], op0=ALU.mult, op1=ALU.mult)
                P.inc(ins, "dv")
            sc_k2 = ("dv", P.v("dv"))

            for i in range(NKEY // 512):
                @block.tensor
                def _(t, i=i, sc_k2=sc_k2):
                    P.w(t, *sc_k2)
                    P.w(t, "dv", P.v("dv"))
                    ins = t.matmul(psX[0:1, :], onesc[:], mhjunk[:, i * 512:(i + 1) * 512],
                                   start=True, stop=True)
                    P.inc(ins, "pe")
                pe_now = P.v("pe")

                @block.vector
                def _(v, h=h, i=i, pe_now=pe_now):
                    P.w(v, "pe", pe_now)
                    v.reduce_max(st1[:, 1:2], psX[0:1, :], axis=AXX)
                    v.drain()
                    if i == 0:
                        ins = v.tensor_copy(st1[:, 3:4], st1[:, 1:2])
                    else:
                        ins = v.tensor_tensor(st1[:, 3:4], st1[:, 3:4], st1[:, 1:2], op=ALU.max)
                    v.drain()
                    P.inc(ins, "dv")

            @block.vector
            def _(v):
                ins = v.tensor_mul(st1[:, 4:5], st1[:, 2:3], st1[:, 3:4])
                P.inc(ins, "dv")
            dv_m = P.v("dv")

            @block.scalar
            def _(s, dv_m=dv_m):
                P.w(s, "dv", dv_m)
                s.activation(st1[:, 4:5], st1[:, 4:5], AF.Ln)
                s.drain()
                ins = s.activation(st1[:, 4:5], st1[:, 4:5], AF.Exp, scale=0.5)
                P.inc(ins, "sc")
            sc_m = P.v("sc")

            @block.vector
            def _(v, sc_m=sc_m):
                P.w(v, "sc", sc_m)
                v.tensor_scalar(st1[:, 4:5], st1[:, 4:5], -1.02, -1.0,
                                op0=ALU.mult, op1=ALU.add)
                v.drain()
                ins = v.tensor_copy(st1b[:, 0:1], st1[:, 4:5])
                P.inc(ins, "dv")
            dv_mb = P.v("dv")

            @block.tensor
            def _(t, dv_mb=dv_mb):
                P.w(t, "dv", dv_mb)
                ins = t.matmul(psX[:, 0:1], ones_r[:], st1b[:, 0:1], start=True, stop=True)
                P.inc(ins, "pe")
            pe_b = P.v("pe")

            @block.vector
            def _(v, hs=hs, pe_b=pe_b):
                P.w(v, "pe", pe_b)
                ins = v.tensor_copy(bias_h[:, hs:hs + 1], psX[:, 0:1])
                P.inc(ins, "dv")
            sc_bias = ("dv", P.v("dv"))

            def emit_PV(g, h=h, hs=hs):
                gs3 = g % 2

                @block.tensor
                def _(t, g=g, gs3=gs3):
                    P.w(t, "sc", exp_hist_pergroup[g])
                    prevdv = pv_dv.get((h, g - 1)) if g >= 1 else (pv_dv.get((h - 1, G3 - 1)) if h >= 1 else None)
                    if prevdv is not None:
                        P.w(t, "dv", prevdv)
                    for kc in range(NKC):
                        ins = t.matmul(psV3[:], vbuf[hs][:, kc * 128:(kc + 1) * 128],
                                       PTb[gs3][:, kc * 512:(kc + 1) * 512],
                                       start=(kc == 0), stop=(kc == NKC - 1))
                    P.inc(ins, "pe")
                pe_pv = P.v("pe")

                @block.vector
                def _(v, g=g, gs3=gs3, pe_pv=pe_pv):
                    P.w(v, "pe", pe_pv)
                    P.w(v, *bcast_dv[g])
                    for qt in range(4):
                        ins = v.tensor_mul(valuesT[:, h * TOWN + g * 512 + qt * 128: h * TOWN + g * 512 + (qt + 1) * 128],
                                           psV3[:, qt * 128:(qt + 1) * 128],
                                           bcast_sb[:, gs3 * 512 + qt * 128: gs3 * 512 + (qt + 1) * 128])
                    P.inc(ins, "dv")
                pv_dv[(h, g)] = P.v("dv")

            for g in range(G3):
                gs3 = g % 2
                den_mm = {}

                sc0 = P.v("sc")
                pe_s_list = []
                for kp in range(NKC // 2):
                    sslot = kp % 3

                    @block.tensor
                    def _(t, g=g, kp=kp, sslot=sslot, sc0=sc0):
                        if (g, kp) == (0, 0):
                            P.w(t, *sc_bias)
                        if kp >= 3:
                            P.wfut(t, "sc", sc0 + kp - 2)
                        for u in range(2):
                            kc = 2 * kp + u
                            ins = t.matmul(psS[sslot][:, u * 512:(u + 1) * 512],
                                           kbuf[hs][:, kc * 128:(kc + 1) * 128],
                                           qbuf[hs][:, g * 512:(g + 1) * 512],
                                           start=True, stop=True)
                        P.inc(ins, "pe")
                    pe_s_list.append(P.v("pe"))

                if g >= 1:
                    emit_PV(g - 1)

                @block.scalar
                def _(s, g=g, pe_s_list=pe_s_list):
                    for kp in range(NKC // 2):
                        P.w(s, "pe", pe_s_list[kp])
                        if kp == 0:
                            prevpv = pv_dv.get((h, g - 2)) if g >= 2 else \
                                (pv_dv.get((h - 1, G3 - 2 + g)) if h >= 1 else None)
                            if prevpv is not None:
                                P.w(s, "dv", prevpv)
                        ins = s.activation(PTb[g % 2][:, kp * 1024:(kp + 1) * 1024],
                                           psS[kp % 3][:, 0:1024], AF.Exp,
                                           bias=bias_h[:, hs:hs + 1])
                        P.inc(ins, "sc")
                assert P.v("sc") == sc0 + NKC // 2
                for kp in range(NKC // 2):
                    exp_hist[kp % 3] = sc0 + kp + 1

                @block.tensor
                def _(t, g=g, sc0=sc0):
                    prevrd = recd_dv.get(g - 1)
                    if prevrd is not None:
                        P.w(t, "dv", prevrd)
                    prevbc = bcast_dv.get(g - 1)
                    if prevbc is not None:
                        P.w(t, *prevbc)
                    for kp in range(NKC // 2):
                        P.w(t, "sc", sc0 + kp + 1)
                        for u in range(2):
                            kc = 2 * kp + u
                            ins = t.matmul(psX[0:1, :], onesc[:],
                                           PTb[g % 2][:, kc * 512:(kc + 1) * 512],
                                           start=(kc == 0), stop=(kc == NKC - 1))
                    P.inc(ins, "pe")
                pe_den = P.v("pe")

                @block.vector
                def _(v, g=g, pe_den=pe_den):
                    P.w(v, "pe", pe_den)
                    v.tensor_scalar(recd_sb[0:1, 0:512], psX[0:1, :], P40, None, op0=ALU.mult)
                    v.drain()
                    v.reciprocal(recd_sb[0:1, 0:512], recd_sb[0:1, 0:512])
                    v.drain()
                    ins = v.tensor_scalar(recd_sb[0:1, 0:512], recd_sb[0:1, 0:512], P40, None, op0=ALU.mult)
                    P.inc(ins, "dv")
                recd_dv[g] = P.v("dv")
                dv_rd = P.v("dv")

                @block.tensor
                def _(t, g=g, dv_rd=dv_rd):
                    P.w(t, "dv", dv_rd)
                    prevbc = bcast_dv.get(g - 1)
                    if prevbc is not None:
                        P.w(t, *prevbc)
                    ins = t.matmul(psX[:, 0:512], ones_rf[:], recd_sb[0:1, 0:512],
                                   start=True, stop=True)
                    P.inc(ins, "pe")
                pe_bc = P.v("pe")

                @block.vector
                def _(v, g=g, gs3=gs3, pe_bc=pe_bc):
                    P.w(v, "pe", pe_bc)
                    prevpv = pv_dv.get((h, g - 2))
                    if prevpv is not None:
                        P.w(v, "dv", prevpv)
                    ins = v.tensor_copy(bcast_sb[:, gs3 * 512:(gs3 + 1) * 512], psX[:, 0:512])
                    P.inc(ins, "dv")
                bcast_dv[g] = ("dv", P.v("dv"))

                exp_hist_pergroup[g] = sc0 + NKC // 2
            emit_PV(G3 - 1)
            head_done_pe[h] = P.v("pe")
        p3_vt_dv = P.v("dv")

        if debug:
            @block.gpsimd
            def _(g):
                P.w(g, "dv", p3_vt_dv)
                P.dma(g, vt_dbg, valuesT[:], "gv0")

        p3_end = P.snap()
        p3ctx.close()

        for engsec in (block.tensor, block.vector, block.scalar, block.sync, block.gpsimd):
            @engsec
            def _(e):
                for nm, val in p3_end.items():
                    P.w(e, nm, val)

        NT4 = G3 * 4
        SGL = [(s, min(8, NT4 - s)) for s in range(0, NT4, 8)]

        bbrow = ctx.enter_context(nc.sbuf_tensor("bbrow", [1, D], BF16))
        AW = ctx.enter_context(nc.sbuf_tensor("AW", [128, NCH * 136], BF16))
        Bfl = ctx.enter_context(nc.sbuf_tensor("Bfl", [128, D], BF16))
        fsums = ctx.enter_context(nc.sbuf_tensor("fsums", [128, 64], F32))
        fssq = ctx.enter_context(nc.sbuf_tensor("fssq", [128, 64], F32))

        p4ctx = ExitStack()
        wo_sb = p4ctx.enter_context(nc.sbuf_tensor("wo_sb", [128, 8 * D], BF16, side="right"))
        attnsb = p4ctx.enter_context(nc.sbuf_tensor("attnsb", [128, D], F32, side="right"))
        _xo = p4ctx.enter_context(nc.sbuf_tensor("xo0", [128, D], F32, side="right"))
        _x2b = p4ctx.enter_context(nc.sbuf_tensor("x2b0", [128, D], F32, side="right"))
        _y2b = p4ctx.enter_context(nc.sbuf_tensor("y2b0", [128, D], BF16, side="right"))
        xo = [_xo, _xo]
        x2b = [_x2b, _x2b]
        y2b = [_y2b, _y2b]
        psW = p4ctx.enter_context(nc.psum_tensor("psW", [128, 3072], F32))
        psC = p4ctx.enter_context(nc.psum_tensor("psC", [128, 2048], BF16))
        p4sctx = ExitStack()
        A_nat = p4sctx.enter_context(nc.sbuf_tensor("A_nat", [128, D], BF16, side="right"))
        Wr_nat = p4sctx.enter_context(nc.sbuf_tensor("Wr_nat", [8, D], BF16, side="right"))
        Bexp = [p4sctx.enter_context(nc.sbuf_tensor(f"Bexp{i}", [128, 1024], BF16, side="right")) for i in range(2)]

        @block.gpsimd
        def _(g):
            P.dma(g, bbrow[:], bb, "gl")
            P.dma(g, A_nat[:], Ast.rearrange("e r d -> (e r) d"), "gl")
            P.dma(g, Wr_nat[:], Wr, "gl")
        aw_gl = P.v("gl")

        aw_sc = []
        for ch in range(NCH):
            cslot = ch % 2

            @block.tensor
            def _(t, ch=ch, cslot=cslot):
                if ch == 0:
                    P.w(t, "gl", aw_gl)
                if ch >= 2:
                    P.w(t, *aw_sc[ch - 2])
                t.transpose(psC[:, cslot * 1024: cslot * 1024 + 128],
                            A_nat[:, ch * 128:(ch + 1) * 128], ident[:])
                ins = t.transpose(psC[:, cslot * 1024 + 128: cslot * 1024 + 136],
                                  Wr_nat[:, ch * 128:(ch + 1) * 128], ident[0:8, 0:8])
                P.inc(ins, "pe")
            pe_now = P.v("pe")

            @block.vector
            def _(v, ch=ch, cslot=cslot, pe_now=pe_now):
                P.w(v, "pe", pe_now)
                v.tensor_copy(AW[:, ch * 136: ch * 136 + 128], psC[:, cslot * 1024: cslot * 1024 + 128])
                v.drain()
                ins = v.tensor_copy(AW[:, ch * 136 + 128: (ch + 1) * 136],
                                    psC[:, cslot * 1024 + 128: cslot * 1024 + 136])
                P.inc(ins, "dv")
            aw_sc.append(("dv", P.v("dv")))

        bexp_gl = {}
        bfl_sc = []
        for ep in range(4):
            es = ep % 2

            @block.gpsimd
            def _(g, ep=ep, es=es):
                if ep >= 2:
                    P.w(g, *bfl_sc[(ep - 2) * NCH + NCH - 1])
                P.need("gb%d" % es)
                for u in range(2):
                    srcb = Bst[2 * ep + u].rearrange("(oc p) r -> p oc r", p=128)
                    dstb = Bexp[es][:].rearrange("p (oc u r) -> p oc u r", u=2, r=16)[:, :, u, :]
                    P.dma(g, dstb, srcb, "gb%d" % es)
            bexp_gl[ep] = ("gb%d" % es, P.v("gb%d" % es))

            for oc in range(NCH):
                bs = oc % 2

                @block.tensor
                def _(t, ep=ep, oc=oc, es=es, bs=bs):
                    if oc == 0:
                        P.w(t, *bexp_gl[ep])
                    if ep == 0 and oc == 0:
                        P.w(t, *aw_sc[-1])
                    if len(bfl_sc) >= 2:
                        P.w(t, *bfl_sc[-2])
                    ins = t.transpose(psC[0:32, bs * 1024 + 256: bs * 1024 + 384],
                                      Bexp[es][:, oc * 32:(oc + 1) * 32], ident[:])
                    P.inc(ins, "pe")
                pe_now = P.v("pe")

                @block.vector
                def _(v, ep=ep, oc=oc, bs=bs, pe_now=pe_now):
                    P.w(v, "pe", pe_now)
                    ins = v.tensor_copy(Bfl[ep * 32:(ep + 1) * 32, oc * 128:(oc + 1) * 128],
                                        psC[0:32, bs * 1024 + 256: bs * 1024 + 384])
                    P.inc(ins, "dv")
                bfl_sc.append(("dv", P.v("dv")))

        cst_end = P.snap()
        p4sctx.close()
        for engsec in (block.tensor, block.vector, block.scalar, block.sync, block.gpsimd):
            @engsec
            def _(e, snap=cst_end):
                for nm, val in snap.items():
                    P.w(e, nm, val)

        P.need("gwo")

        @block.gpsimd
        def _(g):
            srcw = wo.rearrange("(c p) f -> p c f", p=128)
            P.dma(g, wo_sb[:].rearrange("p (c f) -> p c f", c=8), srcw, "gwo")
        wo_gl = P.v("gwo")

        wo_evac = {}
        x2done_dv = {}
        y2_done_sc = {}
        x2sp_gs = {}
        y2sp_gs = {}
        P.need("gx0"); P.need("gx1")

        for tt in range(NT4):
            ts4 = tt % 2
            sb4 = 40 + ts4 * 12

            @block.sync
            def _(s, tt=tt, ts4=ts4):
                prev = x2done_dv.get(tt - 1)
                if prev:
                    P.w(s, "dv", prev)
                P.need("sxo%d" % ts4)
                P.dma(s, xo[ts4][:], xa[tt * 128:(tt + 1) * 128, :], "sxo%d" % ts4)
            sl_xo = P.v("sxo%d" % ts4)

            pe_parts = []
            for part, (obase, olen) in enumerate([(0, 3072), (3072, 1024)]):
                @block.tensor
                def _(t, tt=tt, part=part, obase=obase, olen=olen):
                    if tt == 0 and part == 0:
                        P.w(t, "gwo", wo_gl)
                        P.w(t, "dv", p3_vt_dv)
                    prev = wo_evac.get((tt, part - 1), wo_evac.get((tt - 1, 1)))
                    if prev:
                        P.wfut(t, "sc", prev)
                    for ch in range(8):
                        for oc in range(olen // 512):
                            ins = t.matmul(psW[:, oc * 512:(oc + 1) * 512],
                                           valuesT[:, ch * TOWN + tt * 128:(ch * TOWN) + (tt + 1) * 128],
                                           wo_sb[:, ch * D + obase + oc * 512: ch * D + obase + (oc + 1) * 512],
                                           start=(ch == 0), stop=(ch == 7))
                    P.inc(ins, "pe")
                pe_parts.append(P.v("pe"))
                wo_evac[(tt, part)] = P.v("sc") + part + 1

            @block.scalar
            def _(s, tt=tt, sb4=sb4, pe_parts=pe_parts):
                prevx2 = x2done_dv.get(tt - 1)
                if prevx2:
                    P.w(s, "dv", prevx2)
                for part, (obase, olen) in enumerate([(0, 3072), (3072, 1024)]):
                    P.w(s, "pe", pe_parts[part])
                    ins = s.activation(attnsb[:, obase:obase + olen], psW[:, 0:olen], AF.Identity,
                                       accum_out=stat[:, sb4 + part: sb4 + part + 1])
                    P.inc(ins, "sc")
            sc_attn = P.v("sc")
            assert wo_evac[(tt, 0)] == sc_attn - 1 and wo_evac[(tt, 1)] == sc_attn

            @block.vector
            def _(v, tt=tt, ts4=ts4, sb4=sb4, sc_attn=sc_attn):
                P.w(v, "sc", sc_attn)
                prev = y2sp_gs.get(tt - 1)
                if prev:
                    P.w(v, *prev)
                v.scalar_tensor_tensor(y2b[ts4][:, 0:2048], attnsb[:, 0:2048], 1.0,
                                       attnsb[:, 0:2048], op0=ALU.mult, op1=ALU.mult,
                                       accum_out=stat[:, sb4 + 2: sb4 + 3])
                v.drain()
                v.scalar_tensor_tensor(y2b[ts4][:, 0:2048], attnsb[:, 2048:4096], 1.0,
                                       attnsb[:, 2048:4096], op0=ALU.mult, op1=ALU.mult,
                                       accum_out=stat[:, sb4 + 3: sb4 + 4])
                v.drain()
                v.tensor_tensor(stat[:, sb4 + 4: sb4 + 5], stat[:, sb4: sb4 + 1],
                                stat[:, sb4 + 1: sb4 + 2], op=ALU.add)
                v.drain()
                v.tensor_scalar(stat[:, sb4 + 4: sb4 + 5], stat[:, sb4 + 4: sb4 + 5],
                                -1.0 / D, None, op0=ALU.mult)
                v.drain()
                v.tensor_tensor(stat[:, sb4 + 5: sb4 + 6], stat[:, sb4 + 2: sb4 + 3],
                                stat[:, sb4 + 3: sb4 + 4], op=ALU.add)
                v.drain()
                v.scalar_tensor_tensor(stat[:, sb4 + 6: sb4 + 7], stat[:, sb4 + 4: sb4 + 5],
                                       -1.0, stat[:, sb4 + 4: sb4 + 5], op0=ALU.mult, op1=ALU.mult)
                v.drain()
                v.scalar_tensor_tensor(stat[:, sb4 + 6: sb4 + 7], stat[:, sb4 + 5: sb4 + 6],
                                       1.0 / D, stat[:, sb4 + 6: sb4 + 7], op0=ALU.mult, op1=ALU.add)
                v.drain()
                ins = v.tensor_scalar(stat[:, sb4 + 6: sb4 + 7], stat[:, sb4 + 6: sb4 + 7],
                                      EPS, None, op0=ALU.add)
                P.inc(ins, "dv")
            dv_rec2 = P.v("dv")

            @block.scalar
            def _(s, sb4=sb4, dv_rec2=dv_rec2):
                P.w(s, "dv", dv_rec2)
                s.activation(stat[:, sb4 + 7: sb4 + 8], stat[:, sb4 + 6: sb4 + 7], AF.Ln)
                s.drain()
                ins = s.activation(stat[:, sb4 + 7: sb4 + 8], stat[:, sb4 + 7: sb4 + 8], AF.Exp, scale=-0.5)
                P.inc(ins, "sc")
            sc_r2 = P.v("sc")

            @block.vector
            def _(v, tt=tt, ts4=ts4, sb4=sb4, sc_r2=sc_r2, sl_xo=sl_xo):
                P.w(v, "sc", sc_r2)
                P.w(v, "sxo%d" % ts4, sl_xo)
                prevsp = x2sp_gs.get(tt - 1)
                if prevsp:
                    P.w(v, *prevsp)
                prevy2 = y2_done_sc.get(tt - 1)
                if prevy2:
                    P.w(v, "sc", prevy2)
                v.drain()
                v.tensor_scalar(attnsb[:], attnsb[:], stat[:, sb4 + 4: sb4 + 5], None, op0=ALU.add)
                v.drain()
                ins = v.scalar_tensor_tensor(x2b[ts4][:], attnsb[:], stat[:, sb4 + 7: sb4 + 8],
                                             xo[ts4][:], op0=ALU.mult, op1=ALU.add)
                P.inc(ins, "dv")
            x2done_dv[tt] = P.v("dv")
            dv_x2 = P.v("dv")

            @block.gpsimd
            def _(g, tt=tt, ts4=ts4, dv_x2=dv_x2):
                P.w(g, "dv", dv_x2)
                P.dma(g, x2_sp[tt], x2b[ts4][:], "gx%d" % ts4)
            x2sp_gs[tt] = ("gx%d" % ts4, P.v("gx%d" % ts4))

            @block.vector
            def _(v, tt=tt, ts4=ts4, sb4=sb4):
                v.drain()
                v.scalar_tensor_tensor(y2b[ts4][:], x2b[ts4][:], 1.0, x2b[ts4][:],
                                       op0=ALU.mult, op1=ALU.mult,
                                       accum_out=stat[:, sb4 + 9: sb4 + 10])
                v.drain()
                v.reduce_sum(stat[:, sb4 + 10: sb4 + 11], x2b[ts4][:], axis=AXX)
                v.drain()
                v.tensor_scalar(stat[:, sb4 + 10: sb4 + 11], stat[:, sb4 + 10: sb4 + 11],
                                -1.0 / D, None, op0=ALU.mult)
                v.drain()
                v.scalar_tensor_tensor(stat[:, sb4 + 11: sb4 + 12], stat[:, sb4 + 10: sb4 + 11],
                                       -1.0, stat[:, sb4 + 10: sb4 + 11], op0=ALU.mult, op1=ALU.mult)
                v.drain()
                v.scalar_tensor_tensor(stat[:, sb4 + 11: sb4 + 12], stat[:, sb4 + 9: sb4 + 10],
                                       1.0 / D, stat[:, sb4 + 11: sb4 + 12], op0=ALU.mult, op1=ALU.add)
                v.drain()
                ins = v.tensor_scalar(stat[:, sb4 + 11: sb4 + 12], stat[:, sb4 + 11: sb4 + 12],
                                      EPS, None, op0=ALU.add)
                P.inc(ins, "dv")
            dv_rec3 = P.v("dv")

            @block.scalar
            def _(s, tt=tt, ts4=ts4, sb4=sb4, dv_rec3=dv_rec3):
                P.w(s, "dv", dv_rec3)
                s.activation(stat[:, sb4 + 11: sb4 + 12], stat[:, sb4 + 11: sb4 + 12], AF.Ln)
                s.drain()
                s.activation(stat[:, sb4 + 11: sb4 + 12], stat[:, sb4 + 11: sb4 + 12], AF.Exp, scale=-0.5)
                s.activation(y2b[ts4][:], x2b[ts4][:], AF.Identity,
                             bias=stat[:, sb4 + 10: sb4 + 11])
                s.drain()
                ins = s.activation(y2b[ts4][:], y2b[ts4][:], AF.Identity,
                                   scale=stat[:, sb4 + 11: sb4 + 12])
                P.inc(ins, "sc")
            y2_done_sc[tt] = P.v("sc")
            sc_y2 = P.v("sc")

            @block.gpsimd
            def _(g, tt=tt, ts4=ts4, sc_y2=sc_y2):
                P.w(g, "sc", sc_y2)
                P.dma(g, y2_sp[tt], y2b[ts4][:], "gy%d" % ts4)
            y2sp_gs[tt] = ("gy%d" % ts4, P.v("gy%d" % ts4))

        p4_end = P.snap()
        p4ctx.close()
        p3vctx.close()

        for engsec in (block.tensor, block.vector, block.scalar, block.sync, block.gpsimd):
            @engsec
            def _(e):
                for nm, val in p4_end.items():
                    P.w(e, nm, val)

        p5ctx = ExitStack()
        _y2rd = p5ctx.enter_context(nc.sbuf_tensor("y2rd0", [128, D], BF16, side="right"))
        y2rd = [_y2rd, _y2rd]
        y2T = p5ctx.enter_context(nc.sbuf_tensor("y2T", [128, NCH * 1024], BF16, side="right"))
        hT_sb = p5ctx.enter_context(nc.sbuf_tensor("hT_sb", [128, 8 * 128], BF16, side="right"))
        hp_sb = p5ctx.enter_context(nc.sbuf_tensor("hp_sb", [128, 128], F32, side="right"))
        cmb = p5ctx.enter_context(nc.sbuf_tensor("cmb", [128, 32], F32, side="right"))
        _fst = p5ctx.enter_context(nc.sbuf_tensor("fstage0", [128, 8 * 512], BF16, side="right"))
        fstage = [_fst, _fst]
        fjunk = p5ctx.enter_context(nc.sbuf_tensor("fjunk", [128, 512], BF16, side="right"))
        psF = p5ctx.enter_context(nc.psum_tensor("psF", [128, 4096], F32))

        P.need("gw0"); P.need("gw1"); P.need("swt0"); P.need("swt1")

        P.need("gf0"); P.need("gf1"); P.need("sf0"); P.need("sf1")
        P.need("sxx0"); P.need("sxx1")

        y2t_tr = {}
        hT_done = {}
        base_done_sc = {}
        fspill = {}
        wbt_tr = {}
        wbn_hist = {0: [], 1: []}
        wbr_hist = {0: [], 1: []}
        base_mm_pe = {}
        y2rd_hist = {0: [], 1: []}
        out_done = []

        for sgi, (sg0, sgn) in enumerate(SGL):
            sgctx = ExitStack()
            WbTb = [sgctx.enter_context(nc.sbuf_tensor(f"WbTb{sgi}_{i}", [128, NCH * 512], BF16, side="right")) for i in range(2)]
            wbn = [sgctx.enter_context(nc.sbuf_tensor(f"wbn{sgi}_{i}", [128, 2048], BF16, side="right")) for i in range(2)]
            wbr = [sgctx.enter_context(nc.sbuf_tensor(f"wbr{sgi}_{i}", [128, 2048], F32, side="right")) for i in range(2)]

            for j in range(sgn):
                tt = sg0 + j
                js = j % 2

                @block.sync
                def _(s, tt=tt, js=js, sgi=sgi, j=j):
                    P.w(s, *y2sp_gs[tt])
                    for jsx in (0, 1):
                        if y2rd_hist[jsx]:
                            P.w(s, *y2rd_hist[jsx][-1])
                    P.dma(s, y2rd[js][:], y2_sp[tt], "sxx%d" % js)
                sl_y2 = P.v("sxx%d" % js)

                @block.sync
                def _(s, tt=tt, js=js, sgi=sgi, j=j, sl_y2=sl_y2):
                    P.w(s, "sxx%d" % js, sl_y2)
                    if sgi > 0 and j == 0:
                        P.w(s, "pe", base_done_sc[("pe", sgi - 1)])
                    srcy = y2rd[js][:].rearrange("p (c f) -> p c f", c=NCH)
                    dsty = y2T[:].rearrange("p (c f) -> p c f", c=NCH)[:, :, j * 128:(j + 1) * 128]
                    P.inc(s.dma_start_transpose(dsty, srcy), "swt%d" % js, 16)
                y2t_tr[(sgi, j)] = ("swt%d" % js, P.v("swt%d" % js))
                y2rd_hist[js].append(y2t_tr[(sgi, j)])

                @block.tensor
                def _(t, tt=tt, j=j, sgi=sgi):
                    P.w(t, *y2t_tr[(sgi, j)])
                    if (sgi, j) == (0, 0):
                        P.w(t, *bfl_sc[-1])
                    prev = hT_done.get((sgi, j - 2))
                    if prev:
                        P.w(t, *prev)
                    pbase = (j % 2) * 512
                    for ch in range(NCH):
                        ins = t.matmul(psF[:, pbase: pbase + 136],
                                       y2T[:, ch * 1024 + j * 128: ch * 1024 + (j + 1) * 128],
                                       AW[:, ch * 136:(ch + 1) * 136],
                                       start=(ch == 0), stop=(ch == NCH - 1))
                    P.inc(ins, "pe")
                pe_h = P.v("pe")

                cb = (j % 2) * 16

                @block.vector
                def _(v, j=j, cb=cb, pe_h=pe_h):
                    pbase = (j % 2) * 512
                    P.w(v, "pe", pe_h)
                    lg = psF[:, pbase + 128: pbase + 136]
                    v.reduce_max(cmb[:, cb: cb + 1], lg, axis=AXX)
                    v.drain()
                    v.tensor_scalar(cmb[:, cb + 8: cb + 16], lg, cmb[:, cb: cb + 1], None, op0=ALU.is_ge)
                    v.drain()
                    v.scalar_tensor_tensor(cmb[:, cb + 8: cb + 16], cmb[:, cb + 8: cb + 16], -1e30,
                                           lg, op0=ALU.mult, op1=ALU.add)
                    v.drain()
                    v.reduce_max(cmb[:, cb + 1: cb + 2], cmb[:, cb + 8: cb + 16], axis=AXX)
                    v.drain()
                    v.tensor_scalar(cmb[:, cb + 2: cb + 3], cmb[:, cb: cb + 1], -1.0, None, op0=ALU.mult)
                    v.drain()
                    ins = v.tensor_scalar(cmb[:, cb + 8: cb + 16], lg, cmb[:, cb + 1: cb + 2], None,
                                          op0=ALU.is_ge)
                    P.inc(ins, "dv")
                dv_c1 = P.v("dv")

                @block.scalar
                def _(s, j=j, cb=cb, dv_c1=dv_c1):
                    pbase = (j % 2) * 512
                    P.w(s, "dv", dv_c1)
                    ins = s.activation(cmb[:, cb + 4: cb + 12][:, 0:8], psF[:, pbase + 128: pbase + 136],
                                       AF.Exp, bias=cmb[:, cb + 2: cb + 3], scale=1.0)
                    P.inc(ins, "sc")
                sc_c1 = P.v("sc")

                @block.vector
                def _(v, cb=cb, sc_c1=sc_c1):
                    P.w(v, "sc", sc_c1)
                    v.tensor_mul(cmb[:, cb + 4: cb + 12], cmb[:, cb + 4: cb + 12], cmb[:, cb + 8: cb + 16])
                    v.drain()
                    v.reduce_sum(cmb[:, cb + 3: cb + 4], cmb[:, cb + 4: cb + 12], axis=AXX)
                    v.drain()
                    v.reciprocal(cmb[:, cb + 3: cb + 4], cmb[:, cb + 3: cb + 4])
                    v.drain()
                    ins = v.tensor_scalar(cmb[:, cb + 4: cb + 12], cmb[:, cb + 4: cb + 12],
                                          cmb[:, cb + 3: cb + 4], 2.0, op0=ALU.mult, op1=ALU.mult)
                    P.inc(ins, "dv")
                dv_c2 = P.v("dv")

                @block.vector
                def _(v, j=j, cb=cb, dv_c2=dv_c2):
                    pbase = (j % 2) * 512
                    v.drain()
                    for e8 in range(8):
                        ins = v.tensor_scalar(hp_sb[:, e8 * 16:(e8 + 1) * 16],
                                              psF[:, pbase + e8 * 16: pbase + (e8 + 1) * 16],
                                              cmb[:, cb + 4 + e8: cb + 5 + e8], None, op0=ALU.mult)
                    P.inc(ins, "dv")
                dv_hp = P.v("dv")

                @block.tensor
                def _(t, j=j, dv_hp=dv_hp):
                    pbase = (j % 2) * 512
                    P.w(t, "dv", dv_hp)
                    ins = t.transpose(psF[:, pbase + 256: pbase + 384], hp_sb[:], identf[:])
                    P.inc(ins, "pe")
                pe_hT = P.v("pe")

                @block.vector
                def _(v, j=j, pe_hT=pe_hT):
                    pbase = (j % 2) * 512
                    P.w(v, "pe", pe_hT)
                    ins = v.tensor_copy(hT_sb[:, j * 128:(j + 1) * 128], psF[:, pbase + 256: pbase + 384])
                    P.inc(ins, "dv")
                hT_done[(sgi, j)] = ("dv", P.v("dv"))
            sc_prep = P.v("sc")
            dv_prep = P.v("dv")

            evacA = {}
            evacB = {}

            def prefetch_pass(p):
                for ot2 in range(8):
                    ot, half = ot2 // 2, ot2 % 2
                    ws = ot2 % 2

                    @block.gpsimd
                    def _(g, p=p, ot=ot, half=half, ws=ws):
                        if wbr_hist[ws]:
                            P.w(g, "dv", wbr_hist[ws][-1])
                        rows = Wb[p * 512 + ot * 128: p * 512 + (ot + 1) * 128,
                                  half * 2048:(half + 1) * 2048]
                        P.dma(g, wbr[ws][:], rows, "gw%d" % ws)
                    gl_wb = P.v("gw%d" % ws)

                    @block.vector
                    def _(v, ws=ws, gl_wb=gl_wb):
                        P.w(v, "gw%d" % ws, gl_wb)
                        if wbn_hist[ws]:
                            P.w(v, *wbn_hist[ws][-1])
                        ins = v.tensor_copy(wbn[ws][:], wbr[ws][:])
                        P.inc(ins, "dv")
                    wbr_hist[ws].append(P.v("dv"))
                    dv_cast = P.v("dv")

                    tsem = "sf%d%d" % (p % 2, ot2 % 2)
                    P.need(tsem)

                    @block.sync
                    def _(s, sgi=sgi, p=p, ot=ot, half=half, ws=ws, dv_cast=dv_cast, tsem=tsem):
                        P.w(s, "dv", dv_cast)
                        if ot == 0 and half == 0:
                            prevpe = base_mm_pe.get((sgi, p - 2))
                            if prevpe:
                                P.w(s, "pe", prevpe)
                        srcw = wbn[ws][:].rearrange("p (c f) -> p c f", c=16)
                        dstw = WbTb[p % 2][:, ot * D + half * 2048: ot * D + (half + 1) * 2048].rearrange(
                            "p (c f) -> p c f", c=16)
                        P.inc(s.dma_start_transpose(dstw, srcw), tsem, 16)
                    wbt_tr[(sgi, p, ot2)] = (tsem, P.v(tsem))
                    wbn_hist[ws].append((tsem, P.v(tsem)))

            prefetch_pass(0)
            prefetch_pass(1)

            for p in range(8):
                halves = [(0, min(4, sgn)), (4, max(0, sgn - 4))]
                pe_halves = []
                for hb, (j0, jn) in enumerate(halves):
                    if jn == 0:
                        pe_halves.append(P.v("pe"))
                        continue

                    @block.tensor
                    def _(t, sgi=sgi, p=p, hb=hb, j0=j0, jn=jn, sc_prep=sc_prep, dv_prep=dv_prep):
                        if hb == 0:
                            for ot2 in range(8):
                                P.w(t, *wbt_tr[(sgi, p, ot2)])
                        if p == 0 and hb == 0:
                            P.w(t, "sc", sc_prep)
                            P.w(t, "dv", dv_prep)
                        prev = evacA.get((sgi, p - 1)) if hb == 0 else evacB.get((sgi, p - 1))
                        if prev is None and sgi >= 1:
                            prev = evacA.get((sgi - 1, 7)) if hb == 0 else evacB.get((sgi - 1, 7))
                        if prev:
                            P.w(t, "sc", prev)
                        wtv = WbTb[p % 2][:].rearrange("p (o c j) -> p o c j", o=4, j=128)
                        for ch in range(NCH):
                            for j in range(j0, j0 + jn):
                                t.matmul(psF[:, j * 512:(j + 1) * 512],
                                         y2T[:, ch * 1024 + j * 128: ch * 1024 + (j + 1) * 128],
                                         wtv[:, :, ch, :],
                                         start=(ch == 0), stop=False)
                        for j in range(j0, j0 + jn):
                            t.matmul(psF[:, j * 512:(j + 1) * 512],
                                     hT_sb[:, j * 128:(j + 1) * 128],
                                     Bfl[:, p * 512:(p + 1) * 512],
                                     start=False, stop=False)
                        for j in range(j0, j0 + jn):
                            ins = t.matmul(psF[:, j * 512:(j + 1) * 512],
                                           ones_r[:], bbrow[0:1, p * 512:(p + 1) * 512],
                                           start=False, stop=True)
                        P.inc(ins, "pe")
                    pe_halves.append(P.v("pe"))
                base_mm_pe[(sgi, p)] = P.v("pe")

                @block.scalar
                def _(s, sgi=sgi, p=p, sgn=sgn, pe_halves=pe_halves):
                    prev = fspill.get((sgi, p - 1), fspill.get((sgi - 1, 7)))
                    if prev:
                        P.w(s, *prev)
                    for hb, (j0, jn) in enumerate([(0, min(4, sgn)), (4, max(0, sgn - 4))]):
                        if jn == 0:
                            continue
                        P.w(s, "pe", pe_halves[hb])
                        for j in range(j0, j0 + jn):
                            ins = s.activation(fstage[p % 2][:, j * 512:(j + 1) * 512],
                                               psF[:, j * 512:(j + 1) * 512], AF.Identity,
                                               accum_out=fsums[:, p * 8 + j: p * 8 + j + 1])
                            P.inc(ins, "sc")
                nA = min(4, sgn)
                evacA[(sgi, p)] = P.v("sc") - max(0, sgn - 4)
                evacB[(sgi, p)] = P.v("sc")
                sc_evac5 = P.v("sc")

                @block.vector
                def _(v, sgi=sgi, p=p, sgn=sgn, sc_evac5=sc_evac5):
                    P.w(v, "sc", sc_evac5)
                    for j in range(sgn):
                        v.drain()
                        ins = v.scalar_tensor_tensor(fjunk[:], fstage[p % 2][:, j * 512:(j + 1) * 512],
                                                     1.0, fstage[p % 2][:, j * 512:(j + 1) * 512],
                                                     op0=ALU.mult, op1=ALU.mult,
                                                     accum_out=fssq[:, p * 8 + j: p * 8 + j + 1])
                    P.inc(ins, "dv")
                dv_ssq5 = P.v("dv")

                @block.gpsimd
                def _(g, sgi=sgi, sg0=sg0, p=p, sgn=sgn, sc_evac5=sc_evac5, dv_ssq5=dv_ssq5):
                    P.w(g, "sc", sc_evac5)
                    P.w(g, "dv", dv_ssq5)
                    for j in range(sgn):
                        P.dma(g, f_sp[sg0 + j][:, p * 512:(p + 1) * 512],
                              fstage[p % 2][:, j * 512:(j + 1) * 512], "gf%d" % (p % 2))
                fspill[(sgi, p)] = ("gf%d" % (p % 2), P.v("gf%d" % (p % 2)))

                if p + 2 < 8:
                    prefetch_pass(p + 2)
            base_done_sc[("pe", sgi)] = P.v("pe")
            sg_base_end = P.snap()

            sgctx.close()
            for engsec in (block.tensor, block.vector, block.scalar, block.sync, block.gpsimd):
                @engsec
                def _(e, snap=sg_base_end):
                    for nm, val in snap.items():
                        P.w(e, nm, val)

            apctx = ExitStack()
            fread = [apctx.enter_context(nc.sbuf_tensor(f"fread{sgi}_{i}", [128, D], BF16, side="right")) for i in range(2)]
            xb2 = [apctx.enter_context(nc.sbuf_tensor(f"xb2{sgi}_{i}", [128, D], F32, side="right")) for i in range(2)]
            outsb = [apctx.enter_context(nc.sbuf_tensor(f"outsb{sgi}_{i}", [128, D], F32, side="right")) for i in range(2)]

            ap_dv = {}
            out_st = {}
            for j in range(sgn):
                tt = sg0 + j
                js = j % 2
                ab = 40 + js * 12

                P.need("sap%d" % js)

                @block.sync
                def _(s, tt=tt, j=j, js=js, sgi=sgi):
                    prev = ap_dv.get(j - 2)
                    if prev:
                        P.w(s, "dv", prev)
                    P.w(s, *fspill[(sgi, 7)])
                    P.w(s, *x2sp_gs[tt])
                    P.dma(s, fread[js][:], f_sp[tt], "sap%d" % js)
                    P.dma(s, xb2[js][:], x2_sp[tt], "sap%d" % js)
                sl_ap = P.v("sap%d" % js)

                @block.vector
                def _(v, j=j, js=js, ab=ab, sl_ap=sl_ap):
                    v.tensor_reduce(stat[:, ab: ab + 1], fsums[:, j: 57 + j: 8][:, 0:8], axis=AXX, op=ALU.add)
                    v.drain()
                    v.tensor_reduce(stat[:, ab + 1: ab + 2], fssq[:, j: 57 + j: 8][:, 0:8], axis=AXX, op=ALU.add)
                    v.drain()
                    v.tensor_scalar(stat[:, ab: ab + 1], stat[:, ab: ab + 1], -1.0 / D, None, op0=ALU.mult)
                    v.drain()
                    v.scalar_tensor_tensor(stat[:, ab + 2: ab + 3], stat[:, ab: ab + 1], -1.0,
                                           stat[:, ab: ab + 1], op0=ALU.mult, op1=ALU.mult)
                    v.drain()
                    v.scalar_tensor_tensor(stat[:, ab + 2: ab + 3], stat[:, ab + 1: ab + 2], 1.0 / D,
                                           stat[:, ab + 2: ab + 3], op0=ALU.mult, op1=ALU.add)
                    v.drain()
                    ins = v.tensor_scalar(stat[:, ab + 2: ab + 3], stat[:, ab + 2: ab + 3], EPS, None, op0=ALU.add)
                    P.inc(ins, "dv")
                dv_r4 = P.v("dv")

                @block.scalar
                def _(s, j=j, js=js, ab=ab, dv_r4=dv_r4, sl_ap=sl_ap):
                    P.w(s, "dv", dv_r4)
                    P.w(s, "sap%d" % js, sl_ap)
                    prevst = out_st.get(j - 2)
                    if prevst:
                        P.w(s, *prevst)
                    s.activation(stat[:, ab + 2: ab + 3], stat[:, ab + 2: ab + 3], AF.Ln)
                    s.drain()
                    s.activation(stat[:, ab + 2: ab + 3], stat[:, ab + 2: ab + 3], AF.Exp, scale=-0.5)
                    s.activation(outsb[js][:], fread[js][:], AF.Identity,
                                 bias=stat[:, ab: ab + 1])
                    s.drain()
                    ins = s.activation(outsb[js][:], outsb[js][:], AF.Identity,
                                       scale=stat[:, ab + 2: ab + 3])
                    P.inc(ins, "sc")
                sc_ln4 = P.v("sc")

                @block.vector
                def _(v, js=js, sc_ln4=sc_ln4):
                    P.w(v, "sc", sc_ln4)
                    v.drain()
                    ins = v.tensor_add(outsb[js][:], outsb[js][:], xb2[js][:])
                    P.inc(ins, "dv")
                ap_dv[j] = P.v("dv")
                dv_out = P.v("dv")

                P.need("so%d" % js)

                @block.sync
                def _(s, tt=tt, js=js, dv_out=dv_out):
                    P.w(s, "dv", dv_out)
                    P.dma(s, out[tt * 128:(tt + 1) * 128, :], outsb[js][:], "so%d" % js)
                out_st[j] = ("so%d" % js, P.v("so%d" % js))
                out_done.append(out_st[j])
            ap_end = P.snap()
            apctx.close()
            for engsec in (block.tensor, block.vector, block.scalar, block.sync, block.gpsimd):
                @engsec
                def _(e, snap=ap_end):
                    for nm, val in snap.items():
                        P.w(e, nm, val)

        @block.sync
        def _(s):
            for nm in ("so0", "so1"):
                if nm in P.sems:
                    s.wait_ge(P.sems[nm], P.v(nm))

    return nc


def _in_maps(inputs):
    x = np.ascontiguousarray(np.asarray(inputs["L0_attn_x"], dtype=np.float32))
    wq = np.ascontiguousarray(np.asarray(inputs["L0_attn_wq"], dtype=np.float32))
    wk = np.ascontiguousarray(np.asarray(inputs["L0_attn_wk"], dtype=np.float32))
    wv = np.ascontiguousarray(np.asarray(inputs["L0_attn_wv"], dtype=np.float32))
    wo = np.ascontiguousarray(np.asarray(inputs["L0_attn_wo"], dtype=np.float32))
    Wb = np.ascontiguousarray(np.asarray(inputs["L0_ffn_W_base"], dtype=np.float32))
    bb = np.ascontiguousarray(np.asarray(inputs["L0_ffn_b_base"], dtype=np.float32)).reshape(1, D)
    Wr = np.ascontiguousarray(np.asarray(inputs["L0_ffn_W_router"], dtype=np.float32))
    Ast = np.ascontiguousarray(np.asarray(inputs["L0_ffn_A_stack"], dtype=np.float32))
    Bst = np.ascontiguousarray(np.asarray(inputs["L0_ffn_B_stack"], dtype=np.float32))
    maps = []
    for c in range(8):
        b, half = c // 2, c % 2
        xr = np.ascontiguousarray(np.roll(x[b], -half * TOWN, axis=0))
        maps.append({"xa": xr, "wq": wq, "wk": wk, "wv": wv, "wo": wo,
                     "Wb": Wb, "bb": bb, "Wr": Wr, "Ast": Ast, "Bst": Bst})
    return maps


def kernel(**inputs):
    nc = build_nc(debug=False)
    maps = _in_maps(inputs)
    res = run_bass_kernel_spmd(nc, maps, core_ids=list(range(8)))
    outf = np.empty((4, S, D), dtype=np.float32)
    for c in range(8):
        b, half = c // 2, c % 2
        outf[b, half * TOWN:(half + 1) * TOWN] = res.results[c]["out"]
    return outf


# version 14
# speedup vs baseline: 1.6421x; 1.0584x over previous
import numpy as np
import ml_dtypes
from contextlib import ExitStack
import concourse.bass as bass
import concourse.mybir as mybir
from concourse.bass_utils import run_bass_kernel_spmd

F32 = mybir.dt.float32
BF16 = mybir.dt.bfloat16
AF = mybir.ActivationFunctionType
ALU = mybir.AluOpType
AXX = mybir.AxisListType.X

S = 4096
D = 4096
HKV = 8
DH = 128
KV = 1024
TOWN = 2048
NCH = 32
NT = 16
EPS = 1e-5
RSD = float(1.0 / np.sqrt(128.0))
P40 = float(2.0 ** 40)

SEM_NAMES = ["gl", "so", "pe", "dv", "sc", "gp",
             "sx0", "sx1", "sx2", "sw0", "sw1", "sy0", "sy1",
             "st0", "st1", "gy0", "gy1",
             "gk0", "gk1", "gq0", "gq1", "gv0", "gv1"]


class Prog:

    def __init__(self, nc, ctx):
        self.nc = nc
        self.ctx = ctx
        self.sems = {}
        self.cnt = {}
        for name in SEM_NAMES:
            self._mk(name)

    def _mk(self, name):
        self.sems[name] = self.ctx.enter_context(self.nc.semaphore(name))
        self.cnt[name] = 0

    def need(self, name):
        if name not in self.sems:
            self._mk(name)

    def inc(self, instr, name, by=1):
        self.cnt[name] += by
        instr.then_inc(self.sems[name], by)
        return self.cnt[name]

    def dma(self, eng, outap, inap, name, **kw):
        ins = eng.dma_start(out=outap, in_=inap, **kw)
        return self.inc(ins, name, 16)

    def w(self, eng, name, val):
        assert val <= self.cnt[name], f"wait {name}>={val} emitted={self.cnt[name]}"
        if val > 0:
            eng.wait_ge(self.sems[name], val)

    def wfut(self, eng, name, val):
        if val > 0:
            eng.wait_ge(self.sems[name], val)

    def v(self, name):
        return self.cnt[name]

    def snap(self):
        return dict(self.cnt)

    def wall(self, eng, snapshot):
        for name, val in snapshot.items():
            self.w(eng, name, val)


def build_nc(debug=False, phases=7, n_tiles=32, n_groups=8):
    n_groups_own = max(1, n_groups // 2)
    nc = bass.Bass("TRN2", target_bir_lowering=False)

    xa = nc.dram_tensor("xa", [S, D], F32, kind="ExternalInput").ap()
    wq = nc.dram_tensor("wq", [D, D], F32, kind="ExternalInput").ap()
    wk = nc.dram_tensor("wk", [D, KV], F32, kind="ExternalInput").ap()
    wv = nc.dram_tensor("wv", [D, KV], F32, kind="ExternalInput").ap()
    wo = nc.dram_tensor("wo", [KV, D], F32, kind="ExternalInput").ap()
    Wb = nc.dram_tensor("Wb", [D, D], F32, kind="ExternalInput").ap()
    bb = nc.dram_tensor("bb", [1, D], F32, kind="ExternalInput").ap()
    Wr = nc.dram_tensor("Wr", [HKV, D], F32, kind="ExternalInput").ap()
    Ast = nc.dram_tensor("Ast", [8, 16, D], F32, kind="ExternalInput").ap()
    Bst = nc.dram_tensor("Bst", [8, D, 16], F32, kind="ExternalInput").ap()
    out = nc.dram_tensor("out", [TOWN, D], F32, kind="ExternalOutput").ap()

    ident_bf_d = nc.inline_tensor(np.eye(128, dtype=ml_dtypes.bfloat16), name="ident_bf").ap()
    ident_f_d = nc.inline_tensor(np.eye(128, dtype=np.float32), name="ident_f").ap()

    skind = "ExternalOutput" if debug else "Internal"
    y1t_sp = nc.dram_tensor("y1t_sp", [32, 128, D], BF16, kind=skind).ap()
    q_sp = nc.dram_tensor("q_sp", [HKV, 128, TOWN], BF16, kind=skind).ap()
    k_sp = nc.dram_tensor("k_sp", [HKV, 128, S], BF16, kind=skind).ap()
    v_sp = nc.dram_tensor("v_sp", [HKV, NCH, 128, 128], BF16, kind=skind).ap()
    x2_sp = nc.dram_tensor("x2_sp", [NT, 128, D], F32, kind=skind).ap()
    y2_sp = nc.dram_tensor("y2_sp", [NT, 128, D], BF16, kind=skind).ap()
    f_sp = nc.dram_tensor("f_sp", [NT, 128, D], BF16, kind=skind).ap()
    vt_dbg = None
    if debug:
        vt_dbg = nc.dram_tensor("vt_dbg", [128, HKV * TOWN], BF16, kind="ExternalOutput").ap()

    ctx = ExitStack()
    P = Prog(nc, ctx)

    ident = ctx.enter_context(nc.sbuf_tensor("ident", [128, 128], BF16))
    identf = ctx.enter_context(nc.sbuf_tensor("identf", [128, 128], F32))
    ones_r = ctx.enter_context(nc.sbuf_tensor("ones_r", [1, 128], BF16))
    ones_rf = ctx.enter_context(nc.sbuf_tensor("ones_rf", [1, 128], F32))
    onesc = ctx.enter_context(nc.sbuf_tensor("onesc", [128, 1], BF16))
    stat = ctx.enter_context(nc.sbuf_tensor("stat", [128, 64], F32))
    st1 = ctx.enter_context(nc.sbuf_tensor("st1", [1, 16], F32))
    st1b = ctx.enter_context(nc.sbuf_tensor("st1b", [1, 16], BF16))

    block = ctx.enter_context(nc.Block())

    with ctx:
        @block.gpsimd
        def _(g):
            P.dma(g, ident[:], ident_bf_d, "gl")
            P.dma(g, identf[:], ident_f_d, "gl")

        @block.vector
        def _(v):
            v.memset(ones_r[:], 1.0)
            v.memset(ones_rf[:], 1.0)
            P.inc(v.memset(onesc[:], 1.0), "dv")

        p2rctx = ExitStack()
        wk_h = p2rctx.enter_context(nc.sbuf_tensor("wk_h", [128, NCH * 512], BF16, side="right"))
        wv_h = p2rctx.enter_context(nc.sbuf_tensor("wv_h", [128, NCH * 512], BF16, side="right"))
        y1g = [p2rctx.enter_context(nc.sbuf_tensor(f"y1g{i}", [128, NCH * 512], BF16, side="right")) for i in range(2)]
        kstg = [p2rctx.enter_context(nc.sbuf_tensor(f"kstg{i}", [128, 512], BF16, side="right")) for i in range(2)]
        vstg = [p2rctx.enter_context(nc.sbuf_tensor(f"vstg{i}", [128, 512], BF16, side="right")) for i in range(2)]
        ps2a = p2rctx.enter_context(nc.psum_tensor("ps2a", [128, 2048], F32))
        ps2b = p2rctx.enter_context(nc.psum_tensor("ps2b", [128, 1024], F32))

        p1ctx = ExitStack()
        xin = [p1ctx.enter_context(nc.sbuf_tensor(f"xin{i}", [128, D], F32, side="right")) for i in range(3)]
        yb = [p1ctx.enter_context(nc.sbuf_tensor(f"yb{i}", [128, D], BF16, side="right")) for i in range(2)]
        _ytr = p1ctx.enter_context(nc.sbuf_tensor("ytr0", [128, D], BF16, side="right"))
        ytr = [_ytr, _ytr]

        apply_sc, spill_gs, tpose_tr = [], [], []
        wkv_gl = {}

        def emit_wkv_cast(H, wait_pe=0):
            @block.gpsimd
            def _(g, H=H, wait_pe=wait_pe):
                if wait_pe:
                    P.w(g, "pe", wait_pe)
                src_k = wk[:, H * 512:(H + 1) * 512].rearrange("(c p) f -> p c f", p=128)
                P.dma(g, wk_h[:].rearrange("p (c f) -> p c f", c=NCH), src_k, "gl")
                src_v = wv[:, H * 512:(H + 1) * 512].rearrange("(c p) f -> p c f", p=128)
                P.dma(g, wv_h[:].rearrange("p (c f) -> p c f", c=NCH), src_v, "gl")
            wkv_gl[H] = P.v("gl")

        emit_wkv_cast(0)

        ld_sx = {}

        def emit_p1_load(u):
            if u >= n_tiles:
                return
            sl = u % 3

            @block.sync
            def _(s, u=u, sl=sl):
                if u >= 3:
                    P.w(s, "sc", apply_sc[u - 3])
                P.dma(s, xin[sl][:], xa[u * 128:(u + 1) * 128, :], "sx%d" % sl)
            ld_sx[u] = ("sx%d" % sl, P.v("sx%d" % sl))

        def emit_p1_tile(t):
            slot = t % 3
            yslot = t % 2
            sb = slot * 8
            sl_now = ld_sx[t][1]

            @block.vector
            def _(v, t=t, slot=slot, sb=sb, sl_now=sl_now):
                P.w(v, "sx%d" % slot, sl_now)
                v.reduce_sum(stat[:, sb: sb + 1], xin[slot][:], axis=AXX)
                v.drain()
                if t >= 2:
                    P.w(v, *tpose_tr[t - 2])
                v.scalar_tensor_tensor(yb[t % 2][:, 0:2048], xin[slot][:, 0:2048], 1.0,
                                       xin[slot][:, 0:2048],
                                       op0=ALU.mult, op1=ALU.mult,
                                       accum_out=stat[:, sb + 1: sb + 2])
                v.scalar_tensor_tensor(yb[t % 2][:, 2048:4096], xin[slot][:, 2048:4096], 1.0,
                                       xin[slot][:, 2048:4096],
                                       op0=ALU.mult, op1=ALU.mult,
                                       accum_out=stat[:, sb + 5: sb + 6])
                v.drain()
                v.tensor_scalar(stat[:, sb + 2: sb + 3], stat[:, sb: sb + 1], -1.0 / D, None, op0=ALU.mult)
                v.tensor_tensor(stat[:, sb + 1: sb + 2], stat[:, sb + 1: sb + 2],
                                stat[:, sb + 5: sb + 6], op=ALU.add)
                v.drain()
                v.scalar_tensor_tensor(stat[:, sb + 3: sb + 4], stat[:, sb + 2: sb + 3], -1.0,
                                       stat[:, sb + 2: sb + 3], op0=ALU.mult, op1=ALU.mult)
                v.drain()
                v.scalar_tensor_tensor(stat[:, sb + 3: sb + 4], stat[:, sb + 1: sb + 2], 1.0 / D,
                                       stat[:, sb + 3: sb + 4], op0=ALU.mult, op1=ALU.add)
                v.drain()
                ins = v.tensor_scalar(stat[:, sb + 3: sb + 4], stat[:, sb + 3: sb + 4], EPS, None, op0=ALU.add)
                P.inc(ins, "dv")
            dv_rec = P.v("dv")

            @block.scalar
            def _(s, t=t, slot=slot, yslot=yslot, sb=sb, dv_rec=dv_rec):
                P.w(s, "dv", dv_rec)
                if t >= 2:
                    P.w(s, *tpose_tr[t - 2])
                s.activation(stat[:, sb + 4: sb + 5], stat[:, sb + 3: sb + 4], AF.Ln)
                s.activation(yb[yslot][:], xin[slot][:], AF.Identity,
                             bias=stat[:, sb + 2: sb + 3])
                s.drain()
                s.activation(stat[:, sb + 4: sb + 5], stat[:, sb + 4: sb + 5], AF.Exp, scale=-0.5)
                s.drain()
                ins = s.activation(yb[yslot][:], yb[yslot][:], AF.Identity,
                                   scale=stat[:, sb + 4: sb + 5])
                P.inc(ins, "sc")
            apply_sc.append(P.v("sc"))
            sc_now = P.v("sc")

            @block.sync
            def _(s, t=t, yslot=yslot, sc_now=sc_now):
                P.w(s, "sc", sc_now)
                if t >= 1:
                    P.w(s, *spill_gs[t - 1])
                srcy = yb[yslot][:].rearrange("p (c f) -> p c f", c=NCH)
                dsty = ytr[yslot][:].rearrange("p (c f) -> p c f", c=NCH)
                P.inc(s.dma_start_transpose(dsty, srcy), "st%d" % yslot, 16)
            tpose_tr.append(("st%d" % yslot, P.v("st%d" % yslot)))
            tr_now = P.v("st%d" % yslot)

            @block.gpsimd
            def _(g, t=t, yslot=yslot, tr_now=tr_now):
                P.w(g, "st%d" % yslot, tr_now)
                P.dma(g, y1t_sp[t], ytr[yslot][:], "gy%d" % yslot)
            spill_gs.append(("gy%d" % yslot, P.v("gy%d" % yslot)))

        y1g_sl = {}
        y1g_done_pe = {}
        evac_hist = {"k": [], "q": [], "v": []}
        spill_hist = {"k": [], "q": [], "v": []}
        qkv_spill_snap = {}

        def emit_y1g_load(key, g, gslot, prev_pe):
            @block.sync
            def _(s, g=g, gslot=gslot, prev_pe=prev_pe):
                if prev_pe:
                    P.w(s, "pe", prev_pe)
                P.w(s, *spill_gs[4 * g + 3])
                P.w(s, *spill_gs[4 * g + 2])
                for j in range(4):
                    srcg = y1t_sp[4 * g + j].rearrange("p (c f) -> p c f", c=NCH)
                    dstg = y1g[gslot][:].rearrange("p (c f) -> p c f", c=NCH)[:, :, j * 128:(j + 1) * 128]
                    P.dma(s, dstg, srcg, "sy%d" % gslot)
            y1g_sl[key] = ("sy%d" % gslot, P.v("sy%d" % gslot))

        def emit_k_unit(H, g, ct):
            gslot = g % 2
            n = len(evac_hist["k"])
            pslot = ct % 2

            @block.tensor
            def _(t, H=H, g=g, ct=ct, pslot=pslot, gslot=gslot, n=n):
                P.w(t, *y1g_sl[("kv", H, g)])
                if g == 0 and ct == 0:
                    P.w(t, "gl", wkv_gl[H])
                if n >= 2:
                    P.w(t, "dv", evac_hist["k"][n - 2])
                for ch in range(NCH):
                    ins = t.matmul(ps2a[:, pslot * 512:(pslot + 1) * 512],
                                   wk_h[:, ch * 512 + ct * 128: ch * 512 + (ct + 1) * 128],
                                   y1g[gslot][:, ch * 512:(ch + 1) * 512],
                                   start=(ch == 0), stop=(ch == NCH - 1))
                P.inc(ins, "pe")
            pe_now = P.v("pe")

            @block.vector
            def _(v, pe_now=pe_now, pslot=pslot, n=n):
                P.w(v, "pe", pe_now)
                if n >= 2:
                    P.w(v, *spill_hist["k"][n - 2])
                ins = v.tensor_copy(kstg[n % 2][:], ps2a[:, pslot * 512:(pslot + 1) * 512])
                P.inc(ins, "dv")
            evac_hist["k"].append(P.v("dv"))
            dv_now = P.v("dv")
            gsem = "gk%d" % (n % 2)

            @block.gpsimd
            def _(gp, H=H, g=g, ct=ct, dv_now=dv_now, n=n, gsem=gsem):
                P.w(gp, "dv", dv_now)
                P.dma(gp, k_sp[H * 4 + ct][:, g * 512:(g + 1) * 512], kstg[n % 2][:], gsem)
            spill_hist["k"].append((gsem, P.v(gsem)))

        def emit_v_unit(H, g, tt):
            gslot = g % 2
            n = len(evac_hist["v"])
            pslot = tt % 2

            @block.tensor
            def _(t, H=H, g=g, tt=tt, pslot=pslot, gslot=gslot, n=n):
                if n >= 2:
                    P.w(t, "dv", evac_hist["v"][n - 2])
                for ch in range(NCH):
                    ins = t.matmul(ps2b[:, pslot * 512:(pslot + 1) * 512],
                                   y1g[gslot][:, ch * 512 + tt * 128: ch * 512 + (tt + 1) * 128],
                                   wv_h[:, ch * 512:(ch + 1) * 512],
                                   start=(ch == 0), stop=(ch == NCH - 1))
                P.inc(ins, "pe")
            pe_now = P.v("pe")
            if tt == 3:
                y1g_done_pe[("kv", H, g)] = pe_now

            @block.vector
            def _(v, pe_now=pe_now, pslot=pslot, n=n):
                P.w(v, "pe", pe_now)
                if n >= 2:
                    P.w(v, *spill_hist["v"][n - 2])
                ins = v.tensor_copy(vstg[n % 2][:], ps2b[:, pslot * 512:(pslot + 1) * 512])
                P.inc(ins, "dv")
            evac_hist["v"].append(P.v("dv"))
            dv_now = P.v("dv")
            gsem = "gv%d" % (n % 2)

            @block.gpsimd
            def _(gp, H=H, g=g, tt=tt, dv_now=dv_now, n=n, gsem=gsem):
                P.w(gp, "dv", dv_now)
                for i in range(4):
                    P.dma(gp, v_sp[H * 4 + i, 4 * g + tt], vstg[n % 2][:, i * 128:(i + 1) * 128], gsem)
            spill_hist["v"].append((gsem, P.v(gsem)))

        def emit_kv_units(H, g, units):
            for u in units:
                if u < 4:
                    emit_k_unit(H, g, u)
                else:
                    emit_v_unit(H, g, u - 4)

        emit_p1_load(0)
        emit_p1_load(1)
        for t in range(n_tiles):
            emit_p1_load(t + 2)
            emit_p1_tile(t)
            g = t // 4
            ph = t % 4
            if ph == 3:
                prev_pe = y1g_done_pe.get(("kv", 0, g - 2), 0)
                emit_y1g_load(("kv", 0, g), g, g % 2, prev_pe)
            if g >= 1:
                emit_kv_units(0, g - 1, [2 * ph, 2 * ph + 1])
        emit_kv_units(0, n_groups - 1, list(range(8)))
        last_h0_pe = P.v("pe")
        emit_wkv_cast(1, wait_pe=last_h0_pe)
        for g in range(n_groups):
            prev_pe = y1g_done_pe.get(("kv", 1, g - 2), y1g_done_pe.get(("kv", 0, g + n_groups - 2), 0))
            emit_y1g_load(("kv", 1, g), g, g % 2, prev_pe)
            emit_kv_units(1, g, list(range(8)))

        kv_end = P.snap()
        for hh in range(8):
            HH, cth = hh // 4, hh % 4
            w = {}
            nk = (HH * n_groups + (n_groups - 1)) * 4 + cth
            w.update(dict([spill_hist["k"][nk]]))
            for sl in range(2):
                lastv = None
                for i, ent in enumerate(spill_hist["v"]):
                    if i < HH * n_groups * 4 or i >= (HH + 1) * n_groups * 4:
                        continue
                    if ent[0] == "gv%d" % sl:
                        lastv = ent
                if lastv:
                    w[lastv[0]] = max(w.get(lastv[0], 0), lastv[1])
            qkv_spill_snap[hh] = w

        p1_end = P.snap()
        p1ctx.close()

        p2qctx = ExitStack()
        wqin = [p2qctx.enter_context(nc.sbuf_tensor(f"wqin{i}", [128, TOWN], F32, side="right")) for i in range(2)]
        wacc = [p2qctx.enter_context(nc.sbuf_tensor(f"wacc{i}", [128, 512], F32, side="right")) for i in range(2)]
        wq_eff = p2qctx.enter_context(nc.sbuf_tensor("wq_eff", [128, NCH * 512], BF16, side="right"))
        qstg = [p2qctx.enter_context(nc.sbuf_tensor(f"qstg{i}", [128, 512], BF16, side="right")) for i in range(2)]

        for engsec in (block.tensor, block.vector, block.scalar, block.sync, block.gpsimd):
            @engsec
            def _(e):
                for nm, val in p1_end.items():
                    P.w(e, nm, val)

        presum_list = []

        def emit_presum_chunk(H, ch, wait_pe=0):
            n = len(presum_list)
            slot = n % 2

            @block.sync
            def _(s, H=H, ch=ch, slot=slot, n=n):
                if n >= 2:
                    P.w(s, "dv", presum_list[n - 2])
                P.dma(s, wqin[slot][:], wq[ch * 128:(ch + 1) * 128, H * TOWN:(H + 1) * TOWN], "sw%d" % slot)
            sl_now = P.v("sw%d" % slot)

            @block.vector
            def _(v, H=H, ch=ch, slot=slot, sl_now=sl_now, wait_pe=wait_pe):
                P.w(v, "sw%d" % slot, sl_now)
                if wait_pe and ch == 0:
                    P.w(v, "pe", wait_pe)
                srcq = wqin[slot][:].rearrange("p (kv g d) -> p kv g d", kv=4, g=4)
                acc = wacc[slot][:].rearrange("p (kv d) -> p kv d", kv=4)
                v.tensor_add(acc, srcq[:, :, 0, :], srcq[:, :, 1, :])
                v.drain()
                v.tensor_add(acc, acc, srcq[:, :, 2, :])
                v.drain()
                v.tensor_add(acc, acc, srcq[:, :, 3, :])
                v.drain()
                ins = v.tensor_scalar(wq_eff[:, ch * 512:(ch + 1) * 512], wacc[slot][:],
                                      RSD, None, op0=ALU.mult)
                P.inc(ins, "dv")
            presum_list.append(P.v("dv"))

        for H in range(2):
            wait_pe = 0
            if H == 1:
                wait_pe = P.v("pe")
            for ch in range(NCH):
                emit_presum_chunk(H, ch, wait_pe=wait_pe)
            presum_done = P.v("dv")

            for g in range(n_groups_own):
                gslot = g % 2
                prev_pe = y1g_done_pe.get(("q", H, g - 2), y1g_done_pe.get(("q", H - 1, g + n_groups_own - 2), 0))
                if H == 0 and g < 2:
                    prev_pe = 0
                emit_y1g_load(("q", H, g), g, gslot, prev_pe or kv_end["pe"])

                for ct in range(4):
                    n = len(evac_hist["q"])
                    pslot = 2 + ct % 2

                    @block.tensor
                    def _(t, H=H, g=g, ct=ct, pslot=pslot, gslot=gslot, n=n, presum_done=presum_done):
                        P.w(t, *y1g_sl[("q", H, g)])
                        if g == 0 and ct == 0:
                            P.w(t, "dv", presum_done)
                        if n >= 2:
                            P.w(t, "dv", evac_hist["q"][n - 2])
                        for ch in range(NCH):
                            ins = t.matmul(ps2a[:, pslot * 512:(pslot + 1) * 512],
                                           wq_eff[:, ch * 512 + ct * 128: ch * 512 + (ct + 1) * 128],
                                           y1g[gslot][:, ch * 512:(ch + 1) * 512],
                                           start=(ch == 0), stop=(ch == NCH - 1))
                        P.inc(ins, "pe")
                    pe_now = P.v("pe")
                    if ct == 3:
                        y1g_done_pe[("q", H, g)] = pe_now

                    @block.vector
                    def _(v, pe_now=pe_now, pslot=pslot, n=n):
                        P.w(v, "pe", pe_now)
                        if n >= 2:
                            P.w(v, *spill_hist["q"][n - 2])
                        ins = v.tensor_copy(qstg[n % 2][:], ps2a[:, pslot * 512:(pslot + 1) * 512])
                        P.inc(ins, "dv")
                    evac_hist["q"].append(P.v("dv"))
                    dv_now = P.v("dv")
                    gsem = "gq%d" % (n % 2)

                    @block.gpsimd
                    def _(gp, H=H, g=g, ct=ct, dv_now=dv_now, n=n, gsem=gsem):
                        P.w(gp, "dv", dv_now)
                        P.dma(gp, q_sp[H * 4 + ct][:, g * 512:(g + 1) * 512], qstg[n % 2][:], gsem)
                    spill_hist["q"].append((gsem, P.v(gsem)))

        for hh in range(8):
            HH, cth = hh // 4, hh % 4
            nq = (HH * n_groups_own + (n_groups_own - 1)) * 4 + cth
            ent = spill_hist["q"][nq]
            qkv_spill_snap[hh] = dict(qkv_spill_snap[hh])
            qkv_spill_snap[hh][ent[0]] = max(qkv_spill_snap[hh].get(ent[0], 0), ent[1])
        p2qctx.close()

        p2_end = P.snap()
        p2rctx.close()

        for engsec in (block.tensor, block.vector, block.scalar, block.sync, block.gpsimd):
            @engsec
            def _(e):
                for nm, val in p2_end.items():
                    P.w(e, nm, val)

        p3vctx = ExitStack()
        valuesT = p3vctx.enter_context(nc.sbuf_tensor("valuesT", [128, HKV * TOWN], BF16, side="right"))
        p3ctx = ExitStack()
        kbuf = [p3ctx.enter_context(nc.sbuf_tensor(f"kbuf{i}", [128, S], BF16, side="right")) for i in range(2)]
        vbuf = [p3ctx.enter_context(nc.sbuf_tensor(f"vbuf{i}", [128, S], BF16, side="right")) for i in range(2)]
        qbuf = [p3ctx.enter_context(nc.sbuf_tensor(f"qbuf{i}", [128, TOWN], BF16, side="right")) for i in range(2)]
        PTb = [p3ctx.enter_context(nc.sbuf_tensor(f"PTb{i}", [128, 4 * S], BF16, side="right")) for i in range(2)]
        recd_sb = p3ctx.enter_context(nc.sbuf_tensor("recd_sb", [1, 512], F32, side="right"))
        mhjunk = p3ctx.enter_context(nc.sbuf_tensor("mhjunk", [128, S], BF16, side="right"))
        bias_h = p3ctx.enter_context(nc.sbuf_tensor("bias_h", [128, 2], F32, side="right"))
        bcast_sb = p3ctx.enter_context(nc.sbuf_tensor("bcast_sb", [128, 1024], F32, side="right"))
        psS = [p3ctx.enter_context(nc.psum_tensor(f"psS{i}", [128, 1024], F32)) for i in range(3)]
        psV3 = p3ctx.enter_context(nc.psum_tensor("psV3", [128, 512], F32))
        psX = p3ctx.enter_context(nc.psum_tensor("psX", [128, 512], F32))

        for nm in ("gld0", "gld1", "sp0", "sp1", "sp2", "sp3"):
            P.need(nm)

        @block.vector
        def _(v):
            P.inc(v.memset(valuesT[:], 0.0), "dv")

        NKEY = n_groups * 512
        NKC = NKEY // 128
        G3 = n_groups_own
        head_done_pe = {}
        pv_dv = {}
        recd_dv = {}
        bcast_dv = {}
        exp_hist_pergroup = {}
        exp_hist = {}
        n_heads = min(HKV, 8)

        for h in range(n_heads):
            hs = h % 2
            gld = "gld%d" % hs
            recd_dv = {}
            bcast_dv = dict(bcast_dv)
            exp_hist_pergroup = {}

            @block.gpsimd
            def _(g, h=h, hs=hs, gld=gld):
                if h >= 2:
                    P.w(g, "pe", head_done_pe[h - 2])
                for nm, val in qkv_spill_snap[h].items():
                    P.w(g, nm, val)
                P.dma(g, kbuf[hs][:, 0:NKEY], k_sp[h][:, 0:NKEY], gld)
                P.dma(g, vbuf[hs][:, 0:NKEY], v_sp[h][0:NKC].rearrange("c p j -> p c j"), gld)
                P.dma(g, qbuf[hs][:, 0:G3 * 512], q_sp[h][:, 0:G3 * 512], gld)
            gld_now = P.v(gld)

            @block.vector
            def _(v, h=h, hs=hs, gld_now=gld_now, gld=gld):
                P.w(v, gld, gld_now)
                if h >= 1:
                    P.w(v, "pe", P.v("pe"))
                ins = v.scalar_tensor_tensor(mhjunk[:, 0:G3 * 512], qbuf[hs][:, 0:G3 * 512], 1.0,
                                             qbuf[hs][:, 0:G3 * 512], op0=ALU.mult, op1=ALU.mult)
                P.inc(ins, "dv")
            sc_q2 = ("dv", P.v("dv"))

            for i in range(G3):
                @block.tensor
                def _(t, i=i, sc_q2=sc_q2):
                    P.w(t, *sc_q2)
                    if i > 0:
                        P.w(t, "dv", P.v("dv"))
                    ins = t.matmul(psX[0:1, :], onesc[:], mhjunk[:, i * 512:(i + 1) * 512],
                                   start=True, stop=True)
                    P.inc(ins, "pe")
                pe_now = P.v("pe")

                @block.vector
                def _(v, h=h, i=i, pe_now=pe_now):
                    P.w(v, "pe", pe_now)
                    v.reduce_max(st1[:, 1:2], psX[0:1, :], axis=AXX)
                    v.drain()
                    if i == 0:
                        ins = v.tensor_copy(st1[:, 2:3], st1[:, 1:2])
                    else:
                        ins = v.tensor_tensor(st1[:, 2:3], st1[:, 2:3], st1[:, 1:2], op=ALU.max)
                    v.drain()
                    P.inc(ins, "dv")

            pe_q2done = P.v("pe")

            @block.vector
            def _(v, h=h, hs=hs, pe_q2done=pe_q2done):
                P.w(v, "pe", pe_q2done)
                ins = v.scalar_tensor_tensor(mhjunk[:, 0:NKEY], kbuf[hs][:, 0:NKEY], 1.0,
                                             kbuf[hs][:, 0:NKEY], op0=ALU.mult, op1=ALU.mult)
                P.inc(ins, "dv")
            sc_k2 = ("dv", P.v("dv"))

            for i in range(NKEY // 512):
                @block.tensor
                def _(t, i=i, sc_k2=sc_k2):
                    P.w(t, *sc_k2)
                    P.w(t, "dv", P.v("dv"))
                    ins = t.matmul(psX[0:1, :], onesc[:], mhjunk[:, i * 512:(i + 1) * 512],
                                   start=True, stop=True)
                    P.inc(ins, "pe")
                pe_now = P.v("pe")

                @block.vector
                def _(v, h=h, i=i, pe_now=pe_now):
                    P.w(v, "pe", pe_now)
                    v.reduce_max(st1[:, 1:2], psX[0:1, :], axis=AXX)
                    v.drain()
                    if i == 0:
                        ins = v.tensor_copy(st1[:, 3:4], st1[:, 1:2])
                    else:
                        ins = v.tensor_tensor(st1[:, 3:4], st1[:, 3:4], st1[:, 1:2], op=ALU.max)
                    v.drain()
                    P.inc(ins, "dv")

            @block.vector
            def _(v):
                ins = v.tensor_mul(st1[:, 4:5], st1[:, 2:3], st1[:, 3:4])
                P.inc(ins, "dv")
            dv_m = P.v("dv")

            @block.scalar
            def _(s, dv_m=dv_m):
                P.w(s, "dv", dv_m)
                s.activation(st1[:, 4:5], st1[:, 4:5], AF.Ln)
                s.drain()
                ins = s.activation(st1[:, 4:5], st1[:, 4:5], AF.Exp, scale=0.5)
                P.inc(ins, "sc")
            sc_m = P.v("sc")

            @block.vector
            def _(v, sc_m=sc_m):
                P.w(v, "sc", sc_m)
                v.tensor_scalar(st1[:, 4:5], st1[:, 4:5], -1.02, -1.0,
                                op0=ALU.mult, op1=ALU.add)
                v.drain()
                ins = v.tensor_copy(st1b[:, 0:1], st1[:, 4:5])
                P.inc(ins, "dv")
            dv_mb = P.v("dv")

            @block.tensor
            def _(t, dv_mb=dv_mb):
                P.w(t, "dv", dv_mb)
                ins = t.matmul(psX[:, 0:1], ones_r[:], st1b[:, 0:1], start=True, stop=True)
                P.inc(ins, "pe")
            pe_b = P.v("pe")

            @block.vector
            def _(v, hs=hs, pe_b=pe_b):
                P.w(v, "pe", pe_b)
                ins = v.tensor_copy(bias_h[:, hs:hs + 1], psX[:, 0:1])
                P.inc(ins, "dv")
            sc_bias = ("dv", P.v("dv"))

            def emit_PV(g, h=h, hs=hs):
                gs3 = g % 2

                @block.tensor
                def _(t, g=g, gs3=gs3):
                    P.w(t, "sc", exp_hist_pergroup[g])
                    prevdv = pv_dv.get((h, g - 1)) if g >= 1 else (pv_dv.get((h - 1, G3 - 1)) if h >= 1 else None)
                    if prevdv is not None:
                        P.w(t, "dv", prevdv)
                    for kc in range(NKC):
                        ins = t.matmul(psV3[:], vbuf[hs][:, kc * 128:(kc + 1) * 128],
                                       PTb[gs3][:, kc * 512:(kc + 1) * 512],
                                       start=(kc == 0), stop=(kc == NKC - 1))
                    P.inc(ins, "pe")
                pe_pv = P.v("pe")

                @block.vector
                def _(v, g=g, gs3=gs3, pe_pv=pe_pv):
                    P.w(v, "pe", pe_pv)
                    P.w(v, *bcast_dv[g])
                    for qt in range(4):
                        ins = v.tensor_mul(valuesT[:, h * TOWN + g * 512 + qt * 128: h * TOWN + g * 512 + (qt + 1) * 128],
                                           psV3[:, qt * 128:(qt + 1) * 128],
                                           bcast_sb[:, gs3 * 512 + qt * 128: gs3 * 512 + (qt + 1) * 128])
                    P.inc(ins, "dv")
                pv_dv[(h, g)] = P.v("dv")

            for g in range(G3):
                gs3 = g % 2
                den_mm = {}

                sc0 = P.v("sc")
                pe_s_list = []
                for kp in range(NKC // 2):
                    sslot = kp % 3

                    @block.tensor
                    def _(t, g=g, kp=kp, sslot=sslot, sc0=sc0):
                        if (g, kp) == (0, 0):
                            P.w(t, *sc_bias)
                        if kp >= 3:
                            P.wfut(t, "sc", sc0 + kp - 2)
                        for u in range(2):
                            kc = 2 * kp + u
                            ins = t.matmul(psS[sslot][:, u * 512:(u + 1) * 512],
                                           kbuf[hs][:, kc * 128:(kc + 1) * 128],
                                           qbuf[hs][:, g * 512:(g + 1) * 512],
                                           start=True, stop=True)
                        P.inc(ins, "pe")
                    pe_s_list.append(P.v("pe"))

                if g >= 1:
                    emit_PV(g - 1)

                @block.scalar
                def _(s, g=g, pe_s_list=pe_s_list):
                    for kp in range(NKC // 2):
                        P.w(s, "pe", pe_s_list[kp])
                        if kp == 0:
                            prevpv = pv_dv.get((h, g - 2)) if g >= 2 else \
                                (pv_dv.get((h - 1, G3 - 2 + g)) if h >= 1 else None)
                            if prevpv is not None:
                                P.w(s, "dv", prevpv)
                        ins = s.activation(PTb[g % 2][:, kp * 1024:(kp + 1) * 1024],
                                           psS[kp % 3][:, 0:1024], AF.Exp,
                                           bias=bias_h[:, hs:hs + 1])
                        P.inc(ins, "sc")
                assert P.v("sc") == sc0 + NKC // 2
                for kp in range(NKC // 2):
                    exp_hist[kp % 3] = sc0 + kp + 1

                @block.tensor
                def _(t, g=g, sc0=sc0):
                    prevrd = recd_dv.get(g - 1)
                    if prevrd is not None:
                        P.w(t, "dv", prevrd)
                    prevbc = bcast_dv.get(g - 1)
                    if prevbc is not None:
                        P.w(t, *prevbc)
                    for kp in range(NKC // 2):
                        P.w(t, "sc", sc0 + kp + 1)
                        for u in range(2):
                            kc = 2 * kp + u
                            ins = t.matmul(psX[0:1, :], onesc[:],
                                           PTb[g % 2][:, kc * 512:(kc + 1) * 512],
                                           start=(kc == 0), stop=(kc == NKC - 1))
                    P.inc(ins, "pe")
                pe_den = P.v("pe")

                @block.vector
                def _(v, g=g, pe_den=pe_den):
                    P.w(v, "pe", pe_den)
                    v.tensor_scalar(recd_sb[0:1, 0:512], psX[0:1, :], P40, None, op0=ALU.mult)
                    v.drain()
                    v.reciprocal(recd_sb[0:1, 0:512], recd_sb[0:1, 0:512])
                    v.drain()
                    ins = v.tensor_scalar(recd_sb[0:1, 0:512], recd_sb[0:1, 0:512], P40, None, op0=ALU.mult)
                    P.inc(ins, "dv")
                recd_dv[g] = P.v("dv")
                dv_rd = P.v("dv")

                @block.tensor
                def _(t, g=g, dv_rd=dv_rd):
                    P.w(t, "dv", dv_rd)
                    prevbc = bcast_dv.get(g - 1)
                    if prevbc is not None:
                        P.w(t, *prevbc)
                    ins = t.matmul(psX[:, 0:512], ones_rf[:], recd_sb[0:1, 0:512],
                                   start=True, stop=True)
                    P.inc(ins, "pe")
                pe_bc = P.v("pe")

                @block.vector
                def _(v, g=g, gs3=gs3, pe_bc=pe_bc):
                    P.w(v, "pe", pe_bc)
                    prevpv = pv_dv.get((h, g - 2))
                    if prevpv is not None:
                        P.w(v, "dv", prevpv)
                    ins = v.tensor_copy(bcast_sb[:, gs3 * 512:(gs3 + 1) * 512], psX[:, 0:512])
                    P.inc(ins, "dv")
                bcast_dv[g] = ("dv", P.v("dv"))

                exp_hist_pergroup[g] = sc0 + NKC // 2
            emit_PV(G3 - 1)
            head_done_pe[h] = P.v("pe")
        p3_vt_dv = P.v("dv")

        if debug:
            @block.gpsimd
            def _(g):
                P.w(g, "dv", p3_vt_dv)
                P.dma(g, vt_dbg, valuesT[:], "gv0")

        p3_end = P.snap()
        p3ctx.close()

        for engsec in (block.tensor, block.vector, block.scalar, block.sync, block.gpsimd):
            @engsec
            def _(e):
                for nm, val in p3_end.items():
                    P.w(e, nm, val)

        NT4 = G3 * 4
        SGL = [(s, min(8, NT4 - s)) for s in range(0, NT4, 8)]

        bbrow = ctx.enter_context(nc.sbuf_tensor("bbrow", [1, D], BF16))
        AW = ctx.enter_context(nc.sbuf_tensor("AW", [128, NCH * 136], BF16))
        Bfl = ctx.enter_context(nc.sbuf_tensor("Bfl", [128, D], BF16))
        fsums = ctx.enter_context(nc.sbuf_tensor("fsums", [128, 64], F32))
        fssq = ctx.enter_context(nc.sbuf_tensor("fssq", [128, 64], F32))

        p4ctx = ExitStack()
        wo_sb = p4ctx.enter_context(nc.sbuf_tensor("wo_sb", [128, 8 * D], BF16, side="right"))
        attnsb_b = [p4ctx.enter_context(nc.sbuf_tensor(f"attnsb{i}", [128, D], F32, side="right")) for i in range(2)]
        _xo = p4ctx.enter_context(nc.sbuf_tensor("xo0", [128, D], F32, side="right"))
        _x2b = p4ctx.enter_context(nc.sbuf_tensor("x2b0", [128, D], F32, side="right"))
        _y2b = p4ctx.enter_context(nc.sbuf_tensor("y2b0", [128, D], BF16, side="right"))
        xo = [_xo, _xo]
        x2b = [_x2b, _x2b]
        y2b = [_y2b, _y2b]
        psW = p4ctx.enter_context(nc.psum_tensor("psW", [128, 3072], F32))
        psC = p4ctx.enter_context(nc.psum_tensor("psC", [128, 2048], BF16))
        p4sctx = ExitStack()
        A_nat = p4sctx.enter_context(nc.sbuf_tensor("A_nat", [128, 2048], BF16, side="right"))
        Wr_nat = p4sctx.enter_context(nc.sbuf_tensor("Wr_nat", [8, 2048], BF16, side="right"))
        Bexp = [p4sctx.enter_context(nc.sbuf_tensor(f"Bexp{i}", [128, 1024], BF16, side="right")) for i in range(2)]

        @block.gpsimd
        def _(g):
            P.dma(g, bbrow[:], bb, "gl")
            P.dma(g, A_nat[:], Ast.rearrange("e r d -> (e r) d")[:, 0:2048], "gl")
            P.dma(g, Wr_nat[:], Wr[:, 0:2048], "gl")
        aw_gl = P.v("gl")

        aw_sc = []
        wr_pe15 = [None]
        for ch in range(NCH):
            cslot = ch % 2

            if ch == 16:
                @block.gpsimd
                def _(g):
                    P.w(g, "pe", wr_pe15[0])
                    P.dma(g, Wr_nat[:], Wr[:, 2048:4096], "gl")
                    P.dma(g, A_nat[:], Ast.rearrange("e r d -> (e r) d")[:, 2048:4096], "gl")
                wr_gl2 = P.v("gl")

            @block.tensor
            def _(t, ch=ch, cslot=cslot):
                if ch == 0:
                    P.w(t, "gl", aw_gl)
                if ch == 16:
                    P.w(t, "gl", wr_gl2)
                if ch >= 2:
                    P.w(t, *aw_sc[ch - 2])
                t.transpose(psC[:, cslot * 1024: cslot * 1024 + 128],
                            A_nat[:, (ch % 16) * 128: (ch % 16 + 1) * 128], ident[:])
                ins = t.transpose(psC[:, cslot * 1024 + 128: cslot * 1024 + 136],
                                  Wr_nat[:, (ch % 16) * 128: (ch % 16 + 1) * 128], ident[0:8, 0:8])
                P.inc(ins, "pe")
            pe_now = P.v("pe")
            if ch == 15:
                wr_pe15[0] = pe_now

            @block.vector
            def _(v, ch=ch, cslot=cslot, pe_now=pe_now):
                P.w(v, "pe", pe_now)
                v.tensor_copy(AW[:, ch * 136: ch * 136 + 128], psC[:, cslot * 1024: cslot * 1024 + 128])
                v.drain()
                ins = v.tensor_copy(AW[:, ch * 136 + 128: (ch + 1) * 136],
                                    psC[:, cslot * 1024 + 128: cslot * 1024 + 136])
                P.inc(ins, "dv")
            aw_sc.append(("dv", P.v("dv")))

        bexp_gl = {}
        bfl_sc = []
        for ep in range(4):
            es = ep % 2

            @block.gpsimd
            def _(g, ep=ep, es=es):
                if ep >= 2:
                    P.w(g, *bfl_sc[(ep - 2) * NCH + NCH - 1])
                P.need("gb%d" % es)
                for u in range(2):
                    srcb = Bst[2 * ep + u].rearrange("(oc p) r -> p oc r", p=128)
                    dstb = Bexp[es][:].rearrange("p (oc u r) -> p oc u r", u=2, r=16)[:, :, u, :]
                    P.dma(g, dstb, srcb, "gb%d" % es)
            bexp_gl[ep] = ("gb%d" % es, P.v("gb%d" % es))

            for oc in range(NCH):
                bs = oc % 2

                @block.tensor
                def _(t, ep=ep, oc=oc, es=es, bs=bs):
                    if oc == 0:
                        P.w(t, *bexp_gl[ep])
                    if ep == 0 and oc == 0:
                        P.w(t, *aw_sc[-1])
                    if len(bfl_sc) >= 2:
                        P.w(t, *bfl_sc[-2])
                    ins = t.transpose(psC[0:32, bs * 1024 + 256: bs * 1024 + 384],
                                      Bexp[es][:, oc * 32:(oc + 1) * 32], ident[:])
                    P.inc(ins, "pe")
                pe_now = P.v("pe")

                @block.vector
                def _(v, ep=ep, oc=oc, bs=bs, pe_now=pe_now):
                    P.w(v, "pe", pe_now)
                    ins = v.tensor_copy(Bfl[ep * 32:(ep + 1) * 32, oc * 128:(oc + 1) * 128],
                                        psC[0:32, bs * 1024 + 256: bs * 1024 + 384])
                    P.inc(ins, "dv")
                bfl_sc.append(("dv", P.v("dv")))

        cst_end = P.snap()
        p4sctx.close()
        for engsec in (block.tensor, block.vector, block.scalar, block.sync, block.gpsimd):
            @engsec
            def _(e, snap=cst_end):
                for nm, val in snap.items():
                    P.w(e, nm, val)

        P.need("gwo")

        @block.gpsimd
        def _(g):
            srcw = wo.rearrange("(c p) f -> p c f", p=128)
            P.dma(g, wo_sb[:].rearrange("p (c f) -> p c f", c=8), srcw, "gwo")
        wo_gl = P.v("gwo")

        wo_evac = {}
        x2done_dv = {}
        y2_done_sc = {}
        x2sp_gs = {}
        y2sp_gs = {}
        P.need("gx0"); P.need("gx1")

        for tt in range(NT4):
            ts4 = tt % 2
            sb4 = 40 + ts4 * 12

            @block.sync
            def _(s, tt=tt, ts4=ts4):
                prev = x2done_dv.get(tt - 1)
                if prev:
                    P.w(s, "dv", prev)
                P.need("sxo%d" % ts4)
                P.dma(s, xo[ts4][:], xa[tt * 128:(tt + 1) * 128, :], "sxo%d" % ts4)
            sl_xo = P.v("sxo%d" % ts4)

            pe_parts = []
            for part, (obase, olen) in enumerate([(0, 3072), (3072, 1024)]):
                @block.tensor
                def _(t, tt=tt, part=part, obase=obase, olen=olen):
                    if tt == 0 and part == 0:
                        P.w(t, "gwo", wo_gl)
                        P.w(t, "dv", p3_vt_dv)
                    prev = wo_evac.get((tt, part - 1), wo_evac.get((tt - 1, 1)))
                    if prev:
                        P.wfut(t, "sc", prev)
                    for ch in range(8):
                        for oc in range(olen // 512):
                            ins = t.matmul(psW[:, oc * 512:(oc + 1) * 512],
                                           valuesT[:, ch * TOWN + tt * 128:(ch * TOWN) + (tt + 1) * 128],
                                           wo_sb[:, ch * D + obase + oc * 512: ch * D + obase + (oc + 1) * 512],
                                           start=(ch == 0), stop=(ch == 7))
                    P.inc(ins, "pe")
                pe_parts.append(P.v("pe"))
                wo_evac[(tt, part)] = P.v("sc") + part + 1

            @block.scalar
            def _(s, tt=tt, sb4=sb4, pe_parts=pe_parts):
                prevx2 = x2done_dv.get(tt - 2)
                if prevx2:
                    P.w(s, "dv", prevx2)
                for part, (obase, olen) in enumerate([(0, 3072), (3072, 1024)]):
                    P.w(s, "pe", pe_parts[part])
                    ins = s.activation(attnsb_b[tt % 2][:, obase:obase + olen], psW[:, 0:olen], AF.Identity,
                                       accum_out=stat[:, sb4 + part: sb4 + part + 1])
                    P.inc(ins, "sc")
            sc_attn = P.v("sc")
            assert wo_evac[(tt, 0)] == sc_attn - 1 and wo_evac[(tt, 1)] == sc_attn

            @block.vector
            def _(v, tt=tt, ts4=ts4, sb4=sb4, sc_attn=sc_attn):
                P.w(v, "sc", sc_attn)
                prev = y2sp_gs.get(tt - 1)
                if prev:
                    P.w(v, *prev)
                v.scalar_tensor_tensor(y2b[ts4][:, 0:2048], attnsb_b[tt % 2][:, 0:2048], 1.0,
                                       attnsb_b[tt % 2][:, 0:2048], op0=ALU.mult, op1=ALU.mult,
                                       accum_out=stat[:, sb4 + 2: sb4 + 3])
                v.drain()
                v.scalar_tensor_tensor(y2b[ts4][:, 0:2048], attnsb_b[tt % 2][:, 2048:4096], 1.0,
                                       attnsb_b[tt % 2][:, 2048:4096], op0=ALU.mult, op1=ALU.mult,
                                       accum_out=stat[:, sb4 + 3: sb4 + 4])
                v.drain()
                v.tensor_tensor(stat[:, sb4 + 4: sb4 + 5], stat[:, sb4: sb4 + 1],
                                stat[:, sb4 + 1: sb4 + 2], op=ALU.add)
                v.drain()
                v.tensor_scalar(stat[:, sb4 + 4: sb4 + 5], stat[:, sb4 + 4: sb4 + 5],
                                -1.0 / D, None, op0=ALU.mult)
                v.drain()
                v.tensor_tensor(stat[:, sb4 + 5: sb4 + 6], stat[:, sb4 + 2: sb4 + 3],
                                stat[:, sb4 + 3: sb4 + 4], op=ALU.add)
                v.drain()
                v.scalar_tensor_tensor(stat[:, sb4 + 6: sb4 + 7], stat[:, sb4 + 4: sb4 + 5],
                                       -1.0, stat[:, sb4 + 4: sb4 + 5], op0=ALU.mult, op1=ALU.mult)
                v.drain()
                v.scalar_tensor_tensor(stat[:, sb4 + 6: sb4 + 7], stat[:, sb4 + 5: sb4 + 6],
                                       1.0 / D, stat[:, sb4 + 6: sb4 + 7], op0=ALU.mult, op1=ALU.add)
                v.drain()
                ins = v.tensor_scalar(stat[:, sb4 + 6: sb4 + 7], stat[:, sb4 + 6: sb4 + 7],
                                      EPS, None, op0=ALU.add)
                P.inc(ins, "dv")
            dv_rec2 = P.v("dv")

            @block.scalar
            def _(s, sb4=sb4, dv_rec2=dv_rec2):
                P.w(s, "dv", dv_rec2)
                s.activation(stat[:, sb4 + 7: sb4 + 8], stat[:, sb4 + 6: sb4 + 7], AF.Ln)
                s.drain()
                ins = s.activation(stat[:, sb4 + 7: sb4 + 8], stat[:, sb4 + 7: sb4 + 8], AF.Exp, scale=-0.5)
                P.inc(ins, "sc")
            sc_r2 = P.v("sc")

            @block.vector
            def _(v, tt=tt, ts4=ts4, sb4=sb4, sc_r2=sc_r2, sl_xo=sl_xo):
                P.w(v, "sc", sc_r2)
                P.w(v, "sxo%d" % ts4, sl_xo)
                prevsp = x2sp_gs.get(tt - 1)
                if prevsp:
                    P.w(v, *prevsp)
                prevy2 = y2_done_sc.get(tt - 1)
                if prevy2:
                    P.w(v, "sc", prevy2)
                v.drain()
                v.tensor_scalar(attnsb_b[tt % 2][:], attnsb_b[tt % 2][:],
                                stat[:, sb4 + 4: sb4 + 5], None, op0=ALU.add)
                v.drain()
                ins = v.scalar_tensor_tensor(x2b[ts4][:], attnsb_b[tt % 2][:], stat[:, sb4 + 7: sb4 + 8],
                                             xo[ts4][:], op0=ALU.mult, op1=ALU.add)
                P.inc(ins, "dv")
            x2done_dv[tt] = P.v("dv")
            dv_x2 = P.v("dv")

            @block.gpsimd
            def _(g, tt=tt, ts4=ts4, dv_x2=dv_x2):
                P.w(g, "dv", dv_x2)
                P.dma(g, x2_sp[tt], x2b[ts4][:], "gx%d" % ts4)
            x2sp_gs[tt] = ("gx%d" % ts4, P.v("gx%d" % ts4))

            @block.vector
            def _(v, tt=tt, ts4=ts4, sb4=sb4):
                v.drain()
                v.scalar_tensor_tensor(y2b[ts4][:], x2b[ts4][:], 1.0, x2b[ts4][:],
                                       op0=ALU.mult, op1=ALU.mult,
                                       accum_out=stat[:, sb4 + 9: sb4 + 10])
                v.drain()
                v.reduce_sum(stat[:, sb4 + 10: sb4 + 11], x2b[ts4][:], axis=AXX)
                v.drain()
                v.tensor_scalar(stat[:, sb4 + 10: sb4 + 11], stat[:, sb4 + 10: sb4 + 11],
                                -1.0 / D, None, op0=ALU.mult)
                v.drain()
                v.scalar_tensor_tensor(stat[:, sb4 + 11: sb4 + 12], stat[:, sb4 + 10: sb4 + 11],
                                       -1.0, stat[:, sb4 + 10: sb4 + 11], op0=ALU.mult, op1=ALU.mult)
                v.drain()
                v.scalar_tensor_tensor(stat[:, sb4 + 11: sb4 + 12], stat[:, sb4 + 9: sb4 + 10],
                                       1.0 / D, stat[:, sb4 + 11: sb4 + 12], op0=ALU.mult, op1=ALU.add)
                v.drain()
                ins = v.tensor_scalar(stat[:, sb4 + 11: sb4 + 12], stat[:, sb4 + 11: sb4 + 12],
                                      EPS, None, op0=ALU.add)
                P.inc(ins, "dv")
            dv_rec3 = P.v("dv")

            @block.scalar
            def _(s, tt=tt, ts4=ts4, sb4=sb4, dv_rec3=dv_rec3):
                P.w(s, "dv", dv_rec3)
                s.activation(stat[:, sb4 + 11: sb4 + 12], stat[:, sb4 + 11: sb4 + 12], AF.Ln)
                s.drain()
                s.activation(stat[:, sb4 + 11: sb4 + 12], stat[:, sb4 + 11: sb4 + 12], AF.Exp, scale=-0.5)
                s.activation(y2b[ts4][:], x2b[ts4][:], AF.Identity,
                             bias=stat[:, sb4 + 10: sb4 + 11])
                s.drain()
                ins = s.activation(y2b[ts4][:], y2b[ts4][:], AF.Identity,
                                   scale=stat[:, sb4 + 11: sb4 + 12])
                P.inc(ins, "sc")
            y2_done_sc[tt] = P.v("sc")
            sc_y2 = P.v("sc")

            @block.gpsimd
            def _(g, tt=tt, ts4=ts4, sc_y2=sc_y2):
                P.w(g, "sc", sc_y2)
                P.dma(g, y2_sp[tt], y2b[ts4][:], "gy%d" % ts4)
            y2sp_gs[tt] = ("gy%d" % ts4, P.v("gy%d" % ts4))

        p4_end = P.snap()
        p4ctx.close()
        p3vctx.close()

        for engsec in (block.tensor, block.vector, block.scalar, block.sync, block.gpsimd):
            @engsec
            def _(e):
                for nm, val in p4_end.items():
                    P.w(e, nm, val)

        p5ctx = ExitStack()
        _y2rd = p5ctx.enter_context(nc.sbuf_tensor("y2rd0", [128, D], BF16, side="right"))
        y2rd = [_y2rd, _y2rd]
        y2T = p5ctx.enter_context(nc.sbuf_tensor("y2T", [128, NCH * 1024], BF16, side="right"))
        hT_sb = p5ctx.enter_context(nc.sbuf_tensor("hT_sb", [128, 8 * 128], BF16, side="right"))
        hp_sb = p5ctx.enter_context(nc.sbuf_tensor("hp_sb", [128, 128], F32, side="right"))
        cmb = p5ctx.enter_context(nc.sbuf_tensor("cmb", [128, 32], F32, side="right"))
        _fst = p5ctx.enter_context(nc.sbuf_tensor("fstage0", [128, 8 * 512], BF16, side="right"))
        fstage = [_fst, _fst]
        fjunk = p5ctx.enter_context(nc.sbuf_tensor("fjunk", [128, 512], BF16, side="right"))
        psF = p5ctx.enter_context(nc.psum_tensor("psF", [128, 4096], F32))

        P.need("gw0"); P.need("gw1"); P.need("swt0"); P.need("swt1")

        P.need("gf0"); P.need("gf1"); P.need("sf0"); P.need("sf1")
        P.need("sxx0"); P.need("sxx1")

        y2t_tr = {}
        hT_done = {}
        base_done_sc = {}
        fspill = {}
        wbt_tr = {}
        wbn_hist = {0: [], 1: []}
        wbr_hist = {0: [], 1: []}
        base_mm_pe = {}
        y2rd_hist = {0: [], 1: []}
        out_done = []

        for sgi, (sg0, sgn) in enumerate(SGL):
            sgctx = ExitStack()
            WbTb = [sgctx.enter_context(nc.sbuf_tensor(f"WbTb{sgi}_{i}", [128, NCH * 512], BF16, side="right")) for i in range(2)]
            wbn = [sgctx.enter_context(nc.sbuf_tensor(f"wbn{sgi}_{i}", [128, 2048], BF16, side="right")) for i in range(2)]
            wbr = [sgctx.enter_context(nc.sbuf_tensor(f"wbr{sgi}_{i}", [128, 2048], F32, side="right")) for i in range(2)]

            for j in range(sgn):
                tt = sg0 + j
                js = j % 2

                @block.sync
                def _(s, tt=tt, js=js, sgi=sgi, j=j):
                    P.w(s, *y2sp_gs[tt])
                    for jsx in (0, 1):
                        if y2rd_hist[jsx]:
                            P.w(s, *y2rd_hist[jsx][-1])
                    P.dma(s, y2rd[js][:], y2_sp[tt], "sxx%d" % js)
                sl_y2 = P.v("sxx%d" % js)

                @block.sync
                def _(s, tt=tt, js=js, sgi=sgi, j=j, sl_y2=sl_y2):
                    P.w(s, "sxx%d" % js, sl_y2)
                    if sgi > 0 and j == 0:
                        P.w(s, "pe", base_done_sc[("pe", sgi - 1)])
                    srcy = y2rd[js][:].rearrange("p (c f) -> p c f", c=NCH)
                    dsty = y2T[:].rearrange("p (c f) -> p c f", c=NCH)[:, :, j * 128:(j + 1) * 128]
                    P.inc(s.dma_start_transpose(dsty, srcy), "swt%d" % js, 16)
                y2t_tr[(sgi, j)] = ("swt%d" % js, P.v("swt%d" % js))
                y2rd_hist[js].append(y2t_tr[(sgi, j)])

                @block.tensor
                def _(t, tt=tt, j=j, sgi=sgi):
                    P.w(t, *y2t_tr[(sgi, j)])
                    if (sgi, j) == (0, 0):
                        P.w(t, *bfl_sc[-1])
                    prev = hT_done.get((sgi, j - 2))
                    if prev:
                        P.w(t, *prev)
                    pbase = (j % 2) * 512
                    for ch in range(NCH):
                        ins = t.matmul(psF[:, pbase: pbase + 136],
                                       y2T[:, ch * 1024 + j * 128: ch * 1024 + (j + 1) * 128],
                                       AW[:, ch * 136:(ch + 1) * 136],
                                       start=(ch == 0), stop=(ch == NCH - 1))
                    P.inc(ins, "pe")
                pe_h = P.v("pe")

                cb = (j % 2) * 16

                @block.vector
                def _(v, j=j, cb=cb, pe_h=pe_h):
                    pbase = (j % 2) * 512
                    P.w(v, "pe", pe_h)
                    lg = psF[:, pbase + 128: pbase + 136]
                    v.reduce_max(cmb[:, cb: cb + 1], lg, axis=AXX)
                    v.drain()
                    v.tensor_scalar(cmb[:, cb + 8: cb + 16], lg, cmb[:, cb: cb + 1], None, op0=ALU.is_ge)
                    v.drain()
                    v.scalar_tensor_tensor(cmb[:, cb + 8: cb + 16], cmb[:, cb + 8: cb + 16], -1e30,
                                           lg, op0=ALU.mult, op1=ALU.add)
                    v.drain()
                    v.reduce_max(cmb[:, cb + 1: cb + 2], cmb[:, cb + 8: cb + 16], axis=AXX)
                    v.drain()
                    v.tensor_scalar(cmb[:, cb + 2: cb + 3], cmb[:, cb: cb + 1], -1.0, None, op0=ALU.mult)
                    v.drain()
                    ins = v.tensor_scalar(cmb[:, cb + 8: cb + 16], lg, cmb[:, cb + 1: cb + 2], None,
                                          op0=ALU.is_ge)
                    P.inc(ins, "dv")
                dv_c1 = P.v("dv")

                @block.scalar
                def _(s, j=j, cb=cb, dv_c1=dv_c1):
                    pbase = (j % 2) * 512
                    P.w(s, "dv", dv_c1)
                    ins = s.activation(cmb[:, cb + 4: cb + 12][:, 0:8], psF[:, pbase + 128: pbase + 136],
                                       AF.Exp, bias=cmb[:, cb + 2: cb + 3], scale=1.0)
                    P.inc(ins, "sc")
                sc_c1 = P.v("sc")

                @block.vector
                def _(v, cb=cb, sc_c1=sc_c1):
                    P.w(v, "sc", sc_c1)
                    v.tensor_mul(cmb[:, cb + 4: cb + 12], cmb[:, cb + 4: cb + 12], cmb[:, cb + 8: cb + 16])
                    v.drain()
                    v.reduce_sum(cmb[:, cb + 3: cb + 4], cmb[:, cb + 4: cb + 12], axis=AXX)
                    v.drain()
                    v.reciprocal(cmb[:, cb + 3: cb + 4], cmb[:, cb + 3: cb + 4])
                    v.drain()
                    ins = v.tensor_scalar(cmb[:, cb + 4: cb + 12], cmb[:, cb + 4: cb + 12],
                                          cmb[:, cb + 3: cb + 4], 2.0, op0=ALU.mult, op1=ALU.mult)
                    P.inc(ins, "dv")
                dv_c2 = P.v("dv")

                @block.vector
                def _(v, j=j, cb=cb, dv_c2=dv_c2):
                    pbase = (j % 2) * 512
                    v.drain()
                    for e8 in range(8):
                        ins = v.tensor_scalar(hp_sb[:, e8 * 16:(e8 + 1) * 16],
                                              psF[:, pbase + e8 * 16: pbase + (e8 + 1) * 16],
                                              cmb[:, cb + 4 + e8: cb + 5 + e8], None, op0=ALU.mult)
                    P.inc(ins, "dv")
                dv_hp = P.v("dv")

                @block.tensor
                def _(t, j=j, dv_hp=dv_hp):
                    pbase = (j % 2) * 512
                    P.w(t, "dv", dv_hp)
                    ins = t.transpose(psF[:, pbase + 256: pbase + 384], hp_sb[:], identf[:])
                    P.inc(ins, "pe")
                pe_hT = P.v("pe")

                @block.vector
                def _(v, j=j, pe_hT=pe_hT):
                    pbase = (j % 2) * 512
                    P.w(v, "pe", pe_hT)
                    ins = v.tensor_copy(hT_sb[:, j * 128:(j + 1) * 128], psF[:, pbase + 256: pbase + 384])
                    P.inc(ins, "dv")
                hT_done[(sgi, j)] = ("dv", P.v("dv"))
            sc_prep = P.v("sc")
            dv_prep = P.v("dv")

            evacA = {}
            evacB = {}

            def prefetch_pass(p):
                for ot2 in range(8):
                    ot, half = ot2 // 2, ot2 % 2
                    ws = ot2 % 2

                    @block.gpsimd
                    def _(g, p=p, ot=ot, half=half, ws=ws):
                        if wbr_hist[ws]:
                            P.w(g, "dv", wbr_hist[ws][-1])
                        rows = Wb[p * 512 + ot * 128: p * 512 + (ot + 1) * 128,
                                  half * 2048:(half + 1) * 2048]
                        P.dma(g, wbr[ws][:], rows, "gw%d" % ws)
                    gl_wb = P.v("gw%d" % ws)

                    @block.vector
                    def _(v, ws=ws, gl_wb=gl_wb):
                        P.w(v, "gw%d" % ws, gl_wb)
                        if wbn_hist[ws]:
                            P.w(v, *wbn_hist[ws][-1])
                        ins = v.tensor_copy(wbn[ws][:], wbr[ws][:])
                        P.inc(ins, "dv")
                    wbr_hist[ws].append(P.v("dv"))
                    dv_cast = P.v("dv")

                    tsem = "sf%d%d" % (p % 2, ot2 % 2)
                    P.need(tsem)

                    @block.sync
                    def _(s, sgi=sgi, p=p, ot=ot, half=half, ws=ws, dv_cast=dv_cast, tsem=tsem):
                        P.w(s, "dv", dv_cast)
                        if ot == 0 and half == 0:
                            prevpe = base_mm_pe.get((sgi, p - 2))
                            if prevpe:
                                P.w(s, "pe", prevpe)
                        srcw = wbn[ws][:].rearrange("p (c f) -> p c f", c=16)
                        dstw = WbTb[p % 2][:, ot * D + half * 2048: ot * D + (half + 1) * 2048].rearrange(
                            "p (c f) -> p c f", c=16)
                        P.inc(s.dma_start_transpose(dstw, srcw), tsem, 16)
                    wbt_tr[(sgi, p, ot2)] = (tsem, P.v(tsem))
                    wbn_hist[ws].append((tsem, P.v(tsem)))

            prefetch_pass(0)
            prefetch_pass(1)

            for p in range(8):
                halves = [(0, min(4, sgn)), (4, max(0, sgn - 4))]
                pe_halves = []
                for hb, (j0, jn) in enumerate(halves):
                    if jn == 0:
                        pe_halves.append(P.v("pe"))
                        continue

                    @block.tensor
                    def _(t, sgi=sgi, p=p, hb=hb, j0=j0, jn=jn, sc_prep=sc_prep, dv_prep=dv_prep):
                        if hb == 0:
                            for ot2 in range(8):
                                P.w(t, *wbt_tr[(sgi, p, ot2)])
                        if p == 0 and hb == 0:
                            P.w(t, "sc", sc_prep)
                            P.w(t, "dv", dv_prep)
                        prev = evacA.get((sgi, p - 1)) if hb == 0 else evacB.get((sgi, p - 1))
                        if prev is None and sgi >= 1:
                            prev = evacA.get((sgi - 1, 7)) if hb == 0 else evacB.get((sgi - 1, 7))
                        if prev:
                            P.w(t, "sc", prev)
                        wtv = WbTb[p % 2][:].rearrange("p (o c j) -> p o c j", o=4, j=128)
                        for ch in range(NCH):
                            for j in range(j0, j0 + jn):
                                t.matmul(psF[:, j * 512:(j + 1) * 512],
                                         y2T[:, ch * 1024 + j * 128: ch * 1024 + (j + 1) * 128],
                                         wtv[:, :, ch, :],
                                         start=(ch == 0), stop=False)
                        for j in range(j0, j0 + jn):
                            t.matmul(psF[:, j * 512:(j + 1) * 512],
                                     hT_sb[:, j * 128:(j + 1) * 128],
                                     Bfl[:, p * 512:(p + 1) * 512],
                                     start=False, stop=False)
                        for j in range(j0, j0 + jn):
                            ins = t.matmul(psF[:, j * 512:(j + 1) * 512],
                                           ones_r[:], bbrow[0:1, p * 512:(p + 1) * 512],
                                           start=False, stop=True)
                        P.inc(ins, "pe")
                    pe_halves.append(P.v("pe"))
                base_mm_pe[(sgi, p)] = P.v("pe")

                @block.scalar
                def _(s, sgi=sgi, p=p, sgn=sgn, pe_halves=pe_halves):
                    prev = fspill.get((sgi, p - 1), fspill.get((sgi - 1, 7)))
                    if prev:
                        P.w(s, *prev)
                    for hb, (j0, jn) in enumerate([(0, min(4, sgn)), (4, max(0, sgn - 4))]):
                        if jn == 0:
                            continue
                        P.w(s, "pe", pe_halves[hb])
                        for j in range(j0, j0 + jn):
                            ins = s.activation(fstage[p % 2][:, j * 512:(j + 1) * 512],
                                               psF[:, j * 512:(j + 1) * 512], AF.Identity,
                                               accum_out=fsums[:, p * 8 + j: p * 8 + j + 1])
                            P.inc(ins, "sc")
                nA = min(4, sgn)
                evacA[(sgi, p)] = P.v("sc") - max(0, sgn - 4)
                evacB[(sgi, p)] = P.v("sc")
                sc_evac5 = P.v("sc")

                @block.vector
                def _(v, sgi=sgi, p=p, sgn=sgn, sc_evac5=sc_evac5):
                    P.w(v, "sc", sc_evac5)
                    for j in range(sgn):
                        v.drain()
                        ins = v.scalar_tensor_tensor(fjunk[:], fstage[p % 2][:, j * 512:(j + 1) * 512],
                                                     1.0, fstage[p % 2][:, j * 512:(j + 1) * 512],
                                                     op0=ALU.mult, op1=ALU.mult,
                                                     accum_out=fssq[:, p * 8 + j: p * 8 + j + 1])
                    P.inc(ins, "dv")
                dv_ssq5 = P.v("dv")

                @block.gpsimd
                def _(g, sgi=sgi, sg0=sg0, p=p, sgn=sgn, sc_evac5=sc_evac5, dv_ssq5=dv_ssq5):
                    P.w(g, "sc", sc_evac5)
                    P.w(g, "dv", dv_ssq5)
                    for j in range(sgn):
                        P.dma(g, f_sp[sg0 + j][:, p * 512:(p + 1) * 512],
                              fstage[p % 2][:, j * 512:(j + 1) * 512], "gf%d" % (p % 2))
                fspill[(sgi, p)] = ("gf%d" % (p % 2), P.v("gf%d" % (p % 2)))

                if p + 2 < 8:
                    prefetch_pass(p + 2)
            base_done_sc[("pe", sgi)] = P.v("pe")
            sg_base_end = P.snap()

            sgctx.close()
            for engsec in (block.tensor, block.vector, block.scalar, block.sync, block.gpsimd):
                @engsec
                def _(e, snap=sg_base_end):
                    for nm, val in snap.items():
                        P.w(e, nm, val)

            apctx = ExitStack()
            fread = [apctx.enter_context(nc.sbuf_tensor(f"fread{sgi}_{i}", [128, D], BF16, side="right")) for i in range(2)]
            xb2 = [apctx.enter_context(nc.sbuf_tensor(f"xb2{sgi}_{i}", [128, D], F32, side="right")) for i in range(2)]
            outsb = [apctx.enter_context(nc.sbuf_tensor(f"outsb{sgi}_{i}", [128, D], F32, side="right")) for i in range(2)]

            ap_dv = {}
            out_st = {}
            for j in range(sgn):
                tt = sg0 + j
                js = j % 2
                ab = 40 + js * 12

                P.need("sap%d" % js)

                @block.sync
                def _(s, tt=tt, j=j, js=js, sgi=sgi):
                    prev = ap_dv.get(j - 2)
                    if prev:
                        P.w(s, "dv", prev)
                    P.w(s, *fspill[(sgi, 7)])
                    P.w(s, *x2sp_gs[tt])
                    P.dma(s, fread[js][:], f_sp[tt], "sap%d" % js)
                    P.dma(s, xb2[js][:], x2_sp[tt], "sap%d" % js)
                sl_ap = P.v("sap%d" % js)

                @block.vector
                def _(v, j=j, js=js, ab=ab, sl_ap=sl_ap):
                    v.tensor_reduce(stat[:, ab: ab + 1], fsums[:, j: 57 + j: 8][:, 0:8], axis=AXX, op=ALU.add)
                    v.drain()
                    v.tensor_reduce(stat[:, ab + 1: ab + 2], fssq[:, j: 57 + j: 8][:, 0:8], axis=AXX, op=ALU.add)
                    v.drain()
                    v.tensor_scalar(stat[:, ab: ab + 1], stat[:, ab: ab + 1], -1.0 / D, None, op0=ALU.mult)
                    v.drain()
                    v.scalar_tensor_tensor(stat[:, ab + 2: ab + 3], stat[:, ab: ab + 1], -1.0,
                                           stat[:, ab: ab + 1], op0=ALU.mult, op1=ALU.mult)
                    v.drain()
                    v.scalar_tensor_tensor(stat[:, ab + 2: ab + 3], stat[:, ab + 1: ab + 2], 1.0 / D,
                                           stat[:, ab + 2: ab + 3], op0=ALU.mult, op1=ALU.add)
                    v.drain()
                    ins = v.tensor_scalar(stat[:, ab + 2: ab + 3], stat[:, ab + 2: ab + 3], EPS, None, op0=ALU.add)
                    P.inc(ins, "dv")
                dv_r4 = P.v("dv")

                @block.scalar
                def _(s, j=j, js=js, ab=ab, dv_r4=dv_r4, sl_ap=sl_ap):
                    P.w(s, "dv", dv_r4)
                    P.w(s, "sap%d" % js, sl_ap)
                    prevst = out_st.get(j - 2)
                    if prevst:
                        P.w(s, *prevst)
                    s.activation(stat[:, ab + 2: ab + 3], stat[:, ab + 2: ab + 3], AF.Ln)
                    s.drain()
                    s.activation(stat[:, ab + 2: ab + 3], stat[:, ab + 2: ab + 3], AF.Exp, scale=-0.5)
                    s.activation(outsb[js][:], fread[js][:], AF.Identity,
                                 bias=stat[:, ab: ab + 1])
                    s.drain()
                    ins = s.activation(outsb[js][:], outsb[js][:], AF.Identity,
                                       scale=stat[:, ab + 2: ab + 3])
                    P.inc(ins, "sc")
                sc_ln4 = P.v("sc")

                @block.vector
                def _(v, js=js, sc_ln4=sc_ln4):
                    P.w(v, "sc", sc_ln4)
                    v.drain()
                    ins = v.tensor_add(outsb[js][:], outsb[js][:], xb2[js][:])
                    P.inc(ins, "dv")
                ap_dv[j] = P.v("dv")
                dv_out = P.v("dv")

                P.need("so%d" % js)

                @block.sync
                def _(s, tt=tt, js=js, dv_out=dv_out):
                    P.w(s, "dv", dv_out)
                    P.dma(s, out[tt * 128:(tt + 1) * 128, :], outsb[js][:], "so%d" % js)
                out_st[j] = ("so%d" % js, P.v("so%d" % js))
                out_done.append(out_st[j])
            ap_end = P.snap()
            apctx.close()
            for engsec in (block.tensor, block.vector, block.scalar, block.sync, block.gpsimd):
                @engsec
                def _(e, snap=ap_end):
                    for nm, val in snap.items():
                        P.w(e, nm, val)

        @block.sync
        def _(s):
            for nm in ("so0", "so1"):
                if nm in P.sems:
                    s.wait_ge(P.sems[nm], P.v(nm))

    return nc


def _in_maps(inputs):
    x = np.ascontiguousarray(np.asarray(inputs["L0_attn_x"], dtype=np.float32))
    wq = np.ascontiguousarray(np.asarray(inputs["L0_attn_wq"], dtype=np.float32))
    wk = np.ascontiguousarray(np.asarray(inputs["L0_attn_wk"], dtype=np.float32))
    wv = np.ascontiguousarray(np.asarray(inputs["L0_attn_wv"], dtype=np.float32))
    wo = np.ascontiguousarray(np.asarray(inputs["L0_attn_wo"], dtype=np.float32))
    Wb = np.ascontiguousarray(np.asarray(inputs["L0_ffn_W_base"], dtype=np.float32))
    bb = np.ascontiguousarray(np.asarray(inputs["L0_ffn_b_base"], dtype=np.float32)).reshape(1, D)
    Wr = np.ascontiguousarray(np.asarray(inputs["L0_ffn_W_router"], dtype=np.float32))
    Ast = np.ascontiguousarray(np.asarray(inputs["L0_ffn_A_stack"], dtype=np.float32))
    Bst = np.ascontiguousarray(np.asarray(inputs["L0_ffn_B_stack"], dtype=np.float32))
    maps = []
    for c in range(8):
        b, half = c // 2, c % 2
        xr = np.ascontiguousarray(np.roll(x[b], -half * TOWN, axis=0))
        maps.append({"xa": xr, "wq": wq, "wk": wk, "wv": wv, "wo": wo,
                     "Wb": Wb, "bb": bb, "Wr": Wr, "Ast": Ast, "Bst": Bst})
    return maps


def kernel(**inputs):
    nc = build_nc(debug=False)
    maps = _in_maps(inputs)
    res = run_bass_kernel_spmd(nc, maps, core_ids=list(range(8)))
    outf = np.empty((4, S, D), dtype=np.float32)
    for c in range(8):
        b, half = c // 2, c % 2
        outf[b, half * TOWN:(half + 1) * TOWN] = res.results[c]["out"]
    return outf
